# version 23
# baseline (speedup 1.0000x reference)
import math
import contextlib
import numpy as np
import ml_dtypes
import concourse.bass as bass
import concourse.mybir as mybir
from concourse.bass_utils import run_bass_kernel_spmd

F32 = mybir.dt.float32
BF16 = mybir.dt.bfloat16
I32 = mybir.dt.int32
AF = mybir.ActivationFunctionType
ALU = mybir.AluOpType
BF = ml_dtypes.bfloat16

D = 1024
TOK = 2048
NCHUNK = 8
CH = 256
HID = 3072
NHC = HID // 128
C_A = 92.0
C_B = 4.0
ENGS = ("T", "V", "S", "G", "SY")
_DEBUG_TAPS = False
ARCOLS = 47000
ARB = ARCOLS * 4


def _bucket_of_n(n):
    ret = 16 if n < 0 else 0
    n = abs(n)
    if n < 8:
        return ret + n
    v = 8 + int(np.float32(np.log(np.float32(n) / np.float32(8.0)))
                / np.float32(math.log(128.0 / 8.0)) * np.float32(8))
    return ret + min(v, 15)


def _host_consts():
    oh = np.zeros((32, 1280), np.float32)
    for i in range(1280):
        d = i - 640
        if abs(d) <= 90:
            oh[_bucket_of_n(-d), i] = 1.0
    freq = np.exp(np.arange(0, D, 2, dtype=np.float32)
                  * np.float32(-np.log(10000.0) / D))
    ang = np.arange(4096, dtype=np.float32)[:, None] * freq[None, :]
    cosT = np.ascontiguousarray(np.cos(ang).T.astype(np.float32))
    sinT = np.ascontiguousarray(np.sin(ang).T.astype(np.float32))
    i64 = np.zeros((128, 64), np.float32)
    i64[:64] = np.eye(64, dtype=np.float32)
    i64[64:] = np.eye(64, dtype=np.float32)
    return oh, cosT, sinT, i64


class Prog:
    def __init__(self, nc, stack):
        self.nc = nc
        self.stack = stack
        self.ops = {e: [] for e in ENGS}
        self.bufs = {}
        self.dsems = {}

    def dma_sem(self, name):
        if name not in self.dsems:
            sem = self.stack.enter_context(self.nc.semaphore("ds_" + name))
            self.dsems[name] = {"sem": sem, "n": 0}
        return self.dsems[name]

    def _deps(self, reads, writes, eng, selfsync):
        deps = []
        for b in reads:
            st = self.bufs.get(b)
            if st and st["w"] is not None:
                deps.append(st["w"])
        for b in writes:
            st = self.bufs.get(b)
            if st:
                if st["w"] is not None:
                    deps.append(st["w"])
                deps.extend(st["r"])
        out, seen = [], set()
        for d in deps:
            if d[0] == "E" and d[1] == eng and not selfsync:
                continue
            if d[0] == "D":
                d = ("D", d[1], self.dsems[d[1]]["n"])
            if d not in seen:
                seen.add(d)
                out.append(d)
        return out

    def _register(self, tok, reads, writes):
        for b in writes:
            self.bufs[b] = {"w": tok, "r": []}
        for b in reads:
            st = self.bufs.setdefault(b, {"w": None, "r": []})
            st["r"].append(tok)

    def op(self, eng, fn, reads=(), writes=(), selfsync=True):
        idx = len(self.ops[eng])
        o = {"kind": "op", "fn": fn,
             "deps": self._deps(reads, writes, eng, selfsync),
             "need_inc": False, "inc_no": None}
        self.ops[eng].append(o)
        tok = ("E", eng, idx)
        self._register(tok, reads, writes)
        return tok

    def dma(self, eng, stream, fn, reads=(), writes=()):
        ds = self.dma_sem(stream)
        deps = self._deps(reads, writes, eng, True)
        ds["n"] += 16
        o = {"kind": "dma", "fn": fn, "deps": deps, "sem": ds["sem"],
             "need_inc": False, "inc_no": None}
        self.ops[eng].append(o)
        tok = ("D", stream, ds["n"])
        self._register(tok, reads, writes)
        return tok

    def async_op(self, eng, stream, fn, reads=(), writes=()):
        ds = self.dma_sem(stream)
        deps = self._deps(reads, writes, eng, True)
        ds["n"] += 1
        o = {"kind": "dma", "fn": fn, "deps": deps, "sem": ds["sem"],
             "inc_amt": 1, "need_inc": False, "inc_no": None}
        self.ops[eng].append(o)
        tok = ("D", stream, ds["n"])
        self._register(tok, reads, writes)
        return tok

    def barrier(self):
        lasts = []
        for e in ENGS:
            if self.ops[e]:
                lasts.append(("E", e, len(self.ops[e]) - 1))
        for name, ds in self.dsems.items():
            lasts.append(("D", name, ds["n"]))
        for e in ENGS:
            deps = [d for d in lasts if not (d[0] == "E" and d[1] == e)]
            self.ops[e].append({"kind": "op", "fn": lambda eng: eng.nop(),
                                "deps": deps, "need_inc": False, "inc_no": None})
        self.bufs = {}

    def finalize(self):
        for e in ENGS:
            for o in self.ops[e]:
                for d in o["deps"]:
                    if d[0] == "E":
                        self.ops[d[1]][d[2]]["need_inc"] = True
        for e in ENGS:
            c = 0
            for o in self.ops[e]:
                if o["need_inc"]:
                    c += 1
                    o["inc_no"] = c

    def emit(self, ename, eng, sems):
        waited = {}
        for o in self.ops[ename]:
            for d in o["deps"]:
                if d[0] == "E":
                    _, e2, idx = d
                    cnt = self.ops[e2][idx]["inc_no"]
                    key = ("E", e2)
                    if waited.get(key, 0) >= cnt:
                        continue
                    waited[key] = cnt
                    eng.wait_ge(sems[e2], cnt)
                else:
                    _, sname, cnt = d
                    key = ("D", sname)
                    if waited.get(key, 0) >= cnt:
                        continue
                    waited[key] = cnt
                    eng.wait_ge(self.dsems[sname]["sem"], cnt)
            inst = o["fn"](eng)
            if o["kind"] == "dma":
                inst.then_inc(o["sem"], o.get("inc_amt", 16))
                if o["need_inc"]:
                    eng.nop().then_inc(sems[ename], 1)
            elif o["need_inc"]:
                inst.then_inc(sems[ename], 1)


def build_program():
    nc = bass.Bass("TRN2", target_bir_lowering=False)
    dt = nc.dram_tensor
    ins = {}
    xT = dt("xT", [D, TOK], F32, kind="ExternalInput").ap()
    for nm in ["aWq", "aWk", "aWv", "aWo", "aW1", "aW2", "bWq", "bWk", "bWv", "bWo"]:
        ins[nm] = dt(nm, [D, D], F32, kind="ExternalInput").ap()
    bias_keys = ["abq", "abk", "abv", "abo", "ab1", "ab2", "bbq", "bbk", "bbv", "bbo"]
    for nm in bias_keys:
        ins[nm] = dt(nm, [128, 8], F32, kind="ExternalInput").ap()
    ins["relflat"] = dt("relflat", [1, 512], F32, kind="ExternalInput").ap()
    ins["relT"] = dt("relT", [32, 16], F32, kind="ExternalInput").ap()
    ins["onehot"] = dt("onehot", [32, 1280], F32, kind="ExternalInput").ap()
    ins["cosT"] = dt("cosT", [512, TOK], F32, kind="ExternalInput").ap()
    ins["sinT"] = dt("sinT", [512, TOK], F32, kind="ExternalInput").ap()
    ins["i64"] = dt("i64", [128, 64], F32, kind="ExternalInput").ap()
    ins["wg"] = dt("wg", [NHC, D, 128], I32, kind="ExternalInput").ap()
    ins["wu"] = dt("wu", [NHC, D, 128], I32, kind="ExternalInput").ap()
    ins["wd"] = dt("wd", [8, HID, 128], I32, kind="ExternalInput").ap()
    yT = dt("yT", [D, TOK], F32, kind="ExternalOutput").ap()
    dbg_taps = []
    if _DEBUG_TAPS:
        for i in range(3):
            dbg_taps.append(dt("dbg%d" % i, [D, TOK], F32, kind="ExternalOutput").ap())
        dbgA = dt("dbgA", [128, 2048], F32, kind="ExternalOutput").ap()
        dbgB = dt("dbgB", [128, 4352], F32, kind="ExternalOutput").ap()
        dbgC = dt("dbgC", [128, 2080], F32, kind="ExternalOutput").ap()
        dbgD = dt("dbgD", [128, 4096], F32, kind="ExternalOutput").ap()
        dbgE = dt("dbgE", [128, 1152], F32, kind="ExternalOutput").ap()
    xbuf = dt("xbuf", [D, TOK], F32, kind="Internal").ap()
    bandh_t = dt("bandh", [2, 1280], BF16, kind="Internal")
    bandh = bandh_t.ap()
    cc_in_k = dt("cc_in_k", [D, TOK], BF16, kind="Internal").ap()
    cc_out_k = dt("cc_out_k", [2 * D, TOK], BF16, kind="Internal").ap()
    cc_in_v = dt("cc_in_v", [D, TOK], BF16, kind="Internal").ap()
    cc_out_v = dt("cc_out_v", [2 * D, TOK], BF16, kind="Internal").ap()

    stack = contextlib.ExitStack()
    E = stack.enter_context
    AR = E(nc.sbuf_tensor("AR", [128, ARCOLS], F32))
    biases = E(nc.sbuf_tensor("biases", [128, 80], F32))
    consts = E(nc.sbuf_tensor("consts", [128, 8], F32))
    rel_sb = E(nc.sbuf_tensor("rel_sb", [32, 16], F32))
    relc_sb = E(nc.sbuf_tensor("relc_sb", [32, 1], BF16))
    oh_sb = E(nc.sbuf_tensor("oh_sb", [32, 1280], BF16))
    wbc = E(nc.sbuf_tensor("wbc", [64, 512], F32))
    wbcn = E(nc.sbuf_tensor("wbcn", [64, 512], F32))
    i64_sb = E(nc.sbuf_tensor("i64_sb", [128, 64], BF16))
    ones_c = E(nc.sbuf_tensor("ones_c", [128, 1], F32))
    ones_cb = E(nc.sbuf_tensor("ones_cb", [128, 1], BF16))
    ones_r = E(nc.sbuf_tensor("ones_r", [1, 128], F32))
    ones_rb = E(nc.sbuf_tensor("ones_rb", [1, 128], BF16))
    ones512 = E(nc.sbuf_tensor("ones512", [1, 512], BF16))
    band_sb = E(nc.sbuf_tensor("band_sb", [1, 1280], BF16))
    recip_sb = E(nc.sbuf_tensor("recip_sb", [1, 512], F32))
    w15row = E(nc.sbuf_tensor("w15row", [1, 1024], BF16))
    totT = E(nc.sbuf_tensor("totT", [1, 128], BF16))
    rb_sb = E(nc.sbuf_tensor("rb_sb", [64, 512], F32))
    rowA = E(nc.sbuf_tensor("rowA", [1, 512], F32))
    rowB = E(nc.sbuf_tensor("rowB", [1, 512], F32))
    rowC = E(nc.sbuf_tensor("rowC", [1, 520], F32))
    zcol64 = E(nc.sbuf_tensor("zcol64", [64, 1], F32))
    PB = [E(nc.psum_tensor(f"pb{i}", [128, 1024], F32)) for i in range(4)]
    sems = {e: E(nc.semaphore("sem_" + e)) for e in ENGS}

    p = Prog(nc, stack)
    arf = AR[:]
    arb_all = bass.AP(arf.tensor, 0, list(arf.ap)).bitcast(BF16)

    def v32(off_b, cols):
        assert off_b % 4 == 0 and off_b + cols * 4 <= ARB, (off_b, cols)
        return arf[:, off_b // 4: off_b // 4 + cols]

    def vbf(off_b, cols):
        assert off_b % 2 == 0 and off_b + cols * 2 <= ARB, (off_b, cols)
        return arb_all[:, off_b // 2: off_b // 2 + cols]

    def T(fn, reads=(), writes=()):
        return p.op("T", fn, reads, writes, selfsync=False)

    def V(fn, reads=(), writes=()):
        return p.op("V", fn, reads, writes)

    def S(fn, reads=(), writes=()):
        return p.op("S", fn, reads, writes)

    def G(fn, reads=(), writes=()):
        return p.op("G", fn, reads, writes)

    def bcol(nm, oc):
        i = bias_keys.index(nm)
        return biases[:, 8 * i + oc: 8 * i + oc + 1]

    def free_dims(ap0, dims):
        return bass.AP(ap0.tensor, ap0.offset, [list(ap0.ap[0])] + dims)

    V(lambda e: e.memset(ones_c[:], 1.0), writes=["ones_c"])
    V(lambda e: e.memset(ones_cb[:], 1.0), writes=["ones_cb"])
    V(lambda e: e.memset(ones_r[:], 1.0), writes=["ones_r"])
    V(lambda e: e.memset(ones_rb[:], 1.0), writes=["ones_rb"])
    V(lambda e: e.memset(ones512[:], 1.0), writes=["ones512"])
    V(lambda e: e.memset(zcol64[:], 0.0), writes=["zc"])
    for col, val in [(0, -C_A), (1, 1.0), (2, -C_B), (3, 0.125),
                     (4, 1e-5), (5, -0.5), (6, 0.0)]:
        V(lambda e, c=col, vv=val: e.memset(consts[:, c:c + 1], vv), writes=["consts"])
    p.dma("G", "cinit", lambda e: e.dma_start(rel_sb[:], ins["relT"]), writes=["rel_sb"])
    p.dma("G", "cinit", lambda e: e.dma_start(oh_sb[:], ins["onehot"]), writes=["oh_sb"])
    p.dma("G", "cinit", lambda e: e.dma_start(i64_sb[:], ins["i64"]), writes=["i64_sb"])
    relf = rowB[0:1, 0:512]
    p.dma("G", "cinit", lambda e: e.dma_start(relf, ins["relflat"]), writes=["relf"])
    for i, nm in enumerate(bias_keys):
        p.dma("G", "cinit",
              lambda e, i=i, nm=nm: e.dma_start(biases[:, 8 * i:8 * i + 8], ins[nm]),
              writes=["biases"])
    T(lambda e: e.matmul(PB[3][0:64, 0:512], ones_r[0:1, 0:64], relf,
                         start=True, stop=True),
      reads=["ones_r", "relf"], writes=["pb3a"])
    V(lambda e: e.tensor_copy(wbc[:], PB[3][0:64, 0:512]), reads=["pb3a"], writes=["wbc"])
    V(lambda e: e.tensor_scalar(wbcn[:], wbc[:], -1.0, None, ALU.mult),
      reads=["wbc"], writes=["wbcn"])

    def ln_stats(xg, xk, xsq, sqk, gw, eps, scale, pba, pbb, ka, kb, onesq):
        V(lambda e: e.tensor_tensor(xsq, xg, xg, ALU.mult), reads=[xk], writes=[sqk])
        for kc in range(8):
            T(lambda e, kc=kc: e.matmul(pba[0:1, 0:gw], ones_c[:],
                                        xg[:, kc * gw:(kc + 1) * gw],
                                        start=(kc == 0), stop=(kc == 7)),
              reads=["ones_c", xk], writes=[ka])
        for kc in range(8):
            T(lambda e, kc=kc: e.matmul(pbb[0:1, 0:gw], onesq[:],
                                        xsq[:, kc * gw:(kc + 1) * gw],
                                        start=(kc == 0), stop=(kc == 7)),
              reads=["ones_cb", sqk], writes=[kb])
        V(lambda e: e.tensor_scalar(rowA[0:1, 0:gw], pba[0:1, 0:gw],
                                    1.0 / D, None, ALU.mult),
          reads=[ka], writes=["rowAm"])
        V(lambda e: e.tensor_scalar(rowB[0:1, 0:gw], pbb[0:1, 0:gw],
                                    1.0 / D, None, ALU.mult),
          reads=[kb], writes=["rowBe"])
        V(lambda e: e.tensor_tensor(rowC[0:1, 0:gw], rowA[0:1, 0:gw],
                                    rowA[0:1, 0:gw], ALU.mult),
          reads=["rowAm"], writes=["rowCm2"])
        V(lambda e: e.tensor_tensor(rowB[0:1, 0:gw], rowB[0:1, 0:gw],
                                    rowC[0:1, 0:gw], ALU.subtract),
          reads=["rowBe", "rowCm2"], writes=["rowBv"])
        epsc = consts[0:1, 4:5] if eps else consts[0:1, 6:7]
        S(lambda e: e.activation(rowB[0:1, 0:gw], rowB[0:1, 0:gw], AF.Ln,
                                 bias=epsc, scale=consts[0:1, 1:2]),
          reads=["rowBv", "consts"], writes=["rowBl"])
        S(lambda e: e.activation(rowB[0:1, 0:gw], rowB[0:1, 0:gw], AF.Exp,
                                 bias=consts[0:1, 6:7], scale=consts[0:1, 5:6]),
          reads=["rowBl", "consts"], writes=["rowBr"])
        if scale != 1.0:
            V(lambda e: e.tensor_scalar(rowB[0:1, 0:gw], rowB[0:1, 0:gw],
                                        scale, None, ALU.mult),
              reads=["rowBr"], writes=["rowBr"])
        V(lambda e: e.tensor_tensor(rowA[0:1, 0:gw], rowA[0:1, 0:gw],
                                    rowB[0:1, 0:gw], ALU.mult),
          reads=["rowAm", "rowBr"], writes=["rowAb"])
        T(lambda e: e.matmul(pba[:, 0:gw], ones_r[:], rowB[0:1, 0:gw],
                             start=True, stop=True),
          reads=["ones_r", "rowBr"], writes=[ka])
        T(lambda e: e.matmul(pbb[:, 0:gw], ones_r[:], rowA[0:1, 0:gw],
                             start=True, stop=True),
          reads=["ones_r", "rowAb"], writes=[kb])

    def r3(x, gw):
        return x.rearrange("p (kc w) -> p kc w", kc=8)

    def bc3g(pb, gw):
        a = pb[:, 0:gw]
        return bass.AP(a.tensor, a.offset, [list(a.ap[0]), [0, 8], list(a.ap[1])])

    def ln_apply(xg3, xk, tmp3, tk, out3, ok, gw, pba, pbb, ka, kb):
        V(lambda e: e.tensor_tensor(tmp3, xg3, bc3g(pba, gw), ALU.mult),
          reads=[xk, ka], writes=[tk])
        V(lambda e: e.tensor_tensor(out3, tmp3, bc3g(pbb, gw), ALU.subtract),
          reads=[tk, kb], writes=[ok])

    def proj(wv, wkkey, mvfull, mvkey, ntok, evac):
        ntg = max(1, ntok // 1024)
        tg_w = min(ntok, 1024)
        nq = max(1, tg_w // 512)
        qw = min(tg_w, 512)
        for oc in range(8):
            for tg in range(ntg):
                pbi = (oc * ntg + tg) % 2
                pbks = ["pb%da" % pbi] + (["pb%db" % pbi] if nq > 1 else [])
                for kc in range(8):
                    for q in range(nq):
                        mv = mvfull[:, kc * ntok + tg * tg_w + q * qw:
                                    kc * ntok + tg * tg_w + (q + 1) * qw]
                        pbdst = PB[pbi][:, q * 512: q * 512 + qw]
                        mk = mvkey(tg, q) if callable(mvkey) else mvkey
                        T(lambda e, kc=kc, oc=oc, mv=mv, pbdst=pbdst: e.matmul(
                            pbdst, wv[:, kc * 1024 + oc * 128: kc * 1024 + (oc + 1) * 128],
                            mv, start=(kc == 0), stop=(kc == 7)),
                          reads=[wkkey, mk], writes=[pbks[q]])
                evac(oc, tg, PB[pbi][:, 0:tg_w], pbks)

    def phase_attn_a():
        oXC = 0
        oW = oXC + 16384
        oQv = oW + 65536
        oKv = oQv + 16384
        oVv = oKv + 16384
        oVT = oVv + 8512
        oBG = oVT + 8320
        oHK = oBG + 17408
        oXN = oHK + 4608
        oAS = oXN + 8192
        oSQ = oAS + 8192
        oOT = oSQ + 4096
        oPP = oOT + 4096
        oSC = oPP + 4096
        assert oSC + 4096 <= ARB, oSC

        pbPROJ = PB[3][:, 0:512]
        pbA = PB[3][:, 512:768]
        pbB = PB[3][:, 768:1024]
        pbM = PB[3][:, 512:1024]

        Vvv = vbf(oVv, 4256)
        OT = vbf(oOT, 2048)
        ff1 = vbf(oOT, 2048)

        def Qvp(c):
            return vbf(oQv + (c % 2) * 8192, 4096)

        def Kvp(c):
            return vbf(oKv + (c % 2) * 8192, 4096)

        def vtp(c):
            return vbf(oVT + (c % 2) * 4160, 2080)

        def bigp(c):
            return vbf(oBG + (c % 2) * 8704, 4352)

        def hkp(c):
            return vbf(oHK + (c % 2) * 2304, 1152)

        wslot = [vbf(oW + i * 16384, 8192) for i in range(4)]
        STREAM = [("aWq", 0), ("aWk", 0), ("aWv", 0),
                  ("aWq", 1), ("aWk", 1), ("aWv", 1)]
        for c in range(1, NCHUNK):
            STREAM += [("aWo", c - 1), ("aW1", c - 1), ("aW2", c - 1)]
            if c + 1 < NCHUNK:
                STREAM += [("aWq", c + 1), ("aWk", c + 1), ("aWv", c + 1)]
        STREAM += [("aWo", NCHUNK - 1), ("aW1", NCHUNK - 1), ("aW2", NCHUNK - 1)]
        jof = {key: j for j, key in enumerate(STREAM)}

        def wload(j):
            if j >= len(STREAM):
                return
            nm, c = STREAM[j]
            sl = j % 4
            srcw = ins[nm].rearrange("(kc p) d -> p kc d", p=128)
            p.dma("G", "wsl%d" % sl,
                  lambda e, srcw=srcw, sl=sl: e.dma_start(wslot[sl], srcw),
                  writes=["R_WSL%d" % sl])

        def projS(j, mvfull, mvkey, evac):
            sl = j % 4
            wv = wslot[sl]
            wk = "R_WSL%d" % sl
            for oc in range(8):
                for h in range(2):
                    for kc in range(4 * h, 4 * h + 4):
                        T(lambda e, kc=kc, oc=oc, wv=wv, mvfull=mvfull: e.matmul(
                            pbPROJ[:, 0:256],
                            wv[:, kc * 1024 + oc * 128: kc * 1024 + (oc + 1) * 128],
                            mvfull[:, kc * 256:(kc + 1) * 256],
                            start=(kc == 0), stop=(kc == 7)),
                          reads=[wk, mvkey], writes=["pb3p"])
                    if h == 0:
                        yield (lambda: None)
                yield (lambda oc=oc: evac(oc))
            yield (lambda j=j: wload(j + 4))

        def bcolr(nm, oc, r0):
            i = bias_keys.index(nm)
            return biases[r0:r0 + 64, 8 * i + oc: 8 * i + oc + 1]

        def mk_evq(c):
            par = c % 2
            Qv = Qvp(c)
            keys = ("a_Qv%d" % par, "a_Qv2_%d" % par)

            def evq(oc):
                for m, r0 in ((2 * oc, 0), (2 * oc + 1, 64)):
                    src = pbPROJ[r0:r0 + 64, 0:256]
                    scb = bcolr("abq", oc, r0)
                    for half in range(2):
                        dst = Qv[half * 64:half * 64 + 64, m:4096:16]
                        V(lambda e, dst=dst, src=src, scb=scb: e.tensor_scalar(
                            dst, src, scb, None, ALU.add),
                          reads=["pb3p", "biases"], writes=[keys[half]])
            return evq

        def mk_evk(c):
            par = c % 2
            Kv = Kvp(c)
            keys = ("a_Kv%d" % par, "a_Kv2_%d" % par)

            def evk(oc):
                for m, r0 in ((2 * oc, 0), (2 * oc + 1, 64)):
                    src3 = pbPROJ[r0:r0 + 64, 0:256].rearrange("p (a r) -> p a r", a=32)
                    scb = bcolr("abk", oc, r0)
                    for half in range(2):
                        base = Kv[half * 64:half * 64 + 64, (127 - m):(128 - m)]
                        dst = free_dims(base, [[128, 32], [-16, 8]])
                        V(lambda e, dst=dst, src3=src3, scb=scb: e.tensor_scalar(
                            dst, src3, scb, None, ALU.add),
                          reads=["pb3p", "biases"], writes=[keys[half]])
            return evk

        def mk_evv(c):
            def evv(oc):
                for m, r0 in ((2 * oc, 0), (2 * oc + 1, 64)):
                    src = pbPROJ[r0:r0 + 64, 0:256]
                    src3 = src.rearrange("p (a r) -> p a r", a=32)
                    scb = bcolr("abv", oc, r0)
                    dstn = Vvv[0:64, m:4096:16]
                    V(lambda e, dstn=dstn, src=src, scb=scb: e.tensor_scalar(
                        dstn, src, scb, None, ALU.add),
                      reads=["pb3p", "biases"], writes=["a_Vv"])
                    dstr = free_dims(Vvv[64:128, (127 - m):(128 - m)], [[128, 32], [-16, 8]])
                    V(lambda e, dstr=dstr, src3=src3, scb=scb: e.tensor_scalar(
                        dstr, src3, scb, None, ALU.add),
                      reads=["pb3p", "biases"], writes=["a_Vvr"])
            return evv

        def P1_thunks(c):
            ci = c % 2
            xc = v32(oXC + ci * 8192, 2048)
            xck = "a_xc%d" % ci
            xn = vbf(oXN + ci * 4096, 2048)
            xnk = "R_XN%d" % ci
            srcx = xT.rearrange("(kc p) t -> p kc t", p=128)[:, :, c * CH:(c + 1) * CH]
            yield lambda: p.dma("SY", "xin%d" % ci,
                                lambda e, srcx=srcx, xc=xc: e.dma_start(xc, srcx),
                                writes=[xck])

            def _band():
                V(lambda e, c=c: e.tensor_copy(relc_sb[:], rel_sb[0:32, c:c + 1]),
                  reads=["rel_sb"], writes=["relc"])
                for pc in range(4):
                    T(lambda e, pc=pc: e.matmul(pbM[0:1, 0:320],
                                                relc_sb[:],
                                                oh_sb[:, pc * 320:(pc + 1) * 320],
                                                start=True, stop=True),
                      reads=["relc", "oh_sb"], writes=["pb3b"])
                    V(lambda e, pc=pc: e.tensor_copy(band_sb[0:1, pc * 320:(pc + 1) * 320],
                                                     pbM[0:1, 0:320]),
                      reads=["pb3b"], writes=["band_sb"])
                p.dma("SY", "bando%d" % ci,
                      lambda e, ci=ci: e.dma_start(bandh[ci:ci + 1, :], band_sb[0:1, :1280]),
                      reads=["band_sb"], writes=["bandh%d" % ci])
            yield _band
            xsq = vbf(oSQ, 2048)

            def _ln1(xc=xc, xck=xck, xn=xn, xnk=xnk):
                ln_stats(xc, xck, xsq, "R_SQ", CH, True, 1.0,
                         pbA, pbB, "pb3b", "pb3b", ones_cb)
                ln_apply(r3(xc, CH), xck, r3(xsq, CH), "R_SQ",
                         r3(xn, CH), xnk, CH, pbA, pbB, "pb3b", "pb3b")
            yield _ln1
            yield from projS(jof[("aWq", c)], xn, xnk, mk_evq(c))
            yield from projS(jof[("aWk", c)], xn, xnk, mk_evk(c))
            yield from projS(jof[("aWv", c)], xn, xnk, mk_evv(c))

        def P2_thunks(c):
            ci = c % 2
            vt = vtp(c)
            big = bigp(c)
            hk = hkp(c)
            for tt in range(0, 32, 4):
                def _tr(tt=tt):
                    for t in range(tt, tt + 4):
                        T(lambda e, t=t: e.matmul(pbM[:, 0:64], Vvv[64:128, t * 128:(t + 1) * 128],
                                                  i64_sb[64:128, :], start=True, stop=True,
                                                  tile_position=(64, 0)),
                          reads=["a_Vvr", "i64_sb"], writes=["pb3b"])
                        V(lambda e, t=t, vt=vt: e.tensor_copy(vt[:, t * 65:t * 65 + 64], pbM[:, 0:64]),
                          reads=["pb3b"], writes=["a_vtok%d" % ci])
                yield _tr

            def _scan():
                zcb = zcol64[:].broadcast_to([64, 4255])
                V(lambda e, zcb=zcb, big=big: e.tensor_tensor_scan(big[0:64, 97:4352], zcb,
                                                                   Vvv[0:64, 0:4255], 0.0,
                                                                   ALU.add, ALU.add),
                  reads=["zc", "a_Vv", "a_Vvp"], writes=["a_ps%d" % ci])
                T(lambda e, big=big: e.matmul(pbM[0:1, 0:64], big[0:64, 4192:4193],
                                              i64_sb[0:64, :], start=True, stop=True),
                  reads=["a_ps%d" % ci, "i64_sb"], writes=["pb3b"])
                V(lambda e, ci=ci: e.tensor_copy(totT[:, ci * 64:ci * 64 + 64], pbM[0:1, 0:64]),
                  reads=["pb3b"], writes=["totT%d" % ci])
                V(lambda e, c=c, ci=ci: e.tensor_scalar(w15row[:, ci * 512:ci * 512 + 512],
                                                        ones512[:],
                                                        wbc[0:1, 240 + c:241 + c], None, ALU.mult),
                  reads=["ones512", "wbc"], writes=["w15%d" % ci])
            yield _scan

            def _hk():
                hsrc = bass.AP(bandh_t, ci * 1280 + 1, [[1, 128], [1, 1152]])
                p.dma("SY", "hank",
                      lambda e, hsrc=hsrc, hk=hk: e.dma_start(hk, hsrc),
                      reads=["bandh%d" % ci], writes=["a_hk%d" % ci])
            yield _hk

        def A_emit(c, feeder):
            par = c % 2
            Qv = Qvp(c)
            Kv = Kvp(c)
            vt = vtp(c)
            big = bigp(c)
            hk = hkp(c)
            kq, kq2 = "a_Qv%d" % par, "a_Qv2_%d" % par
            kk, kk2 = "a_Kv%d" % par, "a_Kv2_%d" % par
            vtk = "a_vtok%d" % par
            psk = "a_ps%d" % par
            outk = "a_out%d" % par
            hkk = "a_hk%d" % par
            totc = totT[:, par * 64:par * 64 + 64]
            w15c = w15row[:, par * 512:par * 512 + 512]
            pend = [None]

            def flush_combine():
                if pend[0] is not None:
                    pend[0]()
                    pend[0] = None

            for g in range(8):
                q0 = g * 512
                pvh, bdh = ("pb2a", "pb2b") if g % 2 == 0 else ("pb2b", "pb2a")
                pvo = 0 if g % 2 == 0 else 512
                bdo = 512 - pvo

                def emit_pv(pi, pvh=pvh, pvo=pvo):
                    pp = vbf(oPP + (pi % 2) * 2048, 1024)
                    ppk = "a_pp%d" % (pi % 2)
                    for j, tj in enumerate((2 * pi, 2 * pi + 1)):
                        T(lambda e, tj=tj, pp=pp, j=j, pvo=pvo: e.matmul(
                            PB[2][0:65, pvo:pvo + 512], vt[:, tj * 65:tj * 65 + 65],
                            pp[:, j * 512:(j + 1) * 512],
                            start=(tj == 0), stop=(tj == 31)),
                          reads=[vtk, ppk], writes=[pvh])
                for pi in range(16):
                    t0, t1 = 2 * pi, 2 * pi + 1
                    sp = PB[pi % 2]
                    ska, skb = "pb%da" % (pi % 2), "pb%db" % (pi % 2)
                    T(lambda e, t0=t0, sp=sp, q0=q0: e.matmul(
                        sp[:, 0:512], Kv[0:64, t0 * 128:(t0 + 1) * 128],
                        Qv[0:64, q0:q0 + 512], start=True, stop=True,
                        tile_position=(0, 0)),
                      reads=[kk, kq], writes=[ska])
                    T(lambda e, t1=t1, sp=sp, q0=q0: e.matmul(
                        sp[:, 512:1024], Kv[64:128, t1 * 128:(t1 + 1) * 128],
                        Qv[64:128, q0:q0 + 512], start=True, stop=True,
                        tile_position=(64, 0)),
                      reads=[kk2, kq2], writes=[skb])
                    pp = vbf(oPP + (pi % 2) * 2048, 1024)
                    ppk = "a_pp%d" % (pi % 2)
                    S(lambda e, pp=pp, sp=sp: e.activation(
                        pp, sp[:, :], AF.Exp, bias=consts[:, 0:1], scale=consts[:, 1:2]),
                      reads=[ska, skb, "consts"], writes=[ppk])
                    if pi > 0:
                        emit_pv(pi - 1)
                    if pi == 4:
                        flush_combine()
                    else:
                        th = next(feeder, None)
                        if th is not None:
                            th()
                        if pi % 4 == 2:
                            th = next(feeder, None)
                            if th is not None:
                                th()
                emit_pv(15)
                first_band = True
                for d0 in (128, 0, -128, -256, -384, -512):
                    k0 = q0 - d0
                    if k0 < 0 or k0 >= 4096:
                        continue
                    t = k0 // 128
                    T(lambda e, t=t, d0=d0, fb=first_band, bdo=bdo: e.matmul(
                        PB[2][0:64, bdo:bdo + 512], vt[:, t * 65:t * 65 + 64],
                        hk[:, 512 + d0:512 + d0 + 512],
                        start=fb, stop=False),
                      reads=[vtk, hkk], writes=[bdh])
                    first_band = False
                T(lambda e, bdo=bdo: e.matmul(PB[2][0:64, bdo:bdo + 512], totc, w15c,
                                              start=False, stop=True),
                  reads=["totT%d" % par, "w15%d" % par], writes=[bdh])
                o2e = v32(oSC + 2048, 512)[0:64, :]
                V(lambda e, o2e=o2e, bdo=bdo: e.tensor_copy(o2e, PB[2][0:64, bdo:bdo + 512]),
                  reads=[bdh], writes=["R_O2"])
                V(lambda e, pvo=pvo: e.reciprocal(recip_sb[:], PB[2][64:65, pvo:pvo + 512]),
                  reads=[pvh], writes=["recip"])

                def mk_combine(q0=q0, pvh=pvh, pvo=pvo, c=c, big=big, psk=psk, outk=outk):
                    def go():
                        T(lambda e: e.matmul(pbM[0:64, 0:512], ones_r[0:1, 0:64],
                                             recip_sb[:], start=True, stop=True),
                          reads=["ones_r", "recip"], writes=["pb3b"])
                        V(lambda e: e.tensor_copy(rb_sb[:], pbM[0:64, 0:512]),
                          reads=["pb3b"], writes=["rb"])
                        o1 = v32(oSC, 512)[0:64, :]
                        o2 = v32(oSC + 2048, 512)[0:64, :]
                        V(lambda e, o1=o1: e.tensor_tensor(o1, PB[2][0:64, pvo:pvo + 512],
                                                           rb_sb[:], ALU.mult),
                          reads=[pvh, "rb"], writes=["R_O1"])
                        V(lambda e, o2=o2, o1=o1: e.tensor_tensor(
                            o2, o1, o2, ALU.add),
                          reads=["R_O1", "R_O2"], writes=["R_O2"])
                        V(lambda e, o1=o1: e.scalar_tensor_tensor(
                            o1, big[0:64, q0 + 6:q0 + 518],
                            wbc[0:64, 496 + c:497 + c], o2, ALU.mult, ALU.add),
                          reads=[psk, "wbc", "R_O2"], writes=["R_O1"])
                        V(lambda e, o1=o1: e.scalar_tensor_tensor(
                            big[64:128, q0:q0 + 512], big[0:64, q0 + 187:q0 + 699],
                            wbcn[0:64, 240 + c:241 + c], o1, ALU.mult, ALU.add),
                          reads=[psk, "wbcn", "R_O1"], writes=[outk])
                    return go
                pend[0] = mk_combine()
            flush_combine()

        def T_thunks(c):
            ci = c % 2
            xc = v32(oXC + ci * 8192, 2048)
            xck = "a_xc%d" % ci
            xn = vbf(oXN + ci * 4096, 2048)
            xnk = "R_XN%d" % ci
            as2 = v32(oAS, 2048)
            xsq2 = vbf(oSQ, 2048)
            ab = vbf(oXN + ci * 4096, 2048)
            big = bigp(c)
            outk = "a_out%d" % ci
            for m in range(16):
                def _uv(m=m):
                    dsto = OT[(m % 2) * 64:(m % 2) * 64 + 64,
                              (m // 2) * 256:(m // 2) * 256 + 256]
                    V(lambda e, dsto=dsto, m=m, big=big: e.tensor_copy(
                        dsto, big[64:128, m:4096:16]),
                      reads=[outk], writes=["R_OT"])
                yield _uv

            def evo(oc):
                V(lambda e, oc=oc: e.scalar_tensor_tensor(
                    as2[:, oc * 256:(oc + 1) * 256], pbPROJ[:, 0:256], bcol("abo", oc),
                    ab[:, oc * 256:(oc + 1) * 256], ALU.add, ALU.add),
                  reads=["pb3p", "biases", xnk], writes=["R_AS"])
            yield from projS(jof[("aWo", c)], OT, "R_OT", evo)

            def _ln2(ab=ab, xnk=xnk):
                ln_stats(as2, "R_AS", xsq2, "R_SQ", CH, False, 1.0,
                         pbA, pbB, "pb3b", "pb3b", ones_cb)
                ln_apply(r3(as2, CH), "R_AS", r3(xsq2, CH), "R_SQ",
                         r3(ab, CH), xnk, CH, pbA, pbB, "pb3b", "pb3b")
            yield _ln2

            def ev1(oc):
                V(lambda e, oc=oc: e.tensor_scalar(
                    ff1[:, oc * 256:(oc + 1) * 256], pbPROJ[:, 0:256],
                    bcol("ab1", oc), None, ALU.add),
                  reads=["pb3p", "biases"], writes=["R_OT"])
            yield from projS(jof[("aW1", c)], ab, xnk, ev1)

            def ev2(oc):
                V(lambda e, oc=oc: e.scalar_tensor_tensor(
                    as2[:, oc * 256:(oc + 1) * 256], pbPROJ[:, 0:256], bcol("ab2", oc),
                    ab[:, oc * 256:(oc + 1) * 256], ALU.add, ALU.add),
                  reads=["pb3p", "biases", xnk], writes=["R_AS"])
            yield from projS(jof[("aW2", c)], ff1, "R_OT", ev2)

            def _ln3():
                ln_stats(as2, "R_AS", xsq2, "R_SQ", CH, False, 1.0,
                         pbA, pbB, "pb3b", "pb3b", ones_cb)
                as23 = r3(as2, CH)
                xc3 = r3(xc, CH)
                V(lambda e: e.tensor_tensor(as23, as23, bc3g(pbA, CH), ALU.mult),
                  reads=["R_AS", "pb3b"], writes=["R_AS"])
                V(lambda e: e.tensor_tensor(as23, as23, bc3g(pbB, CH), ALU.subtract),
                  reads=["R_AS", "pb3b"], writes=["R_AS"])
                V(lambda e: e.tensor_tensor(as23, as23, xc3, ALU.add),
                  reads=["R_AS", xck], writes=["R_AS"])
                dstx = xbuf.rearrange("(kc p) t -> p kc t", p=128)[:, :, c * CH:(c + 1) * CH]
                p.dma("SY", "xout",
                      lambda e, dstx=dstx, as2=as2: e.dma_start(dstx, as2),
                      reads=["R_AS"], writes=[])
            yield _ln3

        def drain(gen):
            for th in gen:
                th()

        import itertools
        for ci2 in range(2):
            V(lambda e, ci2=ci2: e.memset(
                free_dims(vbf(oVT + ci2 * 4160, 2080)[:, 64:65], [[65, 32]]), 1.0),
              writes=["a_vtok%d" % ci2])
            V(lambda e, ci2=ci2: e.memset(vbf(oBG + ci2 * 8704, 4352)[0:64, 0:97], 0.0),
              writes=["a_ps%d" % ci2])
        V(lambda e: e.memset(Vvv[0:64, 4096:4256], 0.0), writes=["a_Vvp"])
        for j in range(4):
            wload(j)
        drain(P1_thunks(0))
        drain(P2_thunks(0))
        carry = iter(())
        for c in range(NCHUNK):
            parts = [carry]
            if c + 1 < NCHUNK:
                parts += [P1_thunks(c + 1), P2_thunks(c + 1)]
            feeder = itertools.chain(*parts)
            A_emit(c, feeder)
            drain(feeder)
            carry = T_thunks(c)
        drain(carry)

    def phase_ffn(last):
        oXG = 0
        oSQ = oXG + 32768
        oXN = oSQ + 8192
        oH = oXN + 32768
        oWG = oH + 98304
        oGS = oWG + 8192
        assert oGS + 4096 <= ARB, oGS
        oWD = 0
        oXS = oWD + 12288
        src_in = xbuf
        dst_out = yT if last else xbuf
        xnf = vbf(oXN, 16384)
        for g in range(4):
            xg = v32(oXG + (g % 2) * 16384, 4096)
            xgk = "R_FXG%d" % (g % 2)
            srcx = src_in.rearrange("(kc p) t -> p kc t", p=128)[:, :, g * 512:(g + 1) * 512]
            p.dma("SY", "fxin%d" % (g % 2), lambda e, srcx=srcx, xg=xg: e.dma_start(xg, srcx),
                  writes=[xgk])
            xsq = vbf(oSQ, 4096)
            ln_stats(xg, xgk, xsq, "R_FSQ", 512, True, 1.0 / 127.0,
                     PB[3][:, 0:512], PB[3][:, 512:1024], "pb3a", "pb3b", ones_cb)
            xn3 = free_dims(xnf[0:128, g * 512:g * 512 + 1], [[2048, 8], [1, 512]])
            ln_apply(r3(xg, 512), xgk, r3(xsq, 512), "R_FSQ", xn3, "f_xn%d" % g, 512,
                     PB[3][:, 0:512], PB[3][:, 512:1024], "pb3a", "pb3b")

        H = vbf(oH, 49152)
        for hc in range(NHC):
            hi = hc % 2
            wgv = vbf(oWG + hi * 2048, 1024)
            wuv = vbf(oWG + 4096 + hi * 2048, 1024)
            sg = ins["wg"][hc, :, :].rearrange("(kc p) h -> p kc h", p=128)
            su = ins["wu"][hc, :, :].rearrange("(kc p) h -> p kc h", p=128)
            p.dma("G", "fwg%d" % hi, lambda e, sg=sg, wgv=wgv: e.dma_start(wgv, sg),
                  writes=["f_wg%d" % hi])
            p.dma("G", "fwu%d" % hi, lambda e, su=su, wuv=wuv: e.dma_start(wuv, su),
                  writes=["f_wu%d" % hi])
            for seg in range(4):
                sb = seg % 2
                gk = "pb0a" if sb == 0 else "pb0b"
                uk = "pb1a" if sb == 0 else "pb1b"
                gdep = "f_xn%d" % seg
                for kc in range(8):
                    mv = xnf[:, kc * 2048 + seg * 512: kc * 2048 + (seg + 1) * 512]
                    T(lambda e, mv=mv, wgv=wgv, kc=kc, sb=sb: e.matmul(
                        PB[0][:, sb * 512:(sb + 1) * 512],
                        wgv[:, kc * 128:(kc + 1) * 128], mv,
                        start=(kc == 0), stop=(kc == 7)),
                      reads=["f_wg%d" % hi, gdep], writes=[gk])
                for kc in range(8):
                    mv = xnf[:, kc * 2048 + seg * 512: kc * 2048 + (seg + 1) * 512]
                    T(lambda e, mv=mv, wuv=wuv, kc=kc, sb=sb: e.matmul(
                        PB[1][:, sb * 512:(sb + 1) * 512],
                        wuv[:, kc * 128:(kc + 1) * 128], mv,
                        start=(kc == 0), stop=(kc == 7)),
                      reads=["f_wu%d" % hi, gdep], writes=[uk])
                gs = vbf(oGS + sb * 2048, 512)
                S(lambda e, gs=gs, sb=sb: e.activation(gs, PB[0][:, sb * 512:(sb + 1) * 512],
                                                       AF.Silu),
                  reads=[gk], writes=["f_gs%d" % sb])
                hdst = H[:, hc * 2048 + seg * 512: hc * 2048 + (seg + 1) * 512]
                V(lambda e, hdst=hdst, gs=gs, sb=sb: e.tensor_tensor(
                    hdst, gs, PB[1][:, sb * 512:(sb + 1) * 512], ALU.mult),
                  reads=["f_gs%d" % sb, uk], writes=["f_H%d" % hc])

        for oc in range(8):
            oi = oc % 2
            wdv = vbf(oWD + oi * 6144, 3072)
            sd = ins["wd"][oc, :, :].rearrange("(kt p) d -> p kt d", p=128)
            p.dma("G", "fwd%d" % oi, lambda e, sd=sd, wdv=wdv: e.dma_start(wdv, sd),
                  writes=["f_wd%d" % oi, "R_FXG0"])
            for seg in range(4):
                sb = seg % 2
                dk = "pb2a" if sb == 0 else "pb2b"
                for kt in range(NHC):
                    mv = H[:, kt * 2048 + seg * 512: kt * 2048 + (seg + 1) * 512]
                    T(lambda e, kt=kt, mv=mv, wdv=wdv, sb=sb: e.matmul(
                        PB[2][:, sb * 512:(sb + 1) * 512],
                        wdv[:, kt * 128:(kt + 1) * 128], mv,
                        start=(kt == 0), stop=(kt == NHC - 1)),
                      reads=["f_wd%d" % oi, "f_H%d" % kt], writes=[dk])
                xs = v32(oXS + seg * 2048, 512)
                xsk = "R_FXG0" if seg < 2 else "R_FXG1"
                srcx = src_in[oc * 128:(oc + 1) * 128, seg * 512:(seg + 1) * 512]
                p.dma("SY", "fx2%d" % seg,
                      lambda e, srcx=srcx, xs=xs: e.dma_start(xs, srcx),
                      writes=[xsk])
                xo = v32(oXG + 28672 + (seg % 2) * 2048, 512)
                xok = "R_FXG1"
                V(lambda e, xo=xo, xs=xs, sb=sb: e.scalar_tensor_tensor(
                    xo, PB[2][:, sb * 512:(sb + 1) * 512], 1.0 / 127.0, xs,
                    ALU.mult, ALU.add),
                  reads=[dk, xsk], writes=[xok])
                dstx = dst_out[oc * 128:(oc + 1) * 128, seg * 512:(seg + 1) * 512]
                p.dma("SY", "fxo%d" % (seg % 2),
                      lambda e, dstx=dstx, xo=xo: e.dma_start(dstx, xo),
                      reads=[xok], writes=[])

    def phase_attn_b():
        oXG = 0
        oSQ = oXG + 16384
        oCS = oSQ + 16384
        oRX = oCS + 16384
        oQT = oRX + 32768
        oST = oQT + 32768
        oW = oST + 17408
        oPP = 0
        oXS = 4096
        oXO2 = oXS + 8192
        assert oW + 32768 <= ARB

        rxn = vbf(oRX, 16384)

        def lnb(g):
            xg = v32(oXG, 4096)
            xgk = "R_BXG"
            srcg = xbuf.rearrange("(kc p) t -> p kc t", p=128)[:, :, g * 512:(g + 1) * 512]
            p.dma("SY", "bxin", lambda e, srcg=srcg, xg=xg: e.dma_start(xg, srcg),
                  writes=[xgk])
            xsq = v32(oSQ, 4096)
            ln_stats(xg, xgk, xsq, "R_BSQ", 512, True, 1.0,
                     PB[3][:, 0:512], PB[3][:, 512:1024], "pb3a", "pb3b", ones_c)
            xn = v32(oSQ, 4096)
            ln_apply(r3(xg, 512), xgk, r3(xg, 512), "R_BXG", r3(xn, 512), "R_BSQ", 512,
                     PB[3][:, 0:512], PB[3][:, 512:1024], "pb3a", "pb3b")
            for cc in range(4):
                csrc = ins["cosT"][cc * 128:(cc + 1) * 128, g * 512:(g + 1) * 512]
                ssrc = ins["sinT"][cc * 128:(cc + 1) * 128, g * 512:(g + 1) * 512]
                cosb = v32(oCS, 512)
                sinb = v32(oCS + 8192, 512)
                p.dma("SY", "bcos", lambda e, csrc=csrc, cosb=cosb: e.dma_start(cosb, csrc),
                      writes=["b_cos"])
                p.dma("SY", "bsin", lambda e, ssrc=ssrc, sinb=sinb: e.dma_start(sinb, ssrc),
                      writes=["b_sin"])
                x1 = xn[:, cc * 512:(cc + 1) * 512]
                x2 = xn[:, (cc + 4) * 512:(cc + 5) * 512]
                t1 = v32(oCS + 2048, 512)
                t2 = v32(oCS + 8192 + 2048, 512)
                r1 = rxn[:, cc * 2048 + g * 512: cc * 2048 + (g + 1) * 512]
                r2 = rxn[:, (cc + 4) * 2048 + g * 512: (cc + 4) * 2048 + (g + 1) * 512]
                V(lambda e, t1=t1, x1=x1, cosb=cosb: e.tensor_tensor(t1, x1, cosb, ALU.mult),
                  reads=["R_BSQ", "b_cos"], writes=["b_t1"])
                V(lambda e, t2=t2, x2=x2, sinb=sinb: e.tensor_tensor(t2, x2, sinb, ALU.mult),
                  reads=["R_BSQ", "b_sin"], writes=["b_t2"])
                V(lambda e, r1=r1, t1=t1, t2=t2: e.tensor_tensor(r1, t1, t2, ALU.subtract),
                  reads=["b_t1", "b_t2"], writes=["R_RX", "R_RXg%d" % g])
                V(lambda e, t1=t1, x1=x1, sinb=sinb: e.tensor_tensor(t1, x1, sinb, ALU.mult),
                  reads=["R_BSQ", "b_sin"], writes=["b_t1"])
                V(lambda e, t2=t2, x2=x2, cosb=cosb: e.tensor_tensor(t2, x2, cosb, ALU.mult),
                  reads=["R_BSQ", "b_cos"], writes=["b_t2"])
                V(lambda e, r2=r2, t1=t1, t2=t2: e.tensor_tensor(r2, t1, t2, ALU.add),
                  reads=["b_t1", "b_t2"], writes=["R_RX", "R_RXg%d" % g])

        wslots = [vbf(oW, 8192), vbf(oW + 16384, 8192)]
        QTb = vbf(oQT, 16384)
        stg = [vbf(oST + i * 4352, 2048) for i in range(4)]

        def proj_b(nm, sl, tg):
            bias_nm = {"bWq": "bbq", "bWk": "bbk", "bWv": "bbv", "bWo": "bbo"}[nm]
            ccdst = {"bWk": cc_in_k, "bWv": cc_in_v}.get(nm)
            for oc in range(8):
                pbi = (oc + tg) % 2
                pbks = ["pb%da" % pbi, "pb%db" % pbi]
                for kc in range(8):
                    for q in range(2):
                        mv = rxn[:, kc * 2048 + tg * 1024 + q * 512:
                                 kc * 2048 + tg * 1024 + (q + 1) * 512]
                        T(lambda e, kc=kc, oc=oc, mv=mv, q=q, pbi=pbi, sl=sl: e.matmul(
                            PB[pbi][:, q * 512:(q + 1) * 512],
                            wslots[sl][:, kc * 1024 + oc * 128: kc * 1024 + (oc + 1) * 128],
                            mv, start=(kc == 0), stop=(kc == 7)),
                          reads=["R_BW%d" % sl, "R_RXg%d" % (tg * 2 + q)],
                          writes=[pbks[q]])
                if nm == "bWq":
                    dst = QTb[:, oc * 2048 + tg * 1024: oc * 2048 + (tg + 1) * 1024]
                    S(lambda e, dst=dst, oc=oc, pbi=pbi: e.activation(
                        dst, PB[pbi][:, 0:1024], AF.Identity,
                        bias=bcol("bbq", oc), scale=consts[:, 1:2]),
                      reads=pbks + ["biases", "consts"], writes=["b_QT"])
                else:
                    slot = (0 if nm == "bWk" else 2) + (oc + tg) % 2
                    sgb = stg[slot][:, 0:1024]
                    skey = "R_VT%d" % slot
                    S(lambda e, sgb=sgb, oc=oc, pbi=pbi, bias_nm=bias_nm: e.activation(
                        sgb, PB[pbi][:, 0:1024], AF.Identity,
                        bias=bcol(bias_nm, oc), scale=consts[:, 1:2]),
                      reads=pbks + ["biases", "consts"], writes=[skey])
                    dd = ccdst[oc * 128:(oc + 1) * 128, tg * 1024:(tg + 1) * 1024]
                    p.dma("G", "bstg%d" % slot,
                          lambda e, dd=dd, sgb=sgb: e.dma_start(dd, sgb),
                          reads=[skey], writes=["cc_in" + nm])

        srcw = ins["bWk"].rearrange("(kc p) d -> p kc d", p=128)
        p.dma("G", "bw0", lambda e, srcw=srcw: e.dma_start(wslots[0], srcw),
              writes=["R_BW0"])
        srcw = ins["bWv"].rearrange("(kc p) d -> p kc d", p=128)
        p.dma("G", "bw1", lambda e, srcw=srcw: e.dma_start(wslots[1], srcw),
              writes=["R_BW1"])
        lnb(0)
        lnb(1)
        proj_b("bWk", 0, 0)
        lnb(2)
        proj_b("bWv", 1, 0)
        lnb(3)
        proj_b("bWk", 0, 1)
        p.async_op("G", "cck", lambda e: e.collective_compute(
            "AllGather", ALU.bypass, [[0, 1], [2, 3], [4, 5], [6, 7]],
            [cc_in_k.opt()], [cc_out_k.opt()]),
            reads=["cc_inbWk", "R_VT0", "R_VT1"], writes=["cc_k"])
        proj_b("bWv", 1, 1)
        p.async_op("G", "ccv", lambda e: e.collective_compute(
            "AllGather", ALU.bypass, [[0, 1], [2, 3], [4, 5], [6, 7]],
            [cc_in_v.opt()], [cc_out_v.opt()]),
            reads=["cc_inbWv", "R_VT2", "R_VT3"], writes=["cc_v"])
        srcw = ins["bWq"].rearrange("(kc p) d -> p kc d", p=128)
        p.dma("G", "bw0b", lambda e, srcw=srcw: e.dma_start(wslots[0], srcw),
              writes=["R_BW0"])
        proj_b("bWq", 0, 0)
        proj_b("bWq", 0, 1)

        woview = wslots[0]
        srcw = ins["bWo"].rearrange("(kc p) d -> p kc d", p=128)
        p.dma("G", "bwo", lambda e, srcw=srcw: e.dma_start(woview, srcw),
              writes=["R_BW0"])

        OTb = vbf(oW + 16384, 16384)
        kp = [vbf(oRX + i * 8192, 4096) for i in range(2)]
        vp = [vbf(oRX + 16384 + i * 8192, 4096) for i in range(2)]
        vtb = [vbf(oST + i * 4352, 2080) for i in range(4)]

        def stage_thunks(gp):
            h0, h1 = 2 * gp, 2 * gp + 1
            pi = gp % 2
            kpk, vpk = "R_RXK%d" % pi, "R_RXV%d" % pi
            for (hh, rows) in ((h0, (0, 64)), (h1, (64, 128))):
                for (hs, cols) in ((0, (0, 2048)), (1, (2048, 4096))):
                    def _ld(hh=hh, rows=rows, hs=hs, cols=cols):
                        srck = cc_out_k[hs * D + hh * 64: hs * D + hh * 64 + 64, :]
                        dstk = kp[pi][rows[0]:rows[1], cols[0]:cols[1]]
                        p.dma("SY", "bkp%d" % pi,
                              lambda e, dstk=dstk, srck=srck: e.dma_start(dstk, srck),
                              reads=["cc_k"], writes=[kpk] + (["R_RX", "R_RXg0", "R_RXg1",
                                                               "R_RXg2", "R_RXg3"] if gp < 2 else []))
                        srcv = cc_out_v[hs * D + hh * 64: hs * D + hh * 64 + 64, :]
                        dstv = vp[pi][rows[0]:rows[1], cols[0]:cols[1]]
                        p.dma("SY", "bvp%d" % pi,
                              lambda e, dstv=dstv, srcv=srcv: e.dma_start(dstv, srcv),
                              reads=["cc_v"], writes=[vpk] + (["R_RX", "R_RXg0", "R_RXg1",
                                                               "R_RXg2", "R_RXg3"] if gp < 2 else []))
                    yield _ld
            s0, s1 = 2 * pi, 2 * pi + 1
            k0n, k1n = "R_VT%d" % s0, "R_VT%d" % s1
            def _ms():
                V(lambda e: e.memset(free_dims(vtb[s0][:, 64:65], [[65, 32]]), 1.0),
                  writes=[k0n])
                V(lambda e: e.memset(free_dims(vtb[s1][:, 64:65], [[65, 32]]), 1.0),
                  writes=[k1n])
            yield _ms
            for tt in range(0, 32, 4):
                def _tr(tt=tt):
                    for t in range(tt, tt + 4):
                        T(lambda e, t=t: e.matmul(
                            PB[3][:, 0:64], vp[pi][0:64, t * 128:(t + 1) * 128],
                            i64_sb[0:64, :], start=True, stop=True,
                            tile_position=(0, 0)),
                          reads=[vpk, "i64_sb"], writes=["pb3a"])
                        V(lambda e, t=t: e.tensor_copy(vtb[s0][:, t * 65:t * 65 + 64],
                                                       PB[3][:, 0:64]),
                          reads=["pb3a"], writes=[k0n])
                        T(lambda e, t=t: e.matmul(
                            PB[3][:, 512:576], vp[pi][64:128, t * 128:(t + 1) * 128],
                            i64_sb[64:128, :], start=True, stop=True,
                            tile_position=(64, 0)),
                          reads=[vpk, "i64_sb"], writes=["pb3b"])
                        V(lambda e, t=t: e.tensor_copy(vtb[s1][:, t * 65:t * 65 + 64],
                                                       PB[3][:, 512:576]),
                          reads=["pb3b"], writes=[k1n])
                yield _tr

        def A_b(gp, feeder):
            h0, h1 = 2 * gp, 2 * gp + 1
            pi = gp % 2
            kpk = "R_RXK%d" % pi
            s0, s1 = 2 * pi, 2 * pi + 1
            k0n, k1n = "R_VT%d" % s0, "R_VT%d" % s1
            for g in range(4):
                q0g = g * 512
                q_h0 = QTb[0:64, (h0 // 2) * 2048 + q0g: (h0 // 2) * 2048 + q0g + 512]
                q_h1 = QTb[64:128, (h1 // 2) * 2048 + q0g: (h1 // 2) * 2048 + q0g + 512]
                def emit_pvb(t):
                    pp = vbf(oPP + (t % 2) * 2048, 1024)
                    ppk = "b_pp%d" % (t % 2)
                    T(lambda e, t=t, pp=pp, s0=s0: e.matmul(
                        PB[2][0:65, 0:512], vtb[s0][:, t * 65:t * 65 + 65], pp[:, 0:512],
                        start=(t == 0), stop=(t == 31)),
                      reads=[k0n, ppk], writes=["pb2a"])
                    T(lambda e, t=t, pp=pp, s1=s1: e.matmul(
                        PB[2][0:65, 512:1024], vtb[s1][:, t * 65:t * 65 + 65], pp[:, 512:1024],
                        start=(t == 0), stop=(t == 31)),
                      reads=[k1n, ppk], writes=["pb2b"])
                for t in range(32):
                    sp = PB[t % 2]
                    ska, skb = "pb%da" % (t % 2), "pb%db" % (t % 2)
                    T(lambda e, t=t, sp=sp, q_h0=q_h0, pi=pi: e.matmul(
                        sp[:, 0:512], kp[pi][0:64, t * 128:(t + 1) * 128], q_h0,
                        start=True, stop=True, tile_position=(0, 0)),
                      reads=[kpk, "b_QT"], writes=[ska])
                    T(lambda e, t=t, sp=sp, q_h1=q_h1, pi=pi: e.matmul(
                        sp[:, 512:1024], kp[pi][64:128, t * 128:(t + 1) * 128], q_h1,
                        start=True, stop=True, tile_position=(64, 0)),
                      reads=[kpk, "b_QT"], writes=[skb])
                    pp = vbf(oPP + (t % 2) * 2048, 1024)
                    ppk = "b_pp%d" % (t % 2)
                    S(lambda e, pp=pp, sp=sp: e.activation(
                        pp, sp[:, :], AF.Exp, bias=consts[:, 2:3], scale=consts[:, 3:4]),
                      reads=[ska, skb, "consts"], writes=[ppk])
                    if t > 0:
                        emit_pvb(t - 1)
                    if t % 2 == 0:
                        th = next(feeder, None)
                        if th is not None:
                            th()
                emit_pvb(31)
                for hi2, pvk in ((0, "pb2a"), (1, "pb2b")):
                    hh = h0 if hi2 == 0 else h1
                    V(lambda e, hi2=hi2: e.reciprocal(
                        recip_sb[:], PB[2][64:65, hi2 * 512: hi2 * 512 + 512]),
                      reads=[pvk], writes=["recip"])
                    T(lambda e: e.matmul(PB[3][0:64, 512:1024], ones_r[0:1, 0:64],
                                         recip_sb[:], start=True, stop=True),
                      reads=["ones_r", "recip"], writes=["pb3b"])
                    V(lambda e: e.tensor_copy(rb_sb[:], PB[3][0:64, 512:1024]),
                      reads=["pb3b"], writes=["rb"])
                    dsto = OTb[(hh % 2) * 64:(hh % 2) * 64 + 64,
                               (hh // 2) * 2048 + q0g: (hh // 2) * 2048 + q0g + 512]
                    V(lambda e, dsto=dsto, hi2=hi2: e.tensor_tensor(
                        dsto, PB[2][0:64, hi2 * 512: hi2 * 512 + 512], rb_sb[:], ALU.mult),
                      reads=[pvk, "rb"], writes=["R_BW1"])

        import itertools as _it
        for th in stage_thunks(0):
            th()
        for gp in range(8):
            feeder = stage_thunks(gp + 1) if gp < 7 else iter(())
            A_b(gp, feeder)
            for th in feeder:
                th()

        def evwo(oc, tg, pb, pbks):
            oi = (oc * 2 + tg) % 2
            xs = v32(oXS + oi * 4096, 1024)
            xsk = "R_BXG"
            srcx = xbuf[oc * 128:(oc + 1) * 128, tg * 1024:(tg + 1) * 1024]
            p.dma("SY", "bx2%d" % oi,
                  lambda e, srcx=srcx, xs=xs: e.dma_start(xs, srcx),
                  writes=[xsk, "R_BXG", "R_BXG", "R_BXG"])
            xo = v32(oXO2 + oi * 4096, 1024)
            xok = "R_BSQ"
            V(lambda e, xo=xo, xs=xs, pb=pb, oc=oc: e.scalar_tensor_tensor(
                xo, pb, bcol("bbo", oc), xs, ALU.add, ALU.add),
              reads=list(pbks) + ["biases", xsk], writes=["R_BSQ", "R_BXG"])
            dstx = xbuf[oc * 128:(oc + 1) * 128, tg * 1024:(tg + 1) * 1024]
            p.dma("SY", "bxo%d" % oi,
                  lambda e, dstx=dstx, xo=xo: e.dma_start(dstx, xo),
                  reads=[xok], writes=[])
        proj(woview, "R_BW0", OTb, "R_BW1", TOK, evwo)

    def dbg_tap(i):
        if not _DEBUG_TAPS:
            return
        for kc in range(8):
            t = v32(0, 2048)
            p.dma("SY", "dbgl", lambda e, kc=kc, t=t: e.dma_start(
                t, xbuf[kc * 128:(kc + 1) * 128, :]), writes=["dbgt"])
            p.dma("SY", "dbgs", lambda e, kc=kc, i=i, t=t: e.dma_start(
                dbg_taps[i][kc * 128:(kc + 1) * 128, :], t), reads=["dbgt"], writes=[])
        p.barrier()

    phase_attn_a()
    p.barrier()
    dbg_tap(0)
    phase_ffn(last=False)
    p.barrier()
    dbg_tap(1)
    phase_attn_b()
    p.barrier()
    dbg_tap(2)
    phase_ffn(last=True)
    p.barrier()

    p.finalize()
    with nc.Block() as block:
        @block.tensor
        def _(eng):
            p.emit("T", eng, sems)

        @block.vector
        def _(eng):
            p.emit("V", eng, sems)

        @block.scalar
        def _(eng):
            p.emit("S", eng, sems)

        @block.gpsimd
        def _(eng):
            p.emit("G", eng, sems)

        @block.sync
        def _(eng):
            p.emit("SY", eng, sems)
    stack.close()
    return nc


_NC_CACHE = None


def kernel(L0_attn_a_x, L0_attn_a_Wq, L0_attn_a_bq, L0_attn_a_Wk, L0_attn_a_bk,
           L0_attn_a_Wv, L0_attn_a_bv, L0_attn_a_Wo, L0_attn_a_bo,
           L0_attn_a_rel_bias, L0_attn_a_W1, L0_attn_a_b1, L0_attn_a_W2,
           L0_attn_a_b2, L0_attn_b_Wq, L0_attn_b_bq, L0_attn_b_Wk, L0_attn_b_bk,
           L0_attn_b_Wv, L0_attn_b_bv, L0_attn_b_Wo, L0_attn_b_bo,
           L0_ffn_w_gate, L0_ffn_w_up, L0_ffn_w_down):
    global _NC_CACHE
    oh, cosT, sinT, i64 = _host_consts()

    def wT(w):
        return np.ascontiguousarray(np.asarray(w, np.float32).T)

    def bR(b):
        return np.ascontiguousarray(np.asarray(b, np.float32).reshape(8, 128).T)

    x = np.asarray(L0_attn_a_x, np.float32)
    common = {
        "aWq": wT(L0_attn_a_Wq), "aWk": wT(L0_attn_a_Wk), "aWv": wT(L0_attn_a_Wv),
        "aWo": wT(L0_attn_a_Wo), "aW1": wT(L0_attn_a_W1), "aW2": wT(L0_attn_a_W2),
        "bWq": wT(L0_attn_b_Wq), "bWk": wT(L0_attn_b_Wk), "bWv": wT(L0_attn_b_Wv),
        "bWo": wT(L0_attn_b_Wo),
        "abq": bR(L0_attn_a_bq), "abk": bR(L0_attn_a_bk), "abv": bR(L0_attn_a_bv),
        "abo": bR(L0_attn_a_bo), "ab1": bR(L0_attn_a_b1), "ab2": bR(L0_attn_a_b2),
        "bbq": bR(L0_attn_b_bq), "bbk": bR(L0_attn_b_bk), "bbv": bR(L0_attn_b_bv),
        "bbo": bR(L0_attn_b_bo),
        "onehot": oh, "i64": i64,
        "wg": np.ascontiguousarray(
            np.asarray(L0_ffn_w_gate, np.int32).T.reshape(D, NHC, 128).transpose(1, 0, 2)),
        "wu": np.ascontiguousarray(
            np.asarray(L0_ffn_w_up, np.int32).T.reshape(D, NHC, 128).transpose(1, 0, 2)),
        "wd": np.ascontiguousarray(
            np.asarray(L0_ffn_w_down, np.int32).T.reshape(HID, 8, 128).transpose(1, 0, 2)),
    }
    in_maps = []
    for core in range(8):
        b, hh = core // 2, core % 2
        m = dict(common)
        m["xT"] = np.ascontiguousarray(x[b, hh * TOK:(hh + 1) * TOK, :].T)
        relroll = np.ascontiguousarray(
            np.roll(np.asarray(L0_attn_a_rel_bias, np.float32), -8 * hh, axis=1))
        m["relT"] = relroll
        m["relflat"] = np.ascontiguousarray(relroll.reshape(1, 512))
        m["cosT"] = np.ascontiguousarray(cosT[:, hh * TOK:(hh + 1) * TOK])
        m["sinT"] = np.ascontiguousarray(sinT[:, hh * TOK:(hh + 1) * TOK])
        in_maps.append(m)

    if _NC_CACHE is None:
        _NC_CACHE = build_program()
    res = run_bass_kernel_spmd(_NC_CACHE, in_maps, core_ids=list(range(8)))
    out = np.empty((4, 4096, D), np.float32)
    for core in range(8):
        b, hh = core // 2, core % 2
        out[b, hh * TOK:(hh + 1) * TOK, :] = res.results[core]["yT"].T
    return out


# version 24
# speedup vs baseline: 1.0164x; 1.0164x over previous
import math
import contextlib
import numpy as np
import ml_dtypes
import concourse.bass as bass
import concourse.mybir as mybir
from concourse.bass_utils import run_bass_kernel_spmd

F32 = mybir.dt.float32
BF16 = mybir.dt.bfloat16
I32 = mybir.dt.int32
AF = mybir.ActivationFunctionType
ALU = mybir.AluOpType
BF = ml_dtypes.bfloat16

D = 1024
TOK = 2048
NCHUNK = 8
CH = 256
HID = 3072
NHC = HID // 128
C_A = 92.0
C_B = 4.0
ENGS = ("T", "V", "S", "G", "SY")
_DEBUG_TAPS = False
ARCOLS = 47000
ARB = ARCOLS * 4


def _bucket_of_n(n):
    ret = 16 if n < 0 else 0
    n = abs(n)
    if n < 8:
        return ret + n
    v = 8 + int(np.float32(np.log(np.float32(n) / np.float32(8.0)))
                / np.float32(math.log(128.0 / 8.0)) * np.float32(8))
    return ret + min(v, 15)


def _host_consts():
    oh = np.zeros((32, 1280), np.float32)
    for i in range(1280):
        d = i - 640
        if abs(d) <= 90:
            oh[_bucket_of_n(-d), i] = 1.0
    freq = np.exp(np.arange(0, D, 2, dtype=np.float32)
                  * np.float32(-np.log(10000.0) / D))
    ang = np.arange(4096, dtype=np.float32)[:, None] * freq[None, :]
    cosT = np.ascontiguousarray(np.cos(ang).T.astype(np.float32))
    sinT = np.ascontiguousarray(np.sin(ang).T.astype(np.float32))
    i64 = np.zeros((128, 64), np.float32)
    i64[:64] = np.eye(64, dtype=np.float32)
    i64[64:] = np.eye(64, dtype=np.float32)
    return oh, cosT, sinT, i64


class Prog:
    def __init__(self, nc, stack):
        self.nc = nc
        self.stack = stack
        self.ops = {e: [] for e in ENGS}
        self.bufs = {}
        self.dsems = {}

    def dma_sem(self, name):
        if name not in self.dsems:
            sem = self.stack.enter_context(self.nc.semaphore("ds_" + name))
            self.dsems[name] = {"sem": sem, "n": 0}
        return self.dsems[name]

    def _deps(self, reads, writes, eng, selfsync):
        deps = []
        for b in reads:
            st = self.bufs.get(b)
            if st and st["w"] is not None:
                deps.append(st["w"])
        for b in writes:
            st = self.bufs.get(b)
            if st:
                if st["w"] is not None:
                    deps.append(st["w"])
                deps.extend(st["r"])
        out, seen = [], set()
        for d in deps:
            if d[0] == "E" and d[1] == eng and not selfsync:
                continue
            if d[0] == "D":
                d = ("D", d[1], self.dsems[d[1]]["n"])
            if d not in seen:
                seen.add(d)
                out.append(d)
        return out

    def _register(self, tok, reads, writes):
        for b in writes:
            self.bufs[b] = {"w": tok, "r": []}
        for b in reads:
            st = self.bufs.setdefault(b, {"w": None, "r": []})
            st["r"].append(tok)

    def op(self, eng, fn, reads=(), writes=(), selfsync=True):
        idx = len(self.ops[eng])
        o = {"kind": "op", "fn": fn,
             "deps": self._deps(reads, writes, eng, selfsync),
             "need_inc": False, "inc_no": None}
        self.ops[eng].append(o)
        tok = ("E", eng, idx)
        self._register(tok, reads, writes)
        return tok

    def dma(self, eng, stream, fn, reads=(), writes=()):
        ds = self.dma_sem(stream)
        deps = self._deps(reads, writes, eng, True)
        ds["n"] += 16
        o = {"kind": "dma", "fn": fn, "deps": deps, "sem": ds["sem"],
             "need_inc": False, "inc_no": None}
        self.ops[eng].append(o)
        tok = ("D", stream, ds["n"])
        self._register(tok, reads, writes)
        return tok

    def async_op(self, eng, stream, fn, reads=(), writes=()):
        ds = self.dma_sem(stream)
        deps = self._deps(reads, writes, eng, True)
        ds["n"] += 1
        o = {"kind": "dma", "fn": fn, "deps": deps, "sem": ds["sem"],
             "inc_amt": 1, "need_inc": False, "inc_no": None}
        self.ops[eng].append(o)
        tok = ("D", stream, ds["n"])
        self._register(tok, reads, writes)
        return tok

    def barrier(self):
        lasts = []
        for e in ENGS:
            if self.ops[e]:
                lasts.append(("E", e, len(self.ops[e]) - 1))
        for name, ds in self.dsems.items():
            lasts.append(("D", name, ds["n"]))
        for e in ENGS:
            deps = [d for d in lasts if not (d[0] == "E" and d[1] == e)]
            self.ops[e].append({"kind": "op", "fn": lambda eng: eng.nop(),
                                "deps": deps, "need_inc": False, "inc_no": None})
        self.bufs = {}

    def finalize(self):
        for e in ENGS:
            for o in self.ops[e]:
                for d in o["deps"]:
                    if d[0] == "E":
                        self.ops[d[1]][d[2]]["need_inc"] = True
        for e in ENGS:
            c = 0
            for o in self.ops[e]:
                if o["need_inc"]:
                    c += 1
                    o["inc_no"] = c

    def emit(self, ename, eng, sems):
        waited = {}
        for o in self.ops[ename]:
            for d in o["deps"]:
                if d[0] == "E":
                    _, e2, idx = d
                    cnt = self.ops[e2][idx]["inc_no"]
                    key = ("E", e2)
                    if waited.get(key, 0) >= cnt:
                        continue
                    waited[key] = cnt
                    eng.wait_ge(sems[e2], cnt)
                else:
                    _, sname, cnt = d
                    key = ("D", sname)
                    if waited.get(key, 0) >= cnt:
                        continue
                    waited[key] = cnt
                    eng.wait_ge(self.dsems[sname]["sem"], cnt)
            inst = o["fn"](eng)
            if o["kind"] == "dma":
                inst.then_inc(o["sem"], o.get("inc_amt", 16))
                if o["need_inc"]:
                    eng.nop().then_inc(sems[ename], 1)
            elif o["need_inc"]:
                inst.then_inc(sems[ename], 1)


def build_program():
    nc = bass.Bass("TRN2", target_bir_lowering=False)
    dt = nc.dram_tensor
    ins = {}
    xT = dt("xT", [D, TOK], F32, kind="ExternalInput").ap()
    for nm in ["aWq", "aWk", "aWv", "aWo", "aW1", "aW2", "bWq", "bWk", "bWv", "bWo"]:
        ins[nm] = dt(nm, [D, D], F32, kind="ExternalInput").ap()
    bias_keys = ["abq", "abk", "abv", "abo", "ab1", "ab2", "bbq", "bbk", "bbv", "bbo"]
    for nm in bias_keys:
        ins[nm] = dt(nm, [128, 8], F32, kind="ExternalInput").ap()
    ins["relflat"] = dt("relflat", [1, 512], F32, kind="ExternalInput").ap()
    ins["relT"] = dt("relT", [32, 16], F32, kind="ExternalInput").ap()
    ins["onehot"] = dt("onehot", [32, 1280], F32, kind="ExternalInput").ap()
    ins["cosT"] = dt("cosT", [512, TOK], F32, kind="ExternalInput").ap()
    ins["sinT"] = dt("sinT", [512, TOK], F32, kind="ExternalInput").ap()
    ins["i64"] = dt("i64", [128, 64], F32, kind="ExternalInput").ap()
    ins["wg"] = dt("wg", [NHC, D, 128], I32, kind="ExternalInput").ap()
    ins["wu"] = dt("wu", [NHC, D, 128], I32, kind="ExternalInput").ap()
    ins["wd"] = dt("wd", [8, HID, 128], I32, kind="ExternalInput").ap()
    yT = dt("yT", [D, TOK], F32, kind="ExternalOutput").ap()
    dbg_taps = []
    if _DEBUG_TAPS:
        for i in range(3):
            dbg_taps.append(dt("dbg%d" % i, [D, TOK], F32, kind="ExternalOutput").ap())
        dbgA = dt("dbgA", [128, 2048], F32, kind="ExternalOutput").ap()
        dbgB = dt("dbgB", [128, 4352], F32, kind="ExternalOutput").ap()
        dbgC = dt("dbgC", [128, 2080], F32, kind="ExternalOutput").ap()
        dbgD = dt("dbgD", [128, 4096], F32, kind="ExternalOutput").ap()
        dbgE = dt("dbgE", [128, 1152], F32, kind="ExternalOutput").ap()
    xbuf = dt("xbuf", [D, TOK], F32, kind="Internal").ap()
    bandh_t = dt("bandh", [2, 1280], BF16, kind="Internal")
    bandh = bandh_t.ap()
    cc_in_k = dt("cc_in_k", [D, TOK], BF16, kind="Internal").ap()
    cc_out_k = dt("cc_out_k", [2 * D, TOK], BF16, kind="Internal").ap()
    cc_in_v = dt("cc_in_v", [D, TOK], BF16, kind="Internal").ap()
    cc_out_v = dt("cc_out_v", [2 * D, TOK], BF16, kind="Internal").ap()

    stack = contextlib.ExitStack()
    E = stack.enter_context
    AR = E(nc.sbuf_tensor("AR", [128, ARCOLS], F32))
    biases = E(nc.sbuf_tensor("biases", [128, 80], F32))
    consts = E(nc.sbuf_tensor("consts", [128, 8], F32))
    rel_sb = E(nc.sbuf_tensor("rel_sb", [32, 16], F32))
    relc_sb = E(nc.sbuf_tensor("relc_sb", [32, 1], BF16))
    oh_sb = E(nc.sbuf_tensor("oh_sb", [32, 1280], BF16))
    wbc = E(nc.sbuf_tensor("wbc", [64, 512], F32))
    wbcn = E(nc.sbuf_tensor("wbcn", [64, 512], F32))
    i64_sb = E(nc.sbuf_tensor("i64_sb", [128, 64], BF16))
    ones_c = E(nc.sbuf_tensor("ones_c", [128, 1], F32))
    ones_cb = E(nc.sbuf_tensor("ones_cb", [128, 1], BF16))
    ones_r = E(nc.sbuf_tensor("ones_r", [1, 128], F32))
    ones_rb = E(nc.sbuf_tensor("ones_rb", [1, 128], BF16))
    ones512 = E(nc.sbuf_tensor("ones512", [1, 512], BF16))
    band_sb = E(nc.sbuf_tensor("band_sb", [1, 1280], BF16))
    recip_sb = E(nc.sbuf_tensor("recip_sb", [1, 512], F32))
    w15row = E(nc.sbuf_tensor("w15row", [1, 1024], BF16))
    totT = E(nc.sbuf_tensor("totT", [1, 128], BF16))
    rb_sb = E(nc.sbuf_tensor("rb_sb", [64, 512], F32))
    rowA = E(nc.sbuf_tensor("rowA", [1, 512], F32))
    rowB = E(nc.sbuf_tensor("rowB", [1, 512], F32))
    rowC = E(nc.sbuf_tensor("rowC", [1, 520], F32))
    zcol64 = E(nc.sbuf_tensor("zcol64", [64, 1], F32))
    PB = [E(nc.psum_tensor(f"pb{i}", [128, 1024], F32)) for i in range(4)]
    sems = {e: E(nc.semaphore("sem_" + e)) for e in ENGS}

    p = Prog(nc, stack)
    arf = AR[:]
    arb_all = bass.AP(arf.tensor, 0, list(arf.ap)).bitcast(BF16)

    def v32(off_b, cols):
        assert off_b % 4 == 0 and off_b + cols * 4 <= ARB, (off_b, cols)
        return arf[:, off_b // 4: off_b // 4 + cols]

    def vbf(off_b, cols):
        assert off_b % 2 == 0 and off_b + cols * 2 <= ARB, (off_b, cols)
        return arb_all[:, off_b // 2: off_b // 2 + cols]

    def T(fn, reads=(), writes=()):
        return p.op("T", fn, reads, writes, selfsync=False)

    def V(fn, reads=(), writes=()):
        return p.op("V", fn, reads, writes)

    def S(fn, reads=(), writes=()):
        return p.op("S", fn, reads, writes)

    def G(fn, reads=(), writes=()):
        return p.op("G", fn, reads, writes)

    def bcol(nm, oc):
        i = bias_keys.index(nm)
        return biases[:, 8 * i + oc: 8 * i + oc + 1]

    def free_dims(ap0, dims):
        return bass.AP(ap0.tensor, ap0.offset, [list(ap0.ap[0])] + dims)

    V(lambda e: e.memset(ones_c[:], 1.0), writes=["ones_c"])
    V(lambda e: e.memset(ones_cb[:], 1.0), writes=["ones_cb"])
    V(lambda e: e.memset(ones_r[:], 1.0), writes=["ones_r"])
    V(lambda e: e.memset(ones_rb[:], 1.0), writes=["ones_rb"])
    V(lambda e: e.memset(ones512[:], 1.0), writes=["ones512"])
    V(lambda e: e.memset(zcol64[:], 0.0), writes=["zc"])
    for col, val in [(0, -C_A), (1, 1.0), (2, -C_B), (3, 0.125),
                     (4, 1e-5), (5, -0.5), (6, 0.0)]:
        V(lambda e, c=col, vv=val: e.memset(consts[:, c:c + 1], vv), writes=["consts"])
    p.dma("G", "cinit", lambda e: e.dma_start(rel_sb[:], ins["relT"]), writes=["rel_sb"])
    p.dma("G", "cinit", lambda e: e.dma_start(oh_sb[:], ins["onehot"]), writes=["oh_sb"])
    p.dma("G", "cinit", lambda e: e.dma_start(i64_sb[:], ins["i64"]), writes=["i64_sb"])
    relf = rowB[0:1, 0:512]
    p.dma("G", "cinit", lambda e: e.dma_start(relf, ins["relflat"]), writes=["relf"])
    for i, nm in enumerate(bias_keys):
        p.dma("G", "cinit",
              lambda e, i=i, nm=nm: e.dma_start(biases[:, 8 * i:8 * i + 8], ins[nm]),
              writes=["biases"])
    T(lambda e: e.matmul(PB[3][0:64, 0:512], ones_r[0:1, 0:64], relf,
                         start=True, stop=True),
      reads=["ones_r", "relf"], writes=["pb3a"])
    V(lambda e: e.tensor_copy(wbc[:], PB[3][0:64, 0:512]), reads=["pb3a"], writes=["wbc"])
    V(lambda e: e.tensor_scalar(wbcn[:], wbc[:], -1.0, None, ALU.mult),
      reads=["wbc"], writes=["wbcn"])

    def ln_stats(xg, xk, xsq, sqk, gw, eps, scale, pba, pbb, ka, kb, onesq):
        V(lambda e: e.tensor_tensor(xsq, xg, xg, ALU.mult), reads=[xk], writes=[sqk])
        for kc in range(8):
            T(lambda e, kc=kc: e.matmul(pba[0:1, 0:gw], ones_c[:],
                                        xg[:, kc * gw:(kc + 1) * gw],
                                        start=(kc == 0), stop=(kc == 7)),
              reads=["ones_c", xk], writes=[ka])
        for kc in range(8):
            T(lambda e, kc=kc: e.matmul(pbb[0:1, 0:gw], onesq[:],
                                        xsq[:, kc * gw:(kc + 1) * gw],
                                        start=(kc == 0), stop=(kc == 7)),
              reads=["ones_cb", sqk], writes=[kb])
        V(lambda e: e.tensor_scalar(rowA[0:1, 0:gw], pba[0:1, 0:gw],
                                    1.0 / D, None, ALU.mult),
          reads=[ka], writes=["rowAm"])
        V(lambda e: e.tensor_scalar(rowB[0:1, 0:gw], pbb[0:1, 0:gw],
                                    1.0 / D, None, ALU.mult),
          reads=[kb], writes=["rowBe"])
        V(lambda e: e.tensor_tensor(rowC[0:1, 0:gw], rowA[0:1, 0:gw],
                                    rowA[0:1, 0:gw], ALU.mult),
          reads=["rowAm"], writes=["rowCm2"])
        V(lambda e: e.tensor_tensor(rowB[0:1, 0:gw], rowB[0:1, 0:gw],
                                    rowC[0:1, 0:gw], ALU.subtract),
          reads=["rowBe", "rowCm2"], writes=["rowBv"])
        epsc = consts[0:1, 4:5] if eps else consts[0:1, 6:7]
        S(lambda e: e.activation(rowB[0:1, 0:gw], rowB[0:1, 0:gw], AF.Ln,
                                 bias=epsc, scale=consts[0:1, 1:2]),
          reads=["rowBv", "consts"], writes=["rowBl"])
        S(lambda e: e.activation(rowB[0:1, 0:gw], rowB[0:1, 0:gw], AF.Exp,
                                 bias=consts[0:1, 6:7], scale=consts[0:1, 5:6]),
          reads=["rowBl", "consts"], writes=["rowBr"])
        if scale != 1.0:
            V(lambda e: e.tensor_scalar(rowB[0:1, 0:gw], rowB[0:1, 0:gw],
                                        scale, None, ALU.mult),
              reads=["rowBr"], writes=["rowBr"])
        V(lambda e: e.tensor_tensor(rowA[0:1, 0:gw], rowA[0:1, 0:gw],
                                    rowB[0:1, 0:gw], ALU.mult),
          reads=["rowAm", "rowBr"], writes=["rowAb"])
        T(lambda e: e.matmul(pba[:, 0:gw], ones_r[:], rowB[0:1, 0:gw],
                             start=True, stop=True),
          reads=["ones_r", "rowBr"], writes=[ka])
        T(lambda e: e.matmul(pbb[:, 0:gw], ones_r[:], rowA[0:1, 0:gw],
                             start=True, stop=True),
          reads=["ones_r", "rowAb"], writes=[kb])

    def r3(x, gw):
        return x.rearrange("p (kc w) -> p kc w", kc=8)

    def bc3g(pb, gw):
        a = pb[:, 0:gw]
        return bass.AP(a.tensor, a.offset, [list(a.ap[0]), [0, 8], list(a.ap[1])])

    def ln_apply(xg3, xk, tmp3, tk, out3, ok, gw, pba, pbb, ka, kb):
        V(lambda e: e.tensor_tensor(tmp3, xg3, bc3g(pba, gw), ALU.mult),
          reads=[xk, ka], writes=[tk])
        V(lambda e: e.tensor_tensor(out3, tmp3, bc3g(pbb, gw), ALU.subtract),
          reads=[tk, kb], writes=[ok])

    def proj(wv, wkkey, mvfull, mvkey, ntok, evac):
        ntg = max(1, ntok // 1024)
        tg_w = min(ntok, 1024)
        nq = max(1, tg_w // 512)
        qw = min(tg_w, 512)
        for oc in range(8):
            for tg in range(ntg):
                pbi = (oc * ntg + tg) % 2
                pbks = ["pb%da" % pbi] + (["pb%db" % pbi] if nq > 1 else [])
                for kc in range(8):
                    for q in range(nq):
                        mv = mvfull[:, kc * ntok + tg * tg_w + q * qw:
                                    kc * ntok + tg * tg_w + (q + 1) * qw]
                        pbdst = PB[pbi][:, q * 512: q * 512 + qw]
                        mk = mvkey(tg, q) if callable(mvkey) else mvkey
                        T(lambda e, kc=kc, oc=oc, mv=mv, pbdst=pbdst: e.matmul(
                            pbdst, wv[:, kc * 1024 + oc * 128: kc * 1024 + (oc + 1) * 128],
                            mv, start=(kc == 0), stop=(kc == 7)),
                          reads=[wkkey, mk], writes=[pbks[q]])
                evac(oc, tg, PB[pbi][:, 0:tg_w], pbks)

    def phase_attn_a():
        oXC = 0
        oW = oXC + 16384
        oQv = oW + 65536
        oKv = oQv + 16384
        oVv = oKv + 16384
        oVT = oVv + 8512
        oBG = oVT + 8320
        oHK = oBG + 17408
        oXN = oHK + 4608
        oAS = oXN + 8192
        oSQ = oAS + 8192
        oOT = oSQ + 4096
        oPP = oOT + 4096
        oSC = oPP + 4096
        assert oSC + 4096 <= ARB, oSC

        pbPROJ = PB[3][:, 0:512]
        pbA = PB[3][:, 512:768]
        pbB = PB[3][:, 768:1024]
        pbM = PB[3][:, 512:1024]

        Vvv = vbf(oVv, 4256)
        OT = vbf(oOT, 2048)
        ff1 = vbf(oOT, 2048)

        def Qvp(c):
            return vbf(oQv + (c % 2) * 8192, 4096)

        def Kvp(c):
            return vbf(oKv + (c % 2) * 8192, 4096)

        def vtp(c):
            return vbf(oVT + (c % 2) * 4160, 2080)

        def bigp(c):
            return vbf(oBG + (c % 2) * 8704, 4352)

        def hkp(c):
            return vbf(oHK + (c % 2) * 2304, 1152)

        wslot = [vbf(oW + i * 16384, 8192) for i in range(4)]
        STREAM = [("aWq", 0), ("aWk", 0), ("aWv", 0),
                  ("aWq", 1), ("aWk", 1), ("aWv", 1)]
        for c in range(1, NCHUNK):
            STREAM += [("aWo", c - 1), ("aW1", c - 1), ("aW2", c - 1)]
            if c + 1 < NCHUNK:
                STREAM += [("aWq", c + 1), ("aWk", c + 1), ("aWv", c + 1)]
        STREAM += [("aWo", NCHUNK - 1), ("aW1", NCHUNK - 1), ("aW2", NCHUNK - 1)]
        jof = {key: j for j, key in enumerate(STREAM)}

        def wload(j):
            if j >= len(STREAM):
                return
            nm, c = STREAM[j]
            sl = j % 4
            srcw = ins[nm].rearrange("(kc p) d -> p kc d", p=128)
            p.dma("G", "wsl%d" % sl,
                  lambda e, srcw=srcw, sl=sl: e.dma_start(wslot[sl], srcw),
                  writes=["R_WSL%d" % sl])

        def projS(j, mvfull, mvkey, evac):
            sl = j % 4
            wv = wslot[sl]
            wk = "R_WSL%d" % sl
            for oc in range(8):
                for h in range(2):
                    for kc in range(4 * h, 4 * h + 4):
                        T(lambda e, kc=kc, oc=oc, wv=wv, mvfull=mvfull: e.matmul(
                            pbPROJ[:, 0:256],
                            wv[:, kc * 1024 + oc * 128: kc * 1024 + (oc + 1) * 128],
                            mvfull[:, kc * 256:(kc + 1) * 256],
                            start=(kc == 0), stop=(kc == 7)),
                          reads=[wk, mvkey], writes=["pb3p"])
                    if h == 0:
                        yield (lambda: None)
                yield (lambda oc=oc: evac(oc))
            yield (lambda j=j: wload(j + 4))

        def bcolr(nm, oc, r0):
            i = bias_keys.index(nm)
            return biases[r0:r0 + 64, 8 * i + oc: 8 * i + oc + 1]

        def mk_evq(c):
            par = c % 2
            Qv = Qvp(c)
            keys = ("a_Qv%d" % par, "a_Qv2_%d" % par)

            def evq(oc):
                for m, r0 in ((2 * oc, 0), (2 * oc + 1, 64)):
                    src = pbPROJ[r0:r0 + 64, 0:256]
                    scb = bcolr("abq", oc, r0)
                    for half in range(2):
                        dst = Qv[half * 64:half * 64 + 64, m:4096:16]
                        V(lambda e, dst=dst, src=src, scb=scb: e.tensor_scalar(
                            dst, src, scb, None, ALU.add),
                          reads=["pb3p", "biases"], writes=[keys[half]])
            return evq

        def mk_evk(c):
            par = c % 2
            Kv = Kvp(c)
            keys = ("a_Kv%d" % par, "a_Kv2_%d" % par)

            def evk(oc):
                for m, r0 in ((2 * oc, 0), (2 * oc + 1, 64)):
                    src3 = pbPROJ[r0:r0 + 64, 0:256].rearrange("p (a r) -> p a r", a=32)
                    scb = bcolr("abk", oc, r0)
                    for half in range(2):
                        base = Kv[half * 64:half * 64 + 64, (127 - m):(128 - m)]
                        dst = free_dims(base, [[128, 32], [-16, 8]])
                        V(lambda e, dst=dst, src3=src3, scb=scb: e.tensor_scalar(
                            dst, src3, scb, None, ALU.add),
                          reads=["pb3p", "biases"], writes=[keys[half]])
            return evk

        def mk_evv(c):
            def evv(oc):
                for m, r0 in ((2 * oc, 0), (2 * oc + 1, 64)):
                    src = pbPROJ[r0:r0 + 64, 0:256]
                    src3 = src.rearrange("p (a r) -> p a r", a=32)
                    scb = bcolr("abv", oc, r0)
                    dstn = Vvv[0:64, m:4096:16]
                    V(lambda e, dstn=dstn, src=src, scb=scb: e.tensor_scalar(
                        dstn, src, scb, None, ALU.add),
                      reads=["pb3p", "biases"], writes=["a_Vv"])
                    dstr = free_dims(Vvv[64:128, (127 - m):(128 - m)], [[128, 32], [-16, 8]])
                    V(lambda e, dstr=dstr, src3=src3, scb=scb: e.tensor_scalar(
                        dstr, src3, scb, None, ALU.add),
                      reads=["pb3p", "biases"], writes=["a_Vvr"])
            return evv

        def P1_thunks(c):
            ci = c % 2
            xc = v32(oXC + ci * 8192, 2048)
            xck = "a_xc%d" % ci
            xn = vbf(oXN + ci * 4096, 2048)
            xnk = "R_XN%d" % ci
            srcx = xT.rearrange("(kc p) t -> p kc t", p=128)[:, :, c * CH:(c + 1) * CH]
            yield lambda: p.dma("SY", "xin%d" % ci,
                                lambda e, srcx=srcx, xc=xc: e.dma_start(xc, srcx),
                                writes=[xck])
            xsq = vbf(oSQ, 2048)

            def _ln1(xc=xc, xck=xck, xn=xn, xnk=xnk):
                ln_stats(xc, xck, xsq, "R_SQ", CH, True, 1.0,
                         pbA, pbB, "pb3b", "pb3b", ones_cb)
                ln_apply(r3(xc, CH), xck, r3(xsq, CH), "R_SQ",
                         r3(xn, CH), xnk, CH, pbA, pbB, "pb3b", "pb3b")
            yield _ln1
            yield from projS(jof[("aWq", c)], xn, xnk, mk_evq(c))
            yield from projS(jof[("aWk", c)], xn, xnk, mk_evk(c))
            yield from projS(jof[("aWv", c)], xn, xnk, mk_evv(c))

        def P2_thunks(c):
            ci = c % 2
            vt = vtp(c)
            big = bigp(c)
            hk = hkp(c)

            def _band():
                V(lambda e, c=c: e.tensor_copy(relc_sb[:], rel_sb[0:32, c:c + 1]),
                  reads=["rel_sb"], writes=["relc"])
                for pc in range(4):
                    T(lambda e, pc=pc: e.matmul(pbM[0:1, 0:320],
                                                relc_sb[:],
                                                oh_sb[:, pc * 320:(pc + 1) * 320],
                                                start=True, stop=True),
                      reads=["relc", "oh_sb"], writes=["pb3b"])
                    V(lambda e, pc=pc: e.tensor_copy(band_sb[0:1, pc * 320:(pc + 1) * 320],
                                                     pbM[0:1, 0:320]),
                      reads=["pb3b"], writes=["band_sb"])
                p.dma("SY", "bando%d" % ci,
                      lambda e, ci=ci: e.dma_start(bandh[ci:ci + 1, :], band_sb[0:1, :1280]),
                      reads=["band_sb"], writes=["bandh%d" % ci])
            yield _band
            pbT = PB[3][:, 256:512]
            for tt in range(0, 32, 4):
                def _tr(tt=tt):
                    for j in range(4):
                        t = tt + j
                        T(lambda e, t=t, j=j: e.matmul(pbT[:, j * 64:j * 64 + 64],
                                                       Vvv[64:128, t * 128:(t + 1) * 128],
                                                       i64_sb[64:128, :], start=True, stop=True,
                                                       tile_position=(64, 0)),
                          reads=["a_Vvr", "i64_sb"], writes=["pb3t"])
                    dst = free_dims(vt[:, tt * 65:tt * 65 + 1], [[65, 4], [1, 64]])
                    src4 = pbT.rearrange("p (a b) -> p a b", a=4)
                    V(lambda e, dst=dst, src4=src4: e.tensor_copy(dst, src4),
                      reads=["pb3t"], writes=["a_vtok%d" % ci])
                yield _tr

            def _scan():
                zcb = zcol64[:].broadcast_to([64, 4255])
                V(lambda e, zcb=zcb, big=big: e.tensor_tensor_scan(big[0:64, 97:4352], zcb,
                                                                   Vvv[0:64, 0:4255], 0.0,
                                                                   ALU.add, ALU.add),
                  reads=["zc", "a_Vv", "a_Vvp"], writes=["a_ps%d" % ci])
                T(lambda e, big=big: e.matmul(pbM[0:1, 0:64], big[0:64, 4192:4193],
                                              i64_sb[0:64, :], start=True, stop=True),
                  reads=["a_ps%d" % ci, "i64_sb"], writes=["pb3b"])
                V(lambda e, ci=ci: e.tensor_copy(totT[:, ci * 64:ci * 64 + 64], pbM[0:1, 0:64]),
                  reads=["pb3b"], writes=["totT%d" % ci])
                V(lambda e, c=c, ci=ci: e.tensor_scalar(w15row[:, ci * 512:ci * 512 + 512],
                                                        ones512[:],
                                                        wbc[0:1, 240 + c:241 + c], None, ALU.mult),
                  reads=["ones512", "wbc"], writes=["w15%d" % ci])
            yield _scan

            def _hk():
                hsrc = bass.AP(bandh_t, ci * 1280 + 1, [[1, 128], [1, 1152]])
                p.dma("SY", "hank",
                      lambda e, hsrc=hsrc, hk=hk: e.dma_start(hk, hsrc),
                      reads=["bandh%d" % ci], writes=["a_hk%d" % ci])
            yield _hk

        def A_emit(c, feeder):
            par = c % 2
            Qv = Qvp(c)
            Kv = Kvp(c)
            vt = vtp(c)
            big = bigp(c)
            hk = hkp(c)
            kq, kq2 = "a_Qv%d" % par, "a_Qv2_%d" % par
            kk, kk2 = "a_Kv%d" % par, "a_Kv2_%d" % par
            vtk = "a_vtok%d" % par
            psk = "a_ps%d" % par
            outk = "a_out%d" % par
            hkk = "a_hk%d" % par
            totc = totT[:, par * 64:par * 64 + 64]
            w15c = w15row[:, par * 512:par * 512 + 512]
            pend = [None]

            def flush_combine():
                if pend[0] is not None:
                    pend[0]()
                    pend[0] = None

            for g in range(8):
                q0 = g * 512
                pvh, bdh = ("pb2a", "pb2b") if g % 2 == 0 else ("pb2b", "pb2a")
                pvo = 0 if g % 2 == 0 else 512
                bdo = 512 - pvo

                def emit_pv(pi, pvh=pvh, pvo=pvo):
                    pp = vbf(oPP + (pi % 2) * 2048, 1024)
                    ppk = "a_pp%d" % (pi % 2)
                    for j, tj in enumerate((2 * pi, 2 * pi + 1)):
                        T(lambda e, tj=tj, pp=pp, j=j, pvo=pvo: e.matmul(
                            PB[2][0:65, pvo:pvo + 512], vt[:, tj * 65:tj * 65 + 65],
                            pp[:, j * 512:(j + 1) * 512],
                            start=(tj == 0), stop=(tj == 31)),
                          reads=[vtk, ppk], writes=[pvh])
                for pi in range(16):
                    t0, t1 = 2 * pi, 2 * pi + 1
                    sp = PB[pi % 2]
                    ska, skb = "pb%da" % (pi % 2), "pb%db" % (pi % 2)
                    T(lambda e, t0=t0, sp=sp, q0=q0: e.matmul(
                        sp[:, 0:512], Kv[0:64, t0 * 128:(t0 + 1) * 128],
                        Qv[0:64, q0:q0 + 512], start=True, stop=True,
                        tile_position=(0, 0)),
                      reads=[kk, kq], writes=[ska])
                    T(lambda e, t1=t1, sp=sp, q0=q0: e.matmul(
                        sp[:, 512:1024], Kv[64:128, t1 * 128:(t1 + 1) * 128],
                        Qv[64:128, q0:q0 + 512], start=True, stop=True,
                        tile_position=(64, 0)),
                      reads=[kk2, kq2], writes=[skb])
                    pp = vbf(oPP + (pi % 2) * 2048, 1024)
                    ppk = "a_pp%d" % (pi % 2)
                    S(lambda e, pp=pp, sp=sp: e.activation(
                        pp, sp[:, :], AF.Exp, bias=consts[:, 0:1], scale=consts[:, 1:2]),
                      reads=[ska, skb, "consts"], writes=[ppk])
                    if pi > 0:
                        emit_pv(pi - 1)
                    if pi == 4:
                        flush_combine()
                    else:
                        th = next(feeder, None)
                        if th is not None:
                            th()
                emit_pv(15)
                first_band = True
                for d0 in (128, 0, -128, -256, -384, -512):
                    k0 = q0 - d0
                    if k0 < 0 or k0 >= 4096:
                        continue
                    t = k0 // 128
                    T(lambda e, t=t, d0=d0, fb=first_band, bdo=bdo: e.matmul(
                        PB[2][0:64, bdo:bdo + 512], vt[:, t * 65:t * 65 + 64],
                        hk[:, 512 + d0:512 + d0 + 512],
                        start=fb, stop=False),
                      reads=[vtk, hkk], writes=[bdh])
                    first_band = False
                T(lambda e, bdo=bdo: e.matmul(PB[2][0:64, bdo:bdo + 512], totc, w15c,
                                              start=False, stop=True),
                  reads=["totT%d" % par, "w15%d" % par], writes=[bdh])
                o2e = v32(oSC + 2048, 512)[0:64, :]
                V(lambda e, o2e=o2e, bdo=bdo: e.tensor_copy(o2e, PB[2][0:64, bdo:bdo + 512]),
                  reads=[bdh], writes=["R_O2"])
                V(lambda e, pvo=pvo: e.reciprocal(recip_sb[:], PB[2][64:65, pvo:pvo + 512]),
                  reads=[pvh], writes=["recip"])

                def mk_combine(q0=q0, pvh=pvh, pvo=pvo, c=c, big=big, psk=psk, outk=outk):
                    def go():
                        T(lambda e: e.matmul(pbM[0:64, 0:512], ones_r[0:1, 0:64],
                                             recip_sb[:], start=True, stop=True),
                          reads=["ones_r", "recip"], writes=["pb3b"])
                        V(lambda e: e.tensor_copy(rb_sb[:], pbM[0:64, 0:512]),
                          reads=["pb3b"], writes=["rb"])
                        o1 = v32(oSC, 512)[0:64, :]
                        o2 = v32(oSC + 2048, 512)[0:64, :]
                        V(lambda e, o1=o1: e.tensor_tensor(o1, PB[2][0:64, pvo:pvo + 512],
                                                           rb_sb[:], ALU.mult),
                          reads=[pvh, "rb"], writes=["R_O1"])
                        V(lambda e, o2=o2, o1=o1: e.tensor_tensor(
                            o2, o1, o2, ALU.add),
                          reads=["R_O1", "R_O2"], writes=["R_O2"])
                        V(lambda e, o1=o1: e.scalar_tensor_tensor(
                            o1, big[0:64, q0 + 6:q0 + 518],
                            wbc[0:64, 496 + c:497 + c], o2, ALU.mult, ALU.add),
                          reads=[psk, "wbc", "R_O2"], writes=["R_O1"])
                        V(lambda e, o1=o1: e.scalar_tensor_tensor(
                            big[64:128, q0:q0 + 512], big[0:64, q0 + 187:q0 + 699],
                            wbcn[0:64, 240 + c:241 + c], o1, ALU.mult, ALU.add),
                          reads=[psk, "wbcn", "R_O1"], writes=[outk])
                    return go
                pend[0] = mk_combine()
            flush_combine()

        def T_thunks(c):
            ci = c % 2
            xc = v32(oXC + ci * 8192, 2048)
            xck = "a_xc%d" % ci
            xn = vbf(oXN + ci * 4096, 2048)
            xnk = "R_XN%d" % ci
            as2 = v32(oAS, 2048)
            xsq2 = vbf(oSQ, 2048)
            ab = vbf(oXN + ci * 4096, 2048)
            big = bigp(c)
            outk = "a_out%d" % ci
            for m in range(16):
                def _uv(m=m):
                    dsto = OT[(m % 2) * 64:(m % 2) * 64 + 64,
                              (m // 2) * 256:(m // 2) * 256 + 256]
                    V(lambda e, dsto=dsto, m=m, big=big: e.tensor_copy(
                        dsto, big[64:128, m:4096:16]),
                      reads=[outk], writes=["R_OT"])
                yield _uv

            def evo(oc):
                V(lambda e, oc=oc: e.scalar_tensor_tensor(
                    as2[:, oc * 256:(oc + 1) * 256], pbPROJ[:, 0:256], bcol("abo", oc),
                    ab[:, oc * 256:(oc + 1) * 256], ALU.add, ALU.add),
                  reads=["pb3p", "biases", xnk], writes=["R_AS"])
            yield from projS(jof[("aWo", c)], OT, "R_OT", evo)

            def _ln2(ab=ab, xnk=xnk):
                ln_stats(as2, "R_AS", xsq2, "R_SQ", CH, False, 1.0,
                         pbA, pbB, "pb3b", "pb3b", ones_cb)
                ln_apply(r3(as2, CH), "R_AS", r3(xsq2, CH), "R_SQ",
                         r3(ab, CH), xnk, CH, pbA, pbB, "pb3b", "pb3b")
            yield _ln2

            def ev1(oc):
                V(lambda e, oc=oc: e.tensor_scalar(
                    ff1[:, oc * 256:(oc + 1) * 256], pbPROJ[:, 0:256],
                    bcol("ab1", oc), None, ALU.add),
                  reads=["pb3p", "biases"], writes=["R_OT"])
            yield from projS(jof[("aW1", c)], ab, xnk, ev1)

            def ev2(oc):
                V(lambda e, oc=oc: e.scalar_tensor_tensor(
                    as2[:, oc * 256:(oc + 1) * 256], pbPROJ[:, 0:256], bcol("ab2", oc),
                    ab[:, oc * 256:(oc + 1) * 256], ALU.add, ALU.add),
                  reads=["pb3p", "biases", xnk], writes=["R_AS"])
            yield from projS(jof[("aW2", c)], ff1, "R_OT", ev2)

            def _ln3():
                ln_stats(as2, "R_AS", xsq2, "R_SQ", CH, False, 1.0,
                         pbA, pbB, "pb3b", "pb3b", ones_cb)
                as23 = r3(as2, CH)
                xc3 = r3(xc, CH)
                V(lambda e: e.tensor_tensor(as23, as23, bc3g(pbA, CH), ALU.mult),
                  reads=["R_AS", "pb3b"], writes=["R_AS"])
                V(lambda e: e.tensor_tensor(as23, as23, bc3g(pbB, CH), ALU.subtract),
                  reads=["R_AS", "pb3b"], writes=["R_AS"])
                V(lambda e: e.tensor_tensor(as23, as23, xc3, ALU.add),
                  reads=["R_AS", xck], writes=["R_AS"])
                dstx = xbuf.rearrange("(kc p) t -> p kc t", p=128)[:, :, c * CH:(c + 1) * CH]
                p.dma("SY", "xout",
                      lambda e, dstx=dstx, as2=as2: e.dma_start(dstx, as2),
                      reads=["R_AS"], writes=[])
            yield _ln3

        def drain(gen):
            for th in gen:
                th()

        import itertools
        for ci2 in range(2):
            V(lambda e, ci2=ci2: e.memset(
                free_dims(vbf(oVT + ci2 * 4160, 2080)[:, 64:65], [[65, 32]]), 1.0),
              writes=["a_vtok%d" % ci2])
            V(lambda e, ci2=ci2: e.memset(vbf(oBG + ci2 * 8704, 4352)[0:64, 0:97], 0.0),
              writes=["a_ps%d" % ci2])
        V(lambda e: e.memset(Vvv[0:64, 4096:4256], 0.0), writes=["a_Vvp"])
        for j in range(4):
            wload(j)
        drain(P1_thunks(0))
        drain(P2_thunks(0))
        carry = iter(())
        for c in range(NCHUNK):
            feeder = itertools.chain(
                carry, P1_thunks(c + 1) if c + 1 < NCHUNK else iter(()))
            A_emit(c, feeder)
            drain(feeder)
            if c + 1 < NCHUNK:
                drain(P2_thunks(c + 1))
            carry = T_thunks(c)
        drain(carry)

    def phase_ffn(last):
        oXG = 0
        oSQ = oXG + 32768
        oXN = oSQ + 8192
        oH = oXN + 32768
        oWG = oH + 98304
        oGS = oWG + 8192
        assert oGS + 4096 <= ARB, oGS
        oWD = 0
        oXS = oWD + 12288
        src_in = xbuf
        dst_out = yT if last else xbuf
        xnf = vbf(oXN, 16384)
        for g in range(4):
            xg = v32(oXG + (g % 2) * 16384, 4096)
            xgk = "R_FXG%d" % (g % 2)
            srcx = src_in.rearrange("(kc p) t -> p kc t", p=128)[:, :, g * 512:(g + 1) * 512]
            p.dma("SY", "fxin%d" % (g % 2), lambda e, srcx=srcx, xg=xg: e.dma_start(xg, srcx),
                  writes=[xgk])
            xsq = vbf(oSQ, 4096)
            ln_stats(xg, xgk, xsq, "R_FSQ", 512, True, 1.0 / 127.0,
                     PB[3][:, 0:512], PB[3][:, 512:1024], "pb3a", "pb3b", ones_cb)
            xn3 = free_dims(xnf[0:128, g * 512:g * 512 + 1], [[2048, 8], [1, 512]])
            ln_apply(r3(xg, 512), xgk, r3(xsq, 512), "R_FSQ", xn3, "f_xn%d" % g, 512,
                     PB[3][:, 0:512], PB[3][:, 512:1024], "pb3a", "pb3b")

        H = vbf(oH, 49152)
        for hc in range(NHC):
            hi = hc % 2
            wgv = vbf(oWG + hi * 2048, 1024)
            wuv = vbf(oWG + 4096 + hi * 2048, 1024)
            sg = ins["wg"][hc, :, :].rearrange("(kc p) h -> p kc h", p=128)
            su = ins["wu"][hc, :, :].rearrange("(kc p) h -> p kc h", p=128)
            p.dma("G", "fwg%d" % hi, lambda e, sg=sg, wgv=wgv: e.dma_start(wgv, sg),
                  writes=["f_wg%d" % hi])
            p.dma("G", "fwu%d" % hi, lambda e, su=su, wuv=wuv: e.dma_start(wuv, su),
                  writes=["f_wu%d" % hi])
            for seg in range(4):
                sb = seg % 2
                gk = "pb0a" if sb == 0 else "pb0b"
                uk = "pb1a" if sb == 0 else "pb1b"
                gdep = "f_xn%d" % seg
                for kc in range(8):
                    mv = xnf[:, kc * 2048 + seg * 512: kc * 2048 + (seg + 1) * 512]
                    T(lambda e, mv=mv, wgv=wgv, kc=kc, sb=sb: e.matmul(
                        PB[0][:, sb * 512:(sb + 1) * 512],
                        wgv[:, kc * 128:(kc + 1) * 128], mv,
                        start=(kc == 0), stop=(kc == 7)),
                      reads=["f_wg%d" % hi, gdep], writes=[gk])
                for kc in range(8):
                    mv = xnf[:, kc * 2048 + seg * 512: kc * 2048 + (seg + 1) * 512]
                    T(lambda e, mv=mv, wuv=wuv, kc=kc, sb=sb: e.matmul(
                        PB[1][:, sb * 512:(sb + 1) * 512],
                        wuv[:, kc * 128:(kc + 1) * 128], mv,
                        start=(kc == 0), stop=(kc == 7)),
                      reads=["f_wu%d" % hi, gdep], writes=[uk])
                gs = vbf(oGS + sb * 2048, 512)
                S(lambda e, gs=gs, sb=sb: e.activation(gs, PB[0][:, sb * 512:(sb + 1) * 512],
                                                       AF.Silu),
                  reads=[gk], writes=["f_gs%d" % sb])
                hdst = H[:, hc * 2048 + seg * 512: hc * 2048 + (seg + 1) * 512]
                V(lambda e, hdst=hdst, gs=gs, sb=sb: e.tensor_tensor(
                    hdst, gs, PB[1][:, sb * 512:(sb + 1) * 512], ALU.mult),
                  reads=["f_gs%d" % sb, uk], writes=["f_H%d" % hc])

        for oc in range(8):
            oi = oc % 2
            wdv = vbf(oWD + oi * 6144, 3072)
            sd = ins["wd"][oc, :, :].rearrange("(kt p) d -> p kt d", p=128)
            p.dma("G", "fwd%d" % oi, lambda e, sd=sd, wdv=wdv: e.dma_start(wdv, sd),
                  writes=["f_wd%d" % oi, "R_FXG0"])
            for seg in range(4):
                sb = seg % 2
                dk = "pb2a" if sb == 0 else "pb2b"
                for kt in range(NHC):
                    mv = H[:, kt * 2048 + seg * 512: kt * 2048 + (seg + 1) * 512]
                    T(lambda e, kt=kt, mv=mv, wdv=wdv, sb=sb: e.matmul(
                        PB[2][:, sb * 512:(sb + 1) * 512],
                        wdv[:, kt * 128:(kt + 1) * 128], mv,
                        start=(kt == 0), stop=(kt == NHC - 1)),
                      reads=["f_wd%d" % oi, "f_H%d" % kt], writes=[dk])
                xs = v32(oXS + seg * 2048, 512)
                xsk = "R_FXG0" if seg < 2 else "R_FXG1"
                srcx = src_in[oc * 128:(oc + 1) * 128, seg * 512:(seg + 1) * 512]
                p.dma("SY", "fx2%d" % seg,
                      lambda e, srcx=srcx, xs=xs: e.dma_start(xs, srcx),
                      writes=[xsk])
                xo = v32(oXG + 28672 + (seg % 2) * 2048, 512)
                xok = "R_FXG1"
                V(lambda e, xo=xo, xs=xs, sb=sb: e.scalar_tensor_tensor(
                    xo, PB[2][:, sb * 512:(sb + 1) * 512], 1.0 / 127.0, xs,
                    ALU.mult, ALU.add),
                  reads=[dk, xsk], writes=[xok])
                dstx = dst_out[oc * 128:(oc + 1) * 128, seg * 512:(seg + 1) * 512]
                p.dma("SY", "fxo%d" % (seg % 2),
                      lambda e, dstx=dstx, xo=xo: e.dma_start(dstx, xo),
                      reads=[xok], writes=[])

    def phase_attn_b():
        oXG = 0
        oSQ = oXG + 16384
        oCS = oSQ + 16384
        oRX = oCS + 16384
        oQT = oRX + 32768
        oST = oQT + 32768
        oW = oST + 17408
        oPP = 0
        oXS = 4096
        oXO2 = oXS + 8192
        assert oW + 32768 <= ARB

        rxn = vbf(oRX, 16384)

        def lnb(g):
            xg = v32(oXG, 4096)
            xgk = "R_BXG"
            srcg = xbuf.rearrange("(kc p) t -> p kc t", p=128)[:, :, g * 512:(g + 1) * 512]
            p.dma("SY", "bxin", lambda e, srcg=srcg, xg=xg: e.dma_start(xg, srcg),
                  writes=[xgk])
            xsq = v32(oSQ, 4096)
            ln_stats(xg, xgk, xsq, "R_BSQ", 512, True, 1.0,
                     PB[3][:, 0:512], PB[3][:, 512:1024], "pb3a", "pb3b", ones_c)
            xn = v32(oSQ, 4096)
            ln_apply(r3(xg, 512), xgk, r3(xg, 512), "R_BXG", r3(xn, 512), "R_BSQ", 512,
                     PB[3][:, 0:512], PB[3][:, 512:1024], "pb3a", "pb3b")
            for cc in range(4):
                csrc = ins["cosT"][cc * 128:(cc + 1) * 128, g * 512:(g + 1) * 512]
                ssrc = ins["sinT"][cc * 128:(cc + 1) * 128, g * 512:(g + 1) * 512]
                cosb = v32(oCS, 512)
                sinb = v32(oCS + 8192, 512)
                p.dma("SY", "bcos", lambda e, csrc=csrc, cosb=cosb: e.dma_start(cosb, csrc),
                      writes=["b_cos"])
                p.dma("SY", "bsin", lambda e, ssrc=ssrc, sinb=sinb: e.dma_start(sinb, ssrc),
                      writes=["b_sin"])
                x1 = xn[:, cc * 512:(cc + 1) * 512]
                x2 = xn[:, (cc + 4) * 512:(cc + 5) * 512]
                t1 = v32(oCS + 2048, 512)
                t2 = v32(oCS + 8192 + 2048, 512)
                r1 = rxn[:, cc * 2048 + g * 512: cc * 2048 + (g + 1) * 512]
                r2 = rxn[:, (cc + 4) * 2048 + g * 512: (cc + 4) * 2048 + (g + 1) * 512]
                V(lambda e, t1=t1, x1=x1, cosb=cosb: e.tensor_tensor(t1, x1, cosb, ALU.mult),
                  reads=["R_BSQ", "b_cos"], writes=["b_t1"])
                V(lambda e, t2=t2, x2=x2, sinb=sinb: e.tensor_tensor(t2, x2, sinb, ALU.mult),
                  reads=["R_BSQ", "b_sin"], writes=["b_t2"])
                V(lambda e, r1=r1, t1=t1, t2=t2: e.tensor_tensor(r1, t1, t2, ALU.subtract),
                  reads=["b_t1", "b_t2"], writes=["R_RX", "R_RXg%d" % g])
                V(lambda e, t1=t1, x1=x1, sinb=sinb: e.tensor_tensor(t1, x1, sinb, ALU.mult),
                  reads=["R_BSQ", "b_sin"], writes=["b_t1"])
                V(lambda e, t2=t2, x2=x2, cosb=cosb: e.tensor_tensor(t2, x2, cosb, ALU.mult),
                  reads=["R_BSQ", "b_cos"], writes=["b_t2"])
                V(lambda e, r2=r2, t1=t1, t2=t2: e.tensor_tensor(r2, t1, t2, ALU.add),
                  reads=["b_t1", "b_t2"], writes=["R_RX", "R_RXg%d" % g])

        wslots = [vbf(oW, 8192), vbf(oW + 16384, 8192)]
        QTb = vbf(oQT, 16384)
        stg = [vbf(oST + i * 4352, 2048) for i in range(4)]

        def proj_b(nm, sl, tg):
            bias_nm = {"bWq": "bbq", "bWk": "bbk", "bWv": "bbv", "bWo": "bbo"}[nm]
            ccdst = {"bWk": cc_in_k, "bWv": cc_in_v}.get(nm)
            for oc in range(8):
                pbi = (oc + tg) % 2
                pbks = ["pb%da" % pbi, "pb%db" % pbi]
                for kc in range(8):
                    for q in range(2):
                        mv = rxn[:, kc * 2048 + tg * 1024 + q * 512:
                                 kc * 2048 + tg * 1024 + (q + 1) * 512]
                        T(lambda e, kc=kc, oc=oc, mv=mv, q=q, pbi=pbi, sl=sl: e.matmul(
                            PB[pbi][:, q * 512:(q + 1) * 512],
                            wslots[sl][:, kc * 1024 + oc * 128: kc * 1024 + (oc + 1) * 128],
                            mv, start=(kc == 0), stop=(kc == 7)),
                          reads=["R_BW%d" % sl, "R_RXg%d" % (tg * 2 + q)],
                          writes=[pbks[q]])
                if nm == "bWq":
                    dst = QTb[:, oc * 2048 + tg * 1024: oc * 2048 + (tg + 1) * 1024]
                    S(lambda e, dst=dst, oc=oc, pbi=pbi: e.activation(
                        dst, PB[pbi][:, 0:1024], AF.Identity,
                        bias=bcol("bbq", oc), scale=consts[:, 1:2]),
                      reads=pbks + ["biases", "consts"], writes=["b_QT"])
                else:
                    slot = (0 if nm == "bWk" else 2) + (oc + tg) % 2
                    sgb = stg[slot][:, 0:1024]
                    skey = "R_VT%d" % slot
                    S(lambda e, sgb=sgb, oc=oc, pbi=pbi, bias_nm=bias_nm: e.activation(
                        sgb, PB[pbi][:, 0:1024], AF.Identity,
                        bias=bcol(bias_nm, oc), scale=consts[:, 1:2]),
                      reads=pbks + ["biases", "consts"], writes=[skey])
                    dd = ccdst[oc * 128:(oc + 1) * 128, tg * 1024:(tg + 1) * 1024]
                    p.dma("G", "bstg%d" % slot,
                          lambda e, dd=dd, sgb=sgb: e.dma_start(dd, sgb),
                          reads=[skey], writes=["cc_in" + nm])

        srcw = ins["bWk"].rearrange("(kc p) d -> p kc d", p=128)
        p.dma("G", "bw0", lambda e, srcw=srcw: e.dma_start(wslots[0], srcw),
              writes=["R_BW0"])
        srcw = ins["bWv"].rearrange("(kc p) d -> p kc d", p=128)
        p.dma("G", "bw1", lambda e, srcw=srcw: e.dma_start(wslots[1], srcw),
              writes=["R_BW1"])
        lnb(0)
        lnb(1)
        proj_b("bWk", 0, 0)
        lnb(2)
        proj_b("bWv", 1, 0)
        lnb(3)
        proj_b("bWk", 0, 1)
        p.async_op("G", "cck", lambda e: e.collective_compute(
            "AllGather", ALU.bypass, [[0, 1], [2, 3], [4, 5], [6, 7]],
            [cc_in_k.opt()], [cc_out_k.opt()]),
            reads=["cc_inbWk", "R_VT0", "R_VT1"], writes=["cc_k"])
        proj_b("bWv", 1, 1)
        p.async_op("G", "ccv", lambda e: e.collective_compute(
            "AllGather", ALU.bypass, [[0, 1], [2, 3], [4, 5], [6, 7]],
            [cc_in_v.opt()], [cc_out_v.opt()]),
            reads=["cc_inbWv", "R_VT2", "R_VT3"], writes=["cc_v"])
        srcw = ins["bWq"].rearrange("(kc p) d -> p kc d", p=128)
        p.dma("G", "bw0b", lambda e, srcw=srcw: e.dma_start(wslots[0], srcw),
              writes=["R_BW0"])
        proj_b("bWq", 0, 0)
        proj_b("bWq", 0, 1)

        woview = wslots[0]
        srcw = ins["bWo"].rearrange("(kc p) d -> p kc d", p=128)
        p.dma("G", "bwo", lambda e, srcw=srcw: e.dma_start(woview, srcw),
              writes=["R_BW0"])

        OTb = vbf(oW + 16384, 16384)
        kp = [vbf(oRX + i * 8192, 4096) for i in range(2)]
        vp = [vbf(oRX + 16384 + i * 8192, 4096) for i in range(2)]
        vtb = [vbf(oST + i * 4352, 2080) for i in range(4)]

        def stage_thunks(gp):
            h0, h1 = 2 * gp, 2 * gp + 1
            pi = gp % 2
            kpk, vpk = "R_RXK%d" % pi, "R_RXV%d" % pi
            for (hh, rows) in ((h0, (0, 64)), (h1, (64, 128))):
                for (hs, cols) in ((0, (0, 2048)), (1, (2048, 4096))):
                    def _ld(hh=hh, rows=rows, hs=hs, cols=cols):
                        srck = cc_out_k[hs * D + hh * 64: hs * D + hh * 64 + 64, :]
                        dstk = kp[pi][rows[0]:rows[1], cols[0]:cols[1]]
                        p.dma("SY", "bkp%d" % pi,
                              lambda e, dstk=dstk, srck=srck: e.dma_start(dstk, srck),
                              reads=["cc_k"], writes=[kpk] + (["R_RX", "R_RXg0", "R_RXg1",
                                                               "R_RXg2", "R_RXg3"] if gp < 2 else []))
                        srcv = cc_out_v[hs * D + hh * 64: hs * D + hh * 64 + 64, :]
                        dstv = vp[pi][rows[0]:rows[1], cols[0]:cols[1]]
                        p.dma("SY", "bvp%d" % pi,
                              lambda e, dstv=dstv, srcv=srcv: e.dma_start(dstv, srcv),
                              reads=["cc_v"], writes=[vpk] + (["R_RX", "R_RXg0", "R_RXg1",
                                                               "R_RXg2", "R_RXg3"] if gp < 2 else []))
                    yield _ld
            s0, s1 = 2 * pi, 2 * pi + 1
            k0n, k1n = "R_VT%d" % s0, "R_VT%d" % s1
            def _ms():
                V(lambda e: e.memset(free_dims(vtb[s0][:, 64:65], [[65, 32]]), 1.0),
                  writes=[k0n])
                V(lambda e: e.memset(free_dims(vtb[s1][:, 64:65], [[65, 32]]), 1.0),
                  writes=[k1n])
            yield _ms
            for tt in range(0, 32, 4):
                def _tr(tt=tt):
                    for t in range(tt, tt + 4):
                        T(lambda e, t=t: e.matmul(
                            PB[3][:, 0:64], vp[pi][0:64, t * 128:(t + 1) * 128],
                            i64_sb[0:64, :], start=True, stop=True,
                            tile_position=(0, 0)),
                          reads=[vpk, "i64_sb"], writes=["pb3a"])
                        V(lambda e, t=t: e.tensor_copy(vtb[s0][:, t * 65:t * 65 + 64],
                                                       PB[3][:, 0:64]),
                          reads=["pb3a"], writes=[k0n])
                        T(lambda e, t=t: e.matmul(
                            PB[3][:, 512:576], vp[pi][64:128, t * 128:(t + 1) * 128],
                            i64_sb[64:128, :], start=True, stop=True,
                            tile_position=(64, 0)),
                          reads=[vpk, "i64_sb"], writes=["pb3b"])
                        V(lambda e, t=t: e.tensor_copy(vtb[s1][:, t * 65:t * 65 + 64],
                                                       PB[3][:, 512:576]),
                          reads=["pb3b"], writes=[k1n])
                yield _tr

        def A_b(gp, feeder):
            h0, h1 = 2 * gp, 2 * gp + 1
            pi = gp % 2
            kpk = "R_RXK%d" % pi
            s0, s1 = 2 * pi, 2 * pi + 1
            k0n, k1n = "R_VT%d" % s0, "R_VT%d" % s1
            for g in range(4):
                q0g = g * 512
                q_h0 = QTb[0:64, (h0 // 2) * 2048 + q0g: (h0 // 2) * 2048 + q0g + 512]
                q_h1 = QTb[64:128, (h1 // 2) * 2048 + q0g: (h1 // 2) * 2048 + q0g + 512]
                def emit_pvb(t):
                    pp = vbf(oPP + (t % 2) * 2048, 1024)
                    ppk = "b_pp%d" % (t % 2)
                    T(lambda e, t=t, pp=pp, s0=s0: e.matmul(
                        PB[2][0:65, 0:512], vtb[s0][:, t * 65:t * 65 + 65], pp[:, 0:512],
                        start=(t == 0), stop=(t == 31)),
                      reads=[k0n, ppk], writes=["pb2a"])
                    T(lambda e, t=t, pp=pp, s1=s1: e.matmul(
                        PB[2][0:65, 512:1024], vtb[s1][:, t * 65:t * 65 + 65], pp[:, 512:1024],
                        start=(t == 0), stop=(t == 31)),
                      reads=[k1n, ppk], writes=["pb2b"])
                for t in range(32):
                    sp = PB[t % 2]
                    ska, skb = "pb%da" % (t % 2), "pb%db" % (t % 2)
                    T(lambda e, t=t, sp=sp, q_h0=q_h0, pi=pi: e.matmul(
                        sp[:, 0:512], kp[pi][0:64, t * 128:(t + 1) * 128], q_h0,
                        start=True, stop=True, tile_position=(0, 0)),
                      reads=[kpk, "b_QT"], writes=[ska])
                    T(lambda e, t=t, sp=sp, q_h1=q_h1, pi=pi: e.matmul(
                        sp[:, 512:1024], kp[pi][64:128, t * 128:(t + 1) * 128], q_h1,
                        start=True, stop=True, tile_position=(64, 0)),
                      reads=[kpk, "b_QT"], writes=[skb])
                    pp = vbf(oPP + (t % 2) * 2048, 1024)
                    ppk = "b_pp%d" % (t % 2)
                    S(lambda e, pp=pp, sp=sp: e.activation(
                        pp, sp[:, :], AF.Exp, bias=consts[:, 2:3], scale=consts[:, 3:4]),
                      reads=[ska, skb, "consts"], writes=[ppk])
                    if t > 0:
                        emit_pvb(t - 1)
                    if t % 2 == 0:
                        th = next(feeder, None)
                        if th is not None:
                            th()
                emit_pvb(31)
                for hi2, pvk in ((0, "pb2a"), (1, "pb2b")):
                    hh = h0 if hi2 == 0 else h1
                    V(lambda e, hi2=hi2: e.reciprocal(
                        recip_sb[:], PB[2][64:65, hi2 * 512: hi2 * 512 + 512]),
                      reads=[pvk], writes=["recip"])
                    T(lambda e: e.matmul(PB[3][0:64, 512:1024], ones_r[0:1, 0:64],
                                         recip_sb[:], start=True, stop=True),
                      reads=["ones_r", "recip"], writes=["pb3b"])
                    V(lambda e: e.tensor_copy(rb_sb[:], PB[3][0:64, 512:1024]),
                      reads=["pb3b"], writes=["rb"])
                    dsto = OTb[(hh % 2) * 64:(hh % 2) * 64 + 64,
                               (hh // 2) * 2048 + q0g: (hh // 2) * 2048 + q0g + 512]
                    V(lambda e, dsto=dsto, hi2=hi2: e.tensor_tensor(
                        dsto, PB[2][0:64, hi2 * 512: hi2 * 512 + 512], rb_sb[:], ALU.mult),
                      reads=[pvk, "rb"], writes=["R_BW1"])

        import itertools as _it
        for th in stage_thunks(0):
            th()
        for gp in range(8):
            feeder = stage_thunks(gp + 1) if gp < 7 else iter(())
            A_b(gp, feeder)
            for th in feeder:
                th()

        def evwo(oc, tg, pb, pbks):
            oi = (oc * 2 + tg) % 2
            xs = v32(oXS + oi * 4096, 1024)
            xsk = "R_BXG"
            srcx = xbuf[oc * 128:(oc + 1) * 128, tg * 1024:(tg + 1) * 1024]
            p.dma("SY", "bx2%d" % oi,
                  lambda e, srcx=srcx, xs=xs: e.dma_start(xs, srcx),
                  writes=[xsk, "R_BXG", "R_BXG", "R_BXG"])
            xo = v32(oXO2 + oi * 4096, 1024)
            xok = "R_BSQ"
            V(lambda e, xo=xo, xs=xs, pb=pb, oc=oc: e.scalar_tensor_tensor(
                xo, pb, bcol("bbo", oc), xs, ALU.add, ALU.add),
              reads=list(pbks) + ["biases", xsk], writes=["R_BSQ", "R_BXG"])
            dstx = xbuf[oc * 128:(oc + 1) * 128, tg * 1024:(tg + 1) * 1024]
            p.dma("SY", "bxo%d" % oi,
                  lambda e, dstx=dstx, xo=xo: e.dma_start(dstx, xo),
                  reads=[xok], writes=[])
        proj(woview, "R_BW0", OTb, "R_BW1", TOK, evwo)

    def dbg_tap(i):
        if not _DEBUG_TAPS:
            return
        for kc in range(8):
            t = v32(0, 2048)
            p.dma("SY", "dbgl", lambda e, kc=kc, t=t: e.dma_start(
                t, xbuf[kc * 128:(kc + 1) * 128, :]), writes=["dbgt"])
            p.dma("SY", "dbgs", lambda e, kc=kc, i=i, t=t: e.dma_start(
                dbg_taps[i][kc * 128:(kc + 1) * 128, :], t), reads=["dbgt"], writes=[])
        p.barrier()

    phase_attn_a()
    p.barrier()
    dbg_tap(0)
    phase_ffn(last=False)
    p.barrier()
    dbg_tap(1)
    phase_attn_b()
    p.barrier()
    dbg_tap(2)
    phase_ffn(last=True)
    p.barrier()

    p.finalize()
    with nc.Block() as block:
        @block.tensor
        def _(eng):
            p.emit("T", eng, sems)

        @block.vector
        def _(eng):
            p.emit("V", eng, sems)

        @block.scalar
        def _(eng):
            p.emit("S", eng, sems)

        @block.gpsimd
        def _(eng):
            p.emit("G", eng, sems)

        @block.sync
        def _(eng):
            p.emit("SY", eng, sems)
    stack.close()
    return nc


_NC_CACHE = None


def kernel(L0_attn_a_x, L0_attn_a_Wq, L0_attn_a_bq, L0_attn_a_Wk, L0_attn_a_bk,
           L0_attn_a_Wv, L0_attn_a_bv, L0_attn_a_Wo, L0_attn_a_bo,
           L0_attn_a_rel_bias, L0_attn_a_W1, L0_attn_a_b1, L0_attn_a_W2,
           L0_attn_a_b2, L0_attn_b_Wq, L0_attn_b_bq, L0_attn_b_Wk, L0_attn_b_bk,
           L0_attn_b_Wv, L0_attn_b_bv, L0_attn_b_Wo, L0_attn_b_bo,
           L0_ffn_w_gate, L0_ffn_w_up, L0_ffn_w_down):
    global _NC_CACHE
    oh, cosT, sinT, i64 = _host_consts()

    def wT(w):
        return np.ascontiguousarray(np.asarray(w, np.float32).T)

    def bR(b):
        return np.ascontiguousarray(np.asarray(b, np.float32).reshape(8, 128).T)

    x = np.asarray(L0_attn_a_x, np.float32)
    common = {
        "aWq": wT(L0_attn_a_Wq), "aWk": wT(L0_attn_a_Wk), "aWv": wT(L0_attn_a_Wv),
        "aWo": wT(L0_attn_a_Wo), "aW1": wT(L0_attn_a_W1), "aW2": wT(L0_attn_a_W2),
        "bWq": wT(L0_attn_b_Wq), "bWk": wT(L0_attn_b_Wk), "bWv": wT(L0_attn_b_Wv),
        "bWo": wT(L0_attn_b_Wo),
        "abq": bR(L0_attn_a_bq), "abk": bR(L0_attn_a_bk), "abv": bR(L0_attn_a_bv),
        "abo": bR(L0_attn_a_bo), "ab1": bR(L0_attn_a_b1), "ab2": bR(L0_attn_a_b2),
        "bbq": bR(L0_attn_b_bq), "bbk": bR(L0_attn_b_bk), "bbv": bR(L0_attn_b_bv),
        "bbo": bR(L0_attn_b_bo),
        "onehot": oh, "i64": i64,
        "wg": np.ascontiguousarray(
            np.asarray(L0_ffn_w_gate, np.int32).T.reshape(D, NHC, 128).transpose(1, 0, 2)),
        "wu": np.ascontiguousarray(
            np.asarray(L0_ffn_w_up, np.int32).T.reshape(D, NHC, 128).transpose(1, 0, 2)),
        "wd": np.ascontiguousarray(
            np.asarray(L0_ffn_w_down, np.int32).T.reshape(HID, 8, 128).transpose(1, 0, 2)),
    }
    in_maps = []
    for core in range(8):
        b, hh = core // 2, core % 2
        m = dict(common)
        m["xT"] = np.ascontiguousarray(x[b, hh * TOK:(hh + 1) * TOK, :].T)
        relroll = np.ascontiguousarray(
            np.roll(np.asarray(L0_attn_a_rel_bias, np.float32), -8 * hh, axis=1))
        m["relT"] = relroll
        m["relflat"] = np.ascontiguousarray(relroll.reshape(1, 512))
        m["cosT"] = np.ascontiguousarray(cosT[:, hh * TOK:(hh + 1) * TOK])
        m["sinT"] = np.ascontiguousarray(sinT[:, hh * TOK:(hh + 1) * TOK])
        in_maps.append(m)

    if _NC_CACHE is None:
        _NC_CACHE = build_program()
    res = run_bass_kernel_spmd(_NC_CACHE, in_maps, core_ids=list(range(8)))
    out = np.empty((4, 4096, D), np.float32)
    for core in range(8):
        b, hh = core // 2, core % 2
        out[b, hh * TOK:(hh + 1) * TOK, :] = res.results[core]["yT"].T
    return out


# version 25
# speedup vs baseline: 1.1118x; 1.0938x over previous
import math
import contextlib
import numpy as np
import ml_dtypes
import concourse.bass as bass
import concourse.mybir as mybir
from concourse.bass_utils import run_bass_kernel_spmd

F32 = mybir.dt.float32
BF16 = mybir.dt.bfloat16
I32 = mybir.dt.int32
AF = mybir.ActivationFunctionType
ALU = mybir.AluOpType
BF = ml_dtypes.bfloat16

D = 1024
TOK = 2048
NCHUNK = 8
CH = 256
HID = 3072
NHC = HID // 128
C_A = 92.0
C_B = 4.0
ENGS = ("T", "V", "S", "G", "SY")
_DEBUG_TAPS = False
ARCOLS = 47000
ARB = ARCOLS * 4


def _bucket_of_n(n):
    ret = 16 if n < 0 else 0
    n = abs(n)
    if n < 8:
        return ret + n
    v = 8 + int(np.float32(np.log(np.float32(n) / np.float32(8.0)))
                / np.float32(math.log(128.0 / 8.0)) * np.float32(8))
    return ret + min(v, 15)


def _host_consts():
    oh = np.zeros((32, 1280), np.float32)
    for i in range(1280):
        d = i - 640
        if abs(d) <= 90:
            oh[_bucket_of_n(-d), i] = 1.0
    freq = np.exp(np.arange(0, D, 2, dtype=np.float32)
                  * np.float32(-np.log(10000.0) / D))
    ang = np.arange(4096, dtype=np.float32)[:, None] * freq[None, :]
    cosT = np.ascontiguousarray(np.cos(ang).T.astype(np.float32))
    sinT = np.ascontiguousarray(np.sin(ang).T.astype(np.float32))
    i64 = np.zeros((128, 64), np.float32)
    i64[:64] = np.eye(64, dtype=np.float32)
    i64[64:] = np.eye(64, dtype=np.float32)
    return oh, cosT, sinT, i64


class Prog:
    def __init__(self, nc, stack):
        self.nc = nc
        self.stack = stack
        self.ops = {e: [] for e in ENGS}
        self.bufs = {}
        self.dsems = {}

    def dma_sem(self, name):
        if name not in self.dsems:
            sem = self.stack.enter_context(self.nc.semaphore("ds_" + name))
            self.dsems[name] = {"sem": sem, "n": 0}
        return self.dsems[name]

    def _deps(self, reads, writes, eng, selfsync):
        deps = []
        for b in reads:
            st = self.bufs.get(b)
            if st and st["w"] is not None:
                deps.append(st["w"])
        for b in writes:
            st = self.bufs.get(b)
            if st:
                if st["w"] is not None:
                    deps.append(st["w"])
                deps.extend(st["r"])
        out, seen = [], set()
        for d in deps:
            if d[0] == "E" and d[1] == eng and not selfsync:
                continue
            if d[0] == "D":
                d = ("D", d[1], self.dsems[d[1]]["n"])
            if d not in seen:
                seen.add(d)
                out.append(d)
        return out

    def _register(self, tok, reads, writes):
        for b in writes:
            self.bufs[b] = {"w": tok, "r": []}
        for b in reads:
            st = self.bufs.setdefault(b, {"w": None, "r": []})
            st["r"].append(tok)

    def op(self, eng, fn, reads=(), writes=(), selfsync=True):
        idx = len(self.ops[eng])
        o = {"kind": "op", "fn": fn,
             "deps": self._deps(reads, writes, eng, selfsync),
             "need_inc": False, "inc_no": None}
        self.ops[eng].append(o)
        tok = ("E", eng, idx)
        self._register(tok, reads, writes)
        return tok

    def dma(self, eng, stream, fn, reads=(), writes=()):
        ds = self.dma_sem(stream)
        deps = self._deps(reads, writes, eng, True)
        ds["n"] += 16
        o = {"kind": "dma", "fn": fn, "deps": deps, "sem": ds["sem"],
             "need_inc": False, "inc_no": None}
        self.ops[eng].append(o)
        tok = ("D", stream, ds["n"])
        self._register(tok, reads, writes)
        return tok

    def async_op(self, eng, stream, fn, reads=(), writes=()):
        ds = self.dma_sem(stream)
        deps = self._deps(reads, writes, eng, True)
        ds["n"] += 1
        o = {"kind": "dma", "fn": fn, "deps": deps, "sem": ds["sem"],
             "inc_amt": 1, "need_inc": False, "inc_no": None}
        self.ops[eng].append(o)
        tok = ("D", stream, ds["n"])
        self._register(tok, reads, writes)
        return tok

    def barrier(self):
        lasts = []
        for e in ENGS:
            if self.ops[e]:
                lasts.append(("E", e, len(self.ops[e]) - 1))
        for name, ds in self.dsems.items():
            lasts.append(("D", name, ds["n"]))
        for e in ENGS:
            deps = [d for d in lasts if not (d[0] == "E" and d[1] == e)]
            self.ops[e].append({"kind": "op", "fn": lambda eng: eng.nop(),
                                "deps": deps, "need_inc": False, "inc_no": None})
        self.bufs = {}

    def finalize(self):
        for e in ENGS:
            for o in self.ops[e]:
                for d in o["deps"]:
                    if d[0] == "E":
                        self.ops[d[1]][d[2]]["need_inc"] = True
        for e in ENGS:
            c = 0
            for o in self.ops[e]:
                if o["need_inc"]:
                    c += 1
                    o["inc_no"] = c

    def emit(self, ename, eng, sems):
        waited = {}
        for o in self.ops[ename]:
            for d in o["deps"]:
                if d[0] == "E":
                    _, e2, idx = d
                    cnt = self.ops[e2][idx]["inc_no"]
                    key = ("E", e2)
                    if waited.get(key, 0) >= cnt:
                        continue
                    waited[key] = cnt
                    eng.wait_ge(sems[e2], cnt)
                else:
                    _, sname, cnt = d
                    key = ("D", sname)
                    if waited.get(key, 0) >= cnt:
                        continue
                    waited[key] = cnt
                    eng.wait_ge(self.dsems[sname]["sem"], cnt)
            inst = o["fn"](eng)
            if o["kind"] == "dma":
                inst.then_inc(o["sem"], o.get("inc_amt", 16))
                if o["need_inc"]:
                    eng.nop().then_inc(sems[ename], 1)
            elif o["need_inc"]:
                inst.then_inc(sems[ename], 1)


def build_program():
    nc = bass.Bass("TRN2", target_bir_lowering=False)
    dt = nc.dram_tensor
    ins = {}
    xT = dt("xT", [D, TOK], F32, kind="ExternalInput").ap()
    for nm in ["aWq", "aWk", "aWv", "aWo", "aW1", "aW2", "bWq", "bWk", "bWv", "bWo"]:
        ins[nm] = dt(nm, [D, D], F32, kind="ExternalInput").ap()
    bias_keys = ["abq", "abk", "abv", "abo", "ab1", "ab2", "bbq", "bbk", "bbv", "bbo"]
    for nm in bias_keys:
        ins[nm] = dt(nm, [128, 8], F32, kind="ExternalInput").ap()
    ins["relflat"] = dt("relflat", [1, 512], F32, kind="ExternalInput").ap()
    ins["relT"] = dt("relT", [32, 16], F32, kind="ExternalInput").ap()
    ins["onehot"] = dt("onehot", [32, 1280], F32, kind="ExternalInput").ap()
    ins["cosT"] = dt("cosT", [512, TOK], F32, kind="ExternalInput").ap()
    ins["sinT"] = dt("sinT", [512, TOK], F32, kind="ExternalInput").ap()
    ins["i64"] = dt("i64", [128, 64], F32, kind="ExternalInput").ap()
    ins["wg"] = dt("wg", [NHC, D, 128], I32, kind="ExternalInput").ap()
    ins["wu"] = dt("wu", [NHC, D, 128], I32, kind="ExternalInput").ap()
    ins["wd"] = dt("wd", [8, HID, 128], I32, kind="ExternalInput").ap()
    yT = dt("yT", [D, TOK], F32, kind="ExternalOutput").ap()
    dbg_taps = []
    if _DEBUG_TAPS:
        for i in range(3):
            dbg_taps.append(dt("dbg%d" % i, [D, TOK], F32, kind="ExternalOutput").ap())
        dbgA = dt("dbgA", [128, 2048], F32, kind="ExternalOutput").ap()
        dbgB = dt("dbgB", [128, 4352], F32, kind="ExternalOutput").ap()
        dbgC = dt("dbgC", [128, 2080], F32, kind="ExternalOutput").ap()
        dbgD = dt("dbgD", [128, 4096], F32, kind="ExternalOutput").ap()
        dbgE = dt("dbgE", [128, 1152], F32, kind="ExternalOutput").ap()
    xbuf = dt("xbuf", [D, TOK], F32, kind="Internal").ap()
    bandh_t = dt("bandh", [2, 1280], BF16, kind="Internal")
    bandh = bandh_t.ap()
    cc_in_k = dt("cc_in_k", [D, TOK], BF16, kind="Internal").ap()
    cc_out_k = dt("cc_out_k", [2 * D, TOK], BF16, kind="Internal").ap()
    cc_in_v = dt("cc_in_v", [D, TOK], BF16, kind="Internal").ap()
    cc_out_v = dt("cc_out_v", [2 * D, TOK], BF16, kind="Internal").ap()

    stack = contextlib.ExitStack()
    E = stack.enter_context
    AR = E(nc.sbuf_tensor("AR", [128, ARCOLS], F32))
    biases = E(nc.sbuf_tensor("biases", [128, 80], F32))
    consts = E(nc.sbuf_tensor("consts", [128, 8], F32))
    rel_sb = E(nc.sbuf_tensor("rel_sb", [32, 16], F32))
    relc_sb = E(nc.sbuf_tensor("relc_sb", [32, 1], BF16))
    oh_sb = E(nc.sbuf_tensor("oh_sb", [32, 1280], BF16))
    wbc = E(nc.sbuf_tensor("wbc", [64, 512], F32))
    wbcn = E(nc.sbuf_tensor("wbcn", [64, 512], F32))
    i64_sb = E(nc.sbuf_tensor("i64_sb", [128, 64], BF16))
    ones_c = E(nc.sbuf_tensor("ones_c", [128, 1], F32))
    ones_cb = E(nc.sbuf_tensor("ones_cb", [128, 1], BF16))
    ones_r = E(nc.sbuf_tensor("ones_r", [1, 128], F32))
    ones_rb = E(nc.sbuf_tensor("ones_rb", [1, 128], BF16))
    ones512 = E(nc.sbuf_tensor("ones512", [1, 512], BF16))
    band_sb = E(nc.sbuf_tensor("band_sb", [1, 1280], BF16))
    recip_sb = E(nc.sbuf_tensor("recip_sb", [1, 512], F32))
    w15row = E(nc.sbuf_tensor("w15row", [1, 1024], BF16))
    totT = E(nc.sbuf_tensor("totT", [1, 128], BF16))
    rb_sb = E(nc.sbuf_tensor("rb_sb", [64, 512], F32))
    rowA = E(nc.sbuf_tensor("rowA", [1, 512], F32))
    rowB = E(nc.sbuf_tensor("rowB", [1, 512], F32))
    rowC = E(nc.sbuf_tensor("rowC", [1, 520], F32))
    zcol64 = E(nc.sbuf_tensor("zcol64", [64, 1], F32))
    PB = [E(nc.psum_tensor(f"pb{i}", [128, 1024], F32)) for i in range(4)]
    sems = {e: E(nc.semaphore("sem_" + e)) for e in ENGS}

    p = Prog(nc, stack)
    arf = AR[:]
    arb_all = bass.AP(arf.tensor, 0, list(arf.ap)).bitcast(BF16)

    def v32(off_b, cols):
        assert off_b % 4 == 0 and off_b + cols * 4 <= ARB, (off_b, cols)
        return arf[:, off_b // 4: off_b // 4 + cols]

    def vbf(off_b, cols):
        assert off_b % 2 == 0 and off_b + cols * 2 <= ARB, (off_b, cols)
        return arb_all[:, off_b // 2: off_b // 2 + cols]

    def T(fn, reads=(), writes=()):
        return p.op("T", fn, reads, writes, selfsync=False)

    def V(fn, reads=(), writes=()):
        return p.op("V", fn, reads, writes)

    def S(fn, reads=(), writes=()):
        return p.op("S", fn, reads, writes)

    def G(fn, reads=(), writes=()):
        return p.op("G", fn, reads, writes)

    def bcol(nm, oc):
        i = bias_keys.index(nm)
        return biases[:, 8 * i + oc: 8 * i + oc + 1]

    def free_dims(ap0, dims):
        return bass.AP(ap0.tensor, ap0.offset, [list(ap0.ap[0])] + dims)

    V(lambda e: e.memset(ones_c[:], 1.0), writes=["ones_c"])
    V(lambda e: e.memset(ones_cb[:], 1.0), writes=["ones_cb"])
    V(lambda e: e.memset(ones_r[:], 1.0), writes=["ones_r"])
    V(lambda e: e.memset(ones_rb[:], 1.0), writes=["ones_rb"])
    V(lambda e: e.memset(ones512[:], 1.0), writes=["ones512"])
    V(lambda e: e.memset(zcol64[:], 0.0), writes=["zc"])
    for col, val in [(0, -C_A), (1, 1.0), (2, -C_B), (3, 0.125),
                     (4, 1e-5), (5, -0.5), (6, 0.0)]:
        V(lambda e, c=col, vv=val: e.memset(consts[:, c:c + 1], vv), writes=["consts"])
    p.dma("G", "cinit", lambda e: e.dma_start(rel_sb[:], ins["relT"]), writes=["rel_sb"])
    p.dma("G", "cinit", lambda e: e.dma_start(oh_sb[:], ins["onehot"]), writes=["oh_sb"])
    p.dma("G", "cinit", lambda e: e.dma_start(i64_sb[:], ins["i64"]), writes=["i64_sb"])
    relf = rowB[0:1, 0:512]
    p.dma("G", "cinit", lambda e: e.dma_start(relf, ins["relflat"]), writes=["relf"])
    for i, nm in enumerate(bias_keys):
        p.dma("G", "cinit",
              lambda e, i=i, nm=nm: e.dma_start(biases[:, 8 * i:8 * i + 8], ins[nm]),
              writes=["biases"])
    T(lambda e: e.matmul(PB[3][0:64, 0:512], ones_r[0:1, 0:64], relf,
                         start=True, stop=True),
      reads=["ones_r", "relf"], writes=["pb3a"])
    V(lambda e: e.tensor_copy(wbc[:], PB[3][0:64, 0:512]), reads=["pb3a"], writes=["wbc"])
    V(lambda e: e.tensor_scalar(wbcn[:], wbc[:], -1.0, None, ALU.mult),
      reads=["wbc"], writes=["wbcn"])

    def ln_stats(xg, xk, xsq, sqk, gw, eps, scale, pba, pbb, ka, kb, onesq):
        V(lambda e: e.tensor_tensor(xsq, xg, xg, ALU.mult), reads=[xk], writes=[sqk])
        for kc in range(8):
            T(lambda e, kc=kc: e.matmul(pba[0:1, 0:gw], ones_c[:],
                                        xg[:, kc * gw:(kc + 1) * gw],
                                        start=(kc == 0), stop=(kc == 7)),
              reads=["ones_c", xk], writes=[ka])
        for kc in range(8):
            T(lambda e, kc=kc: e.matmul(pbb[0:1, 0:gw], onesq[:],
                                        xsq[:, kc * gw:(kc + 1) * gw],
                                        start=(kc == 0), stop=(kc == 7)),
              reads=["ones_cb", sqk], writes=[kb])
        V(lambda e: e.tensor_scalar(rowA[0:1, 0:gw], pba[0:1, 0:gw],
                                    1.0 / D, None, ALU.mult),
          reads=[ka], writes=["rowAm"])
        V(lambda e: e.tensor_scalar(rowB[0:1, 0:gw], pbb[0:1, 0:gw],
                                    1.0 / D, None, ALU.mult),
          reads=[kb], writes=["rowBe"])
        V(lambda e: e.tensor_tensor(rowC[0:1, 0:gw], rowA[0:1, 0:gw],
                                    rowA[0:1, 0:gw], ALU.mult),
          reads=["rowAm"], writes=["rowCm2"])
        V(lambda e: e.tensor_tensor(rowB[0:1, 0:gw], rowB[0:1, 0:gw],
                                    rowC[0:1, 0:gw], ALU.subtract),
          reads=["rowBe", "rowCm2"], writes=["rowBv"])
        epsc = consts[0:1, 4:5] if eps else consts[0:1, 6:7]
        S(lambda e: e.activation(rowB[0:1, 0:gw], rowB[0:1, 0:gw], AF.Ln,
                                 bias=epsc, scale=consts[0:1, 1:2]),
          reads=["rowBv", "consts"], writes=["rowBl"])
        S(lambda e: e.activation(rowB[0:1, 0:gw], rowB[0:1, 0:gw], AF.Exp,
                                 bias=consts[0:1, 6:7], scale=consts[0:1, 5:6]),
          reads=["rowBl", "consts"], writes=["rowBr"])
        if scale != 1.0:
            V(lambda e: e.tensor_scalar(rowB[0:1, 0:gw], rowB[0:1, 0:gw],
                                        scale, None, ALU.mult),
              reads=["rowBr"], writes=["rowBr"])
        V(lambda e: e.tensor_tensor(rowA[0:1, 0:gw], rowA[0:1, 0:gw],
                                    rowB[0:1, 0:gw], ALU.mult),
          reads=["rowAm", "rowBr"], writes=["rowAb"])
        T(lambda e: e.matmul(pba[:, 0:gw], ones_r[:], rowB[0:1, 0:gw],
                             start=True, stop=True),
          reads=["ones_r", "rowBr"], writes=[ka])
        T(lambda e: e.matmul(pbb[:, 0:gw], ones_r[:], rowA[0:1, 0:gw],
                             start=True, stop=True),
          reads=["ones_r", "rowAb"], writes=[kb])

    def r3(x, gw):
        return x.rearrange("p (kc w) -> p kc w", kc=8)

    def bc3g(pb, gw):
        a = pb[:, 0:gw]
        return bass.AP(a.tensor, a.offset, [list(a.ap[0]), [0, 8], list(a.ap[1])])

    def ln_apply(xg3, xk, tmp3, tk, out3, ok, gw, pba, pbb, ka, kb):
        V(lambda e: e.tensor_tensor(tmp3, xg3, bc3g(pba, gw), ALU.mult),
          reads=[xk, ka], writes=[tk])
        V(lambda e: e.tensor_tensor(out3, tmp3, bc3g(pbb, gw), ALU.subtract),
          reads=[tk, kb], writes=[ok])

    def proj(wv, wkkey, mvfull, mvkey, ntok, evac):
        ntg = max(1, ntok // 1024)
        tg_w = min(ntok, 1024)
        nq = max(1, tg_w // 512)
        qw = min(tg_w, 512)
        for oc in range(8):
            for tg in range(ntg):
                pbi = (oc * ntg + tg) % 2
                pbks = ["pb%da" % pbi] + (["pb%db" % pbi] if nq > 1 else [])
                for kc in range(8):
                    for q in range(nq):
                        mv = mvfull[:, kc * ntok + tg * tg_w + q * qw:
                                    kc * ntok + tg * tg_w + (q + 1) * qw]
                        pbdst = PB[pbi][:, q * 512: q * 512 + qw]
                        mk = mvkey(tg, q) if callable(mvkey) else mvkey
                        T(lambda e, kc=kc, oc=oc, mv=mv, pbdst=pbdst: e.matmul(
                            pbdst, wv[:, kc * 1024 + oc * 128: kc * 1024 + (oc + 1) * 128],
                            mv, start=(kc == 0), stop=(kc == 7)),
                          reads=[wkkey, mk], writes=[pbks[q]])
                evac(oc, tg, PB[pbi][:, 0:tg_w], pbks)

    def phase_attn_a():
        oXC = 0
        oW = oXC + 16384
        oQKV = oW + 49152
        oQv = oQKV + 12288
        oKv = oQv + 16384
        oVv = oKv + 16384
        oVT = oVv + 8512
        oBG = oVT + 8320
        oHK = oBG + 17408
        oXN = oHK + 4608
        oAS = oXN + 8192
        oSQ = oAS + 8192
        oOT = oSQ + 4096
        oPP = oOT + 4096
        oSC = oPP + 4096
        assert oSC + 4096 <= ARB, oSC

        pbPROJ = PB[3][:, 0:512]
        pbA = PB[3][:, 512:768]
        pbB = PB[3][:, 768:1024]
        pbM = PB[3][:, 512:1024]

        Vvv = vbf(oVv, 4256)
        OT = vbf(oOT, 2048)
        ff1 = vbf(oOT, 2048)

        def Qvp(c):
            return vbf(oQv + (c % 2) * 8192, 4096)

        def Kvp(c):
            return vbf(oKv + (c % 2) * 8192, 4096)

        def vtp(c):
            return vbf(oVT + (c % 2) * 4160, 2080)

        def bigp(c):
            return vbf(oBG + (c % 2) * 8704, 4352)

        def hkp(c):
            return vbf(oHK + (c % 2) * 2304, 1152)

        wslot = [vbf(oW + i * 16384, 8192) for i in range(3)]
        STREAM = [("aWq", 0), ("aWk", 0), ("aWv", 0),
                  ("aWq", 1), ("aWk", 1), ("aWv", 1)]
        for c in range(1, NCHUNK):
            STREAM += [("aWo", c - 1), ("aW1", c - 1), ("aW2", c - 1)]
            if c + 1 < NCHUNK:
                STREAM += [("aWq", c + 1), ("aWk", c + 1), ("aWv", c + 1)]
        STREAM += [("aWo", NCHUNK - 1), ("aW1", NCHUNK - 1), ("aW2", NCHUNK - 1)]
        jof = {key: j for j, key in enumerate(STREAM)}

        def wload(j):
            if j >= len(STREAM):
                return
            nm, c = STREAM[j]
            sl = j % 3
            srcw = ins[nm].rearrange("(kc p) d -> p kc d", p=128)
            p.dma("G", "wsl%d" % sl,
                  lambda e, srcw=srcw, sl=sl: e.dma_start(wslot[sl], srcw),
                  writes=["R_WSL%d" % sl])

        def projS(j, mvfull, mvkey, evac):
            sl = j % 3
            wv = wslot[sl]
            wk = "R_WSL%d" % sl
            for oc in range(8):
                for h in range(2):
                    for kc in range(4 * h, 4 * h + 4):
                        T(lambda e, kc=kc, oc=oc, wv=wv, mvfull=mvfull: e.matmul(
                            pbPROJ[:, 0:256],
                            wv[:, kc * 1024 + oc * 128: kc * 1024 + (oc + 1) * 128],
                            mvfull[:, kc * 256:(kc + 1) * 256],
                            start=(kc == 0), stop=(kc == 7)),
                          reads=[wk, mvkey], writes=["pb3p"])
                    if h == 0:
                        yield (lambda: None)
                yield (lambda oc=oc: evac(oc))
            yield (lambda j=j: wload(j + 3))

        QT, KT, VT = vbf(oQKV, 2048), vbf(oQKV + 4096, 2048), vbf(oQKV + 8192, 2048)
        qdst = {"aWq": 0, "aWk": 4096, "aWv": 8192}
        bnm = {"aWq": "abq", "aWk": "abk", "aWv": "abv"}
        qkey = {"aWq": "R_QT", "aWk": "R_KT", "aWv": "R_VT"}

        def mk_ev(nm):
            def ev(oc):
                dstv = vbf(oQKV + qdst[nm] + oc * 512, 256)
                V(lambda e, dstv=dstv, oc=oc, nm=nm: e.tensor_scalar(
                    dstv, pbPROJ[:, 0:256], bcol(bnm[nm], oc), None, ALU.add),
                  reads=["pb3p", "biases"], writes=[qkey[nm]])
            return ev

        def P1_thunks(c):
            ci = c % 2
            xc = v32(oXC + ci * 8192, 2048)
            xck = "a_xc%d" % ci
            xn = vbf(oXN + ci * 4096, 2048)
            xnk = "R_XN%d" % ci
            srcx = xT.rearrange("(kc p) t -> p kc t", p=128)[:, :, c * CH:(c + 1) * CH]
            yield lambda: p.dma("SY", "xin%d" % ci,
                                lambda e, srcx=srcx, xc=xc: e.dma_start(xc, srcx),
                                writes=[xck])
            xsq = vbf(oSQ, 2048)

            def _ln1(xc=xc, xck=xck, xn=xn, xnk=xnk):
                ln_stats(xc, xck, xsq, "R_SQ", CH, True, 1.0,
                         pbA, pbB, "pb3b", "pb3b", ones_cb)
                ln_apply(r3(xc, CH), xck, r3(xsq, CH), "R_SQ",
                         r3(xn, CH), xnk, CH, pbA, pbB, "pb3b", "pb3b")
            yield _ln1
            yield from projS(jof[("aWq", c)], xn, xnk, mk_ev("aWq"))
            yield from projS(jof[("aWk", c)], xn, xnk, mk_ev("aWk"))
            yield from projS(jof[("aWv", c)], xn, xnk, mk_ev("aWv"))
            par = c % 2
            Qv = Qvp(c)
            Kv = Kvp(c)
            kq, kq2 = "a_Qv%d" % par, "a_Qv2_%d" % par
            kk, kk2 = "a_Kv%d" % par, "a_Kv2_%d" % par
            for m in range(16):
                def _cp(m=m, Qv=Qv, Kv=Kv, kq=kq, kk=kk):
                    def sl(X, m=m):
                        return X[(m % 2) * 64:(m % 2) * 64 + 64,
                                 (m // 2) * 256:(m // 2) * 256 + 256]
                    V(lambda e, m=m, s=sl(QT), Qv=Qv: e.tensor_copy(Qv[0:64, m:4096:16], s),
                      reads=["R_QT"], writes=[kq])
                    V(lambda e, m=m, s=sl(VT): e.tensor_copy(Vvv[0:64, m:4096:16], s),
                      reads=["R_VT"], writes=["a_Vv"])
                    dk = free_dims(Kv[0:64, (127 - m):(128 - m)], [[128, 32], [-16, 8]])
                    sk3 = sl(KT).rearrange("p (a r) -> p a r", a=32)
                    V(lambda e, dk=dk, sk3=sk3: e.tensor_copy(dk, sk3),
                      reads=["R_KT"], writes=[kk])
                    dv = free_dims(Vvv[64:128, (127 - m):(128 - m)], [[128, 32], [-16, 8]])
                    sv3 = sl(VT).rearrange("p (a r) -> p a r", a=32)
                    V(lambda e, dv=dv, sv3=sv3: e.tensor_copy(dv, sv3),
                      reads=["R_VT"], writes=["a_Vvr"])
                yield _cp
            for h4 in range(2):
                yield lambda h4=h4, Qv=Qv, kq=kq, kq2=kq2: V(
                    lambda e, h4=h4, Qv=Qv: e.tensor_copy(
                        Qv[64:128, h4 * 2048:(h4 + 1) * 2048],
                        Qv[0:64, h4 * 2048:(h4 + 1) * 2048]),
                    reads=[kq], writes=[kq2])
                yield lambda h4=h4, Kv=Kv, kk=kk, kk2=kk2: V(
                    lambda e, h4=h4, Kv=Kv: e.tensor_copy(
                        Kv[64:128, h4 * 2048:(h4 + 1) * 2048],
                        Kv[0:64, h4 * 2048:(h4 + 1) * 2048]),
                    reads=[kk], writes=[kk2])

        def P2_thunks(c):
            ci = c % 2
            vt = vtp(c)
            big = bigp(c)
            hk = hkp(c)

            def _band():
                V(lambda e, c=c: e.tensor_copy(relc_sb[:], rel_sb[0:32, c:c + 1]),
                  reads=["rel_sb"], writes=["relc"])
                for pc in range(4):
                    T(lambda e, pc=pc: e.matmul(pbM[0:1, 0:320],
                                                relc_sb[:],
                                                oh_sb[:, pc * 320:(pc + 1) * 320],
                                                start=True, stop=True),
                      reads=["relc", "oh_sb"], writes=["pb3b"])
                    V(lambda e, pc=pc: e.tensor_copy(band_sb[0:1, pc * 320:(pc + 1) * 320],
                                                     pbM[0:1, 0:320]),
                      reads=["pb3b"], writes=["band_sb"])
                p.dma("SY", "bando%d" % ci,
                      lambda e, ci=ci: e.dma_start(bandh[ci:ci + 1, :], band_sb[0:1, :1280]),
                      reads=["band_sb"], writes=["bandh%d" % ci])
            yield _band
            pbT = PB[3][:, 256:512]
            for tt in range(0, 32, 4):
                def _tr(tt=tt):
                    for j in range(4):
                        t = tt + j
                        T(lambda e, t=t, j=j: e.matmul(pbT[:, j * 64:j * 64 + 64],
                                                       Vvv[64:128, t * 128:(t + 1) * 128],
                                                       i64_sb[64:128, :], start=True, stop=True,
                                                       tile_position=(64, 0)),
                          reads=["a_Vvr", "i64_sb"], writes=["pb3t"])
                    dst = free_dims(vt[:, tt * 65:tt * 65 + 1], [[65, 4], [1, 64]])
                    src4 = pbT.rearrange("p (a b) -> p a b", a=4)
                    V(lambda e, dst=dst, src4=src4: e.tensor_copy(dst, src4),
                      reads=["pb3t"], writes=["a_vtok%d" % ci])
                yield _tr

            def _scan():
                zcb = zcol64[:].broadcast_to([64, 4255])
                V(lambda e, zcb=zcb, big=big: e.tensor_tensor_scan(big[0:64, 97:4352], zcb,
                                                                   Vvv[0:64, 0:4255], 0.0,
                                                                   ALU.add, ALU.add),
                  reads=["zc", "a_Vv", "a_Vvp"], writes=["a_ps%d" % ci])
                T(lambda e, big=big: e.matmul(pbM[0:1, 0:64], big[0:64, 4192:4193],
                                              i64_sb[0:64, :], start=True, stop=True),
                  reads=["a_ps%d" % ci, "i64_sb"], writes=["pb3b"])
                V(lambda e, ci=ci: e.tensor_copy(totT[:, ci * 64:ci * 64 + 64], pbM[0:1, 0:64]),
                  reads=["pb3b"], writes=["totT%d" % ci])
                V(lambda e, c=c, ci=ci: e.tensor_scalar(w15row[:, ci * 512:ci * 512 + 512],
                                                        ones512[:],
                                                        wbc[0:1, 240 + c:241 + c], None, ALU.mult),
                  reads=["ones512", "wbc"], writes=["w15%d" % ci])
            yield _scan

            def _hk():
                hsrc = bass.AP(bandh_t, ci * 1280 + 1, [[1, 128], [1, 1152]])
                p.dma("SY", "hank",
                      lambda e, hsrc=hsrc, hk=hk: e.dma_start(hk, hsrc),
                      reads=["bandh%d" % ci], writes=["a_hk%d" % ci])
            yield _hk

        def A_emit(c, feeder):
            par = c % 2
            Qv = Qvp(c)
            Kv = Kvp(c)
            vt = vtp(c)
            big = bigp(c)
            hk = hkp(c)
            kq, kq2 = "a_Qv%d" % par, "a_Qv2_%d" % par
            kk, kk2 = "a_Kv%d" % par, "a_Kv2_%d" % par
            vtk = "a_vtok%d" % par
            psk = "a_ps%d" % par
            outk = "a_out%d" % par
            hkk = "a_hk%d" % par
            totc = totT[:, par * 64:par * 64 + 64]
            w15c = w15row[:, par * 512:par * 512 + 512]
            pend = [None]

            def flush_combine():
                if pend[0] is not None:
                    pend[0]()
                    pend[0] = None

            for g in range(8):
                q0 = g * 512
                pvh, bdh = ("pb2a", "pb2b") if g % 2 == 0 else ("pb2b", "pb2a")
                pvo = 0 if g % 2 == 0 else 512
                bdo = 512 - pvo

                def emit_pv(pi, pvh=pvh, pvo=pvo):
                    pp = vbf(oPP + (pi % 2) * 2048, 1024)
                    ppk = "a_pp%d" % (pi % 2)
                    for j, tj in enumerate((2 * pi, 2 * pi + 1)):
                        T(lambda e, tj=tj, pp=pp, j=j, pvo=pvo: e.matmul(
                            PB[2][0:65, pvo:pvo + 512], vt[:, tj * 65:tj * 65 + 65],
                            pp[:, j * 512:(j + 1) * 512],
                            start=(tj == 0), stop=(tj == 31)),
                          reads=[vtk, ppk], writes=[pvh])
                for pi in range(16):
                    t0, t1 = 2 * pi, 2 * pi + 1
                    sp = PB[pi % 2]
                    ska, skb = "pb%da" % (pi % 2), "pb%db" % (pi % 2)
                    T(lambda e, t0=t0, sp=sp, q0=q0: e.matmul(
                        sp[:, 0:512], Kv[0:64, t0 * 128:(t0 + 1) * 128],
                        Qv[0:64, q0:q0 + 512], start=True, stop=True,
                        tile_position=(0, 0)),
                      reads=[kk, kq], writes=[ska])
                    T(lambda e, t1=t1, sp=sp, q0=q0: e.matmul(
                        sp[:, 512:1024], Kv[64:128, t1 * 128:(t1 + 1) * 128],
                        Qv[64:128, q0:q0 + 512], start=True, stop=True,
                        tile_position=(64, 0)),
                      reads=[kk2, kq2], writes=[skb])
                    pp = vbf(oPP + (pi % 2) * 2048, 1024)
                    ppk = "a_pp%d" % (pi % 2)
                    S(lambda e, pp=pp, sp=sp: e.activation(
                        pp, sp[:, :], AF.Exp, bias=consts[:, 0:1], scale=consts[:, 1:2]),
                      reads=[ska, skb, "consts"], writes=[ppk])
                    if pi > 0:
                        emit_pv(pi - 1)
                    if pi == 4:
                        flush_combine()
                    else:
                        th = next(feeder, None)
                        if th is not None:
                            th()
                emit_pv(15)
                for _ in range(3):
                    th = next(feeder, None)
                    if th is not None:
                        th()
                first_band = True
                for d0 in (128, 0, -128, -256, -384, -512):
                    k0 = q0 - d0
                    if k0 < 0 or k0 >= 4096:
                        continue
                    t = k0 // 128
                    T(lambda e, t=t, d0=d0, fb=first_band, bdo=bdo: e.matmul(
                        PB[2][0:64, bdo:bdo + 512], vt[:, t * 65:t * 65 + 64],
                        hk[:, 512 + d0:512 + d0 + 512],
                        start=fb, stop=False),
                      reads=[vtk, hkk], writes=[bdh])
                    first_band = False
                T(lambda e, bdo=bdo: e.matmul(PB[2][0:64, bdo:bdo + 512], totc, w15c,
                                              start=False, stop=True),
                  reads=["totT%d" % par, "w15%d" % par], writes=[bdh])
                o2e = v32(oSC + 2048, 512)[0:64, :]
                V(lambda e, o2e=o2e, bdo=bdo: e.tensor_copy(o2e, PB[2][0:64, bdo:bdo + 512]),
                  reads=[bdh], writes=["R_O2"])
                V(lambda e, pvo=pvo: e.reciprocal(recip_sb[:], PB[2][64:65, pvo:pvo + 512]),
                  reads=[pvh], writes=["recip"])

                def mk_combine(q0=q0, pvh=pvh, pvo=pvo, c=c, big=big, psk=psk, outk=outk):
                    def go():
                        T(lambda e: e.matmul(pbM[0:64, 0:512], ones_r[0:1, 0:64],
                                             recip_sb[:], start=True, stop=True),
                          reads=["ones_r", "recip"], writes=["pb3b"])
                        V(lambda e: e.tensor_copy(rb_sb[:], pbM[0:64, 0:512]),
                          reads=["pb3b"], writes=["rb"])
                        o1 = v32(oSC, 512)[0:64, :]
                        o2 = v32(oSC + 2048, 512)[0:64, :]
                        V(lambda e, o1=o1: e.tensor_tensor(o1, PB[2][0:64, pvo:pvo + 512],
                                                           rb_sb[:], ALU.mult),
                          reads=[pvh, "rb"], writes=["R_O1"])
                        V(lambda e, o2=o2, o1=o1: e.tensor_tensor(
                            o2, o1, o2, ALU.add),
                          reads=["R_O1", "R_O2"], writes=["R_O2"])
                        V(lambda e, o1=o1: e.scalar_tensor_tensor(
                            o1, big[0:64, q0 + 6:q0 + 518],
                            wbc[0:64, 496 + c:497 + c], o2, ALU.mult, ALU.add),
                          reads=[psk, "wbc", "R_O2"], writes=["R_O1"])
                        V(lambda e, o1=o1: e.scalar_tensor_tensor(
                            big[64:128, q0:q0 + 512], big[0:64, q0 + 187:q0 + 699],
                            wbcn[0:64, 240 + c:241 + c], o1, ALU.mult, ALU.add),
                          reads=[psk, "wbcn", "R_O1"], writes=[outk])
                    return go
                pend[0] = mk_combine()
            flush_combine()

        def T_thunks(c):
            ci = c % 2
            xc = v32(oXC + ci * 8192, 2048)
            xck = "a_xc%d" % ci
            xn = vbf(oXN + ci * 4096, 2048)
            xnk = "R_XN%d" % ci
            as2 = v32(oAS, 2048)
            xsq2 = vbf(oSQ, 2048)
            ab = vbf(oXN + ci * 4096, 2048)
            big = bigp(c)
            outk = "a_out%d" % ci
            for m in range(16):
                def _uv(m=m):
                    dsto = OT[(m % 2) * 64:(m % 2) * 64 + 64,
                              (m // 2) * 256:(m // 2) * 256 + 256]
                    V(lambda e, dsto=dsto, m=m, big=big: e.tensor_copy(
                        dsto, big[64:128, m:4096:16]),
                      reads=[outk], writes=["R_OT"])
                yield _uv

            def evo(oc):
                V(lambda e, oc=oc: e.scalar_tensor_tensor(
                    as2[:, oc * 256:(oc + 1) * 256], pbPROJ[:, 0:256], bcol("abo", oc),
                    ab[:, oc * 256:(oc + 1) * 256], ALU.add, ALU.add),
                  reads=["pb3p", "biases", xnk], writes=["R_AS"])
            yield from projS(jof[("aWo", c)], OT, "R_OT", evo)

            def _ln2(ab=ab, xnk=xnk):
                ln_stats(as2, "R_AS", xsq2, "R_SQ", CH, False, 1.0,
                         pbA, pbB, "pb3b", "pb3b", ones_cb)
                ln_apply(r3(as2, CH), "R_AS", r3(xsq2, CH), "R_SQ",
                         r3(ab, CH), xnk, CH, pbA, pbB, "pb3b", "pb3b")
            yield _ln2

            def ev1(oc):
                V(lambda e, oc=oc: e.tensor_scalar(
                    ff1[:, oc * 256:(oc + 1) * 256], pbPROJ[:, 0:256],
                    bcol("ab1", oc), None, ALU.add),
                  reads=["pb3p", "biases"], writes=["R_OT"])
            yield from projS(jof[("aW1", c)], ab, xnk, ev1)

            def ev2(oc):
                V(lambda e, oc=oc: e.scalar_tensor_tensor(
                    as2[:, oc * 256:(oc + 1) * 256], pbPROJ[:, 0:256], bcol("ab2", oc),
                    ab[:, oc * 256:(oc + 1) * 256], ALU.add, ALU.add),
                  reads=["pb3p", "biases", xnk], writes=["R_AS"])
            yield from projS(jof[("aW2", c)], ff1, "R_OT", ev2)

            def _ln3():
                ln_stats(as2, "R_AS", xsq2, "R_SQ", CH, False, 1.0,
                         pbA, pbB, "pb3b", "pb3b", ones_cb)
                as23 = r3(as2, CH)
                xc3 = r3(xc, CH)
                V(lambda e: e.tensor_tensor(as23, as23, bc3g(pbA, CH), ALU.mult),
                  reads=["R_AS", "pb3b"], writes=["R_AS"])
                V(lambda e: e.tensor_tensor(as23, as23, bc3g(pbB, CH), ALU.subtract),
                  reads=["R_AS", "pb3b"], writes=["R_AS"])
                V(lambda e: e.tensor_tensor(as23, as23, xc3, ALU.add),
                  reads=["R_AS", xck], writes=["R_AS"])
                dstx = xbuf.rearrange("(kc p) t -> p kc t", p=128)[:, :, c * CH:(c + 1) * CH]
                p.dma("SY", "xout",
                      lambda e, dstx=dstx, as2=as2: e.dma_start(dstx, as2),
                      reads=["R_AS"], writes=[])
            yield _ln3

        def drain(gen):
            for th in gen:
                th()

        import itertools
        for ci2 in range(2):
            V(lambda e, ci2=ci2: e.memset(
                free_dims(vbf(oVT + ci2 * 4160, 2080)[:, 64:65], [[65, 32]]), 1.0),
              writes=["a_vtok%d" % ci2])
            V(lambda e, ci2=ci2: e.memset(vbf(oBG + ci2 * 8704, 4352)[0:64, 0:97], 0.0),
              writes=["a_ps%d" % ci2])
        V(lambda e: e.memset(Vvv[0:64, 4096:4256], 0.0), writes=["a_Vvp"])
        for j in range(3):
            wload(j)
        drain(P1_thunks(0))
        drain(P2_thunks(0))
        carry = iter(())
        for c in range(NCHUNK):
            feeder = itertools.chain(
                carry, P1_thunks(c + 1) if c + 1 < NCHUNK else iter(()))
            A_emit(c, feeder)
            drain(feeder)
            if c + 1 < NCHUNK:
                drain(P2_thunks(c + 1))
            carry = T_thunks(c)
        drain(carry)

    def phase_ffn(last):
        oXG = 0
        oSQ = oXG + 32768
        oXN = oSQ + 8192
        oH = oXN + 32768
        oWG = oH + 98304
        oGS = oWG + 8192
        assert oGS + 4096 <= ARB, oGS
        oWD = 0
        oXS = oWD + 12288
        src_in = xbuf
        dst_out = yT if last else xbuf
        xnf = vbf(oXN, 16384)
        for g in range(4):
            xg = v32(oXG + (g % 2) * 16384, 4096)
            xgk = "R_FXG%d" % (g % 2)
            srcx = src_in.rearrange("(kc p) t -> p kc t", p=128)[:, :, g * 512:(g + 1) * 512]
            p.dma("SY", "fxin%d" % (g % 2), lambda e, srcx=srcx, xg=xg: e.dma_start(xg, srcx),
                  writes=[xgk])
            xsq = vbf(oSQ, 4096)
            ln_stats(xg, xgk, xsq, "R_FSQ", 512, True, 1.0 / 127.0,
                     PB[3][:, 0:512], PB[3][:, 512:1024], "pb3a", "pb3b", ones_cb)
            xn3 = free_dims(xnf[0:128, g * 512:g * 512 + 1], [[2048, 8], [1, 512]])
            ln_apply(r3(xg, 512), xgk, r3(xsq, 512), "R_FSQ", xn3, "f_xn%d" % g, 512,
                     PB[3][:, 0:512], PB[3][:, 512:1024], "pb3a", "pb3b")

        H = vbf(oH, 49152)
        for hc in range(NHC):
            hi = hc % 2
            wgv = vbf(oWG + hi * 2048, 1024)
            wuv = vbf(oWG + 4096 + hi * 2048, 1024)
            sg = ins["wg"][hc, :, :].rearrange("(kc p) h -> p kc h", p=128)
            su = ins["wu"][hc, :, :].rearrange("(kc p) h -> p kc h", p=128)
            p.dma("G", "fwg%d" % hi, lambda e, sg=sg, wgv=wgv: e.dma_start(wgv, sg),
                  writes=["f_wg%d" % hi])
            p.dma("G", "fwu%d" % hi, lambda e, su=su, wuv=wuv: e.dma_start(wuv, su),
                  writes=["f_wu%d" % hi])
            for seg in range(4):
                sb = seg % 2
                gk = "pb0a" if sb == 0 else "pb0b"
                uk = "pb1a" if sb == 0 else "pb1b"
                gdep = "f_xn%d" % seg
                for kc in range(8):
                    mv = xnf[:, kc * 2048 + seg * 512: kc * 2048 + (seg + 1) * 512]
                    T(lambda e, mv=mv, wgv=wgv, kc=kc, sb=sb: e.matmul(
                        PB[0][:, sb * 512:(sb + 1) * 512],
                        wgv[:, kc * 128:(kc + 1) * 128], mv,
                        start=(kc == 0), stop=(kc == 7)),
                      reads=["f_wg%d" % hi, gdep], writes=[gk])
                for kc in range(8):
                    mv = xnf[:, kc * 2048 + seg * 512: kc * 2048 + (seg + 1) * 512]
                    T(lambda e, mv=mv, wuv=wuv, kc=kc, sb=sb: e.matmul(
                        PB[1][:, sb * 512:(sb + 1) * 512],
                        wuv[:, kc * 128:(kc + 1) * 128], mv,
                        start=(kc == 0), stop=(kc == 7)),
                      reads=["f_wu%d" % hi, gdep], writes=[uk])
                gs = vbf(oGS + sb * 2048, 512)
                S(lambda e, gs=gs, sb=sb: e.activation(gs, PB[0][:, sb * 512:(sb + 1) * 512],
                                                       AF.Silu),
                  reads=[gk], writes=["f_gs%d" % sb])
                hdst = H[:, hc * 2048 + seg * 512: hc * 2048 + (seg + 1) * 512]
                V(lambda e, hdst=hdst, gs=gs, sb=sb: e.tensor_tensor(
                    hdst, gs, PB[1][:, sb * 512:(sb + 1) * 512], ALU.mult),
                  reads=["f_gs%d" % sb, uk], writes=["f_H%d" % hc])

        for oc in range(8):
            oi = oc % 2
            wdv = vbf(oWD + oi * 6144, 3072)
            sd = ins["wd"][oc, :, :].rearrange("(kt p) d -> p kt d", p=128)
            p.dma("G", "fwd%d" % oi, lambda e, sd=sd, wdv=wdv: e.dma_start(wdv, sd),
                  writes=["f_wd%d" % oi, "R_FXG0"])
            for seg in range(4):
                sb = seg % 2
                dk = "pb2a" if sb == 0 else "pb2b"
                for kt in range(NHC):
                    mv = H[:, kt * 2048 + seg * 512: kt * 2048 + (seg + 1) * 512]
                    T(lambda e, kt=kt, mv=mv, wdv=wdv, sb=sb: e.matmul(
                        PB[2][:, sb * 512:(sb + 1) * 512],
                        wdv[:, kt * 128:(kt + 1) * 128], mv,
                        start=(kt == 0), stop=(kt == NHC - 1)),
                      reads=["f_wd%d" % oi, "f_H%d" % kt], writes=[dk])
                xs = v32(oXS + seg * 2048, 512)
                xsk = "R_FXG0" if seg < 2 else "R_FXG1"
                srcx = src_in[oc * 128:(oc + 1) * 128, seg * 512:(seg + 1) * 512]
                p.dma("SY", "fx2%d" % seg,
                      lambda e, srcx=srcx, xs=xs: e.dma_start(xs, srcx),
                      writes=[xsk])
                xo = v32(oXG + 28672 + (seg % 2) * 2048, 512)
                xok = "R_FXG1"
                V(lambda e, xo=xo, xs=xs, sb=sb: e.scalar_tensor_tensor(
                    xo, PB[2][:, sb * 512:(sb + 1) * 512], 1.0 / 127.0, xs,
                    ALU.mult, ALU.add),
                  reads=[dk, xsk], writes=[xok])
                dstx = dst_out[oc * 128:(oc + 1) * 128, seg * 512:(seg + 1) * 512]
                p.dma("SY", "fxo%d" % (seg % 2),
                      lambda e, dstx=dstx, xo=xo: e.dma_start(dstx, xo),
                      reads=[xok], writes=[])

    def phase_attn_b():
        oXG = 0
        oSQ = oXG + 16384
        oCS = oSQ + 16384
        oRX = oCS + 16384
        oQT = oRX + 32768
        oST = oQT + 32768
        oW = oST + 17408
        oPP = 0
        oXS = 4096
        oXO2 = oXS + 8192
        assert oW + 32768 <= ARB

        rxn = vbf(oRX, 16384)

        def lnb(g):
            xg = v32(oXG, 4096)
            xgk = "R_BXG"
            srcg = xbuf.rearrange("(kc p) t -> p kc t", p=128)[:, :, g * 512:(g + 1) * 512]
            p.dma("SY", "bxin", lambda e, srcg=srcg, xg=xg: e.dma_start(xg, srcg),
                  writes=[xgk])
            xsq = v32(oSQ, 4096)
            ln_stats(xg, xgk, xsq, "R_BSQ", 512, True, 1.0,
                     PB[3][:, 0:512], PB[3][:, 512:1024], "pb3a", "pb3b", ones_c)
            xn = v32(oSQ, 4096)
            ln_apply(r3(xg, 512), xgk, r3(xg, 512), "R_BXG", r3(xn, 512), "R_BSQ", 512,
                     PB[3][:, 0:512], PB[3][:, 512:1024], "pb3a", "pb3b")
            for cc in range(4):
                csrc = ins["cosT"][cc * 128:(cc + 1) * 128, g * 512:(g + 1) * 512]
                ssrc = ins["sinT"][cc * 128:(cc + 1) * 128, g * 512:(g + 1) * 512]
                cosb = v32(oCS, 512)
                sinb = v32(oCS + 8192, 512)
                p.dma("SY", "bcos", lambda e, csrc=csrc, cosb=cosb: e.dma_start(cosb, csrc),
                      writes=["b_cos"])
                p.dma("SY", "bsin", lambda e, ssrc=ssrc, sinb=sinb: e.dma_start(sinb, ssrc),
                      writes=["b_sin"])
                x1 = xn[:, cc * 512:(cc + 1) * 512]
                x2 = xn[:, (cc + 4) * 512:(cc + 5) * 512]
                t1 = v32(oCS + 2048, 512)
                t2 = v32(oCS + 8192 + 2048, 512)
                r1 = rxn[:, cc * 2048 + g * 512: cc * 2048 + (g + 1) * 512]
                r2 = rxn[:, (cc + 4) * 2048 + g * 512: (cc + 4) * 2048 + (g + 1) * 512]
                V(lambda e, t1=t1, x1=x1, cosb=cosb: e.tensor_tensor(t1, x1, cosb, ALU.mult),
                  reads=["R_BSQ", "b_cos"], writes=["b_t1"])
                V(lambda e, t2=t2, x2=x2, sinb=sinb: e.tensor_tensor(t2, x2, sinb, ALU.mult),
                  reads=["R_BSQ", "b_sin"], writes=["b_t2"])
                V(lambda e, r1=r1, t1=t1, t2=t2: e.tensor_tensor(r1, t1, t2, ALU.subtract),
                  reads=["b_t1", "b_t2"], writes=["R_RX", "R_RXg%d" % g])
                V(lambda e, t1=t1, x1=x1, sinb=sinb: e.tensor_tensor(t1, x1, sinb, ALU.mult),
                  reads=["R_BSQ", "b_sin"], writes=["b_t1"])
                V(lambda e, t2=t2, x2=x2, cosb=cosb: e.tensor_tensor(t2, x2, cosb, ALU.mult),
                  reads=["R_BSQ", "b_cos"], writes=["b_t2"])
                V(lambda e, r2=r2, t1=t1, t2=t2: e.tensor_tensor(r2, t1, t2, ALU.add),
                  reads=["b_t1", "b_t2"], writes=["R_RX", "R_RXg%d" % g])

        wslots = [vbf(oW, 8192), vbf(oW + 16384, 8192)]
        QTb = vbf(oQT, 16384)
        stg = [vbf(oST + i * 4352, 2048) for i in range(4)]

        def proj_b(nm, sl, tg):
            bias_nm = {"bWq": "bbq", "bWk": "bbk", "bWv": "bbv", "bWo": "bbo"}[nm]
            ccdst = {"bWk": cc_in_k, "bWv": cc_in_v}.get(nm)
            for oc in range(8):
                pbi = (oc + tg) % 2
                pbks = ["pb%da" % pbi, "pb%db" % pbi]
                for kc in range(8):
                    for q in range(2):
                        mv = rxn[:, kc * 2048 + tg * 1024 + q * 512:
                                 kc * 2048 + tg * 1024 + (q + 1) * 512]
                        T(lambda e, kc=kc, oc=oc, mv=mv, q=q, pbi=pbi, sl=sl: e.matmul(
                            PB[pbi][:, q * 512:(q + 1) * 512],
                            wslots[sl][:, kc * 1024 + oc * 128: kc * 1024 + (oc + 1) * 128],
                            mv, start=(kc == 0), stop=(kc == 7)),
                          reads=["R_BW%d" % sl, "R_RXg%d" % (tg * 2 + q)],
                          writes=[pbks[q]])
                if nm == "bWq":
                    dst = QTb[:, oc * 2048 + tg * 1024: oc * 2048 + (tg + 1) * 1024]
                    S(lambda e, dst=dst, oc=oc, pbi=pbi: e.activation(
                        dst, PB[pbi][:, 0:1024], AF.Identity,
                        bias=bcol("bbq", oc), scale=consts[:, 1:2]),
                      reads=pbks + ["biases", "consts"], writes=["b_QT"])
                else:
                    slot = (0 if nm == "bWk" else 2) + (oc + tg) % 2
                    sgb = stg[slot][:, 0:1024]
                    skey = "R_VT%d" % slot
                    S(lambda e, sgb=sgb, oc=oc, pbi=pbi, bias_nm=bias_nm: e.activation(
                        sgb, PB[pbi][:, 0:1024], AF.Identity,
                        bias=bcol(bias_nm, oc), scale=consts[:, 1:2]),
                      reads=pbks + ["biases", "consts"], writes=[skey])
                    dd = ccdst[oc * 128:(oc + 1) * 128, tg * 1024:(tg + 1) * 1024]
                    p.dma("G", "bstg%d" % slot,
                          lambda e, dd=dd, sgb=sgb: e.dma_start(dd, sgb),
                          reads=[skey], writes=["cc_in" + nm])

        srcw = ins["bWk"].rearrange("(kc p) d -> p kc d", p=128)
        p.dma("G", "bw0", lambda e, srcw=srcw: e.dma_start(wslots[0], srcw),
              writes=["R_BW0"])
        srcw = ins["bWv"].rearrange("(kc p) d -> p kc d", p=128)
        p.dma("G", "bw1", lambda e, srcw=srcw: e.dma_start(wslots[1], srcw),
              writes=["R_BW1"])
        lnb(0)
        lnb(1)
        proj_b("bWk", 0, 0)
        lnb(2)
        proj_b("bWv", 1, 0)
        lnb(3)
        proj_b("bWk", 0, 1)
        p.async_op("G", "cck", lambda e: e.collective_compute(
            "AllGather", ALU.bypass, [[0, 1], [2, 3], [4, 5], [6, 7]],
            [cc_in_k.opt()], [cc_out_k.opt()]),
            reads=["cc_inbWk", "R_VT0", "R_VT1"], writes=["cc_k"])
        proj_b("bWv", 1, 1)
        p.async_op("G", "ccv", lambda e: e.collective_compute(
            "AllGather", ALU.bypass, [[0, 1], [2, 3], [4, 5], [6, 7]],
            [cc_in_v.opt()], [cc_out_v.opt()]),
            reads=["cc_inbWv", "R_VT2", "R_VT3"], writes=["cc_v"])
        srcw = ins["bWq"].rearrange("(kc p) d -> p kc d", p=128)
        p.dma("G", "bw0b", lambda e, srcw=srcw: e.dma_start(wslots[0], srcw),
              writes=["R_BW0"])
        proj_b("bWq", 0, 0)
        proj_b("bWq", 0, 1)

        woview = wslots[0]
        srcw = ins["bWo"].rearrange("(kc p) d -> p kc d", p=128)
        p.dma("G", "bwo", lambda e, srcw=srcw: e.dma_start(woview, srcw),
              writes=["R_BW0"])

        OTb = vbf(oW + 16384, 16384)
        kp = [vbf(oRX + i * 8192, 4096) for i in range(2)]
        vp = [vbf(oRX + 16384 + i * 8192, 4096) for i in range(2)]
        vtb = [vbf(oST + i * 4352, 2080) for i in range(4)]

        def stage_thunks(gp):
            h0, h1 = 2 * gp, 2 * gp + 1
            pi = gp % 2
            kpk, vpk = "R_RXK%d" % pi, "R_RXV%d" % pi
            for (hh, rows) in ((h0, (0, 64)), (h1, (64, 128))):
                for (hs, cols) in ((0, (0, 2048)), (1, (2048, 4096))):
                    def _ld(hh=hh, rows=rows, hs=hs, cols=cols):
                        srck = cc_out_k[hs * D + hh * 64: hs * D + hh * 64 + 64, :]
                        dstk = kp[pi][rows[0]:rows[1], cols[0]:cols[1]]
                        p.dma("SY", "bkp%d" % pi,
                              lambda e, dstk=dstk, srck=srck: e.dma_start(dstk, srck),
                              reads=["cc_k"], writes=[kpk] + (["R_RX", "R_RXg0", "R_RXg1",
                                                               "R_RXg2", "R_RXg3"] if gp < 2 else []))
                        srcv = cc_out_v[hs * D + hh * 64: hs * D + hh * 64 + 64, :]
                        dstv = vp[pi][rows[0]:rows[1], cols[0]:cols[1]]
                        p.dma("SY", "bvp%d" % pi,
                              lambda e, dstv=dstv, srcv=srcv: e.dma_start(dstv, srcv),
                              reads=["cc_v"], writes=[vpk] + (["R_RX", "R_RXg0", "R_RXg1",
                                                               "R_RXg2", "R_RXg3"] if gp < 2 else []))
                    yield _ld
            s0, s1 = 2 * pi, 2 * pi + 1
            k0n, k1n = "R_VT%d" % s0, "R_VT%d" % s1
            def _ms():
                V(lambda e: e.memset(free_dims(vtb[s0][:, 64:65], [[65, 32]]), 1.0),
                  writes=[k0n])
                V(lambda e: e.memset(free_dims(vtb[s1][:, 64:65], [[65, 32]]), 1.0),
                  writes=[k1n])
            yield _ms
            for tt in range(0, 32, 4):
                def _tr(tt=tt):
                    for t in range(tt, tt + 4):
                        T(lambda e, t=t: e.matmul(
                            PB[3][:, 0:64], vp[pi][0:64, t * 128:(t + 1) * 128],
                            i64_sb[0:64, :], start=True, stop=True,
                            tile_position=(0, 0)),
                          reads=[vpk, "i64_sb"], writes=["pb3a"])
                        V(lambda e, t=t: e.tensor_copy(vtb[s0][:, t * 65:t * 65 + 64],
                                                       PB[3][:, 0:64]),
                          reads=["pb3a"], writes=[k0n])
                        T(lambda e, t=t: e.matmul(
                            PB[3][:, 512:576], vp[pi][64:128, t * 128:(t + 1) * 128],
                            i64_sb[64:128, :], start=True, stop=True,
                            tile_position=(64, 0)),
                          reads=[vpk, "i64_sb"], writes=["pb3b"])
                        V(lambda e, t=t: e.tensor_copy(vtb[s1][:, t * 65:t * 65 + 64],
                                                       PB[3][:, 512:576]),
                          reads=["pb3b"], writes=[k1n])
                yield _tr

        def A_b(gp, feeder):
            h0, h1 = 2 * gp, 2 * gp + 1
            pi = gp % 2
            kpk = "R_RXK%d" % pi
            s0, s1 = 2 * pi, 2 * pi + 1
            k0n, k1n = "R_VT%d" % s0, "R_VT%d" % s1
            for g in range(4):
                q0g = g * 512
                q_h0 = QTb[0:64, (h0 // 2) * 2048 + q0g: (h0 // 2) * 2048 + q0g + 512]
                q_h1 = QTb[64:128, (h1 // 2) * 2048 + q0g: (h1 // 2) * 2048 + q0g + 512]
                def emit_pvb(t):
                    pp = vbf(oPP + (t % 2) * 2048, 1024)
                    ppk = "b_pp%d" % (t % 2)
                    T(lambda e, t=t, pp=pp, s0=s0: e.matmul(
                        PB[2][0:65, 0:512], vtb[s0][:, t * 65:t * 65 + 65], pp[:, 0:512],
                        start=(t == 0), stop=(t == 31)),
                      reads=[k0n, ppk], writes=["pb2a"])
                    T(lambda e, t=t, pp=pp, s1=s1: e.matmul(
                        PB[2][0:65, 512:1024], vtb[s1][:, t * 65:t * 65 + 65], pp[:, 512:1024],
                        start=(t == 0), stop=(t == 31)),
                      reads=[k1n, ppk], writes=["pb2b"])
                for t in range(32):
                    sp = PB[t % 2]
                    ska, skb = "pb%da" % (t % 2), "pb%db" % (t % 2)
                    T(lambda e, t=t, sp=sp, q_h0=q_h0, pi=pi: e.matmul(
                        sp[:, 0:512], kp[pi][0:64, t * 128:(t + 1) * 128], q_h0,
                        start=True, stop=True, tile_position=(0, 0)),
                      reads=[kpk, "b_QT"], writes=[ska])
                    T(lambda e, t=t, sp=sp, q_h1=q_h1, pi=pi: e.matmul(
                        sp[:, 512:1024], kp[pi][64:128, t * 128:(t + 1) * 128], q_h1,
                        start=True, stop=True, tile_position=(64, 0)),
                      reads=[kpk, "b_QT"], writes=[skb])
                    pp = vbf(oPP + (t % 2) * 2048, 1024)
                    ppk = "b_pp%d" % (t % 2)
                    S(lambda e, pp=pp, sp=sp: e.activation(
                        pp, sp[:, :], AF.Exp, bias=consts[:, 2:3], scale=consts[:, 3:4]),
                      reads=[ska, skb, "consts"], writes=[ppk])
                    if t > 0:
                        emit_pvb(t - 1)
                    if t % 2 == 0:
                        th = next(feeder, None)
                        if th is not None:
                            th()
                emit_pvb(31)
                for hi2, pvk in ((0, "pb2a"), (1, "pb2b")):
                    hh = h0 if hi2 == 0 else h1
                    V(lambda e, hi2=hi2: e.reciprocal(
                        recip_sb[:], PB[2][64:65, hi2 * 512: hi2 * 512 + 512]),
                      reads=[pvk], writes=["recip"])
                    T(lambda e: e.matmul(PB[3][0:64, 512:1024], ones_r[0:1, 0:64],
                                         recip_sb[:], start=True, stop=True),
                      reads=["ones_r", "recip"], writes=["pb3b"])
                    V(lambda e: e.tensor_copy(rb_sb[:], PB[3][0:64, 512:1024]),
                      reads=["pb3b"], writes=["rb"])
                    dsto = OTb[(hh % 2) * 64:(hh % 2) * 64 + 64,
                               (hh // 2) * 2048 + q0g: (hh // 2) * 2048 + q0g + 512]
                    V(lambda e, dsto=dsto, hi2=hi2: e.tensor_tensor(
                        dsto, PB[2][0:64, hi2 * 512: hi2 * 512 + 512], rb_sb[:], ALU.mult),
                      reads=[pvk, "rb"], writes=["R_BW1"])

        import itertools as _it
        for th in stage_thunks(0):
            th()
        for gp in range(8):
            feeder = stage_thunks(gp + 1) if gp < 7 else iter(())
            A_b(gp, feeder)
            for th in feeder:
                th()

        def evwo(oc, tg, pb, pbks):
            oi = (oc * 2 + tg) % 2
            xs = v32(oXS + oi * 4096, 1024)
            xsk = "R_BXG"
            srcx = xbuf[oc * 128:(oc + 1) * 128, tg * 1024:(tg + 1) * 1024]
            p.dma("SY", "bx2%d" % oi,
                  lambda e, srcx=srcx, xs=xs: e.dma_start(xs, srcx),
                  writes=[xsk, "R_BXG", "R_BXG", "R_BXG"])
            xo = v32(oXO2 + oi * 4096, 1024)
            xok = "R_BSQ"
            V(lambda e, xo=xo, xs=xs, pb=pb, oc=oc: e.scalar_tensor_tensor(
                xo, pb, bcol("bbo", oc), xs, ALU.add, ALU.add),
              reads=list(pbks) + ["biases", xsk], writes=["R_BSQ", "R_BXG"])
            dstx = xbuf[oc * 128:(oc + 1) * 128, tg * 1024:(tg + 1) * 1024]
            p.dma("SY", "bxo%d" % oi,
                  lambda e, dstx=dstx, xo=xo: e.dma_start(dstx, xo),
                  reads=[xok], writes=[])
        proj(woview, "R_BW0", OTb, "R_BW1", TOK, evwo)

    def dbg_tap(i):
        if not _DEBUG_TAPS:
            return
        for kc in range(8):
            t = v32(0, 2048)
            p.dma("SY", "dbgl", lambda e, kc=kc, t=t: e.dma_start(
                t, xbuf[kc * 128:(kc + 1) * 128, :]), writes=["dbgt"])
            p.dma("SY", "dbgs", lambda e, kc=kc, i=i, t=t: e.dma_start(
                dbg_taps[i][kc * 128:(kc + 1) * 128, :], t), reads=["dbgt"], writes=[])
        p.barrier()

    phase_attn_a()
    p.barrier()
    dbg_tap(0)
    phase_ffn(last=False)
    p.barrier()
    dbg_tap(1)
    phase_attn_b()
    p.barrier()
    dbg_tap(2)
    phase_ffn(last=True)
    p.barrier()

    p.finalize()
    with nc.Block() as block:
        @block.tensor
        def _(eng):
            p.emit("T", eng, sems)

        @block.vector
        def _(eng):
            p.emit("V", eng, sems)

        @block.scalar
        def _(eng):
            p.emit("S", eng, sems)

        @block.gpsimd
        def _(eng):
            p.emit("G", eng, sems)

        @block.sync
        def _(eng):
            p.emit("SY", eng, sems)
    stack.close()
    return nc


_NC_CACHE = None


def kernel(L0_attn_a_x, L0_attn_a_Wq, L0_attn_a_bq, L0_attn_a_Wk, L0_attn_a_bk,
           L0_attn_a_Wv, L0_attn_a_bv, L0_attn_a_Wo, L0_attn_a_bo,
           L0_attn_a_rel_bias, L0_attn_a_W1, L0_attn_a_b1, L0_attn_a_W2,
           L0_attn_a_b2, L0_attn_b_Wq, L0_attn_b_bq, L0_attn_b_Wk, L0_attn_b_bk,
           L0_attn_b_Wv, L0_attn_b_bv, L0_attn_b_Wo, L0_attn_b_bo,
           L0_ffn_w_gate, L0_ffn_w_up, L0_ffn_w_down):
    global _NC_CACHE
    oh, cosT, sinT, i64 = _host_consts()

    def wT(w):
        return np.ascontiguousarray(np.asarray(w, np.float32).T)

    def bR(b):
        return np.ascontiguousarray(np.asarray(b, np.float32).reshape(8, 128).T)

    x = np.asarray(L0_attn_a_x, np.float32)
    common = {
        "aWq": wT(L0_attn_a_Wq), "aWk": wT(L0_attn_a_Wk), "aWv": wT(L0_attn_a_Wv),
        "aWo": wT(L0_attn_a_Wo), "aW1": wT(L0_attn_a_W1), "aW2": wT(L0_attn_a_W2),
        "bWq": wT(L0_attn_b_Wq), "bWk": wT(L0_attn_b_Wk), "bWv": wT(L0_attn_b_Wv),
        "bWo": wT(L0_attn_b_Wo),
        "abq": bR(L0_attn_a_bq), "abk": bR(L0_attn_a_bk), "abv": bR(L0_attn_a_bv),
        "abo": bR(L0_attn_a_bo), "ab1": bR(L0_attn_a_b1), "ab2": bR(L0_attn_a_b2),
        "bbq": bR(L0_attn_b_bq), "bbk": bR(L0_attn_b_bk), "bbv": bR(L0_attn_b_bv),
        "bbo": bR(L0_attn_b_bo),
        "onehot": oh, "i64": i64,
        "wg": np.ascontiguousarray(
            np.asarray(L0_ffn_w_gate, np.int32).T.reshape(D, NHC, 128).transpose(1, 0, 2)),
        "wu": np.ascontiguousarray(
            np.asarray(L0_ffn_w_up, np.int32).T.reshape(D, NHC, 128).transpose(1, 0, 2)),
        "wd": np.ascontiguousarray(
            np.asarray(L0_ffn_w_down, np.int32).T.reshape(HID, 8, 128).transpose(1, 0, 2)),
    }
    in_maps = []
    for core in range(8):
        b, hh = core // 2, core % 2
        m = dict(common)
        m["xT"] = np.ascontiguousarray(x[b, hh * TOK:(hh + 1) * TOK, :].T)
        relroll = np.ascontiguousarray(
            np.roll(np.asarray(L0_attn_a_rel_bias, np.float32), -8 * hh, axis=1))
        m["relT"] = relroll
        m["relflat"] = np.ascontiguousarray(relroll.reshape(1, 512))
        m["cosT"] = np.ascontiguousarray(cosT[:, hh * TOK:(hh + 1) * TOK])
        m["sinT"] = np.ascontiguousarray(sinT[:, hh * TOK:(hh + 1) * TOK])
        in_maps.append(m)

    if _NC_CACHE is None:
        _NC_CACHE = build_program()
    res = run_bass_kernel_spmd(_NC_CACHE, in_maps, core_ids=list(range(8)))
    out = np.empty((4, 4096, D), np.float32)
    for core in range(8):
        b, hh = core // 2, core % 2
        out[b, hh * TOK:(hh + 1) * TOK, :] = res.results[core]["yT"].T
    return out


# version 27
# speedup vs baseline: 1.1769x; 1.0586x over previous
import math
import contextlib
import numpy as np
import ml_dtypes
import concourse.bass as bass
import concourse.mybir as mybir
from concourse.bass_utils import run_bass_kernel_spmd

F32 = mybir.dt.float32
BF16 = mybir.dt.bfloat16
I32 = mybir.dt.int32
AF = mybir.ActivationFunctionType
ALU = mybir.AluOpType
BF = ml_dtypes.bfloat16

D = 1024
TOK = 2048
NCHUNK = 8
CH = 256
HID = 3072
NHC = HID // 128
C_A = 92.0
C_B = 4.0
ENGS = ("T", "V", "S", "G", "SY")
_DEBUG_TAPS = False
ARCOLS = 47000
ARB = ARCOLS * 4


def _bucket_of_n(n):
    ret = 16 if n < 0 else 0
    n = abs(n)
    if n < 8:
        return ret + n
    v = 8 + int(np.float32(np.log(np.float32(n) / np.float32(8.0)))
                / np.float32(math.log(128.0 / 8.0)) * np.float32(8))
    return ret + min(v, 15)


def _host_consts():
    oh = np.zeros((32, 1280), np.float32)
    for i in range(1280):
        d = i - 640
        if abs(d) <= 90:
            oh[_bucket_of_n(-d), i] = 1.0
    freq = np.exp(np.arange(0, D, 2, dtype=np.float32)
                  * np.float32(-np.log(10000.0) / D))
    ang = np.arange(4096, dtype=np.float32)[:, None] * freq[None, :]
    cosT = np.ascontiguousarray(np.cos(ang).T.astype(np.float32))
    sinT = np.ascontiguousarray(np.sin(ang).T.astype(np.float32))
    i64 = np.zeros((128, 64), np.float32)
    i64[:64] = np.eye(64, dtype=np.float32)
    i64[64:] = np.eye(64, dtype=np.float32)
    return oh, cosT, sinT, i64


class Prog:
    def __init__(self, nc, stack):
        self.nc = nc
        self.stack = stack
        self.ops = {e: [] for e in ENGS}
        self.bufs = {}
        self.dsems = {}

    def dma_sem(self, name):
        if name not in self.dsems:
            sem = self.stack.enter_context(self.nc.semaphore("ds_" + name))
            self.dsems[name] = {"sem": sem, "n": 0}
        return self.dsems[name]

    def _deps(self, reads, writes, eng, selfsync):
        deps = []
        for b in reads:
            st = self.bufs.get(b)
            if st and st["w"] is not None:
                deps.append(st["w"])
        for b in writes:
            st = self.bufs.get(b)
            if st:
                if st["w"] is not None:
                    deps.append(st["w"])
                deps.extend(st["r"])
        out, seen = [], set()
        for d in deps:
            if d[0] == "E" and d[1] == eng and not selfsync:
                continue
            if d[0] == "D":
                d = ("D", d[1], self.dsems[d[1]]["n"])
            if d not in seen:
                seen.add(d)
                out.append(d)
        return out

    def _register(self, tok, reads, writes):
        for b in writes:
            self.bufs[b] = {"w": tok, "r": []}
        for b in reads:
            st = self.bufs.setdefault(b, {"w": None, "r": []})
            st["r"].append(tok)

    def op(self, eng, fn, reads=(), writes=(), selfsync=True):
        idx = len(self.ops[eng])
        o = {"kind": "op", "fn": fn,
             "deps": self._deps(reads, writes, eng, selfsync),
             "need_inc": False, "inc_no": None}
        self.ops[eng].append(o)
        tok = ("E", eng, idx)
        self._register(tok, reads, writes)
        return tok

    def dma(self, eng, stream, fn, reads=(), writes=()):
        ds = self.dma_sem(stream)
        deps = self._deps(reads, writes, eng, True)
        ds["n"] += 16
        o = {"kind": "dma", "fn": fn, "deps": deps, "sem": ds["sem"],
             "need_inc": False, "inc_no": None}
        self.ops[eng].append(o)
        tok = ("D", stream, ds["n"])
        self._register(tok, reads, writes)
        return tok

    def async_op(self, eng, stream, fn, reads=(), writes=()):
        ds = self.dma_sem(stream)
        deps = self._deps(reads, writes, eng, True)
        ds["n"] += 1
        o = {"kind": "dma", "fn": fn, "deps": deps, "sem": ds["sem"],
             "inc_amt": 1, "need_inc": False, "inc_no": None}
        self.ops[eng].append(o)
        tok = ("D", stream, ds["n"])
        self._register(tok, reads, writes)
        return tok

    def barrier(self):
        lasts = []
        for e in ENGS:
            if self.ops[e]:
                lasts.append(("E", e, len(self.ops[e]) - 1))
        for name, ds in self.dsems.items():
            lasts.append(("D", name, ds["n"]))
        for e in ENGS:
            deps = [d for d in lasts if not (d[0] == "E" and d[1] == e)]
            self.ops[e].append({"kind": "op", "fn": lambda eng: eng.nop(),
                                "deps": deps, "need_inc": False, "inc_no": None})
        self.bufs = {}

    def finalize(self):
        for e in ENGS:
            for o in self.ops[e]:
                for d in o["deps"]:
                    if d[0] == "E":
                        self.ops[d[1]][d[2]]["need_inc"] = True
        for e in ENGS:
            c = 0
            for o in self.ops[e]:
                if o["need_inc"]:
                    c += 1
                    o["inc_no"] = c

    def emit(self, ename, eng, sems):
        waited = {}
        for o in self.ops[ename]:
            for d in o["deps"]:
                if d[0] == "E":
                    _, e2, idx = d
                    cnt = self.ops[e2][idx]["inc_no"]
                    key = ("E", e2)
                    if waited.get(key, 0) >= cnt:
                        continue
                    waited[key] = cnt
                    eng.wait_ge(sems[e2], cnt)
                else:
                    _, sname, cnt = d
                    key = ("D", sname)
                    if waited.get(key, 0) >= cnt:
                        continue
                    waited[key] = cnt
                    eng.wait_ge(self.dsems[sname]["sem"], cnt)
            inst = o["fn"](eng)
            if o["kind"] == "dma":
                inst.then_inc(o["sem"], o.get("inc_amt", 16))
                if o["need_inc"]:
                    eng.nop().then_inc(sems[ename], 1)
            elif o["need_inc"]:
                inst.then_inc(sems[ename], 1)


def build_program():
    nc = bass.Bass("TRN2", target_bir_lowering=False)
    dt = nc.dram_tensor
    ins = {}
    xT = dt("xT", [D, TOK], F32, kind="ExternalInput").ap()
    for nm in ["aWq", "aWk", "aWv", "aWo", "aW1", "aW2", "bWq", "bWk", "bWv", "bWo"]:
        ins[nm] = dt(nm, [D, D], F32, kind="ExternalInput").ap()
    bias_keys = ["abq", "abk", "abv", "abo", "ab1", "ab2", "bbq", "bbk", "bbv", "bbo"]
    for nm in bias_keys:
        ins[nm] = dt(nm, [128, 8], F32, kind="ExternalInput").ap()
    ins["relflat"] = dt("relflat", [1, 512], F32, kind="ExternalInput").ap()
    ins["relT"] = dt("relT", [32, 16], F32, kind="ExternalInput").ap()
    ins["onehot"] = dt("onehot", [32, 1280], F32, kind="ExternalInput").ap()
    ins["cosT"] = dt("cosT", [512, TOK], F32, kind="ExternalInput").ap()
    ins["sinT"] = dt("sinT", [512, TOK], F32, kind="ExternalInput").ap()
    ins["i64"] = dt("i64", [128, 64], F32, kind="ExternalInput").ap()
    ins["wg"] = dt("wg", [NHC, D, 128], I32, kind="ExternalInput").ap()
    ins["wu"] = dt("wu", [NHC, D, 128], I32, kind="ExternalInput").ap()
    ins["wd"] = dt("wd", [8, HID, 128], I32, kind="ExternalInput").ap()
    yT = dt("yT", [D, TOK], F32, kind="ExternalOutput").ap()
    dbg_taps = []
    if _DEBUG_TAPS:
        for i in range(3):
            dbg_taps.append(dt("dbg%d" % i, [D, TOK], F32, kind="ExternalOutput").ap())
        dbgA = dt("dbgA", [128, 2048], F32, kind="ExternalOutput").ap()
        dbgB = dt("dbgB", [128, 4352], F32, kind="ExternalOutput").ap()
        dbgC = dt("dbgC", [128, 2080], F32, kind="ExternalOutput").ap()
        dbgD = dt("dbgD", [128, 4096], F32, kind="ExternalOutput").ap()
        dbgE = dt("dbgE", [128, 1152], F32, kind="ExternalOutput").ap()
    xbuf = dt("xbuf", [D, TOK], F32, kind="Internal").ap()
    bandh_t = dt("bandh", [2, 1280], BF16, kind="Internal")
    bandh = bandh_t.ap()
    cc_in_k = dt("cc_in_k", [D, TOK], BF16, kind="Internal").ap()
    cc_out_k = dt("cc_out_k", [2 * D, TOK], BF16, kind="Internal").ap()
    cc_in_v = dt("cc_in_v", [D, TOK], BF16, kind="Internal").ap()
    cc_out_v = dt("cc_out_v", [2 * D, TOK], BF16, kind="Internal").ap()

    stack = contextlib.ExitStack()
    E = stack.enter_context
    AR = E(nc.sbuf_tensor("AR", [128, ARCOLS], F32))
    biases = E(nc.sbuf_tensor("biases", [128, 80], F32))
    consts = E(nc.sbuf_tensor("consts", [128, 8], F32))
    rel_sb = E(nc.sbuf_tensor("rel_sb", [32, 16], F32))
    relc_sb = E(nc.sbuf_tensor("relc_sb", [32, 1], BF16))
    oh_sb = E(nc.sbuf_tensor("oh_sb", [32, 1280], BF16))
    wbc = E(nc.sbuf_tensor("wbc", [64, 512], F32))
    wbcn = E(nc.sbuf_tensor("wbcn", [64, 512], F32))
    i64_sb = E(nc.sbuf_tensor("i64_sb", [128, 64], BF16))
    ones_c = E(nc.sbuf_tensor("ones_c", [128, 1], F32))
    ones_cb = E(nc.sbuf_tensor("ones_cb", [128, 1], BF16))
    ones_r = E(nc.sbuf_tensor("ones_r", [1, 128], F32))
    ones_rb = E(nc.sbuf_tensor("ones_rb", [1, 128], BF16))
    ones512 = E(nc.sbuf_tensor("ones512", [1, 512], BF16))
    band_sb = E(nc.sbuf_tensor("band_sb", [1, 1280], BF16))
    recip_sb = E(nc.sbuf_tensor("recip_sb", [1, 512], F32))
    w15row = E(nc.sbuf_tensor("w15row", [1, 1024], BF16))
    totT = E(nc.sbuf_tensor("totT", [1, 128], BF16))
    rb_sb = E(nc.sbuf_tensor("rb_sb", [64, 512], F32))
    rowA = E(nc.sbuf_tensor("rowA", [1, 512], F32))
    rowB = E(nc.sbuf_tensor("rowB", [1, 512], F32))
    rowC = E(nc.sbuf_tensor("rowC", [1, 520], F32))
    zcol64 = E(nc.sbuf_tensor("zcol64", [64, 1], F32))
    PB = [E(nc.psum_tensor(f"pb{i}", [128, 1024], F32)) for i in range(4)]
    sems = {e: E(nc.semaphore("sem_" + e)) for e in ENGS}

    p = Prog(nc, stack)
    arf = AR[:]
    arb_all = bass.AP(arf.tensor, 0, list(arf.ap)).bitcast(BF16)

    def v32(off_b, cols):
        assert off_b % 4 == 0 and off_b + cols * 4 <= ARB, (off_b, cols)
        return arf[:, off_b // 4: off_b // 4 + cols]

    def vbf(off_b, cols):
        assert off_b % 2 == 0 and off_b + cols * 2 <= ARB, (off_b, cols)
        return arb_all[:, off_b // 2: off_b // 2 + cols]

    def T(fn, reads=(), writes=()):
        return p.op("T", fn, reads, writes, selfsync=False)

    def V(fn, reads=(), writes=()):
        return p.op("V", fn, reads, writes)

    def S(fn, reads=(), writes=()):
        return p.op("S", fn, reads, writes)

    def G(fn, reads=(), writes=()):
        return p.op("G", fn, reads, writes)

    def bcol(nm, oc):
        i = bias_keys.index(nm)
        return biases[:, 8 * i + oc: 8 * i + oc + 1]

    def free_dims(ap0, dims):
        return bass.AP(ap0.tensor, ap0.offset, [list(ap0.ap[0])] + dims)

    V(lambda e: e.memset(ones_c[:], 1.0), writes=["ones_c"])
    V(lambda e: e.memset(ones_cb[:], 1.0), writes=["ones_cb"])
    V(lambda e: e.memset(ones_r[:], 1.0), writes=["ones_r"])
    V(lambda e: e.memset(ones_rb[:], 1.0), writes=["ones_rb"])
    V(lambda e: e.memset(ones512[:], 1.0), writes=["ones512"])
    V(lambda e: e.memset(zcol64[:], 0.0), writes=["zc"])
    for col, val in [(0, -C_A), (1, 1.0), (2, -C_B), (3, 0.125),
                     (4, 1e-5), (5, -0.5), (6, 0.0)]:
        V(lambda e, c=col, vv=val: e.memset(consts[:, c:c + 1], vv), writes=["consts"])
    p.dma("G", "cinit", lambda e: e.dma_start(rel_sb[:], ins["relT"]), writes=["rel_sb"])
    p.dma("G", "cinit", lambda e: e.dma_start(oh_sb[:], ins["onehot"]), writes=["oh_sb"])
    p.dma("G", "cinit", lambda e: e.dma_start(i64_sb[:], ins["i64"]), writes=["i64_sb"])
    relf = rowB[0:1, 0:512]
    p.dma("G", "cinit", lambda e: e.dma_start(relf, ins["relflat"]), writes=["relf"])
    for i, nm in enumerate(bias_keys):
        p.dma("G", "cinit",
              lambda e, i=i, nm=nm: e.dma_start(biases[:, 8 * i:8 * i + 8], ins[nm]),
              writes=["biases"])
    T(lambda e: e.matmul(PB[3][0:64, 0:512], ones_r[0:1, 0:64], relf,
                         start=True, stop=True),
      reads=["ones_r", "relf"], writes=["pb3a"])
    V(lambda e: e.tensor_copy(wbc[:], PB[3][0:64, 0:512]), reads=["pb3a"], writes=["wbc"])
    V(lambda e: e.tensor_scalar(wbcn[:], wbc[:], -1.0, None, ALU.mult),
      reads=["wbc"], writes=["wbcn"])

    def ln_stats(xg, xk, xsq, sqk, gw, eps, scale, pba, pbb, ka, kb, onesq):
        V(lambda e: e.tensor_tensor(xsq, xg, xg, ALU.mult), reads=[xk], writes=[sqk])
        for kc in range(8):
            T(lambda e, kc=kc: e.matmul(pba[0:1, 0:gw], ones_c[:],
                                        xg[:, kc * gw:(kc + 1) * gw],
                                        start=(kc == 0), stop=(kc == 7)),
              reads=["ones_c", xk], writes=[ka])
        for kc in range(8):
            T(lambda e, kc=kc: e.matmul(pbb[0:1, 0:gw], onesq[:],
                                        xsq[:, kc * gw:(kc + 1) * gw],
                                        start=(kc == 0), stop=(kc == 7)),
              reads=["ones_cb", sqk], writes=[kb])
        V(lambda e: e.tensor_scalar(rowA[0:1, 0:gw], pba[0:1, 0:gw],
                                    1.0 / D, None, ALU.mult),
          reads=[ka], writes=["rowAm"])
        V(lambda e: e.tensor_scalar(rowB[0:1, 0:gw], pbb[0:1, 0:gw],
                                    1.0 / D, None, ALU.mult),
          reads=[kb], writes=["rowBe"])
        V(lambda e: e.tensor_tensor(rowC[0:1, 0:gw], rowA[0:1, 0:gw],
                                    rowA[0:1, 0:gw], ALU.mult),
          reads=["rowAm"], writes=["rowCm2"])
        V(lambda e: e.tensor_tensor(rowB[0:1, 0:gw], rowB[0:1, 0:gw],
                                    rowC[0:1, 0:gw], ALU.subtract),
          reads=["rowBe", "rowCm2"], writes=["rowBv"])
        epsc = consts[0:1, 4:5] if eps else consts[0:1, 6:7]
        S(lambda e: e.activation(rowB[0:1, 0:gw], rowB[0:1, 0:gw], AF.Ln,
                                 bias=epsc, scale=consts[0:1, 1:2]),
          reads=["rowBv", "consts"], writes=["rowBl"])
        S(lambda e: e.activation(rowB[0:1, 0:gw], rowB[0:1, 0:gw], AF.Exp,
                                 bias=consts[0:1, 6:7], scale=consts[0:1, 5:6]),
          reads=["rowBl", "consts"], writes=["rowBr"])
        if scale != 1.0:
            V(lambda e: e.tensor_scalar(rowB[0:1, 0:gw], rowB[0:1, 0:gw],
                                        scale, None, ALU.mult),
              reads=["rowBr"], writes=["rowBr"])
        V(lambda e: e.tensor_tensor(rowA[0:1, 0:gw], rowA[0:1, 0:gw],
                                    rowB[0:1, 0:gw], ALU.mult),
          reads=["rowAm", "rowBr"], writes=["rowAb"])
        T(lambda e: e.matmul(pba[:, 0:gw], ones_r[:], rowB[0:1, 0:gw],
                             start=True, stop=True),
          reads=["ones_r", "rowBr"], writes=[ka])
        T(lambda e: e.matmul(pbb[:, 0:gw], ones_r[:], rowA[0:1, 0:gw],
                             start=True, stop=True),
          reads=["ones_r", "rowAb"], writes=[kb])

    def r3(x, gw):
        return x.rearrange("p (kc w) -> p kc w", kc=8)

    def bc3g(pb, gw):
        a = pb[:, 0:gw]
        return bass.AP(a.tensor, a.offset, [list(a.ap[0]), [0, 8], list(a.ap[1])])

    def ln_apply(xg3, xk, tmp3, tk, out3, ok, gw, pba, pbb, ka, kb):
        V(lambda e: e.tensor_tensor(tmp3, xg3, bc3g(pba, gw), ALU.mult),
          reads=[xk, ka], writes=[tk])
        V(lambda e: e.tensor_tensor(out3, tmp3, bc3g(pbb, gw), ALU.subtract),
          reads=[tk, kb], writes=[ok])

    def proj(wv, wkkey, mvfull, mvkey, ntok, evac):
        ntg = max(1, ntok // 1024)
        tg_w = min(ntok, 1024)
        nq = max(1, tg_w // 512)
        qw = min(tg_w, 512)
        for oc in range(8):
            for tg in range(ntg):
                pbi = (oc * ntg + tg) % 2
                pbks = ["pb%da" % pbi] + (["pb%db" % pbi] if nq > 1 else [])
                for kc in range(8):
                    for q in range(nq):
                        mv = mvfull[:, kc * ntok + tg * tg_w + q * qw:
                                    kc * ntok + tg * tg_w + (q + 1) * qw]
                        pbdst = PB[pbi][:, q * 512: q * 512 + qw]
                        mk = mvkey(tg, q) if callable(mvkey) else mvkey
                        T(lambda e, kc=kc, oc=oc, mv=mv, pbdst=pbdst: e.matmul(
                            pbdst, wv[:, kc * 1024 + oc * 128: kc * 1024 + (oc + 1) * 128],
                            mv, start=(kc == 0), stop=(kc == 7)),
                          reads=[wkkey, mk], writes=[pbks[q]])
                evac(oc, tg, PB[pbi][:, 0:tg_w], pbks)

    def phase_attn_a():
        oXC = 0
        oW = oXC + 16384
        oQKV = oW + 49152
        oQv = oQKV + 12288
        oKv = oQv + 16384
        oVv = oKv + 16384
        oVT = oVv + 8512
        oBG = oVT + 8320
        oHK = oBG + 17408
        oXN = oHK + 4608
        oAS = oXN + 8192
        oSQ = oAS + 8192
        oOT = oSQ + 4096
        oPP = oOT + 4096
        oSC = oPP + 6144
        assert oSC + 4096 <= ARB, oSC

        pbPROJ = PB[3][:, 0:512]
        pbA = PB[3][:, 512:768]
        pbB = PB[3][:, 768:1024]
        pbM = PB[3][:, 512:1024]

        Vvv = vbf(oVv, 4256)
        OT = vbf(oOT, 2048)
        ff1 = vbf(oOT, 2048)

        def Qvp(c):
            return vbf(oQv + (c % 2) * 8192, 4096)

        def Kvp(c):
            return vbf(oKv + (c % 2) * 8192, 4096)

        def vtp(c):
            return vbf(oVT + (c % 2) * 4160, 2080)

        def bigp(c):
            return vbf(oBG + (c % 2) * 8704, 4352)

        def hkp(c):
            return vbf(oHK + (c % 2) * 2304, 1152)

        wslot = [vbf(oW + i * 16384, 8192) for i in range(3)]
        STREAM = [("aWq", 0), ("aWk", 0), ("aWv", 0),
                  ("aWq", 1), ("aWk", 1), ("aWv", 1)]
        for c in range(1, NCHUNK):
            STREAM += [("aWo", c - 1), ("aW1", c - 1), ("aW2", c - 1)]
            if c + 1 < NCHUNK:
                STREAM += [("aWq", c + 1), ("aWk", c + 1), ("aWv", c + 1)]
        STREAM += [("aWo", NCHUNK - 1), ("aW1", NCHUNK - 1), ("aW2", NCHUNK - 1)]
        jof = {key: j for j, key in enumerate(STREAM)}

        def wload(j):
            if j >= len(STREAM):
                return
            nm, c = STREAM[j]
            sl = j % 3
            srcw = ins[nm].rearrange("(kc p) d -> p kc d", p=128)
            p.dma("G", "wsl%d" % sl,
                  lambda e, srcw=srcw, sl=sl: e.dma_start(wslot[sl], srcw),
                  writes=["R_WSL%d" % sl])

        def projS(j, mvfull, mvkey, evac):
            sl = j % 3
            wv = wslot[sl]
            wk = "R_WSL%d" % sl
            for oc in range(8):
                for h in range(2):
                    for kc in range(4 * h, 4 * h + 4):
                        T(lambda e, kc=kc, oc=oc, wv=wv, mvfull=mvfull: e.matmul(
                            pbPROJ[:, 0:256],
                            wv[:, kc * 1024 + oc * 128: kc * 1024 + (oc + 1) * 128],
                            mvfull[:, kc * 256:(kc + 1) * 256],
                            start=(kc == 0), stop=(kc == 7)),
                          reads=[wk, mvkey], writes=["pb3p"])
                    if h == 0:
                        yield (lambda: None)
                yield (lambda oc=oc: evac(oc))
            yield (lambda j=j: wload(j + 3))

        QT, KT, VT = vbf(oQKV, 2048), vbf(oQKV + 4096, 2048), vbf(oQKV + 8192, 2048)
        qdst = {"aWq": 0, "aWk": 4096, "aWv": 8192}
        bnm = {"aWq": "abq", "aWk": "abk", "aWv": "abv"}
        qkey = {"aWq": "R_QT", "aWk": "R_KT", "aWv": "R_VT"}

        def mk_ev(nm):
            def ev(oc):
                dstv = vbf(oQKV + qdst[nm] + oc * 512, 256)
                V(lambda e, dstv=dstv, oc=oc, nm=nm: e.tensor_scalar(
                    dstv, pbPROJ[:, 0:256], bcol(bnm[nm], oc), None, ALU.add),
                  reads=["pb3p", "biases"], writes=[qkey[nm]])
            return ev

        def P1_thunks(c):
            ci = c % 2
            xc = v32(oXC + ci * 8192, 2048)
            xck = "a_xc%d" % ci
            xn = vbf(oXN + ci * 4096, 2048)
            xnk = "R_XN%d" % ci
            srcx = xT.rearrange("(kc p) t -> p kc t", p=128)[:, :, c * CH:(c + 1) * CH]
            yield lambda: p.dma("SY", "xin%d" % ci,
                                lambda e, srcx=srcx, xc=xc: e.dma_start(xc, srcx),
                                writes=[xck])
            xsq = vbf(oSQ, 2048)

            def _ln1(xc=xc, xck=xck, xn=xn, xnk=xnk):
                ln_stats(xc, xck, xsq, "R_SQ", CH, True, 1.0,
                         pbA, pbB, "pb3b", "pb3b", ones_cb)
                ln_apply(r3(xc, CH), xck, r3(xsq, CH), "R_SQ",
                         r3(xn, CH), xnk, CH, pbA, pbB, "pb3b", "pb3b")
            yield _ln1
            yield from projS(jof[("aWq", c)], xn, xnk, mk_ev("aWq"))
            yield from projS(jof[("aWk", c)], xn, xnk, mk_ev("aWk"))
            yield from projS(jof[("aWv", c)], xn, xnk, mk_ev("aWv"))
            par = c % 2
            Qv = Qvp(c)
            Kv = Kvp(c)
            kq, kq2 = "a_Qv%d" % par, "a_Qv2_%d" % par
            kk, kk2 = "a_Kv%d" % par, "a_Kv2_%d" % par
            for m in range(16):
                def _cp(m=m, Qv=Qv, Kv=Kv, kq=kq, kk=kk):
                    def sl(X, m=m):
                        return X[(m % 2) * 64:(m % 2) * 64 + 64,
                                 (m // 2) * 256:(m // 2) * 256 + 256]
                    V(lambda e, m=m, s=sl(QT), Qv=Qv: e.tensor_copy(Qv[0:64, m:4096:16], s),
                      reads=["R_QT"], writes=[kq])
                    V(lambda e, m=m, s=sl(VT): e.tensor_copy(Vvv[0:64, m:4096:16], s),
                      reads=["R_VT"], writes=["a_Vv"])
                    dk = free_dims(Kv[0:64, (127 - m):(128 - m)], [[128, 32], [-16, 8]])
                    sk3 = sl(KT).rearrange("p (a r) -> p a r", a=32)
                    V(lambda e, dk=dk, sk3=sk3: e.tensor_copy(dk, sk3),
                      reads=["R_KT"], writes=[kk])
                    dv = free_dims(Vvv[64:128, (127 - m):(128 - m)], [[128, 32], [-16, 8]])
                    sv3 = sl(VT).rearrange("p (a r) -> p a r", a=32)
                    V(lambda e, dv=dv, sv3=sv3: e.tensor_copy(dv, sv3),
                      reads=["R_VT"], writes=["a_Vvr"])
                yield _cp
            for h4 in range(2):
                yield lambda h4=h4, Qv=Qv, kq=kq, kq2=kq2: V(
                    lambda e, h4=h4, Qv=Qv: e.tensor_copy(
                        Qv[64:128, h4 * 2048:(h4 + 1) * 2048],
                        Qv[0:64, h4 * 2048:(h4 + 1) * 2048]),
                    reads=[kq], writes=[kq2])
                yield lambda h4=h4, Kv=Kv, kk=kk, kk2=kk2: V(
                    lambda e, h4=h4, Kv=Kv: e.tensor_copy(
                        Kv[64:128, h4 * 2048:(h4 + 1) * 2048],
                        Kv[0:64, h4 * 2048:(h4 + 1) * 2048]),
                    reads=[kk], writes=[kk2])

        def P2_thunks(c):
            ci = c % 2
            vt = vtp(c)
            big = bigp(c)
            hk = hkp(c)

            def _band():
                V(lambda e, c=c: e.tensor_copy(relc_sb[:], rel_sb[0:32, c:c + 1]),
                  reads=["rel_sb"], writes=["relc"])
                for pc in range(4):
                    T(lambda e, pc=pc: e.matmul(pbM[0:1, 0:320],
                                                relc_sb[:],
                                                oh_sb[:, pc * 320:(pc + 1) * 320],
                                                start=True, stop=True),
                      reads=["relc", "oh_sb"], writes=["pb3b"])
                    V(lambda e, pc=pc: e.tensor_copy(band_sb[0:1, pc * 320:(pc + 1) * 320],
                                                     pbM[0:1, 0:320]),
                      reads=["pb3b"], writes=["band_sb"])
                p.dma("SY", "bando%d" % ci,
                      lambda e, ci=ci: e.dma_start(bandh[ci:ci + 1, :], band_sb[0:1, :1280]),
                      reads=["band_sb"], writes=["bandh%d" % ci])
            yield _band
            pbT = PB[3][:, 256:512]
            for tt in range(0, 32, 4):
                def _tr(tt=tt):
                    for j in range(4):
                        t = tt + j
                        T(lambda e, t=t, j=j: e.matmul(pbT[:, j * 64:j * 64 + 64],
                                                       Vvv[64:128, t * 128:(t + 1) * 128],
                                                       i64_sb[64:128, :], start=True, stop=True,
                                                       tile_position=(64, 0)),
                          reads=["a_Vvr", "i64_sb"], writes=["pb3t"])
                    dst = free_dims(vt[:, tt * 65:tt * 65 + 1], [[65, 4], [1, 64]])
                    src4 = pbT.rearrange("p (a b) -> p a b", a=4)
                    V(lambda e, dst=dst, src4=src4: e.tensor_copy(dst, src4),
                      reads=["pb3t"], writes=["a_vtok%d" % ci])
                yield _tr

            def _scan():
                zcb = zcol64[:].broadcast_to([64, 4255])
                V(lambda e, zcb=zcb, big=big: e.tensor_tensor_scan(big[0:64, 97:4352], zcb,
                                                                   Vvv[0:64, 0:4255], 0.0,
                                                                   ALU.add, ALU.add),
                  reads=["zc", "a_Vv", "a_Vvp"], writes=["a_ps%d" % ci])
                T(lambda e, big=big: e.matmul(pbM[0:1, 0:64], big[0:64, 4192:4193],
                                              i64_sb[0:64, :], start=True, stop=True),
                  reads=["a_ps%d" % ci, "i64_sb"], writes=["pb3b"])
                V(lambda e, ci=ci: e.tensor_copy(totT[:, ci * 64:ci * 64 + 64], pbM[0:1, 0:64]),
                  reads=["pb3b"], writes=["totT%d" % ci])
                V(lambda e, c=c, ci=ci: e.tensor_scalar(w15row[:, ci * 512:ci * 512 + 512],
                                                        ones512[:],
                                                        wbc[0:1, 240 + c:241 + c], None, ALU.mult),
                  reads=["ones512", "wbc"], writes=["w15%d" % ci])
            yield _scan

            def _hk():
                hsrc = bass.AP(bandh_t, ci * 1280 + 1, [[1, 128], [1, 1152]])
                p.dma("SY", "hank",
                      lambda e, hsrc=hsrc, hk=hk: e.dma_start(hk, hsrc),
                      reads=["bandh%d" % ci], writes=["a_hk%d" % ci])
            yield _hk

        def A_emit(c, feeder):
            par = c % 2
            Qv = Qvp(c)
            Kv = Kvp(c)
            vt = vtp(c)
            big = bigp(c)
            hk = hkp(c)
            kq, kq2 = "a_Qv%d" % par, "a_Qv2_%d" % par
            kk, kk2 = "a_Kv%d" % par, "a_Kv2_%d" % par
            vtk = "a_vtok%d" % par
            psk = "a_ps%d" % par
            outk = "a_out%d" % par
            hkk = "a_hk%d" % par
            totc = totT[:, par * 64:par * 64 + 64]
            w15c = w15row[:, par * 512:par * 512 + 512]
            pend = [None]

            def flush_combine():
                if pend[0] is not None:
                    pend[0]()
                    pend[0] = None

            def draw(n=1):
                for _ in range(n):
                    th = next(feeder, None)
                    if th is not None:
                        th()

            def qg_params(g):
                pvh, bdh = ("pb2a", "pb2b") if g % 2 == 0 else ("pb2b", "pb2a")
                pvo = 0 if g % 2 == 0 else 512
                return pvh, bdh, pvo, 512 - pvo

            def emit_pv(g, pi):
                pvh, _, pvo, _ = qg_params(g)
                gp = g * 16 + pi
                pp = vbf(oPP + (gp % 3) * 2048, 1024)
                ppk = "a_pp%d" % (gp % 3)
                for j, tj in enumerate((2 * pi, 2 * pi + 1)):
                    T(lambda e, tj=tj, pp=pp, j=j, pvo=pvo: e.matmul(
                        PB[2][0:65, pvo:pvo + 512], vt[:, tj * 65:tj * 65 + 65],
                        pp[:, j * 512:(j + 1) * 512],
                        start=(tj == 0), stop=(tj == 31)),
                      reads=[vtk, ppk], writes=[pvh])

            def band_mms(g):
                q0 = g * 512
                _, bdh, _, bdo = qg_params(g)
                mms = []
                first = [True]
                for d0 in (128, 0, -128, -256, -384, -512):
                    k0 = q0 - d0
                    if k0 < 0 or k0 >= 4096:
                        continue
                    t = k0 // 128

                    def mk(t=t, d0=d0, bdo=bdo, bdh=bdh):
                        fb = first[0]
                        first[0] = False
                        def go(fb=fb):
                            T(lambda e, t=t, d0=d0, fb=fb, bdo=bdo: e.matmul(
                                PB[2][0:64, bdo:bdo + 512], vt[:, t * 65:t * 65 + 64],
                                hk[:, 512 + d0:512 + d0 + 512],
                                start=fb, stop=False),
                              reads=[vtk, hkk], writes=[bdh])
                        return go
                    mms.append(mk())

                def w15mm():
                    T(lambda e, bdo=bdo: e.matmul(PB[2][0:64, bdo:bdo + 512], totc, w15c,
                                                  start=False, stop=True),
                      reads=["totT%d" % par, "w15%d" % par], writes=[bdh])
                mms.append(w15mm)
                return mms

            def emit_o2e(g):
                _, bdh, _, bdo = qg_params(g)
                o2e = v32(oSC + 2048, 512)[0:64, :]
                V(lambda e, o2e=o2e, bdo=bdo: e.tensor_copy(o2e, PB[2][0:64, bdo:bdo + 512]),
                  reads=[bdh], writes=["R_O2"])

            def emit_recip(g):
                pvh, _, pvo, _ = qg_params(g)
                V(lambda e, pvo=pvo: e.reciprocal(recip_sb[:], PB[2][64:65, pvo:pvo + 512]),
                  reads=[pvh], writes=["recip"])

            def mk_combine(g):
                q0 = g * 512
                pvh, _, pvo, _ = qg_params(g)

                def go():
                    T(lambda e: e.matmul(pbM[0:64, 0:512], ones_r[0:1, 0:64],
                                         recip_sb[:], start=True, stop=True),
                      reads=["ones_r", "recip"], writes=["pb3b"])
                    V(lambda e: e.tensor_copy(rb_sb[:], pbM[0:64, 0:512]),
                      reads=["pb3b"], writes=["rb"])
                    o1 = v32(oSC, 512)[0:64, :]
                    o2 = v32(oSC + 2048, 512)[0:64, :]
                    V(lambda e, o1=o1: e.tensor_tensor(o1, PB[2][0:64, pvo:pvo + 512],
                                                       rb_sb[:], ALU.mult),
                      reads=[pvh, "rb"], writes=["R_O1"])
                    V(lambda e, o2=o2, o1=o1: e.tensor_tensor(
                        o2, o1, o2, ALU.add),
                      reads=["R_O1", "R_O2"], writes=["R_O2"])
                    V(lambda e, o1=o1: e.scalar_tensor_tensor(
                        o1, big[0:64, q0 + 6:q0 + 518],
                        wbc[0:64, 496 + c:497 + c], o2, ALU.mult, ALU.add),
                      reads=[psk, "wbc", "R_O2"], writes=["R_O1"])
                    V(lambda e, o1=o1: e.scalar_tensor_tensor(
                        big[64:128, q0:q0 + 512], big[0:64, q0 + 187:q0 + 699],
                        wbcn[0:64, 240 + c:241 + c], o1, ALU.mult, ALU.add),
                      reads=[psk, "wbcn", "R_O1"], writes=[outk])
                return go

            bandq = []

            for g in range(8):
                q0 = g * 512
                for pi in range(16):
                    t0, t1 = 2 * pi, 2 * pi + 1
                    sp = PB[pi % 2]
                    ska, skb = "pb%da" % (pi % 2), "pb%db" % (pi % 2)
                    gp = g * 16 + pi
                    T(lambda e, t0=t0, sp=sp, q0=q0: e.matmul(
                        sp[:, 0:512], Kv[0:64, t0 * 128:(t0 + 1) * 128],
                        Qv[0:64, q0:q0 + 512], start=True, stop=True,
                        tile_position=(0, 0)),
                      reads=[kk, kq], writes=[ska])
                    T(lambda e, t1=t1, sp=sp, q0=q0: e.matmul(
                        sp[:, 512:1024], Kv[64:128, t1 * 128:(t1 + 1) * 128],
                        Qv[64:128, q0:q0 + 512], start=True, stop=True,
                        tile_position=(64, 0)),
                      reads=[kk2, kq2], writes=[skb])
                    pp = vbf(oPP + (gp % 3) * 2048, 1024)
                    ppk = "a_pp%d" % (gp % 3)
                    S(lambda e, pp=pp, sp=sp: e.activation(
                        pp, sp[:, :], AF.Exp, bias=consts[:, 0:1], scale=consts[:, 1:2]),
                      reads=[ska, skb, "consts"], writes=[ppk])
                    if pi == 0:
                        if g > 0:
                            emit_pv(g - 1, 14)
                            for mm in bandq[3:]:
                                mm()
                            emit_o2e(g - 1)
                        else:
                            draw()
                    elif pi == 1:
                        if g > 0:
                            emit_pv(g - 1, 15)
                            emit_recip(g - 1)
                            pend[0] = mk_combine(g - 1)
                        else:
                            draw()
                    elif pi == 5:
                        emit_pv(g, 3)
                        flush_combine()
                    elif pi == 15:
                        emit_pv(g, 13)
                        bandq = band_mms(g)
                        for mm in bandq[:3]:
                            mm()
                    else:
                        emit_pv(g, pi - 2)
                        draw()
                        if pi in (7, 9, 11, 13):
                            draw()
            emit_pv(7, 14)
            for mm in bandq[3:]:
                mm()
            emit_o2e(7)
            emit_pv(7, 15)
            emit_recip(7)
            flush_combine()
            mk_combine(7)()

        def T_thunks(c):
            ci = c % 2
            xc = v32(oXC + ci * 8192, 2048)
            xck = "a_xc%d" % ci
            xn = vbf(oXN + ci * 4096, 2048)
            xnk = "R_XN%d" % ci
            as2 = v32(oAS, 2048)
            xsq2 = vbf(oSQ, 2048)
            ab = vbf(oXN + ci * 4096, 2048)
            big = bigp(c)
            outk = "a_out%d" % ci
            for m in range(16):
                def _uv(m=m):
                    dsto = OT[(m % 2) * 64:(m % 2) * 64 + 64,
                              (m // 2) * 256:(m // 2) * 256 + 256]
                    V(lambda e, dsto=dsto, m=m, big=big: e.tensor_copy(
                        dsto, big[64:128, m:4096:16]),
                      reads=[outk], writes=["R_OT"])
                yield _uv

            def evo(oc):
                V(lambda e, oc=oc: e.scalar_tensor_tensor(
                    as2[:, oc * 256:(oc + 1) * 256], pbPROJ[:, 0:256], bcol("abo", oc),
                    ab[:, oc * 256:(oc + 1) * 256], ALU.add, ALU.add),
                  reads=["pb3p", "biases", xnk], writes=["R_AS"])
            yield from projS(jof[("aWo", c)], OT, "R_OT", evo)

            def _ln2(ab=ab, xnk=xnk):
                ln_stats(as2, "R_AS", xsq2, "R_SQ", CH, False, 1.0,
                         pbA, pbB, "pb3b", "pb3b", ones_cb)
                ln_apply(r3(as2, CH), "R_AS", r3(xsq2, CH), "R_SQ",
                         r3(ab, CH), xnk, CH, pbA, pbB, "pb3b", "pb3b")
            yield _ln2

            def ev1(oc):
                V(lambda e, oc=oc: e.tensor_scalar(
                    ff1[:, oc * 256:(oc + 1) * 256], pbPROJ[:, 0:256],
                    bcol("ab1", oc), None, ALU.add),
                  reads=["pb3p", "biases"], writes=["R_OT"])
            yield from projS(jof[("aW1", c)], ab, xnk, ev1)

            def ev2(oc):
                V(lambda e, oc=oc: e.scalar_tensor_tensor(
                    as2[:, oc * 256:(oc + 1) * 256], pbPROJ[:, 0:256], bcol("ab2", oc),
                    ab[:, oc * 256:(oc + 1) * 256], ALU.add, ALU.add),
                  reads=["pb3p", "biases", xnk], writes=["R_AS"])
            yield from projS(jof[("aW2", c)], ff1, "R_OT", ev2)

            def _ln3():
                ln_stats(as2, "R_AS", xsq2, "R_SQ", CH, False, 1.0,
                         pbA, pbB, "pb3b", "pb3b", ones_cb)
                as23 = r3(as2, CH)
                xc3 = r3(xc, CH)
                V(lambda e: e.tensor_tensor(as23, as23, bc3g(pbA, CH), ALU.mult),
                  reads=["R_AS", "pb3b"], writes=["R_AS"])
                V(lambda e: e.tensor_tensor(as23, as23, bc3g(pbB, CH), ALU.subtract),
                  reads=["R_AS", "pb3b"], writes=["R_AS"])
                V(lambda e: e.tensor_tensor(as23, as23, xc3, ALU.add),
                  reads=["R_AS", xck], writes=["R_AS"])
                dstx = xbuf.rearrange("(kc p) t -> p kc t", p=128)[:, :, c * CH:(c + 1) * CH]
                p.dma("SY", "xout",
                      lambda e, dstx=dstx, as2=as2: e.dma_start(dstx, as2),
                      reads=["R_AS"], writes=[])
            yield _ln3

        def drain(gen):
            for th in gen:
                th()

        import itertools
        for ci2 in range(2):
            V(lambda e, ci2=ci2: e.memset(
                free_dims(vbf(oVT + ci2 * 4160, 2080)[:, 64:65], [[65, 32]]), 1.0),
              writes=["a_vtok%d" % ci2])
            V(lambda e, ci2=ci2: e.memset(vbf(oBG + ci2 * 8704, 4352)[0:64, 0:97], 0.0),
              writes=["a_ps%d" % ci2])
        V(lambda e: e.memset(Vvv[0:64, 4096:4256], 0.0), writes=["a_Vvp"])
        for j in range(3):
            wload(j)
        drain(P1_thunks(0))
        drain(P2_thunks(0))
        carry = iter(())
        for c in range(NCHUNK):
            feeder = itertools.chain(
                carry, P1_thunks(c + 1) if c + 1 < NCHUNK else iter(()))
            A_emit(c, feeder)
            drain(feeder)
            if c + 1 < NCHUNK:
                drain(P2_thunks(c + 1))
            carry = T_thunks(c)
        drain(carry)

    def phase_ffn(last):
        oXG = 0
        oSQ = oXG + 32768
        oXN = oSQ + 8192
        oH = oXN + 32768
        oWG = oH + 98304
        oGS = oWG + 8192
        assert oGS + 4096 <= ARB, oGS
        oWD = 0
        oXS = oWD + 12288
        src_in = xbuf
        dst_out = yT if last else xbuf
        xnf = vbf(oXN, 16384)
        for g in range(4):
            xg = v32(oXG + (g % 2) * 16384, 4096)
            xgk = "R_FXG%d" % (g % 2)
            srcx = src_in.rearrange("(kc p) t -> p kc t", p=128)[:, :, g * 512:(g + 1) * 512]
            p.dma("SY", "fxin%d" % (g % 2), lambda e, srcx=srcx, xg=xg: e.dma_start(xg, srcx),
                  writes=[xgk])
            xsq = vbf(oSQ, 4096)
            ln_stats(xg, xgk, xsq, "R_FSQ", 512, True, 1.0 / 127.0,
                     PB[3][:, 0:512], PB[3][:, 512:1024], "pb3a", "pb3b", ones_cb)
            xn3 = free_dims(xnf[0:128, g * 512:g * 512 + 1], [[2048, 8], [1, 512]])
            ln_apply(r3(xg, 512), xgk, r3(xsq, 512), "R_FSQ", xn3, "f_xn%d" % g, 512,
                     PB[3][:, 0:512], PB[3][:, 512:1024], "pb3a", "pb3b")

        H = vbf(oH, 49152)
        for hc in range(NHC):
            hi = hc % 2
            wgv = vbf(oWG + hi * 2048, 1024)
            wuv = vbf(oWG + 4096 + hi * 2048, 1024)
            sg = ins["wg"][hc, :, :].rearrange("(kc p) h -> p kc h", p=128)
            su = ins["wu"][hc, :, :].rearrange("(kc p) h -> p kc h", p=128)
            p.dma("G", "fwg%d" % hi, lambda e, sg=sg, wgv=wgv: e.dma_start(wgv, sg),
                  writes=["f_wg%d" % hi])
            p.dma("G", "fwu%d" % hi, lambda e, su=su, wuv=wuv: e.dma_start(wuv, su),
                  writes=["f_wu%d" % hi])
            for seg in range(4):
                sb = seg % 2
                gk = "pb0a" if sb == 0 else "pb0b"
                uk = "pb1a" if sb == 0 else "pb1b"
                gdep = "f_xn%d" % seg
                for kc in range(8):
                    mv = xnf[:, kc * 2048 + seg * 512: kc * 2048 + (seg + 1) * 512]
                    T(lambda e, mv=mv, wgv=wgv, kc=kc, sb=sb: e.matmul(
                        PB[0][:, sb * 512:(sb + 1) * 512],
                        wgv[:, kc * 128:(kc + 1) * 128], mv,
                        start=(kc == 0), stop=(kc == 7)),
                      reads=["f_wg%d" % hi, gdep], writes=[gk])
                for kc in range(8):
                    mv = xnf[:, kc * 2048 + seg * 512: kc * 2048 + (seg + 1) * 512]
                    T(lambda e, mv=mv, wuv=wuv, kc=kc, sb=sb: e.matmul(
                        PB[1][:, sb * 512:(sb + 1) * 512],
                        wuv[:, kc * 128:(kc + 1) * 128], mv,
                        start=(kc == 0), stop=(kc == 7)),
                      reads=["f_wu%d" % hi, gdep], writes=[uk])
                gs = vbf(oGS + sb * 2048, 512)
                S(lambda e, gs=gs, sb=sb: e.activation(gs, PB[0][:, sb * 512:(sb + 1) * 512],
                                                       AF.Silu),
                  reads=[gk], writes=["f_gs%d" % sb])
                hdst = H[:, hc * 2048 + seg * 512: hc * 2048 + (seg + 1) * 512]
                V(lambda e, hdst=hdst, gs=gs, sb=sb: e.tensor_tensor(
                    hdst, gs, PB[1][:, sb * 512:(sb + 1) * 512], ALU.mult),
                  reads=["f_gs%d" % sb, uk], writes=["f_H%d" % hc])

        for oc in range(8):
            oi = oc % 2
            wdv = vbf(oWD + oi * 6144, 3072)
            sd = ins["wd"][oc, :, :].rearrange("(kt p) d -> p kt d", p=128)
            p.dma("G", "fwd%d" % oi, lambda e, sd=sd, wdv=wdv: e.dma_start(wdv, sd),
                  writes=["f_wd%d" % oi, "R_FXG0"])
            for seg in range(4):
                sb = seg % 2
                dk = "pb2a" if sb == 0 else "pb2b"
                for kt in range(NHC):
                    mv = H[:, kt * 2048 + seg * 512: kt * 2048 + (seg + 1) * 512]
                    T(lambda e, kt=kt, mv=mv, wdv=wdv, sb=sb: e.matmul(
                        PB[2][:, sb * 512:(sb + 1) * 512],
                        wdv[:, kt * 128:(kt + 1) * 128], mv,
                        start=(kt == 0), stop=(kt == NHC - 1)),
                      reads=["f_wd%d" % oi, "f_H%d" % kt], writes=[dk])
                xs = v32(oXS + seg * 2048, 512)
                xsk = "R_FXG0" if seg < 2 else "R_FXG1"
                srcx = src_in[oc * 128:(oc + 1) * 128, seg * 512:(seg + 1) * 512]
                p.dma("SY", "fx2%d" % seg,
                      lambda e, srcx=srcx, xs=xs: e.dma_start(xs, srcx),
                      writes=[xsk])
                xo = v32(oXG + 28672 + (seg % 2) * 2048, 512)
                xok = "R_FXG1"
                V(lambda e, xo=xo, xs=xs, sb=sb: e.scalar_tensor_tensor(
                    xo, PB[2][:, sb * 512:(sb + 1) * 512], 1.0 / 127.0, xs,
                    ALU.mult, ALU.add),
                  reads=[dk, xsk], writes=[xok])
                dstx = dst_out[oc * 128:(oc + 1) * 128, seg * 512:(seg + 1) * 512]
                p.dma("SY", "fxo%d" % (seg % 2),
                      lambda e, dstx=dstx, xo=xo: e.dma_start(dstx, xo),
                      reads=[xok], writes=[])

    def phase_attn_b():
        oXG = 0
        oSQ = oXG + 16384
        oCS = oSQ + 16384
        oRX = oCS + 16384
        oQT = oRX + 32768
        oST = oQT + 32768
        oW = oST + 17408
        oPP = 0
        oXS = 4096
        oXO2 = oXS + 8192
        assert oW + 32768 <= ARB

        rxn = vbf(oRX, 16384)

        def lnb(g):
            xg = v32(oXG, 4096)
            xgk = "R_BXG"
            srcg = xbuf.rearrange("(kc p) t -> p kc t", p=128)[:, :, g * 512:(g + 1) * 512]
            p.dma("SY", "bxin", lambda e, srcg=srcg, xg=xg: e.dma_start(xg, srcg),
                  writes=[xgk])
            xsq = v32(oSQ, 4096)
            ln_stats(xg, xgk, xsq, "R_BSQ", 512, True, 1.0,
                     PB[3][:, 0:512], PB[3][:, 512:1024], "pb3a", "pb3b", ones_c)
            xn = v32(oSQ, 4096)
            ln_apply(r3(xg, 512), xgk, r3(xg, 512), "R_BXG", r3(xn, 512), "R_BSQ", 512,
                     PB[3][:, 0:512], PB[3][:, 512:1024], "pb3a", "pb3b")
            for cc in range(4):
                csrc = ins["cosT"][cc * 128:(cc + 1) * 128, g * 512:(g + 1) * 512]
                ssrc = ins["sinT"][cc * 128:(cc + 1) * 128, g * 512:(g + 1) * 512]
                cosb = v32(oCS, 512)
                sinb = v32(oCS + 8192, 512)
                p.dma("SY", "bcos", lambda e, csrc=csrc, cosb=cosb: e.dma_start(cosb, csrc),
                      writes=["b_cos"])
                p.dma("SY", "bsin", lambda e, ssrc=ssrc, sinb=sinb: e.dma_start(sinb, ssrc),
                      writes=["b_sin"])
                x1 = xn[:, cc * 512:(cc + 1) * 512]
                x2 = xn[:, (cc + 4) * 512:(cc + 5) * 512]
                t1 = v32(oCS + 2048, 512)
                t2 = v32(oCS + 8192 + 2048, 512)
                r1 = rxn[:, cc * 2048 + g * 512: cc * 2048 + (g + 1) * 512]
                r2 = rxn[:, (cc + 4) * 2048 + g * 512: (cc + 4) * 2048 + (g + 1) * 512]
                V(lambda e, t1=t1, x1=x1, cosb=cosb: e.tensor_tensor(t1, x1, cosb, ALU.mult),
                  reads=["R_BSQ", "b_cos"], writes=["b_t1"])
                V(lambda e, t2=t2, x2=x2, sinb=sinb: e.tensor_tensor(t2, x2, sinb, ALU.mult),
                  reads=["R_BSQ", "b_sin"], writes=["b_t2"])
                V(lambda e, r1=r1, t1=t1, t2=t2: e.tensor_tensor(r1, t1, t2, ALU.subtract),
                  reads=["b_t1", "b_t2"], writes=["R_RX", "R_RXg%d" % g])
                V(lambda e, t1=t1, x1=x1, sinb=sinb: e.tensor_tensor(t1, x1, sinb, ALU.mult),
                  reads=["R_BSQ", "b_sin"], writes=["b_t1"])
                V(lambda e, t2=t2, x2=x2, cosb=cosb: e.tensor_tensor(t2, x2, cosb, ALU.mult),
                  reads=["R_BSQ", "b_cos"], writes=["b_t2"])
                V(lambda e, r2=r2, t1=t1, t2=t2: e.tensor_tensor(r2, t1, t2, ALU.add),
                  reads=["b_t1", "b_t2"], writes=["R_RX", "R_RXg%d" % g])

        wslots = [vbf(oW, 8192), vbf(oW + 16384, 8192)]
        QTb = vbf(oQT, 16384)
        stg = [vbf(oST + i * 4352, 2048) for i in range(4)]

        def proj_b(nm, sl, tg):
            bias_nm = {"bWq": "bbq", "bWk": "bbk", "bWv": "bbv", "bWo": "bbo"}[nm]
            ccdst = {"bWk": cc_in_k, "bWv": cc_in_v}.get(nm)
            for oc in range(8):
                pbi = (oc + tg) % 2
                pbks = ["pb%da" % pbi, "pb%db" % pbi]
                for kc in range(8):
                    for q in range(2):
                        mv = rxn[:, kc * 2048 + tg * 1024 + q * 512:
                                 kc * 2048 + tg * 1024 + (q + 1) * 512]
                        T(lambda e, kc=kc, oc=oc, mv=mv, q=q, pbi=pbi, sl=sl: e.matmul(
                            PB[pbi][:, q * 512:(q + 1) * 512],
                            wslots[sl][:, kc * 1024 + oc * 128: kc * 1024 + (oc + 1) * 128],
                            mv, start=(kc == 0), stop=(kc == 7)),
                          reads=["R_BW%d" % sl, "R_RXg%d" % (tg * 2 + q)],
                          writes=[pbks[q]])
                if nm == "bWq":
                    dst = QTb[:, oc * 2048 + tg * 1024: oc * 2048 + (tg + 1) * 1024]
                    S(lambda e, dst=dst, oc=oc, pbi=pbi: e.activation(
                        dst, PB[pbi][:, 0:1024], AF.Identity,
                        bias=bcol("bbq", oc), scale=consts[:, 1:2]),
                      reads=pbks + ["biases", "consts"], writes=["b_QT"])
                else:
                    slot = (0 if nm == "bWk" else 2) + (oc + tg) % 2
                    sgb = stg[slot][:, 0:1024]
                    skey = "R_VT%d" % slot
                    S(lambda e, sgb=sgb, oc=oc, pbi=pbi, bias_nm=bias_nm: e.activation(
                        sgb, PB[pbi][:, 0:1024], AF.Identity,
                        bias=bcol(bias_nm, oc), scale=consts[:, 1:2]),
                      reads=pbks + ["biases", "consts"], writes=[skey])
                    dd = ccdst[oc * 128:(oc + 1) * 128, tg * 1024:(tg + 1) * 1024]
                    p.dma("G", "bstg%d" % slot,
                          lambda e, dd=dd, sgb=sgb: e.dma_start(dd, sgb),
                          reads=[skey], writes=["cc_in" + nm])

        srcw = ins["bWk"].rearrange("(kc p) d -> p kc d", p=128)
        p.dma("G", "bw0", lambda e, srcw=srcw: e.dma_start(wslots[0], srcw),
              writes=["R_BW0"])
        srcw = ins["bWv"].rearrange("(kc p) d -> p kc d", p=128)
        p.dma("G", "bw1", lambda e, srcw=srcw: e.dma_start(wslots[1], srcw),
              writes=["R_BW1"])
        lnb(0)
        lnb(1)
        proj_b("bWk", 0, 0)
        lnb(2)
        proj_b("bWv", 1, 0)
        lnb(3)
        proj_b("bWk", 0, 1)
        p.async_op("G", "cck", lambda e: e.collective_compute(
            "AllGather", ALU.bypass, [[0, 1], [2, 3], [4, 5], [6, 7]],
            [cc_in_k.opt()], [cc_out_k.opt()]),
            reads=["cc_inbWk", "R_VT0", "R_VT1"], writes=["cc_k"])
        proj_b("bWv", 1, 1)
        p.async_op("G", "ccv", lambda e: e.collective_compute(
            "AllGather", ALU.bypass, [[0, 1], [2, 3], [4, 5], [6, 7]],
            [cc_in_v.opt()], [cc_out_v.opt()]),
            reads=["cc_inbWv", "R_VT2", "R_VT3"], writes=["cc_v"])
        srcw = ins["bWq"].rearrange("(kc p) d -> p kc d", p=128)
        p.dma("G", "bw0b", lambda e, srcw=srcw: e.dma_start(wslots[0], srcw),
              writes=["R_BW0"])
        proj_b("bWq", 0, 0)
        proj_b("bWq", 0, 1)

        woview = wslots[0]
        srcw = ins["bWo"].rearrange("(kc p) d -> p kc d", p=128)
        p.dma("G", "bwo", lambda e, srcw=srcw: e.dma_start(woview, srcw),
              writes=["R_BW0"])

        OTb = vbf(oW + 16384, 16384)
        kp = [vbf(oRX + i * 8192, 4096) for i in range(2)]
        vp = [vbf(oRX + 16384 + i * 8192, 4096) for i in range(2)]
        vtb = [vbf(oST + i * 4352, 2080) for i in range(4)]

        def stage_thunks(gp):
            h0, h1 = 2 * gp, 2 * gp + 1
            pi = gp % 2
            kpk, vpk = "R_RXK%d" % pi, "R_RXV%d" % pi
            for (hh, rows) in ((h0, (0, 64)), (h1, (64, 128))):
                for (hs, cols) in ((0, (0, 2048)), (1, (2048, 4096))):
                    def _ld(hh=hh, rows=rows, hs=hs, cols=cols):
                        srck = cc_out_k[hs * D + hh * 64: hs * D + hh * 64 + 64, :]
                        dstk = kp[pi][rows[0]:rows[1], cols[0]:cols[1]]
                        p.dma("SY", "bkp%d" % pi,
                              lambda e, dstk=dstk, srck=srck: e.dma_start(dstk, srck),
                              reads=["cc_k"], writes=[kpk] + (["R_RX", "R_RXg0", "R_RXg1",
                                                               "R_RXg2", "R_RXg3"] if gp < 2 else []))
                        srcv = cc_out_v[hs * D + hh * 64: hs * D + hh * 64 + 64, :]
                        dstv = vp[pi][rows[0]:rows[1], cols[0]:cols[1]]
                        p.dma("SY", "bvp%d" % pi,
                              lambda e, dstv=dstv, srcv=srcv: e.dma_start(dstv, srcv),
                              reads=["cc_v"], writes=[vpk] + (["R_RX", "R_RXg0", "R_RXg1",
                                                               "R_RXg2", "R_RXg3"] if gp < 2 else []))
                    yield _ld
            s0, s1 = 2 * pi, 2 * pi + 1
            k0n, k1n = "R_VT%d" % s0, "R_VT%d" % s1
            def _ms():
                V(lambda e: e.memset(free_dims(vtb[s0][:, 64:65], [[65, 32]]), 1.0),
                  writes=[k0n])
                V(lambda e: e.memset(free_dims(vtb[s1][:, 64:65], [[65, 32]]), 1.0),
                  writes=[k1n])
            yield _ms
            for tt in range(0, 32, 4):
                def _tr(tt=tt):
                    for t in range(tt, tt + 4):
                        T(lambda e, t=t: e.matmul(
                            PB[3][:, 0:64], vp[pi][0:64, t * 128:(t + 1) * 128],
                            i64_sb[0:64, :], start=True, stop=True,
                            tile_position=(0, 0)),
                          reads=[vpk, "i64_sb"], writes=["pb3a"])
                        V(lambda e, t=t: e.tensor_copy(vtb[s0][:, t * 65:t * 65 + 64],
                                                       PB[3][:, 0:64]),
                          reads=["pb3a"], writes=[k0n])
                        T(lambda e, t=t: e.matmul(
                            PB[3][:, 512:576], vp[pi][64:128, t * 128:(t + 1) * 128],
                            i64_sb[64:128, :], start=True, stop=True,
                            tile_position=(64, 0)),
                          reads=[vpk, "i64_sb"], writes=["pb3b"])
                        V(lambda e, t=t: e.tensor_copy(vtb[s1][:, t * 65:t * 65 + 64],
                                                       PB[3][:, 512:576]),
                          reads=["pb3b"], writes=[k1n])
                yield _tr

        def A_b(gp, feeder, carry):
            h0, h1 = 2 * gp, 2 * gp + 1
            pi = gp % 2
            kpk = "R_RXK%d" % pi
            s0, s1 = 2 * pi, 2 * pi + 1
            k0n, k1n = "R_VT%d" % s0, "R_VT%d" % s1

            def draw():
                th = next(feeder, None)
                if th is not None:
                    th()

            for g in range(4):
                q0g = g * 512
                q_h0 = QTb[0:64, (h0 // 2) * 2048 + q0g: (h0 // 2) * 2048 + q0g + 512]
                q_h1 = QTb[64:128, (h1 // 2) * 2048 + q0g: (h1 // 2) * 2048 + q0g + 512]

                def emit_pvb(t, s0=s0, s1=s1, k0n=k0n, k1n=k1n):
                    pp = vbf(oPP + (t % 4) * 2048, 1024)
                    ppk = "b_pp%d" % (t % 4)
                    T(lambda e, t=t, pp=pp, s0=s0: e.matmul(
                        PB[2][0:65, 0:512], vtb[s0][:, t * 65:t * 65 + 65], pp[:, 0:512],
                        start=(t == 0), stop=(t == 31)),
                      reads=[k0n, ppk], writes=["pb2a"])
                    T(lambda e, t=t, pp=pp, s1=s1: e.matmul(
                        PB[2][0:65, 512:1024], vtb[s1][:, t * 65:t * 65 + 65], pp[:, 512:1024],
                        start=(t == 0), stop=(t == 31)),
                      reads=[k1n, ppk], writes=["pb2b"])
                for t in range(32):
                    sp = PB[t % 2]
                    ska, skb = "pb%da" % (t % 2), "pb%db" % (t % 2)
                    T(lambda e, t=t, sp=sp, q_h0=q_h0, pi=pi: e.matmul(
                        sp[:, 0:512], kp[pi][0:64, t * 128:(t + 1) * 128], q_h0,
                        start=True, stop=True, tile_position=(0, 0)),
                      reads=[kpk, "b_QT"], writes=[ska])
                    T(lambda e, t=t, sp=sp, q_h1=q_h1, pi=pi: e.matmul(
                        sp[:, 512:1024], kp[pi][64:128, t * 128:(t + 1) * 128], q_h1,
                        start=True, stop=True, tile_position=(64, 0)),
                      reads=[kpk, "b_QT"], writes=[skb])
                    pp = vbf(oPP + (t % 4) * 2048, 1024)
                    ppk = "b_pp%d" % (t % 4)
                    S(lambda e, pp=pp, sp=sp: e.activation(
                        pp, sp[:, :], AF.Exp, bias=consts[:, 2:3], scale=consts[:, 3:4]),
                      reads=[ska, skb, "consts"], writes=[ppk])
                    if t < 4:
                        if carry is not None:
                            for op in carry[t]:
                                op()
                        else:
                            draw()
                    elif t < 8:
                        draw()
                    elif t <= 14:
                        emit_pvb(2 * t - 16)
                        emit_pvb(2 * t - 15)
                    else:
                        emit_pvb(t - 1)

                def mk_carry(g=g, q0g=q0g, h0=h0, h1=h1, emit_pvb=emit_pvb):
                    def pv31():
                        emit_pvb(31)

                    def rc(hi2, dstr, dk):
                        V(lambda e, hi2=hi2, dstr=dstr: e.reciprocal(
                            dstr[0:1, 0:512], PB[2][64:65, hi2 * 512: hi2 * 512 + 512]),
                          reads=["pb2a" if hi2 == 0 else "pb2b"], writes=[dk])

                    def cm(hi2, srcr, dk):
                        hh = h0 if hi2 == 0 else h1
                        pvk = "pb2a" if hi2 == 0 else "pb2b"
                        T(lambda e, srcr=srcr: e.matmul(PB[3][0:64, 512:1024], ones_r[0:1, 0:64],
                                                        srcr[0:1, 0:512], start=True, stop=True),
                          reads=["ones_r", dk], writes=["pb3b"])
                        V(lambda e: e.tensor_copy(rb_sb[:], PB[3][0:64, 512:1024]),
                          reads=["pb3b"], writes=["rb"])
                        dsto = OTb[(hh % 2) * 64:(hh % 2) * 64 + 64,
                                   (hh // 2) * 2048 + q0g: (hh // 2) * 2048 + q0g + 512]
                        V(lambda e, dsto=dsto, hi2=hi2: e.tensor_tensor(
                            dsto, PB[2][0:64, hi2 * 512: hi2 * 512 + 512], rb_sb[:], ALU.mult),
                          reads=[pvk, "rb"], writes=["R_BW1"])
                    return [
                        [pv31, lambda: rc(0, recip_sb, "recipA")],
                        [lambda: rc(1, rowA, "recipB")],
                        [lambda: cm(0, recip_sb, "recipA")],
                        [lambda: cm(1, rowA, "recipB")],
                    ]
                carry = mk_carry()
            return carry

        for th in stage_thunks(0):
            th()
        carry = None
        for gp in range(8):
            feeder = stage_thunks(gp + 1) if gp < 7 else iter(())
            carry = A_b(gp, feeder, carry)
            for th in feeder:
                th()
        for ops in carry:
            for op in ops:
                op()

        def evwo(oc, tg, pb, pbks):
            oi = (oc * 2 + tg) % 2
            xs = v32(oXS + oi * 4096, 1024)
            xsk = "R_BXG"
            srcx = xbuf[oc * 128:(oc + 1) * 128, tg * 1024:(tg + 1) * 1024]
            p.dma("SY", "bx2%d" % oi,
                  lambda e, srcx=srcx, xs=xs: e.dma_start(xs, srcx),
                  writes=[xsk, "b_pp2", "b_pp3"])
            xo = v32(oXO2 + oi * 4096, 1024)
            xok = "R_BSQ"
            V(lambda e, xo=xo, xs=xs, pb=pb, oc=oc: e.scalar_tensor_tensor(
                xo, pb, bcol("bbo", oc), xs, ALU.add, ALU.add),
              reads=list(pbks) + ["biases", xsk], writes=["R_BSQ", "R_BXG"])
            dstx = xbuf[oc * 128:(oc + 1) * 128, tg * 1024:(tg + 1) * 1024]
            p.dma("SY", "bxo%d" % oi,
                  lambda e, dstx=dstx, xo=xo: e.dma_start(dstx, xo),
                  reads=[xok], writes=[])
        proj(woview, "R_BW0", OTb, "R_BW1", TOK, evwo)

    def dbg_tap(i):
        if not _DEBUG_TAPS:
            return
        for kc in range(8):
            t = v32(0, 2048)
            p.dma("SY", "dbgl", lambda e, kc=kc, t=t: e.dma_start(
                t, xbuf[kc * 128:(kc + 1) * 128, :]), writes=["dbgt"])
            p.dma("SY", "dbgs", lambda e, kc=kc, i=i, t=t: e.dma_start(
                dbg_taps[i][kc * 128:(kc + 1) * 128, :], t), reads=["dbgt"], writes=[])
        p.barrier()

    phase_attn_a()
    p.barrier()
    dbg_tap(0)
    phase_ffn(last=False)
    p.barrier()
    dbg_tap(1)
    phase_attn_b()
    p.barrier()
    dbg_tap(2)
    phase_ffn(last=True)
    p.barrier()

    p.finalize()
    with nc.Block() as block:
        @block.tensor
        def _(eng):
            p.emit("T", eng, sems)

        @block.vector
        def _(eng):
            p.emit("V", eng, sems)

        @block.scalar
        def _(eng):
            p.emit("S", eng, sems)

        @block.gpsimd
        def _(eng):
            p.emit("G", eng, sems)

        @block.sync
        def _(eng):
            p.emit("SY", eng, sems)
    stack.close()
    return nc


_NC_CACHE = None


def kernel(L0_attn_a_x, L0_attn_a_Wq, L0_attn_a_bq, L0_attn_a_Wk, L0_attn_a_bk,
           L0_attn_a_Wv, L0_attn_a_bv, L0_attn_a_Wo, L0_attn_a_bo,
           L0_attn_a_rel_bias, L0_attn_a_W1, L0_attn_a_b1, L0_attn_a_W2,
           L0_attn_a_b2, L0_attn_b_Wq, L0_attn_b_bq, L0_attn_b_Wk, L0_attn_b_bk,
           L0_attn_b_Wv, L0_attn_b_bv, L0_attn_b_Wo, L0_attn_b_bo,
           L0_ffn_w_gate, L0_ffn_w_up, L0_ffn_w_down):
    global _NC_CACHE
    oh, cosT, sinT, i64 = _host_consts()

    def wT(w):
        return np.ascontiguousarray(np.asarray(w, np.float32).T)

    def bR(b):
        return np.ascontiguousarray(np.asarray(b, np.float32).reshape(8, 128).T)

    x = np.asarray(L0_attn_a_x, np.float32)
    common = {
        "aWq": wT(L0_attn_a_Wq), "aWk": wT(L0_attn_a_Wk), "aWv": wT(L0_attn_a_Wv),
        "aWo": wT(L0_attn_a_Wo), "aW1": wT(L0_attn_a_W1), "aW2": wT(L0_attn_a_W2),
        "bWq": wT(L0_attn_b_Wq), "bWk": wT(L0_attn_b_Wk), "bWv": wT(L0_attn_b_Wv),
        "bWo": wT(L0_attn_b_Wo),
        "abq": bR(L0_attn_a_bq), "abk": bR(L0_attn_a_bk), "abv": bR(L0_attn_a_bv),
        "abo": bR(L0_attn_a_bo), "ab1": bR(L0_attn_a_b1), "ab2": bR(L0_attn_a_b2),
        "bbq": bR(L0_attn_b_bq), "bbk": bR(L0_attn_b_bk), "bbv": bR(L0_attn_b_bv),
        "bbo": bR(L0_attn_b_bo),
        "onehot": oh, "i64": i64,
        "wg": np.ascontiguousarray(
            np.asarray(L0_ffn_w_gate, np.int32).T.reshape(D, NHC, 128).transpose(1, 0, 2)),
        "wu": np.ascontiguousarray(
            np.asarray(L0_ffn_w_up, np.int32).T.reshape(D, NHC, 128).transpose(1, 0, 2)),
        "wd": np.ascontiguousarray(
            np.asarray(L0_ffn_w_down, np.int32).T.reshape(HID, 8, 128).transpose(1, 0, 2)),
    }
    in_maps = []
    for core in range(8):
        b, hh = core // 2, core % 2
        m = dict(common)
        m["xT"] = np.ascontiguousarray(x[b, hh * TOK:(hh + 1) * TOK, :].T)
        relroll = np.ascontiguousarray(
            np.roll(np.asarray(L0_attn_a_rel_bias, np.float32), -8 * hh, axis=1))
        m["relT"] = relroll
        m["relflat"] = np.ascontiguousarray(relroll.reshape(1, 512))
        m["cosT"] = np.ascontiguousarray(cosT[:, hh * TOK:(hh + 1) * TOK])
        m["sinT"] = np.ascontiguousarray(sinT[:, hh * TOK:(hh + 1) * TOK])
        in_maps.append(m)

    if _NC_CACHE is None:
        _NC_CACHE = build_program()
    res = run_bass_kernel_spmd(_NC_CACHE, in_maps, core_ids=list(range(8)))
    out = np.empty((4, 4096, D), np.float32)
    for core in range(8):
        b, hh = core // 2, core % 2
        out[b, hh * TOK:(hh + 1) * TOK, :] = res.results[core]["yT"].T
    return out


# version 28
# speedup vs baseline: 1.1802x; 1.0028x over previous
import math
import contextlib
import numpy as np
import ml_dtypes
import concourse.bass as bass
import concourse.mybir as mybir
from concourse.bass_utils import run_bass_kernel_spmd

F32 = mybir.dt.float32
BF16 = mybir.dt.bfloat16
I32 = mybir.dt.int32
AF = mybir.ActivationFunctionType
ALU = mybir.AluOpType
BF = ml_dtypes.bfloat16

D = 1024
TOK = 2048
NCHUNK = 8
CH = 256
HID = 3072
NHC = HID // 128
C_A = 92.0
C_B = 4.0
ENGS = ("T", "V", "S", "G", "SY")
_DEBUG_TAPS = False
ARCOLS = 47000
ARB = ARCOLS * 4


def _bucket_of_n(n):
    ret = 16 if n < 0 else 0
    n = abs(n)
    if n < 8:
        return ret + n
    v = 8 + int(np.float32(np.log(np.float32(n) / np.float32(8.0)))
                / np.float32(math.log(128.0 / 8.0)) * np.float32(8))
    return ret + min(v, 15)


def _host_consts():
    oh = np.zeros((32, 1280), np.float32)
    for i in range(1280):
        d = i - 640
        if abs(d) <= 90:
            oh[_bucket_of_n(-d), i] = 1.0
    freq = np.exp(np.arange(0, D, 2, dtype=np.float32)
                  * np.float32(-np.log(10000.0) / D))
    ang = np.arange(4096, dtype=np.float32)[:, None] * freq[None, :]
    cosT = np.ascontiguousarray(np.cos(ang).T.astype(np.float32))
    sinT = np.ascontiguousarray(np.sin(ang).T.astype(np.float32))
    i64 = np.zeros((128, 64), np.float32)
    i64[:64] = np.eye(64, dtype=np.float32)
    i64[64:] = np.eye(64, dtype=np.float32)
    return oh, cosT, sinT, i64


class Prog:
    def __init__(self, nc, stack):
        self.nc = nc
        self.stack = stack
        self.ops = {e: [] for e in ENGS}
        self.bufs = {}
        self.dsems = {}

    def dma_sem(self, name):
        if name not in self.dsems:
            sem = self.stack.enter_context(self.nc.semaphore("ds_" + name))
            self.dsems[name] = {"sem": sem, "n": 0}
        return self.dsems[name]

    def _deps(self, reads, writes, eng, selfsync):
        deps = []
        for b in reads:
            st = self.bufs.get(b)
            if st and st["w"] is not None:
                deps.append(st["w"])
        for b in writes:
            st = self.bufs.get(b)
            if st:
                if st["w"] is not None:
                    deps.append(st["w"])
                deps.extend(st["r"])
        out, seen = [], set()
        for d in deps:
            if d[0] == "E" and d[1] == eng and not selfsync:
                continue
            if d[0] == "D":
                d = ("D", d[1], self.dsems[d[1]]["n"])
            if d not in seen:
                seen.add(d)
                out.append(d)
        return out

    def _register(self, tok, reads, writes):
        for b in writes:
            self.bufs[b] = {"w": tok, "r": []}
        for b in reads:
            st = self.bufs.setdefault(b, {"w": None, "r": []})
            st["r"].append(tok)

    def op(self, eng, fn, reads=(), writes=(), selfsync=True):
        idx = len(self.ops[eng])
        o = {"kind": "op", "fn": fn,
             "deps": self._deps(reads, writes, eng, selfsync),
             "need_inc": False, "inc_no": None}
        self.ops[eng].append(o)
        tok = ("E", eng, idx)
        self._register(tok, reads, writes)
        return tok

    def dma(self, eng, stream, fn, reads=(), writes=()):
        ds = self.dma_sem(stream)
        deps = self._deps(reads, writes, eng, True)
        ds["n"] += 16
        o = {"kind": "dma", "fn": fn, "deps": deps, "sem": ds["sem"],
             "need_inc": False, "inc_no": None}
        self.ops[eng].append(o)
        tok = ("D", stream, ds["n"])
        self._register(tok, reads, writes)
        return tok

    def async_op(self, eng, stream, fn, reads=(), writes=()):
        ds = self.dma_sem(stream)
        deps = self._deps(reads, writes, eng, True)
        ds["n"] += 1
        o = {"kind": "dma", "fn": fn, "deps": deps, "sem": ds["sem"],
             "inc_amt": 1, "need_inc": False, "inc_no": None}
        self.ops[eng].append(o)
        tok = ("D", stream, ds["n"])
        self._register(tok, reads, writes)
        return tok

    def barrier(self):
        lasts = []
        for e in ENGS:
            if self.ops[e]:
                lasts.append(("E", e, len(self.ops[e]) - 1))
        for name, ds in self.dsems.items():
            lasts.append(("D", name, ds["n"]))
        for e in ENGS:
            deps = [d for d in lasts if not (d[0] == "E" and d[1] == e)]
            self.ops[e].append({"kind": "op", "fn": lambda eng: eng.nop(),
                                "deps": deps, "need_inc": False, "inc_no": None})
        self.bufs = {}

    def finalize(self):
        for e in ENGS:
            for o in self.ops[e]:
                for d in o["deps"]:
                    if d[0] == "E":
                        self.ops[d[1]][d[2]]["need_inc"] = True
        for e in ENGS:
            c = 0
            for o in self.ops[e]:
                if o["need_inc"]:
                    c += 1
                    o["inc_no"] = c

    def emit(self, ename, eng, sems):
        waited = {}
        for o in self.ops[ename]:
            for d in o["deps"]:
                if d[0] == "E":
                    _, e2, idx = d
                    cnt = self.ops[e2][idx]["inc_no"]
                    key = ("E", e2)
                    if waited.get(key, 0) >= cnt:
                        continue
                    waited[key] = cnt
                    eng.wait_ge(sems[e2], cnt)
                else:
                    _, sname, cnt = d
                    key = ("D", sname)
                    if waited.get(key, 0) >= cnt:
                        continue
                    waited[key] = cnt
                    eng.wait_ge(self.dsems[sname]["sem"], cnt)
            inst = o["fn"](eng)
            if o["kind"] == "dma":
                inst.then_inc(o["sem"], o.get("inc_amt", 16))
                if o["need_inc"]:
                    eng.nop().then_inc(sems[ename], 1)
            elif o["need_inc"]:
                inst.then_inc(sems[ename], 1)


def build_program():
    nc = bass.Bass("TRN2", target_bir_lowering=False)
    dt = nc.dram_tensor
    ins = {}
    xT = dt("xT", [D, TOK], F32, kind="ExternalInput").ap()
    for nm in ["aWq", "aWk", "aWv", "aWo", "aW1", "aW2", "bWq", "bWk", "bWv", "bWo"]:
        ins[nm] = dt(nm, [D, D], F32, kind="ExternalInput").ap()
    bias_keys = ["abq", "abk", "abv", "abo", "ab1", "ab2", "bbq", "bbk", "bbv", "bbo"]
    for nm in bias_keys:
        ins[nm] = dt(nm, [128, 8], F32, kind="ExternalInput").ap()
    ins["relflat"] = dt("relflat", [1, 512], F32, kind="ExternalInput").ap()
    ins["relT"] = dt("relT", [32, 16], F32, kind="ExternalInput").ap()
    ins["onehot"] = dt("onehot", [32, 1280], F32, kind="ExternalInput").ap()
    ins["cosT"] = dt("cosT", [512, TOK], F32, kind="ExternalInput").ap()
    ins["sinT"] = dt("sinT", [512, TOK], F32, kind="ExternalInput").ap()
    ins["i64"] = dt("i64", [128, 64], F32, kind="ExternalInput").ap()
    ins["wg"] = dt("wg", [NHC, D, 128], I32, kind="ExternalInput").ap()
    ins["wu"] = dt("wu", [NHC, D, 128], I32, kind="ExternalInput").ap()
    ins["wd"] = dt("wd", [8, HID, 128], I32, kind="ExternalInput").ap()
    yT = dt("yT", [D, TOK], F32, kind="ExternalOutput").ap()
    dbg_taps = []
    if _DEBUG_TAPS:
        for i in range(3):
            dbg_taps.append(dt("dbg%d" % i, [D, TOK], F32, kind="ExternalOutput").ap())
        dbgA = dt("dbgA", [128, 2048], F32, kind="ExternalOutput").ap()
        dbgB = dt("dbgB", [128, 4352], F32, kind="ExternalOutput").ap()
        dbgC = dt("dbgC", [128, 2080], F32, kind="ExternalOutput").ap()
        dbgD = dt("dbgD", [128, 4096], F32, kind="ExternalOutput").ap()
        dbgE = dt("dbgE", [128, 1152], F32, kind="ExternalOutput").ap()
    xbuf = dt("xbuf", [D, TOK], F32, kind="Internal").ap()
    bandh_t = dt("bandh", [2, 1280], BF16, kind="Internal")
    bandh = bandh_t.ap()
    cc_in_k = dt("cc_in_k", [D, TOK], BF16, kind="Internal").ap()
    cc_out_k = dt("cc_out_k", [2 * D, TOK], BF16, kind="Internal").ap()
    cc_in_v = dt("cc_in_v", [D, TOK], BF16, kind="Internal").ap()
    cc_out_v = dt("cc_out_v", [2 * D, TOK], BF16, kind="Internal").ap()

    stack = contextlib.ExitStack()
    E = stack.enter_context
    AR = E(nc.sbuf_tensor("AR", [128, ARCOLS], F32))
    biases = E(nc.sbuf_tensor("biases", [128, 80], F32))
    consts = E(nc.sbuf_tensor("consts", [128, 8], F32))
    rel_sb = E(nc.sbuf_tensor("rel_sb", [32, 16], F32))
    relc_sb = E(nc.sbuf_tensor("relc_sb", [32, 1], BF16))
    oh_sb = E(nc.sbuf_tensor("oh_sb", [32, 1280], BF16))
    wbc = E(nc.sbuf_tensor("wbc", [64, 512], F32))
    wbcn = E(nc.sbuf_tensor("wbcn", [64, 512], F32))
    i64_sb = E(nc.sbuf_tensor("i64_sb", [128, 64], BF16))
    ones_c = E(nc.sbuf_tensor("ones_c", [128, 1], F32))
    ones_cb = E(nc.sbuf_tensor("ones_cb", [128, 1], BF16))
    ones_r = E(nc.sbuf_tensor("ones_r", [1, 128], F32))
    ones_rb = E(nc.sbuf_tensor("ones_rb", [1, 128], BF16))
    ones512 = E(nc.sbuf_tensor("ones512", [1, 512], BF16))
    band_sb = E(nc.sbuf_tensor("band_sb", [1, 1280], BF16))
    recip_sb = E(nc.sbuf_tensor("recip_sb", [1, 512], F32))
    w15row = E(nc.sbuf_tensor("w15row", [1, 1024], BF16))
    totT = E(nc.sbuf_tensor("totT", [1, 128], BF16))
    rb_sb = E(nc.sbuf_tensor("rb_sb", [64, 512], F32))
    rowA = E(nc.sbuf_tensor("rowA", [1, 512], F32))
    rowB = E(nc.sbuf_tensor("rowB", [1, 512], F32))
    rowC = E(nc.sbuf_tensor("rowC", [1, 520], F32))
    zcol64 = E(nc.sbuf_tensor("zcol64", [64, 1], F32))
    PB = [E(nc.psum_tensor(f"pb{i}", [128, 1024], F32)) for i in range(4)]
    sems = {e: E(nc.semaphore("sem_" + e)) for e in ENGS}

    p = Prog(nc, stack)
    arf = AR[:]
    arb_all = bass.AP(arf.tensor, 0, list(arf.ap)).bitcast(BF16)

    def v32(off_b, cols):
        assert off_b % 4 == 0 and off_b + cols * 4 <= ARB, (off_b, cols)
        return arf[:, off_b // 4: off_b // 4 + cols]

    def vbf(off_b, cols):
        assert off_b % 2 == 0 and off_b + cols * 2 <= ARB, (off_b, cols)
        return arb_all[:, off_b // 2: off_b // 2 + cols]

    def T(fn, reads=(), writes=()):
        return p.op("T", fn, reads, writes, selfsync=False)

    def V(fn, reads=(), writes=()):
        return p.op("V", fn, reads, writes)

    def S(fn, reads=(), writes=()):
        return p.op("S", fn, reads, writes)

    def G(fn, reads=(), writes=()):
        return p.op("G", fn, reads, writes)

    def bcol(nm, oc):
        i = bias_keys.index(nm)
        return biases[:, 8 * i + oc: 8 * i + oc + 1]

    def free_dims(ap0, dims):
        return bass.AP(ap0.tensor, ap0.offset, [list(ap0.ap[0])] + dims)

    V(lambda e: e.memset(ones_c[:], 1.0), writes=["ones_c"])
    V(lambda e: e.memset(ones_cb[:], 1.0), writes=["ones_cb"])
    V(lambda e: e.memset(ones_r[:], 1.0), writes=["ones_r"])
    V(lambda e: e.memset(ones_rb[:], 1.0), writes=["ones_rb"])
    V(lambda e: e.memset(ones512[:], 1.0), writes=["ones512"])
    V(lambda e: e.memset(zcol64[:], 0.0), writes=["zc"])
    for col, val in [(0, -C_A), (1, 1.0), (2, -C_B), (3, 0.125),
                     (4, 1e-5), (5, -0.5), (6, 0.0)]:
        V(lambda e, c=col, vv=val: e.memset(consts[:, c:c + 1], vv), writes=["consts"])
    p.dma("G", "cinit", lambda e: e.dma_start(rel_sb[:], ins["relT"]), writes=["rel_sb"])
    p.dma("G", "cinit", lambda e: e.dma_start(oh_sb[:], ins["onehot"]), writes=["oh_sb"])
    p.dma("G", "cinit", lambda e: e.dma_start(i64_sb[:], ins["i64"]), writes=["i64_sb"])
    relf = rowB[0:1, 0:512]
    p.dma("G", "cinit", lambda e: e.dma_start(relf, ins["relflat"]), writes=["relf"])
    for i, nm in enumerate(bias_keys):
        p.dma("G", "cinit",
              lambda e, i=i, nm=nm: e.dma_start(biases[:, 8 * i:8 * i + 8], ins[nm]),
              writes=["biases"])
    T(lambda e: e.matmul(PB[3][0:64, 0:512], ones_r[0:1, 0:64], relf,
                         start=True, stop=True),
      reads=["ones_r", "relf"], writes=["pb3a"])
    V(lambda e: e.tensor_copy(wbc[:], PB[3][0:64, 0:512]), reads=["pb3a"], writes=["wbc"])
    V(lambda e: e.tensor_scalar(wbcn[:], wbc[:], -1.0, None, ALU.mult),
      reads=["wbc"], writes=["wbcn"])

    def ln_stats(xg, xk, xsq, sqk, gw, eps, scale, pba, pbb, ka, kb, onesq):
        V(lambda e: e.tensor_tensor(xsq, xg, xg, ALU.mult), reads=[xk], writes=[sqk])
        for kc in range(8):
            T(lambda e, kc=kc: e.matmul(pba[0:1, 0:gw], ones_c[:],
                                        xg[:, kc * gw:(kc + 1) * gw],
                                        start=(kc == 0), stop=(kc == 7)),
              reads=["ones_c", xk], writes=[ka])
        for kc in range(8):
            T(lambda e, kc=kc: e.matmul(pbb[0:1, 0:gw], onesq[:],
                                        xsq[:, kc * gw:(kc + 1) * gw],
                                        start=(kc == 0), stop=(kc == 7)),
              reads=["ones_cb", sqk], writes=[kb])
        V(lambda e: e.tensor_scalar(rowA[0:1, 0:gw], pba[0:1, 0:gw],
                                    1.0 / D, None, ALU.mult),
          reads=[ka], writes=["rowAm"])
        V(lambda e: e.tensor_scalar(rowB[0:1, 0:gw], pbb[0:1, 0:gw],
                                    1.0 / D, None, ALU.mult),
          reads=[kb], writes=["rowBe"])
        V(lambda e: e.tensor_tensor(rowC[0:1, 0:gw], rowA[0:1, 0:gw],
                                    rowA[0:1, 0:gw], ALU.mult),
          reads=["rowAm"], writes=["rowCm2"])
        V(lambda e: e.tensor_tensor(rowB[0:1, 0:gw], rowB[0:1, 0:gw],
                                    rowC[0:1, 0:gw], ALU.subtract),
          reads=["rowBe", "rowCm2"], writes=["rowBv"])
        epsc = consts[0:1, 4:5] if eps else consts[0:1, 6:7]
        S(lambda e: e.activation(rowB[0:1, 0:gw], rowB[0:1, 0:gw], AF.Ln,
                                 bias=epsc, scale=consts[0:1, 1:2]),
          reads=["rowBv", "consts"], writes=["rowBl"])
        S(lambda e: e.activation(rowB[0:1, 0:gw], rowB[0:1, 0:gw], AF.Exp,
                                 bias=consts[0:1, 6:7], scale=consts[0:1, 5:6]),
          reads=["rowBl", "consts"], writes=["rowBr"])
        if scale != 1.0:
            V(lambda e: e.tensor_scalar(rowB[0:1, 0:gw], rowB[0:1, 0:gw],
                                        scale, None, ALU.mult),
              reads=["rowBr"], writes=["rowBr"])
        V(lambda e: e.tensor_tensor(rowA[0:1, 0:gw], rowA[0:1, 0:gw],
                                    rowB[0:1, 0:gw], ALU.mult),
          reads=["rowAm", "rowBr"], writes=["rowAb"])
        T(lambda e: e.matmul(pba[:, 0:gw], ones_r[:], rowB[0:1, 0:gw],
                             start=True, stop=True),
          reads=["ones_r", "rowBr"], writes=[ka])
        T(lambda e: e.matmul(pbb[:, 0:gw], ones_r[:], rowA[0:1, 0:gw],
                             start=True, stop=True),
          reads=["ones_r", "rowAb"], writes=[kb])

    def r3(x, gw):
        return x.rearrange("p (kc w) -> p kc w", kc=8)

    def bc3g(pb, gw):
        a = pb[:, 0:gw]
        return bass.AP(a.tensor, a.offset, [list(a.ap[0]), [0, 8], list(a.ap[1])])

    def ln_apply(xg3, xk, tmp3, tk, out3, ok, gw, pba, pbb, ka, kb):
        V(lambda e: e.tensor_tensor(tmp3, xg3, bc3g(pba, gw), ALU.mult),
          reads=[xk, ka], writes=[tk])
        V(lambda e: e.tensor_tensor(out3, tmp3, bc3g(pbb, gw), ALU.subtract),
          reads=[tk, kb], writes=[ok])

    def proj(wv, wkkey, mvfull, mvkey, ntok, evac):
        ntg = max(1, ntok // 1024)
        tg_w = min(ntok, 1024)
        nq = max(1, tg_w // 512)
        qw = min(tg_w, 512)
        for oc in range(8):
            for tg in range(ntg):
                pbi = (oc * ntg + tg) % 2
                pbks = ["pb%da" % pbi] + (["pb%db" % pbi] if nq > 1 else [])
                for kc in range(8):
                    for q in range(nq):
                        mv = mvfull[:, kc * ntok + tg * tg_w + q * qw:
                                    kc * ntok + tg * tg_w + (q + 1) * qw]
                        pbdst = PB[pbi][:, q * 512: q * 512 + qw]
                        mk = mvkey(tg, q) if callable(mvkey) else mvkey
                        T(lambda e, kc=kc, oc=oc, mv=mv, pbdst=pbdst: e.matmul(
                            pbdst, wv[:, kc * 1024 + oc * 128: kc * 1024 + (oc + 1) * 128],
                            mv, start=(kc == 0), stop=(kc == 7)),
                          reads=[wkkey, mk], writes=[pbks[q]])
                evac(oc, tg, PB[pbi][:, 0:tg_w], pbks)

    def phase_attn_a():
        oXC = 0
        oW = oXC + 16384
        oQKV = oW + 49152
        oQv = oQKV + 12288
        oKv = oQv + 16384
        oVv = oKv + 16384
        oVT = oVv + 8512
        oBG = oVT + 8320
        oHK = oBG + 17408
        oXN = oHK + 4608
        oAS = oXN + 8192
        oSQ = oAS + 8192
        oOT = oSQ + 4096
        oPP = oOT + 4096
        oSC = oPP + 6144
        assert oSC + 4096 <= ARB, oSC

        pbPROJ = PB[3][:, 0:512]
        pbA = PB[3][:, 512:768]
        pbB = PB[3][:, 768:1024]
        pbM = PB[3][:, 512:1024]

        Vvv = vbf(oVv, 4256)
        OT = vbf(oOT, 2048)
        ff1 = vbf(oOT, 2048)

        def Qvp(c):
            return vbf(oQv + (c % 2) * 8192, 4096)

        def Kvp(c):
            return vbf(oKv + (c % 2) * 8192, 4096)

        def vtp(c):
            return vbf(oVT + (c % 2) * 4160, 2080)

        def bigp(c):
            return vbf(oBG + (c % 2) * 8704, 4352)

        def hkp(c):
            return vbf(oHK + (c % 2) * 2304, 1152)

        wslot = [vbf(oW + i * 16384, 8192) for i in range(3)]
        STREAM = [("aWq", 0), ("aWk", 0), ("aWv", 0),
                  ("aWq", 1), ("aWk", 1), ("aWv", 1)]
        for c in range(1, NCHUNK):
            STREAM += [("aWo", c - 1), ("aW1", c - 1), ("aW2", c - 1)]
            if c + 1 < NCHUNK:
                STREAM += [("aWq", c + 1), ("aWk", c + 1), ("aWv", c + 1)]
        STREAM += [("aWo", NCHUNK - 1), ("aW1", NCHUNK - 1), ("aW2", NCHUNK - 1)]
        jof = {key: j for j, key in enumerate(STREAM)}

        def wload(j):
            if j >= len(STREAM):
                return
            nm, c = STREAM[j]
            sl = j % 3
            srcw = ins[nm].rearrange("(kc p) d -> p kc d", p=128)
            p.dma("G", "wsl%d" % sl,
                  lambda e, srcw=srcw, sl=sl: e.dma_start(wslot[sl], srcw),
                  writes=["R_WSL%d" % sl])

        def projS(j, mvfull, mvkey, evac):
            sl = j % 3
            wv = wslot[sl]
            wk = "R_WSL%d" % sl
            for oc in range(8):
                for h in range(2):
                    for kc in range(4 * h, 4 * h + 4):
                        T(lambda e, kc=kc, oc=oc, wv=wv, mvfull=mvfull: e.matmul(
                            pbPROJ[:, 0:256],
                            wv[:, kc * 1024 + oc * 128: kc * 1024 + (oc + 1) * 128],
                            mvfull[:, kc * 256:(kc + 1) * 256],
                            start=(kc == 0), stop=(kc == 7)),
                          reads=[wk, mvkey], writes=["pb3p"])
                    if h == 0:
                        yield (lambda: None)
                yield (lambda oc=oc: evac(oc))
            yield (lambda j=j: wload(j + 3))

        QT, KT, VT = vbf(oQKV, 2048), vbf(oQKV + 4096, 2048), vbf(oQKV + 8192, 2048)
        qdst = {"aWq": 0, "aWk": 4096, "aWv": 8192}
        bnm = {"aWq": "abq", "aWk": "abk", "aWv": "abv"}
        qkey = {"aWq": "R_QT", "aWk": "R_KT", "aWv": "R_VT"}

        def mk_ev(nm):
            def ev(oc):
                dstv = vbf(oQKV + qdst[nm] + oc * 512, 256)
                V(lambda e, dstv=dstv, oc=oc, nm=nm: e.tensor_scalar(
                    dstv, pbPROJ[:, 0:256], bcol(bnm[nm], oc), None, ALU.add),
                  reads=["pb3p", "biases"], writes=[qkey[nm]])
            return ev

        def P1_thunks(c):
            ci = c % 2
            xc = v32(oXC + ci * 8192, 2048)
            xck = "a_xc%d" % ci
            xn = vbf(oXN + ci * 4096, 2048)
            xnk = "R_XN%d" % ci
            srcx = xT.rearrange("(kc p) t -> p kc t", p=128)[:, :, c * CH:(c + 1) * CH]
            yield lambda: p.dma("SY", "xin%d" % ci,
                                lambda e, srcx=srcx, xc=xc: e.dma_start(xc, srcx),
                                writes=[xck])
            xsq = vbf(oSQ, 2048)

            def _ln1(xc=xc, xck=xck, xn=xn, xnk=xnk):
                ln_stats(xc, xck, xsq, "R_SQ", CH, True, 1.0,
                         pbA, pbB, "pb3b", "pb3b", ones_cb)
                ln_apply(r3(xc, CH), xck, r3(xsq, CH), "R_SQ",
                         r3(xn, CH), xnk, CH, pbA, pbB, "pb3b", "pb3b")
            yield _ln1
            yield from projS(jof[("aWq", c)], xn, xnk, mk_ev("aWq"))
            yield from projS(jof[("aWk", c)], xn, xnk, mk_ev("aWk"))
            yield from projS(jof[("aWv", c)], xn, xnk, mk_ev("aWv"))
            par = c % 2
            Qv = Qvp(c)
            Kv = Kvp(c)
            kq, kq2 = "a_Qv%d" % par, "a_Qv2_%d" % par
            kk, kk2 = "a_Kv%d" % par, "a_Kv2_%d" % par
            for m in range(16):
                def _cp(m=m, Qv=Qv, Kv=Kv, kq=kq, kk=kk):
                    def sl(X, m=m):
                        return X[(m % 2) * 64:(m % 2) * 64 + 64,
                                 (m // 2) * 256:(m // 2) * 256 + 256]
                    V(lambda e, m=m, s=sl(QT), Qv=Qv: e.tensor_copy(Qv[0:64, m:4096:16], s),
                      reads=["R_QT"], writes=[kq])
                    V(lambda e, m=m, s=sl(VT): e.tensor_copy(Vvv[0:64, m:4096:16], s),
                      reads=["R_VT"], writes=["a_Vv"])
                    dk = free_dims(Kv[0:64, (127 - m):(128 - m)], [[128, 32], [-16, 8]])
                    sk3 = sl(KT).rearrange("p (a r) -> p a r", a=32)
                    V(lambda e, dk=dk, sk3=sk3: e.tensor_copy(dk, sk3),
                      reads=["R_KT"], writes=[kk])
                    dv = free_dims(Vvv[64:128, (127 - m):(128 - m)], [[128, 32], [-16, 8]])
                    sv3 = sl(VT).rearrange("p (a r) -> p a r", a=32)
                    V(lambda e, dv=dv, sv3=sv3: e.tensor_copy(dv, sv3),
                      reads=["R_VT"], writes=["a_Vvr"])
                yield _cp
            for h4 in range(2):
                yield lambda h4=h4, Qv=Qv, kq=kq, kq2=kq2: V(
                    lambda e, h4=h4, Qv=Qv: e.tensor_copy(
                        Qv[64:128, h4 * 2048:(h4 + 1) * 2048],
                        Qv[0:64, h4 * 2048:(h4 + 1) * 2048]),
                    reads=[kq], writes=[kq2])
                yield lambda h4=h4, Kv=Kv, kk=kk, kk2=kk2: V(
                    lambda e, h4=h4, Kv=Kv: e.tensor_copy(
                        Kv[64:128, h4 * 2048:(h4 + 1) * 2048],
                        Kv[0:64, h4 * 2048:(h4 + 1) * 2048]),
                    reads=[kk], writes=[kk2])

        def P2_thunks(c):
            ci = c % 2
            vt = vtp(c)
            big = bigp(c)
            hk = hkp(c)

            def _band():
                V(lambda e, c=c: e.tensor_copy(relc_sb[:], rel_sb[0:32, c:c + 1]),
                  reads=["rel_sb"], writes=["relc"])
                for pc in range(4):
                    T(lambda e, pc=pc: e.matmul(pbM[0:1, 0:320],
                                                relc_sb[:],
                                                oh_sb[:, pc * 320:(pc + 1) * 320],
                                                start=True, stop=True),
                      reads=["relc", "oh_sb"], writes=["pb3b"])
                    V(lambda e, pc=pc: e.tensor_copy(band_sb[0:1, pc * 320:(pc + 1) * 320],
                                                     pbM[0:1, 0:320]),
                      reads=["pb3b"], writes=["band_sb"])
                p.dma("SY", "bando%d" % ci,
                      lambda e, ci=ci: e.dma_start(bandh[ci:ci + 1, :], band_sb[0:1, :1280]),
                      reads=["band_sb"], writes=["bandh%d" % ci])
            yield _band
            pbT = PB[3][:, 256:512]
            for tt in range(0, 32, 4):
                def _tr(tt=tt):
                    for j in range(4):
                        t = tt + j
                        T(lambda e, t=t, j=j: e.matmul(pbT[:, j * 64:j * 64 + 64],
                                                       Vvv[64:128, t * 128:(t + 1) * 128],
                                                       i64_sb[64:128, :], start=True, stop=True,
                                                       tile_position=(64, 0)),
                          reads=["a_Vvr", "i64_sb"], writes=["pb3t"])
                    dst = free_dims(vt[:, tt * 65:tt * 65 + 1], [[65, 4], [1, 64]])
                    src4 = pbT.rearrange("p (a b) -> p a b", a=4)
                    V(lambda e, dst=dst, src4=src4: e.tensor_copy(dst, src4),
                      reads=["pb3t"], writes=["a_vtok%d" % ci])
                yield _tr

            def _scan():
                zcb = zcol64[:].broadcast_to([64, 4255])
                V(lambda e, zcb=zcb, big=big: e.tensor_tensor_scan(big[0:64, 97:4352], zcb,
                                                                   Vvv[0:64, 0:4255], 0.0,
                                                                   ALU.add, ALU.add),
                  reads=["zc", "a_Vv", "a_Vvp"], writes=["a_ps%d" % ci])
                T(lambda e, big=big: e.matmul(pbM[0:1, 0:64], big[0:64, 4192:4193],
                                              i64_sb[0:64, :], start=True, stop=True),
                  reads=["a_ps%d" % ci, "i64_sb"], writes=["pb3b"])
                V(lambda e, ci=ci: e.tensor_copy(totT[:, ci * 64:ci * 64 + 64], pbM[0:1, 0:64]),
                  reads=["pb3b"], writes=["totT%d" % ci])
                V(lambda e, c=c, ci=ci: e.tensor_scalar(w15row[:, ci * 512:ci * 512 + 512],
                                                        ones512[:],
                                                        wbc[0:1, 240 + c:241 + c], None, ALU.mult),
                  reads=["ones512", "wbc"], writes=["w15%d" % ci])
            yield _scan

            def _hk():
                hsrc = bass.AP(bandh_t, ci * 1280 + 1, [[1, 128], [1, 1152]])
                p.dma("SY", "hank",
                      lambda e, hsrc=hsrc, hk=hk: e.dma_start(hk, hsrc),
                      reads=["bandh%d" % ci], writes=["a_hk%d" % ci])
            yield _hk

        def A_emit(c, feeder):
            par = c % 2
            Qv = Qvp(c)
            Kv = Kvp(c)
            vt = vtp(c)
            big = bigp(c)
            hk = hkp(c)
            kq, kq2 = "a_Qv%d" % par, "a_Qv2_%d" % par
            kk, kk2 = "a_Kv%d" % par, "a_Kv2_%d" % par
            vtk = "a_vtok%d" % par
            psk = "a_ps%d" % par
            outk = "a_out%d" % par
            hkk = "a_hk%d" % par
            totc = totT[:, par * 64:par * 64 + 64]
            w15c = w15row[:, par * 512:par * 512 + 512]
            pend = [None]

            def flush_combine():
                if pend[0] is not None:
                    pend[0]()
                    pend[0] = None

            def draw(n=1):
                for _ in range(n):
                    th = next(feeder, None)
                    if th is not None:
                        th()

            def qg_params(g):
                pvh, bdh = ("pb2a", "pb2b") if g % 2 == 0 else ("pb2b", "pb2a")
                pvo = 0 if g % 2 == 0 else 512
                return pvh, bdh, pvo, 512 - pvo

            def emit_pv(g, pi):
                pvh, _, pvo, _ = qg_params(g)
                gp = g * 16 + pi
                pp = vbf(oPP + (gp % 3) * 2048, 1024)
                ppk = "a_pp%d" % (gp % 3)
                for j, tj in enumerate((2 * pi, 2 * pi + 1)):
                    T(lambda e, tj=tj, pp=pp, j=j, pvo=pvo: e.matmul(
                        PB[2][0:65, pvo:pvo + 512], vt[:, tj * 65:tj * 65 + 65],
                        pp[:, j * 512:(j + 1) * 512],
                        start=(tj == 0), stop=(tj == 31)),
                      reads=[vtk, ppk], writes=[pvh])

            def band_mms(g):
                q0 = g * 512
                _, bdh, _, bdo = qg_params(g)
                mms = []
                first = [True]
                for d0 in (128, 0, -128, -256, -384, -512):
                    k0 = q0 - d0
                    if k0 < 0 or k0 >= 4096:
                        continue
                    t = k0 // 128

                    def mk(t=t, d0=d0, bdo=bdo, bdh=bdh):
                        fb = first[0]
                        first[0] = False
                        def go(fb=fb):
                            T(lambda e, t=t, d0=d0, fb=fb, bdo=bdo: e.matmul(
                                PB[2][0:64, bdo:bdo + 512], vt[:, t * 65:t * 65 + 64],
                                hk[:, 512 + d0:512 + d0 + 512],
                                start=fb, stop=False),
                              reads=[vtk, hkk], writes=[bdh])
                        return go
                    mms.append(mk())

                def w15mm():
                    T(lambda e, bdo=bdo: e.matmul(PB[2][0:64, bdo:bdo + 512], totc, w15c,
                                                  start=False, stop=True),
                      reads=["totT%d" % par, "w15%d" % par], writes=[bdh])
                mms.append(w15mm)
                return mms

            def emit_o2e(g):
                _, bdh, _, bdo = qg_params(g)
                o2e = v32(oSC + 2048, 512)[0:64, :]
                V(lambda e, o2e=o2e, bdo=bdo: e.tensor_copy(o2e, PB[2][0:64, bdo:bdo + 512]),
                  reads=[bdh], writes=["R_O2"])

            def emit_recip(g):
                pvh, _, pvo, _ = qg_params(g)
                V(lambda e, pvo=pvo: e.reciprocal(recip_sb[:], PB[2][64:65, pvo:pvo + 512]),
                  reads=[pvh], writes=["recip"])

            def mk_combine(g):
                q0 = g * 512
                pvh, _, pvo, _ = qg_params(g)

                def go():
                    T(lambda e: e.matmul(pbM[0:64, 0:512], ones_r[0:1, 0:64],
                                         recip_sb[:], start=True, stop=True),
                      reads=["ones_r", "recip"], writes=["pb3b"])
                    V(lambda e: e.tensor_copy(rb_sb[:], pbM[0:64, 0:512]),
                      reads=["pb3b"], writes=["rb"])
                    o1 = v32(oSC, 512)[0:64, :]
                    o2 = v32(oSC + 2048, 512)[0:64, :]
                    V(lambda e, o1=o1: e.tensor_tensor(o1, PB[2][0:64, pvo:pvo + 512],
                                                       rb_sb[:], ALU.mult),
                      reads=[pvh, "rb"], writes=["R_O1"])
                    V(lambda e, o2=o2, o1=o1: e.tensor_tensor(
                        o2, o1, o2, ALU.add),
                      reads=["R_O1", "R_O2"], writes=["R_O2"])
                    V(lambda e, o1=o1: e.scalar_tensor_tensor(
                        o1, big[0:64, q0 + 6:q0 + 518],
                        wbc[0:64, 496 + c:497 + c], o2, ALU.mult, ALU.add),
                      reads=[psk, "wbc", "R_O2"], writes=["R_O1"])
                    V(lambda e, o1=o1: e.scalar_tensor_tensor(
                        big[64:128, q0:q0 + 512], big[0:64, q0 + 187:q0 + 699],
                        wbcn[0:64, 240 + c:241 + c], o1, ALU.mult, ALU.add),
                      reads=[psk, "wbcn", "R_O1"], writes=[outk])
                return go

            bandq = []

            for g in range(8):
                q0 = g * 512
                for pi in range(16):
                    t0, t1 = 2 * pi, 2 * pi + 1
                    sp = PB[pi % 2]
                    ska, skb = "pb%da" % (pi % 2), "pb%db" % (pi % 2)
                    gp = g * 16 + pi
                    T(lambda e, t0=t0, sp=sp, q0=q0: e.matmul(
                        sp[:, 0:512], Kv[0:64, t0 * 128:(t0 + 1) * 128],
                        Qv[0:64, q0:q0 + 512], start=True, stop=True,
                        tile_position=(0, 0)),
                      reads=[kk, kq], writes=[ska])
                    T(lambda e, t1=t1, sp=sp, q0=q0: e.matmul(
                        sp[:, 512:1024], Kv[64:128, t1 * 128:(t1 + 1) * 128],
                        Qv[64:128, q0:q0 + 512], start=True, stop=True,
                        tile_position=(64, 0)),
                      reads=[kk2, kq2], writes=[skb])
                    pp = vbf(oPP + (gp % 3) * 2048, 1024)
                    ppk = "a_pp%d" % (gp % 3)
                    S(lambda e, pp=pp, sp=sp: e.activation(
                        pp, sp[:, :], AF.Exp, bias=consts[:, 0:1], scale=consts[:, 1:2]),
                      reads=[ska, skb, "consts"], writes=[ppk])
                    if pi == 0:
                        if g > 0:
                            emit_pv(g - 1, 14)
                            for mm in bandq[3:]:
                                mm()
                            emit_o2e(g - 1)
                        else:
                            draw()
                    elif pi == 1:
                        if g > 0:
                            emit_pv(g - 1, 15)
                            emit_recip(g - 1)
                            pend[0] = mk_combine(g - 1)
                        else:
                            draw()
                    elif pi == 5:
                        emit_pv(g, 3)
                        flush_combine()
                    elif pi == 15:
                        emit_pv(g, 13)
                        bandq = band_mms(g)
                        for mm in bandq[:3]:
                            mm()
                    else:
                        emit_pv(g, pi - 2)
                        draw()
                        if pi in (7, 9, 11, 13):
                            draw()
            emit_pv(7, 14)
            for mm in bandq[3:]:
                mm()
            emit_o2e(7)
            emit_pv(7, 15)
            emit_recip(7)
            flush_combine()
            mk_combine(7)()

        def T_thunks(c):
            ci = c % 2
            xc = v32(oXC + ci * 8192, 2048)
            xck = "a_xc%d" % ci
            xn = vbf(oXN + ci * 4096, 2048)
            xnk = "R_XN%d" % ci
            as2 = v32(oAS, 2048)
            xsq2 = vbf(oSQ, 2048)
            ab = vbf(oXN + ci * 4096, 2048)
            big = bigp(c)
            outk = "a_out%d" % ci
            for m in range(16):
                def _uv(m=m):
                    dsto = OT[(m % 2) * 64:(m % 2) * 64 + 64,
                              (m // 2) * 256:(m // 2) * 256 + 256]
                    V(lambda e, dsto=dsto, m=m, big=big: e.tensor_copy(
                        dsto, big[64:128, m:4096:16]),
                      reads=[outk], writes=["R_OT"])
                yield _uv

            def evo(oc):
                V(lambda e, oc=oc: e.scalar_tensor_tensor(
                    as2[:, oc * 256:(oc + 1) * 256], pbPROJ[:, 0:256], bcol("abo", oc),
                    ab[:, oc * 256:(oc + 1) * 256], ALU.add, ALU.add),
                  reads=["pb3p", "biases", xnk], writes=["R_AS"])
            yield from projS(jof[("aWo", c)], OT, "R_OT", evo)

            def _ln2(ab=ab, xnk=xnk):
                ln_stats(as2, "R_AS", xsq2, "R_SQ", CH, False, 1.0,
                         pbA, pbB, "pb3b", "pb3b", ones_cb)
                ln_apply(r3(as2, CH), "R_AS", r3(xsq2, CH), "R_SQ",
                         r3(ab, CH), xnk, CH, pbA, pbB, "pb3b", "pb3b")
            yield _ln2

            def ev1(oc):
                V(lambda e, oc=oc: e.tensor_scalar(
                    ff1[:, oc * 256:(oc + 1) * 256], pbPROJ[:, 0:256],
                    bcol("ab1", oc), None, ALU.add),
                  reads=["pb3p", "biases"], writes=["R_OT"])
            yield from projS(jof[("aW1", c)], ab, xnk, ev1)

            def ev2(oc):
                V(lambda e, oc=oc: e.scalar_tensor_tensor(
                    as2[:, oc * 256:(oc + 1) * 256], pbPROJ[:, 0:256], bcol("ab2", oc),
                    ab[:, oc * 256:(oc + 1) * 256], ALU.add, ALU.add),
                  reads=["pb3p", "biases", xnk], writes=["R_AS"])
            yield from projS(jof[("aW2", c)], ff1, "R_OT", ev2)

            def _ln3():
                ln_stats(as2, "R_AS", xsq2, "R_SQ", CH, False, 1.0,
                         pbA, pbB, "pb3b", "pb3b", ones_cb)
                as23 = r3(as2, CH)
                xc3 = r3(xc, CH)
                V(lambda e: e.tensor_tensor(as23, as23, bc3g(pbA, CH), ALU.mult),
                  reads=["R_AS", "pb3b"], writes=["R_AS"])
                V(lambda e: e.tensor_tensor(as23, as23, bc3g(pbB, CH), ALU.subtract),
                  reads=["R_AS", "pb3b"], writes=["R_AS"])
                V(lambda e: e.tensor_tensor(as23, as23, xc3, ALU.add),
                  reads=["R_AS", xck], writes=["R_AS"])
                dstx = xbuf.rearrange("(kc p) t -> p kc t", p=128)[:, :, c * CH:(c + 1) * CH]
                p.dma("SY", "xout",
                      lambda e, dstx=dstx, as2=as2: e.dma_start(dstx, as2),
                      reads=["R_AS"], writes=[])
            yield _ln3

        def drain(gen):
            for th in gen:
                th()

        import itertools
        for ci2 in range(2):
            V(lambda e, ci2=ci2: e.memset(
                free_dims(vbf(oVT + ci2 * 4160, 2080)[:, 64:65], [[65, 32]]), 1.0),
              writes=["a_vtok%d" % ci2])
            V(lambda e, ci2=ci2: e.memset(vbf(oBG + ci2 * 8704, 4352)[0:64, 0:97], 0.0),
              writes=["a_ps%d" % ci2])
        V(lambda e: e.memset(Vvv[0:64, 4096:4256], 0.0), writes=["a_Vvp"])
        for j in range(3):
            wload(j)
        drain(P1_thunks(0))
        drain(P2_thunks(0))
        carry = iter(())
        for c in range(NCHUNK):
            feeder = itertools.chain(
                carry, P1_thunks(c + 1) if c + 1 < NCHUNK else iter(()))
            A_emit(c, feeder)
            drain(feeder)
            if c + 1 < NCHUNK:
                drain(P2_thunks(c + 1))
            carry = T_thunks(c)
        drain(carry)

    def phase_ffn(last):
        oXG = 0
        oSQ = oXG + 32768
        oXN = oSQ + 8192
        oH = oXN + 32768
        oWG = oH + 98304
        oGS = oWG + 8192
        assert oGS + 4096 <= ARB, oGS
        oWD = 0
        oXS = oWD + 12288
        src_in = xbuf
        dst_out = yT if last else xbuf
        xnf = vbf(oXN, 16384)
        for g in range(4):
            xg = v32(oXG + (g % 2) * 16384, 4096)
            xgk = "R_FXG%d" % (g % 2)
            srcx = src_in.rearrange("(kc p) t -> p kc t", p=128)[:, :, g * 512:(g + 1) * 512]
            p.dma("SY", "fxin%d" % (g % 2), lambda e, srcx=srcx, xg=xg: e.dma_start(xg, srcx),
                  writes=[xgk])
            xsq = vbf(oSQ, 4096)
            ln_stats(xg, xgk, xsq, "R_FSQ", 512, True, 1.0 / 127.0,
                     PB[3][:, 0:512], PB[3][:, 512:1024], "pb3a", "pb3b", ones_cb)
            xn3 = free_dims(xnf[0:128, g * 512:g * 512 + 1], [[2048, 8], [1, 512]])
            ln_apply(r3(xg, 512), xgk, r3(xsq, 512), "R_FSQ", xn3, "f_xn%d" % g, 512,
                     PB[3][:, 0:512], PB[3][:, 512:1024], "pb3a", "pb3b")

        H = vbf(oH, 49152)
        for hc in range(NHC):
            hi = hc % 2
            wgv = vbf(oWG + hi * 2048, 1024)
            wuv = vbf(oWG + 4096 + hi * 2048, 1024)
            sg = ins["wg"][hc, :, :].rearrange("(kc p) h -> p kc h", p=128)
            su = ins["wu"][hc, :, :].rearrange("(kc p) h -> p kc h", p=128)
            p.dma("G", "fwg%d" % hi, lambda e, sg=sg, wgv=wgv: e.dma_start(wgv, sg),
                  writes=["f_wg%d" % hi])
            p.dma("G", "fwu%d" % hi, lambda e, su=su, wuv=wuv: e.dma_start(wuv, su),
                  writes=["f_wu%d" % hi])
            for seg in range(4):
                sb = seg % 2
                gk = "pb0a" if sb == 0 else "pb0b"
                uk = "pb1a" if sb == 0 else "pb1b"
                gdep = "f_xn%d" % seg
                for kc in range(8):
                    mv = xnf[:, kc * 2048 + seg * 512: kc * 2048 + (seg + 1) * 512]
                    T(lambda e, mv=mv, wgv=wgv, kc=kc, sb=sb: e.matmul(
                        PB[0][:, sb * 512:(sb + 1) * 512],
                        wgv[:, kc * 128:(kc + 1) * 128], mv,
                        start=(kc == 0), stop=(kc == 7)),
                      reads=["f_wg%d" % hi, gdep], writes=[gk])
                for kc in range(8):
                    mv = xnf[:, kc * 2048 + seg * 512: kc * 2048 + (seg + 1) * 512]
                    T(lambda e, mv=mv, wuv=wuv, kc=kc, sb=sb: e.matmul(
                        PB[1][:, sb * 512:(sb + 1) * 512],
                        wuv[:, kc * 128:(kc + 1) * 128], mv,
                        start=(kc == 0), stop=(kc == 7)),
                      reads=["f_wu%d" % hi, gdep], writes=[uk])
                gs = vbf(oGS + sb * 2048, 512)
                S(lambda e, gs=gs, sb=sb: e.activation(gs, PB[0][:, sb * 512:(sb + 1) * 512],
                                                       AF.Silu),
                  reads=[gk], writes=["f_gs%d" % sb])
                hdst = H[:, hc * 2048 + seg * 512: hc * 2048 + (seg + 1) * 512]
                V(lambda e, hdst=hdst, gs=gs, sb=sb: e.tensor_tensor(
                    hdst, gs, PB[1][:, sb * 512:(sb + 1) * 512], ALU.mult),
                  reads=["f_gs%d" % sb, uk], writes=["f_H%d" % hc])

        for oc in range(8):
            oi = oc % 2
            wdv = vbf(oWD + oi * 6144, 3072)
            sd = ins["wd"][oc, :, :].rearrange("(kt p) d -> p kt d", p=128)
            p.dma("G", "fwd%d" % oi, lambda e, sd=sd, wdv=wdv: e.dma_start(wdv, sd),
                  writes=["f_wd%d" % oi, "R_FXG0"])
            for seg in range(4):
                sb = seg % 2
                dk = "pb2a" if sb == 0 else "pb2b"
                for kt in range(NHC):
                    mv = H[:, kt * 2048 + seg * 512: kt * 2048 + (seg + 1) * 512]
                    T(lambda e, kt=kt, mv=mv, wdv=wdv, sb=sb: e.matmul(
                        PB[2][:, sb * 512:(sb + 1) * 512],
                        wdv[:, kt * 128:(kt + 1) * 128], mv,
                        start=(kt == 0), stop=(kt == NHC - 1)),
                      reads=["f_wd%d" % oi, "f_H%d" % kt], writes=[dk])
                xs = v32(oXS + seg * 2048, 512)
                xsk = "R_FXG0" if seg < 2 else "R_FXG1"
                srcx = src_in[oc * 128:(oc + 1) * 128, seg * 512:(seg + 1) * 512]
                p.dma("SY", "fx2%d" % seg,
                      lambda e, srcx=srcx, xs=xs: e.dma_start(xs, srcx),
                      writes=[xsk])
                xo = v32(oXG + 28672 + (seg % 2) * 2048, 512)
                xok = "R_FXG1"
                V(lambda e, xo=xo, xs=xs, sb=sb: e.scalar_tensor_tensor(
                    xo, PB[2][:, sb * 512:(sb + 1) * 512], 1.0 / 127.0, xs,
                    ALU.mult, ALU.add),
                  reads=[dk, xsk], writes=[xok])
                dstx = dst_out[oc * 128:(oc + 1) * 128, seg * 512:(seg + 1) * 512]
                p.dma("SY", "fxo%d" % (seg % 2),
                      lambda e, dstx=dstx, xo=xo: e.dma_start(dstx, xo),
                      reads=[xok], writes=[])

    def phase_attn_b():
        oXG = 0
        oSQ = oXG + 16384
        oCS = oSQ + 16384
        oLN = oCS + 12288
        oRX = oLN + 4096
        oQT = oRX + 32768
        oST = oQT + 32768
        oW = oST + 17408
        oPP = 0
        oXS = 8192
        oXO2 = oXS + 8192
        assert oW + 49152 <= ARB

        rxn = vbf(oRX, 16384)
        sbA = v32(oLN, 512)
        sbB = v32(oLN + 2048, 512)

        def xg_slot(g):
            return v32(oXG if g % 2 == 0 else oQT, 4096)

        def xg_pref(g):
            xg = xg_slot(g)
            srcg = xbuf.rearrange("(kc p) t -> p kc t", p=128)[:, :, g * 512:(g + 1) * 512]
            p.dma("SY", "bxin%d" % (g % 2),
                  lambda e, srcg=srcg, xg=xg: e.dma_start(xg, srcg),
                  writes=["R_BXG%d" % (g % 2)])

        def lnb(g):
            gi = g % 2
            xg = xg_slot(g)
            xgk = "R_BXG%d" % gi
            xsq = v32(oSQ, 4096)
            ln_stats(xg, xgk, xsq, "R_BSQ", 512, True, 1.0,
                     PB[3][:, 0:512], PB[3][:, 512:1024], "pb3a", "pb3b", ones_c)
            V(lambda e: e.tensor_copy(sbA[:], PB[3][:, 0:512]),
              reads=["pb3a"], writes=["b_sbA"])
            V(lambda e: e.tensor_copy(sbB[:], PB[3][:, 512:1024]),
              reads=["pb3b"], writes=["b_sbB"])
            xn = v32(oSQ, 4096)
            ln_apply(r3(xg, 512), xgk, r3(xg, 512), xgk, r3(xn, 512), "R_BSQ", 512,
                     sbA, sbB, "b_sbA", "b_sbB")
            if g + 2 < 4:
                xg_pref(g + 2)
            for cc in range(4):
                ci2 = cc % 2
                csrc = ins["cosT"][cc * 128:(cc + 1) * 128, g * 512:(g + 1) * 512]
                ssrc = ins["sinT"][cc * 128:(cc + 1) * 128, g * 512:(g + 1) * 512]
                cosb = v32(oCS + ci2 * 2048, 512)
                sinb = v32(oCS + 6144 + ci2 * 2048, 512)
                p.dma("SY", "bcs%d" % ci2,
                      lambda e, csrc=csrc, cosb=cosb: e.dma_start(cosb, csrc),
                      writes=["b_cos%d" % ci2])
                p.dma("SY", "bsn%d" % ci2,
                      lambda e, ssrc=ssrc, sinb=sinb: e.dma_start(sinb, ssrc),
                      writes=["b_sin%d" % ci2])
                x1 = xn[:, cc * 512:(cc + 1) * 512]
                x2 = xn[:, (cc + 4) * 512:(cc + 5) * 512]
                t1 = v32(oCS + 4096, 512)
                t2 = v32(oCS + 10240, 512)
                r1 = rxn[:, cc * 2048 + g * 512: cc * 2048 + (g + 1) * 512]
                r2 = rxn[:, (cc + 4) * 2048 + g * 512: (cc + 4) * 2048 + (g + 1) * 512]
                V(lambda e, t1=t1, x1=x1, cosb=cosb: e.tensor_tensor(t1, x1, cosb, ALU.mult),
                  reads=["R_BSQ", "b_cos%d" % ci2], writes=["b_t1"])
                V(lambda e, t2=t2, x2=x2, sinb=sinb: e.tensor_tensor(t2, x2, sinb, ALU.mult),
                  reads=["R_BSQ", "b_sin%d" % ci2], writes=["b_t2"])
                V(lambda e, r1=r1, t1=t1, t2=t2: e.tensor_tensor(r1, t1, t2, ALU.subtract),
                  reads=["b_t1", "b_t2"], writes=["R_RX", "R_RXg%d" % g])
                V(lambda e, t1=t1, x1=x1, sinb=sinb: e.tensor_tensor(t1, x1, sinb, ALU.mult),
                  reads=["R_BSQ", "b_sin%d" % ci2], writes=["b_t1"])
                V(lambda e, t2=t2, x2=x2, cosb=cosb: e.tensor_tensor(t2, x2, cosb, ALU.mult),
                  reads=["R_BSQ", "b_cos%d" % ci2], writes=["b_t2"])
                V(lambda e, r2=r2, t1=t1, t2=t2: e.tensor_tensor(r2, t1, t2, ALU.add),
                  reads=["b_t1", "b_t2"], writes=["R_RX", "R_RXg%d" % g])

        wslots = [vbf(oW, 8192), vbf(oW + 16384, 8192)]
        QTb = vbf(oQT, 16384)
        stg = [vbf(oST + i * 4352, 2048) for i in range(4)]

        def proj_b(nm, sl, tg):
            bias_nm = {"bWq": "bbq", "bWk": "bbk", "bWv": "bbv", "bWo": "bbo"}[nm]
            ccdst = {"bWk": cc_in_k, "bWv": cc_in_v}.get(nm)
            for oc in range(8):
                pbi = (oc + tg) % 2
                pbks = ["pb%da" % pbi, "pb%db" % pbi]
                for kc in range(8):
                    for q in range(2):
                        mv = rxn[:, kc * 2048 + tg * 1024 + q * 512:
                                 kc * 2048 + tg * 1024 + (q + 1) * 512]
                        T(lambda e, kc=kc, oc=oc, mv=mv, q=q, pbi=pbi, sl=sl: e.matmul(
                            PB[pbi][:, q * 512:(q + 1) * 512],
                            wslots[sl][:, kc * 1024 + oc * 128: kc * 1024 + (oc + 1) * 128],
                            mv, start=(kc == 0), stop=(kc == 7)),
                          reads=["R_BW%d" % sl, "R_RXg%d" % (tg * 2 + q)],
                          writes=[pbks[q]])
                if nm == "bWq":
                    dst = QTb[:, oc * 2048 + tg * 1024: oc * 2048 + (tg + 1) * 1024]
                    S(lambda e, dst=dst, oc=oc, pbi=pbi: e.activation(
                        dst, PB[pbi][:, 0:1024], AF.Identity,
                        bias=bcol("bbq", oc), scale=consts[:, 1:2]),
                      reads=pbks + ["biases", "consts"], writes=["b_QT"])
                else:
                    slot = (0 if nm == "bWk" else 2) + (oc + tg) % 2
                    sgb = stg[slot][:, 0:1024]
                    skey = "R_VT%d" % slot
                    S(lambda e, sgb=sgb, oc=oc, pbi=pbi, bias_nm=bias_nm: e.activation(
                        sgb, PB[pbi][:, 0:1024], AF.Identity,
                        bias=bcol(bias_nm, oc), scale=consts[:, 1:2]),
                      reads=pbks + ["biases", "consts"], writes=[skey])
                    dd = ccdst[oc * 128:(oc + 1) * 128, tg * 1024:(tg + 1) * 1024]
                    p.dma("G", "bstg%d" % slot,
                          lambda e, dd=dd, sgb=sgb: e.dma_start(dd, sgb),
                          reads=[skey], writes=["cc_in" + nm])

        srcw = ins["bWk"].rearrange("(kc p) d -> p kc d", p=128)
        p.dma("G", "bw0", lambda e, srcw=srcw: e.dma_start(wslots[0], srcw),
              writes=["R_BW0"])
        srcw = ins["bWv"].rearrange("(kc p) d -> p kc d", p=128)
        p.dma("G", "bw1", lambda e, srcw=srcw: e.dma_start(wslots[1], srcw),
              writes=["R_BW1"])
        xg_pref(0)
        xg_pref(1)
        lnb(0)
        lnb(1)
        proj_b("bWk", 0, 0)
        lnb(2)
        lnb(3)
        proj_b("bWk", 0, 1)
        p.async_op("G", "cck", lambda e: e.collective_compute(
            "AllGather", ALU.bypass, [[0, 1], [2, 3], [4, 5], [6, 7]],
            [cc_in_k.opt()], [cc_out_k.opt()]),
            reads=["cc_inbWk", "R_VT0", "R_VT1"], writes=["cc_k"])
        proj_b("bWv", 1, 0)
        proj_b("bWv", 1, 1)
        p.async_op("G", "ccv", lambda e: e.collective_compute(
            "AllGather", ALU.bypass, [[0, 1], [2, 3], [4, 5], [6, 7]],
            [cc_in_v.opt()], [cc_out_v.opt()]),
            reads=["cc_inbWv", "R_VT2", "R_VT3"], writes=["cc_v"])
        srcw = ins["bWq"].rearrange("(kc p) d -> p kc d", p=128)
        p.dma("G", "bw0b", lambda e, srcw=srcw: e.dma_start(wslots[0], srcw),
              writes=["R_BW0"])
        proj_b("bWq", 0, 0)
        proj_b("bWq", 0, 1)

        woview = wslots[0]
        srcw = ins["bWo"].rearrange("(kc p) d -> p kc d", p=128)
        p.dma("G", "bwo", lambda e, srcw=srcw: e.dma_start(woview, srcw),
              writes=["R_BW0"])

        OTb = vbf(oW + 16384, 16384)
        kp = [vbf(oRX + i * 8192, 4096) for i in range(2)]
        vp = [vbf(oRX + 16384 + i * 8192, 4096) for i in range(2)]
        vtb = [vbf(oST + i * 4352, 2080) for i in range(4)]

        def stage_thunks(gp):
            h0, h1 = 2 * gp, 2 * gp + 1
            pi = gp % 2
            kpk, vpk = "R_RXK%d" % pi, "R_RXV%d" % pi
            for (hh, rows) in ((h0, (0, 64)), (h1, (64, 128))):
                for (hs, cols) in ((0, (0, 2048)), (1, (2048, 4096))):
                    def _ld(hh=hh, rows=rows, hs=hs, cols=cols):
                        srck = cc_out_k[hs * D + hh * 64: hs * D + hh * 64 + 64, :]
                        dstk = kp[pi][rows[0]:rows[1], cols[0]:cols[1]]
                        p.dma("SY", "bkp%d" % pi,
                              lambda e, dstk=dstk, srck=srck: e.dma_start(dstk, srck),
                              reads=["cc_k"], writes=[kpk] + (["R_RX", "R_RXg0", "R_RXg1",
                                                               "R_RXg2", "R_RXg3"] if gp < 2 else []))
                        srcv = cc_out_v[hs * D + hh * 64: hs * D + hh * 64 + 64, :]
                        dstv = vp[pi][rows[0]:rows[1], cols[0]:cols[1]]
                        p.dma("SY", "bvp%d" % pi,
                              lambda e, dstv=dstv, srcv=srcv: e.dma_start(dstv, srcv),
                              reads=["cc_v"], writes=[vpk] + (["R_RX", "R_RXg0", "R_RXg1",
                                                               "R_RXg2", "R_RXg3"] if gp < 2 else []))
                    yield _ld
            s0, s1 = 2 * pi, 2 * pi + 1
            k0n, k1n = "R_VT%d" % s0, "R_VT%d" % s1
            def _ms():
                V(lambda e: e.memset(free_dims(vtb[s0][:, 64:65], [[65, 32]]), 1.0),
                  writes=[k0n])
                V(lambda e: e.memset(free_dims(vtb[s1][:, 64:65], [[65, 32]]), 1.0),
                  writes=[k1n])
            yield _ms
            for tt in range(0, 32, 4):
                def _tr(tt=tt):
                    for t in range(tt, tt + 4):
                        T(lambda e, t=t: e.matmul(
                            PB[3][:, 0:64], vp[pi][0:64, t * 128:(t + 1) * 128],
                            i64_sb[0:64, :], start=True, stop=True,
                            tile_position=(0, 0)),
                          reads=[vpk, "i64_sb"], writes=["pb3a"])
                        V(lambda e, t=t: e.tensor_copy(vtb[s0][:, t * 65:t * 65 + 64],
                                                       PB[3][:, 0:64]),
                          reads=["pb3a"], writes=[k0n])
                        T(lambda e, t=t: e.matmul(
                            PB[3][:, 512:576], vp[pi][64:128, t * 128:(t + 1) * 128],
                            i64_sb[64:128, :], start=True, stop=True,
                            tile_position=(64, 0)),
                          reads=[vpk, "i64_sb"], writes=["pb3b"])
                        V(lambda e, t=t: e.tensor_copy(vtb[s1][:, t * 65:t * 65 + 64],
                                                       PB[3][:, 512:576]),
                          reads=["pb3b"], writes=[k1n])
                yield _tr

        def A_b(gp, feeder, carry):
            h0, h1 = 2 * gp, 2 * gp + 1
            pi = gp % 2
            kpk = "R_RXK%d" % pi
            s0, s1 = 2 * pi, 2 * pi + 1
            k0n, k1n = "R_VT%d" % s0, "R_VT%d" % s1

            def draw():
                th = next(feeder, None)
                if th is not None:
                    th()

            for g in range(4):
                q0g = g * 512
                q_h0 = QTb[0:64, (h0 // 2) * 2048 + q0g: (h0 // 2) * 2048 + q0g + 512]
                q_h1 = QTb[64:128, (h1 // 2) * 2048 + q0g: (h1 // 2) * 2048 + q0g + 512]

                def emit_pvb(t, s0=s0, s1=s1, k0n=k0n, k1n=k1n):
                    pp = vbf(oPP + (t % 4) * 2048, 1024)
                    ppk = "b_pp%d" % (t % 4)
                    T(lambda e, t=t, pp=pp, s0=s0: e.matmul(
                        PB[2][0:65, 0:512], vtb[s0][:, t * 65:t * 65 + 65], pp[:, 0:512],
                        start=(t == 0), stop=(t == 31)),
                      reads=[k0n, ppk], writes=["pb2a"])
                    T(lambda e, t=t, pp=pp, s1=s1: e.matmul(
                        PB[2][0:65, 512:1024], vtb[s1][:, t * 65:t * 65 + 65], pp[:, 512:1024],
                        start=(t == 0), stop=(t == 31)),
                      reads=[k1n, ppk], writes=["pb2b"])
                for t in range(32):
                    sp = PB[t % 2]
                    ska, skb = "pb%da" % (t % 2), "pb%db" % (t % 2)
                    T(lambda e, t=t, sp=sp, q_h0=q_h0, pi=pi: e.matmul(
                        sp[:, 0:512], kp[pi][0:64, t * 128:(t + 1) * 128], q_h0,
                        start=True, stop=True, tile_position=(0, 0)),
                      reads=[kpk, "b_QT"], writes=[ska])
                    T(lambda e, t=t, sp=sp, q_h1=q_h1, pi=pi: e.matmul(
                        sp[:, 512:1024], kp[pi][64:128, t * 128:(t + 1) * 128], q_h1,
                        start=True, stop=True, tile_position=(64, 0)),
                      reads=[kpk, "b_QT"], writes=[skb])
                    pp = vbf(oPP + (t % 4) * 2048, 1024)
                    ppk = "b_pp%d" % (t % 4)
                    S(lambda e, pp=pp, sp=sp: e.activation(
                        pp, sp[:, :], AF.Exp, bias=consts[:, 2:3], scale=consts[:, 3:4]),
                      reads=[ska, skb, "consts"], writes=[ppk])
                    if t < 4:
                        if carry is not None:
                            for op in carry[t]:
                                op()
                        else:
                            draw()
                    elif t < 8:
                        draw()
                    elif t <= 14:
                        emit_pvb(2 * t - 16)
                        emit_pvb(2 * t - 15)
                    else:
                        emit_pvb(t - 1)

                def mk_carry(g=g, q0g=q0g, h0=h0, h1=h1, emit_pvb=emit_pvb):
                    def pv31():
                        emit_pvb(31)

                    def rc(hi2, dstr, dk):
                        V(lambda e, hi2=hi2, dstr=dstr: e.reciprocal(
                            dstr[0:1, 0:512], PB[2][64:65, hi2 * 512: hi2 * 512 + 512]),
                          reads=["pb2a" if hi2 == 0 else "pb2b"], writes=[dk])

                    def cm(hi2, srcr, dk):
                        hh = h0 if hi2 == 0 else h1
                        pvk = "pb2a" if hi2 == 0 else "pb2b"
                        T(lambda e, srcr=srcr: e.matmul(PB[3][0:64, 512:1024], ones_r[0:1, 0:64],
                                                        srcr[0:1, 0:512], start=True, stop=True),
                          reads=["ones_r", dk], writes=["pb3b"])
                        V(lambda e: e.tensor_copy(rb_sb[:], PB[3][0:64, 512:1024]),
                          reads=["pb3b"], writes=["rb"])
                        dsto = OTb[(hh % 2) * 64:(hh % 2) * 64 + 64,
                                   (hh // 2) * 2048 + q0g: (hh // 2) * 2048 + q0g + 512]
                        V(lambda e, dsto=dsto, hi2=hi2: e.tensor_tensor(
                            dsto, PB[2][0:64, hi2 * 512: hi2 * 512 + 512], rb_sb[:], ALU.mult),
                          reads=[pvk, "rb"], writes=["R_BW1"])
                    return [
                        [pv31, lambda: rc(0, recip_sb, "recipA")],
                        [lambda: rc(1, rowA, "recipB")],
                        [lambda: cm(0, recip_sb, "recipA")],
                        [lambda: cm(1, rowA, "recipB")],
                    ]
                carry = mk_carry()
            return carry

        for th in stage_thunks(0):
            th()
        carry = None
        for gp in range(8):
            feeder = stage_thunks(gp + 1) if gp < 7 else iter(())
            carry = A_b(gp, feeder, carry)
            for th in feeder:
                th()
        for ops in carry:
            for op in ops:
                op()

        def evwo(oc, tg, pb, pbks):
            oi = (oc * 2 + tg) % 2
            xs = v32(oXS + oi * 4096, 1024)
            xsk = "R_BXG"
            srcx = xbuf[oc * 128:(oc + 1) * 128, tg * 1024:(tg + 1) * 1024]
            p.dma("SY", "bx2%d" % oi,
                  lambda e, srcx=srcx, xs=xs: e.dma_start(xs, srcx),
                  writes=[xsk, "b_pp2", "b_pp3"])
            xo = v32(oXO2 + oi * 4096, 1024)
            xok = "R_BSQ"
            V(lambda e, xo=xo, xs=xs, pb=pb, oc=oc: e.scalar_tensor_tensor(
                xo, pb, bcol("bbo", oc), xs, ALU.add, ALU.add),
              reads=list(pbks) + ["biases", xsk], writes=["R_BSQ", "R_BXG"])
            dstx = xbuf[oc * 128:(oc + 1) * 128, tg * 1024:(tg + 1) * 1024]
            p.dma("SY", "bxo%d" % oi,
                  lambda e, dstx=dstx, xo=xo: e.dma_start(dstx, xo),
                  reads=[xok], writes=[])
        proj(woview, "R_BW0", OTb, "R_BW1", TOK, evwo)

    def dbg_tap(i):
        if not _DEBUG_TAPS:
            return
        for kc in range(8):
            t = v32(0, 2048)
            p.dma("SY", "dbgl", lambda e, kc=kc, t=t: e.dma_start(
                t, xbuf[kc * 128:(kc + 1) * 128, :]), writes=["dbgt"])
            p.dma("SY", "dbgs", lambda e, kc=kc, i=i, t=t: e.dma_start(
                dbg_taps[i][kc * 128:(kc + 1) * 128, :], t), reads=["dbgt"], writes=[])
        p.barrier()

    phase_attn_a()
    p.barrier()
    dbg_tap(0)
    phase_ffn(last=False)
    p.barrier()
    dbg_tap(1)
    phase_attn_b()
    p.barrier()
    dbg_tap(2)
    phase_ffn(last=True)
    p.barrier()

    p.finalize()
    with nc.Block() as block:
        @block.tensor
        def _(eng):
            p.emit("T", eng, sems)

        @block.vector
        def _(eng):
            p.emit("V", eng, sems)

        @block.scalar
        def _(eng):
            p.emit("S", eng, sems)

        @block.gpsimd
        def _(eng):
            p.emit("G", eng, sems)

        @block.sync
        def _(eng):
            p.emit("SY", eng, sems)
    stack.close()
    return nc


_NC_CACHE = None


def kernel(L0_attn_a_x, L0_attn_a_Wq, L0_attn_a_bq, L0_attn_a_Wk, L0_attn_a_bk,
           L0_attn_a_Wv, L0_attn_a_bv, L0_attn_a_Wo, L0_attn_a_bo,
           L0_attn_a_rel_bias, L0_attn_a_W1, L0_attn_a_b1, L0_attn_a_W2,
           L0_attn_a_b2, L0_attn_b_Wq, L0_attn_b_bq, L0_attn_b_Wk, L0_attn_b_bk,
           L0_attn_b_Wv, L0_attn_b_bv, L0_attn_b_Wo, L0_attn_b_bo,
           L0_ffn_w_gate, L0_ffn_w_up, L0_ffn_w_down):
    global _NC_CACHE
    oh, cosT, sinT, i64 = _host_consts()

    def wT(w):
        return np.ascontiguousarray(np.asarray(w, np.float32).T)

    def bR(b):
        return np.ascontiguousarray(np.asarray(b, np.float32).reshape(8, 128).T)

    x = np.asarray(L0_attn_a_x, np.float32)
    common = {
        "aWq": wT(L0_attn_a_Wq), "aWk": wT(L0_attn_a_Wk), "aWv": wT(L0_attn_a_Wv),
        "aWo": wT(L0_attn_a_Wo), "aW1": wT(L0_attn_a_W1), "aW2": wT(L0_attn_a_W2),
        "bWq": wT(L0_attn_b_Wq), "bWk": wT(L0_attn_b_Wk), "bWv": wT(L0_attn_b_Wv),
        "bWo": wT(L0_attn_b_Wo),
        "abq": bR(L0_attn_a_bq), "abk": bR(L0_attn_a_bk), "abv": bR(L0_attn_a_bv),
        "abo": bR(L0_attn_a_bo), "ab1": bR(L0_attn_a_b1), "ab2": bR(L0_attn_a_b2),
        "bbq": bR(L0_attn_b_bq), "bbk": bR(L0_attn_b_bk), "bbv": bR(L0_attn_b_bv),
        "bbo": bR(L0_attn_b_bo),
        "onehot": oh, "i64": i64,
        "wg": np.ascontiguousarray(
            np.asarray(L0_ffn_w_gate, np.int32).T.reshape(D, NHC, 128).transpose(1, 0, 2)),
        "wu": np.ascontiguousarray(
            np.asarray(L0_ffn_w_up, np.int32).T.reshape(D, NHC, 128).transpose(1, 0, 2)),
        "wd": np.ascontiguousarray(
            np.asarray(L0_ffn_w_down, np.int32).T.reshape(HID, 8, 128).transpose(1, 0, 2)),
    }
    in_maps = []
    for core in range(8):
        b, hh = core // 2, core % 2
        m = dict(common)
        m["xT"] = np.ascontiguousarray(x[b, hh * TOK:(hh + 1) * TOK, :].T)
        relroll = np.ascontiguousarray(
            np.roll(np.asarray(L0_attn_a_rel_bias, np.float32), -8 * hh, axis=1))
        m["relT"] = relroll
        m["relflat"] = np.ascontiguousarray(relroll.reshape(1, 512))
        m["cosT"] = np.ascontiguousarray(cosT[:, hh * TOK:(hh + 1) * TOK])
        m["sinT"] = np.ascontiguousarray(sinT[:, hh * TOK:(hh + 1) * TOK])
        in_maps.append(m)

    if _NC_CACHE is None:
        _NC_CACHE = build_program()
    res = run_bass_kernel_spmd(_NC_CACHE, in_maps, core_ids=list(range(8)))
    out = np.empty((4, 4096, D), np.float32)
    for core in range(8):
        b, hh = core // 2, core % 2
        out[b, hh * TOK:(hh + 1) * TOK, :] = res.results[core]["yT"].T
    return out


# version 29 (final)
# speedup vs baseline: 1.1836x; 1.0029x over previous
import math
import contextlib
import numpy as np
import ml_dtypes
import concourse.bass as bass
import concourse.mybir as mybir
from concourse.bass_utils import run_bass_kernel_spmd

F32 = mybir.dt.float32
BF16 = mybir.dt.bfloat16
I32 = mybir.dt.int32
AF = mybir.ActivationFunctionType
ALU = mybir.AluOpType
BF = ml_dtypes.bfloat16

D = 1024
TOK = 2048
NCHUNK = 8
CH = 256
HID = 3072
NHC = HID // 128
C_A = 92.0
C_B = 4.0
ENGS = ("T", "V", "S", "G", "SY")
_DEBUG_TAPS = False
ARCOLS = 47000
ARB = ARCOLS * 4


def _bucket_of_n(n):
    ret = 16 if n < 0 else 0
    n = abs(n)
    if n < 8:
        return ret + n
    v = 8 + int(np.float32(np.log(np.float32(n) / np.float32(8.0)))
                / np.float32(math.log(128.0 / 8.0)) * np.float32(8))
    return ret + min(v, 15)


def _host_consts():
    oh = np.zeros((32, 1280), np.float32)
    for i in range(1280):
        d = i - 640
        if abs(d) <= 90:
            oh[_bucket_of_n(-d), i] = 1.0
    freq = np.exp(np.arange(0, D, 2, dtype=np.float32)
                  * np.float32(-np.log(10000.0) / D))
    ang = np.arange(4096, dtype=np.float32)[:, None] * freq[None, :]
    cosT = np.ascontiguousarray(np.cos(ang).T.astype(np.float32))
    sinT = np.ascontiguousarray(np.sin(ang).T.astype(np.float32))
    i64 = np.zeros((128, 64), np.float32)
    i64[:64] = np.eye(64, dtype=np.float32)
    i64[64:] = np.eye(64, dtype=np.float32)
    return oh, cosT, sinT, i64


class Prog:
    def __init__(self, nc, stack):
        self.nc = nc
        self.stack = stack
        self.ops = {e: [] for e in ENGS}
        self.bufs = {}
        self.dsems = {}

    def dma_sem(self, name):
        if name not in self.dsems:
            sem = self.stack.enter_context(self.nc.semaphore("ds_" + name))
            self.dsems[name] = {"sem": sem, "n": 0}
        return self.dsems[name]

    def _deps(self, reads, writes, eng, selfsync):
        deps = []
        for b in reads:
            st = self.bufs.get(b)
            if st and st["w"] is not None:
                deps.append(st["w"])
        for b in writes:
            st = self.bufs.get(b)
            if st:
                if st["w"] is not None:
                    deps.append(st["w"])
                deps.extend(st["r"])
        out, seen = [], set()
        for d in deps:
            if d[0] == "E" and d[1] == eng and not selfsync:
                continue
            if d[0] == "D":
                d = ("D", d[1], self.dsems[d[1]]["n"])
            if d not in seen:
                seen.add(d)
                out.append(d)
        return out

    def _register(self, tok, reads, writes):
        for b in writes:
            self.bufs[b] = {"w": tok, "r": []}
        for b in reads:
            st = self.bufs.setdefault(b, {"w": None, "r": []})
            st["r"].append(tok)

    def op(self, eng, fn, reads=(), writes=(), selfsync=True):
        idx = len(self.ops[eng])
        o = {"kind": "op", "fn": fn,
             "deps": self._deps(reads, writes, eng, selfsync),
             "need_inc": False, "inc_no": None}
        self.ops[eng].append(o)
        tok = ("E", eng, idx)
        self._register(tok, reads, writes)
        return tok

    def dma(self, eng, stream, fn, reads=(), writes=()):
        ds = self.dma_sem(stream)
        deps = self._deps(reads, writes, eng, True)
        ds["n"] += 16
        o = {"kind": "dma", "fn": fn, "deps": deps, "sem": ds["sem"],
             "need_inc": False, "inc_no": None}
        self.ops[eng].append(o)
        tok = ("D", stream, ds["n"])
        self._register(tok, reads, writes)
        return tok

    def async_op(self, eng, stream, fn, reads=(), writes=()):
        ds = self.dma_sem(stream)
        deps = self._deps(reads, writes, eng, True)
        ds["n"] += 1
        o = {"kind": "dma", "fn": fn, "deps": deps, "sem": ds["sem"],
             "inc_amt": 1, "need_inc": False, "inc_no": None}
        self.ops[eng].append(o)
        tok = ("D", stream, ds["n"])
        self._register(tok, reads, writes)
        return tok

    def barrier(self):
        lasts = []
        for e in ENGS:
            if self.ops[e]:
                lasts.append(("E", e, len(self.ops[e]) - 1))
        for name, ds in self.dsems.items():
            lasts.append(("D", name, ds["n"]))
        for e in ENGS:
            deps = [d for d in lasts if not (d[0] == "E" and d[1] == e)]
            self.ops[e].append({"kind": "op", "fn": lambda eng: eng.nop(),
                                "deps": deps, "need_inc": False, "inc_no": None})
        self.bufs = {}

    def finalize(self):
        for e in ENGS:
            for o in self.ops[e]:
                for d in o["deps"]:
                    if d[0] == "E":
                        self.ops[d[1]][d[2]]["need_inc"] = True
        for e in ENGS:
            c = 0
            for o in self.ops[e]:
                if o["need_inc"]:
                    c += 1
                    o["inc_no"] = c

    def emit(self, ename, eng, sems):
        waited = {}
        for o in self.ops[ename]:
            for d in o["deps"]:
                if d[0] == "E":
                    _, e2, idx = d
                    cnt = self.ops[e2][idx]["inc_no"]
                    key = ("E", e2)
                    if waited.get(key, 0) >= cnt:
                        continue
                    waited[key] = cnt
                    eng.wait_ge(sems[e2], cnt)
                else:
                    _, sname, cnt = d
                    key = ("D", sname)
                    if waited.get(key, 0) >= cnt:
                        continue
                    waited[key] = cnt
                    eng.wait_ge(self.dsems[sname]["sem"], cnt)
            inst = o["fn"](eng)
            if o["kind"] == "dma":
                inst.then_inc(o["sem"], o.get("inc_amt", 16))
                if o["need_inc"]:
                    eng.nop().then_inc(sems[ename], 1)
            elif o["need_inc"]:
                inst.then_inc(sems[ename], 1)


def build_program():
    nc = bass.Bass("TRN2", target_bir_lowering=False)
    dt = nc.dram_tensor
    ins = {}
    xT = dt("xT", [D, TOK], F32, kind="ExternalInput").ap()
    for nm in ["aWq", "aWk", "aWv", "aWo", "aW1", "aW2", "bWq", "bWk", "bWv", "bWo"]:
        ins[nm] = dt(nm, [D, D], F32, kind="ExternalInput").ap()
    bias_keys = ["abq", "abk", "abv", "abo", "ab1", "ab2", "bbq", "bbk", "bbv", "bbo"]
    for nm in bias_keys:
        ins[nm] = dt(nm, [128, 8], F32, kind="ExternalInput").ap()
    ins["relflat"] = dt("relflat", [1, 512], F32, kind="ExternalInput").ap()
    ins["relT"] = dt("relT", [32, 16], F32, kind="ExternalInput").ap()
    ins["onehot"] = dt("onehot", [32, 1280], F32, kind="ExternalInput").ap()
    ins["cosT"] = dt("cosT", [512, TOK], F32, kind="ExternalInput").ap()
    ins["sinT"] = dt("sinT", [512, TOK], F32, kind="ExternalInput").ap()
    ins["i64"] = dt("i64", [128, 64], F32, kind="ExternalInput").ap()
    ins["wg"] = dt("wg", [NHC, D, 128], I32, kind="ExternalInput").ap()
    ins["wu"] = dt("wu", [NHC, D, 128], I32, kind="ExternalInput").ap()
    ins["wd"] = dt("wd", [8, HID, 128], I32, kind="ExternalInput").ap()
    yT = dt("yT", [D, TOK], F32, kind="ExternalOutput").ap()
    dbg_taps = []
    if _DEBUG_TAPS:
        for i in range(3):
            dbg_taps.append(dt("dbg%d" % i, [D, TOK], F32, kind="ExternalOutput").ap())
        dbgA = dt("dbgA", [128, 2048], F32, kind="ExternalOutput").ap()
        dbgB = dt("dbgB", [128, 4352], F32, kind="ExternalOutput").ap()
        dbgC = dt("dbgC", [128, 2080], F32, kind="ExternalOutput").ap()
        dbgD = dt("dbgD", [128, 4096], F32, kind="ExternalOutput").ap()
        dbgE = dt("dbgE", [128, 1152], F32, kind="ExternalOutput").ap()
    xbuf = dt("xbuf", [D, TOK], F32, kind="Internal").ap()
    bandh_t = dt("bandh", [2, 1280], BF16, kind="Internal")
    bandh = bandh_t.ap()
    cc_in_k = dt("cc_in_k", [D, TOK], BF16, kind="Internal").ap()
    cc_out_k = dt("cc_out_k", [2 * D, TOK], BF16, kind="Internal").ap()
    cc_in_v = dt("cc_in_v", [D, TOK], BF16, kind="Internal").ap()
    cc_out_v = dt("cc_out_v", [2 * D, TOK], BF16, kind="Internal").ap()

    stack = contextlib.ExitStack()
    E = stack.enter_context
    AR = E(nc.sbuf_tensor("AR", [128, ARCOLS], F32))
    biases = E(nc.sbuf_tensor("biases", [128, 80], F32))
    consts = E(nc.sbuf_tensor("consts", [128, 8], F32))
    rel_sb = E(nc.sbuf_tensor("rel_sb", [32, 16], F32))
    relc_sb = E(nc.sbuf_tensor("relc_sb", [32, 1], BF16))
    oh_sb = E(nc.sbuf_tensor("oh_sb", [32, 1280], BF16))
    wbc = E(nc.sbuf_tensor("wbc", [64, 512], F32))
    wbcn = E(nc.sbuf_tensor("wbcn", [64, 512], F32))
    i64_sb = E(nc.sbuf_tensor("i64_sb", [128, 64], BF16))
    ones_c = E(nc.sbuf_tensor("ones_c", [128, 1], F32))
    ones_cb = E(nc.sbuf_tensor("ones_cb", [128, 1], BF16))
    ones_r = E(nc.sbuf_tensor("ones_r", [1, 128], F32))
    ones_rb = E(nc.sbuf_tensor("ones_rb", [1, 128], BF16))
    ones512 = E(nc.sbuf_tensor("ones512", [1, 512], BF16))
    band_sb = E(nc.sbuf_tensor("band_sb", [1, 1280], BF16))
    recip_sb = E(nc.sbuf_tensor("recip_sb", [1, 512], F32))
    w15row = E(nc.sbuf_tensor("w15row", [1, 1024], BF16))
    totT = E(nc.sbuf_tensor("totT", [1, 128], BF16))
    rb_sb = E(nc.sbuf_tensor("rb_sb", [64, 512], F32))
    rowA = E(nc.sbuf_tensor("rowA", [1, 512], F32))
    rowB = E(nc.sbuf_tensor("rowB", [1, 512], F32))
    rowC = E(nc.sbuf_tensor("rowC", [1, 520], F32))
    zcol64 = E(nc.sbuf_tensor("zcol64", [64, 1], F32))
    PB = [E(nc.psum_tensor(f"pb{i}", [128, 1024], F32)) for i in range(4)]
    sems = {e: E(nc.semaphore("sem_" + e)) for e in ENGS}

    p = Prog(nc, stack)
    arf = AR[:]
    arb_all = bass.AP(arf.tensor, 0, list(arf.ap)).bitcast(BF16)

    def v32(off_b, cols):
        assert off_b % 4 == 0 and off_b + cols * 4 <= ARB, (off_b, cols)
        return arf[:, off_b // 4: off_b // 4 + cols]

    def vbf(off_b, cols):
        assert off_b % 2 == 0 and off_b + cols * 2 <= ARB, (off_b, cols)
        return arb_all[:, off_b // 2: off_b // 2 + cols]

    def T(fn, reads=(), writes=()):
        return p.op("T", fn, reads, writes, selfsync=False)

    def V(fn, reads=(), writes=()):
        return p.op("V", fn, reads, writes)

    def S(fn, reads=(), writes=()):
        return p.op("S", fn, reads, writes)

    def G(fn, reads=(), writes=()):
        return p.op("G", fn, reads, writes)

    def bcol(nm, oc):
        i = bias_keys.index(nm)
        return biases[:, 8 * i + oc: 8 * i + oc + 1]

    def free_dims(ap0, dims):
        return bass.AP(ap0.tensor, ap0.offset, [list(ap0.ap[0])] + dims)

    V(lambda e: e.memset(ones_c[:], 1.0), writes=["ones_c"])
    V(lambda e: e.memset(ones_cb[:], 1.0), writes=["ones_cb"])
    V(lambda e: e.memset(ones_r[:], 1.0), writes=["ones_r"])
    V(lambda e: e.memset(ones_rb[:], 1.0), writes=["ones_rb"])
    V(lambda e: e.memset(ones512[:], 1.0), writes=["ones512"])
    V(lambda e: e.memset(zcol64[:], 0.0), writes=["zc"])
    for col, val in [(0, -C_A), (1, 1.0), (2, -C_B), (3, 0.125),
                     (4, 1e-5), (5, -0.5), (6, 0.0)]:
        V(lambda e, c=col, vv=val: e.memset(consts[:, c:c + 1], vv), writes=["consts"])
    p.dma("G", "cinit", lambda e: e.dma_start(rel_sb[:], ins["relT"]), writes=["rel_sb"])
    p.dma("G", "cinit", lambda e: e.dma_start(oh_sb[:], ins["onehot"]), writes=["oh_sb"])
    p.dma("G", "cinit", lambda e: e.dma_start(i64_sb[:], ins["i64"]), writes=["i64_sb"])
    relf = rowB[0:1, 0:512]
    p.dma("G", "cinit", lambda e: e.dma_start(relf, ins["relflat"]), writes=["relf"])
    for i, nm in enumerate(bias_keys):
        p.dma("G", "cinit",
              lambda e, i=i, nm=nm: e.dma_start(biases[:, 8 * i:8 * i + 8], ins[nm]),
              writes=["biases"])
    T(lambda e: e.matmul(PB[3][0:64, 0:512], ones_r[0:1, 0:64], relf,
                         start=True, stop=True),
      reads=["ones_r", "relf"], writes=["pb3a"])
    V(lambda e: e.tensor_copy(wbc[:], PB[3][0:64, 0:512]), reads=["pb3a"], writes=["wbc"])
    V(lambda e: e.tensor_scalar(wbcn[:], wbc[:], -1.0, None, ALU.mult),
      reads=["wbc"], writes=["wbcn"])

    def ln_stats(xg, xk, xsq, sqk, gw, eps, scale, pba, pbb, ka, kb, onesq):
        V(lambda e: e.tensor_tensor(xsq, xg, xg, ALU.mult), reads=[xk], writes=[sqk])
        for kc in range(8):
            T(lambda e, kc=kc: e.matmul(pba[0:1, 0:gw], ones_c[:],
                                        xg[:, kc * gw:(kc + 1) * gw],
                                        start=(kc == 0), stop=(kc == 7)),
              reads=["ones_c", xk], writes=[ka])
        for kc in range(8):
            T(lambda e, kc=kc: e.matmul(pbb[0:1, 0:gw], onesq[:],
                                        xsq[:, kc * gw:(kc + 1) * gw],
                                        start=(kc == 0), stop=(kc == 7)),
              reads=["ones_cb", sqk], writes=[kb])
        V(lambda e: e.tensor_scalar(rowA[0:1, 0:gw], pba[0:1, 0:gw],
                                    1.0 / D, None, ALU.mult),
          reads=[ka], writes=["rowAm"])
        V(lambda e: e.tensor_scalar(rowB[0:1, 0:gw], pbb[0:1, 0:gw],
                                    1.0 / D, None, ALU.mult),
          reads=[kb], writes=["rowBe"])
        V(lambda e: e.tensor_tensor(rowC[0:1, 0:gw], rowA[0:1, 0:gw],
                                    rowA[0:1, 0:gw], ALU.mult),
          reads=["rowAm"], writes=["rowCm2"])
        V(lambda e: e.tensor_tensor(rowB[0:1, 0:gw], rowB[0:1, 0:gw],
                                    rowC[0:1, 0:gw], ALU.subtract),
          reads=["rowBe", "rowCm2"], writes=["rowBv"])
        epsc = consts[0:1, 4:5] if eps else consts[0:1, 6:7]
        S(lambda e: e.activation(rowB[0:1, 0:gw], rowB[0:1, 0:gw], AF.Ln,
                                 bias=epsc, scale=consts[0:1, 1:2]),
          reads=["rowBv", "consts"], writes=["rowBl"])
        S(lambda e: e.activation(rowB[0:1, 0:gw], rowB[0:1, 0:gw], AF.Exp,
                                 bias=consts[0:1, 6:7], scale=consts[0:1, 5:6]),
          reads=["rowBl", "consts"], writes=["rowBr"])
        if scale != 1.0:
            V(lambda e: e.tensor_scalar(rowB[0:1, 0:gw], rowB[0:1, 0:gw],
                                        scale, None, ALU.mult),
              reads=["rowBr"], writes=["rowBr"])
        V(lambda e: e.tensor_tensor(rowA[0:1, 0:gw], rowA[0:1, 0:gw],
                                    rowB[0:1, 0:gw], ALU.mult),
          reads=["rowAm", "rowBr"], writes=["rowAb"])
        T(lambda e: e.matmul(pba[:, 0:gw], ones_r[:], rowB[0:1, 0:gw],
                             start=True, stop=True),
          reads=["ones_r", "rowBr"], writes=[ka])
        T(lambda e: e.matmul(pbb[:, 0:gw], ones_r[:], rowA[0:1, 0:gw],
                             start=True, stop=True),
          reads=["ones_r", "rowAb"], writes=[kb])

    def r3(x, gw):
        return x.rearrange("p (kc w) -> p kc w", kc=8)

    def bc3g(pb, gw):
        a = pb[:, 0:gw]
        return bass.AP(a.tensor, a.offset, [list(a.ap[0]), [0, 8], list(a.ap[1])])

    def ln_apply(xg3, xk, tmp3, tk, out3, ok, gw, pba, pbb, ka, kb):
        V(lambda e: e.tensor_tensor(tmp3, xg3, bc3g(pba, gw), ALU.mult),
          reads=[xk, ka], writes=[tk])
        V(lambda e: e.tensor_tensor(out3, tmp3, bc3g(pbb, gw), ALU.subtract),
          reads=[tk, kb], writes=[ok])

    def proj(wv, wkkey, mvfull, mvkey, ntok, evac):
        ntg = max(1, ntok // 1024)
        tg_w = min(ntok, 1024)
        nq = max(1, tg_w // 512)
        qw = min(tg_w, 512)
        for oc in range(8):
            for tg in range(ntg):
                pbi = (oc * ntg + tg) % 2
                pbks = ["pb%da" % pbi] + (["pb%db" % pbi] if nq > 1 else [])
                for kc in range(8):
                    for q in range(nq):
                        mv = mvfull[:, kc * ntok + tg * tg_w + q * qw:
                                    kc * ntok + tg * tg_w + (q + 1) * qw]
                        pbdst = PB[pbi][:, q * 512: q * 512 + qw]
                        mk = mvkey(tg, q) if callable(mvkey) else mvkey
                        T(lambda e, kc=kc, oc=oc, mv=mv, pbdst=pbdst: e.matmul(
                            pbdst, wv[:, kc * 1024 + oc * 128: kc * 1024 + (oc + 1) * 128],
                            mv, start=(kc == 0), stop=(kc == 7)),
                          reads=[wkkey, mk], writes=[pbks[q]])
                evac(oc, tg, PB[pbi][:, 0:tg_w], pbks)

    def phase_attn_a():
        oXC = 0
        oW = oXC + 16384
        oQKV = oW + 49152
        oQv = oQKV + 12288
        oKv = oQv + 16384
        oVv = oKv + 16384
        oVT = oVv + 8512
        oBG = oVT + 8320
        oHK = oBG + 17408
        oXN = oHK + 4608
        oAS = oXN + 8192
        oSQ = oAS + 8192
        oOT = oSQ + 4096
        oPP = oOT + 4096
        oSC = oPP + 6144
        assert oSC + 4096 <= ARB, oSC

        pbPROJ = PB[3][:, 0:512]
        pbA = PB[3][:, 512:768]
        pbB = PB[3][:, 768:1024]
        pbM = PB[3][:, 512:1024]

        Vvv = vbf(oVv, 4256)
        OT = vbf(oOT, 2048)
        ff1 = vbf(oOT, 2048)

        def Qvp(c):
            return vbf(oQv + (c % 2) * 8192, 4096)

        def Kvp(c):
            return vbf(oKv + (c % 2) * 8192, 4096)

        def vtp(c):
            return vbf(oVT + (c % 2) * 4160, 2080)

        def bigp(c):
            return vbf(oBG + (c % 2) * 8704, 4352)

        def hkp(c):
            return vbf(oHK + (c % 2) * 2304, 1152)

        wslot = [vbf(oW + i * 16384, 8192) for i in range(3)]
        STREAM = [("aWq", 0), ("aWk", 0), ("aWv", 0),
                  ("aWq", 1), ("aWk", 1), ("aWv", 1)]
        for c in range(1, NCHUNK):
            STREAM += [("aWo", c - 1), ("aW1", c - 1), ("aW2", c - 1)]
            if c + 1 < NCHUNK:
                STREAM += [("aWq", c + 1), ("aWk", c + 1), ("aWv", c + 1)]
        STREAM += [("aWo", NCHUNK - 1), ("aW1", NCHUNK - 1), ("aW2", NCHUNK - 1)]
        jof = {key: j for j, key in enumerate(STREAM)}

        def wload(j):
            if j >= len(STREAM):
                return
            nm, c = STREAM[j]
            sl = j % 3
            srcw = ins[nm].rearrange("(kc p) d -> p kc d", p=128)
            p.dma("G", "wsl%d" % sl,
                  lambda e, srcw=srcw, sl=sl: e.dma_start(wslot[sl], srcw),
                  writes=["R_WSL%d" % sl])

        def projS(j, mvfull, mvkey, evac):
            sl = j % 3
            wv = wslot[sl]
            wk = "R_WSL%d" % sl
            for oc in range(8):
                for h in range(2):
                    for kc in range(4 * h, 4 * h + 4):
                        T(lambda e, kc=kc, oc=oc, wv=wv, mvfull=mvfull: e.matmul(
                            pbPROJ[:, 0:256],
                            wv[:, kc * 1024 + oc * 128: kc * 1024 + (oc + 1) * 128],
                            mvfull[:, kc * 256:(kc + 1) * 256],
                            start=(kc == 0), stop=(kc == 7)),
                          reads=[wk, mvkey], writes=["pb3p"])
                    if h == 0:
                        yield (lambda: None)
                yield (lambda oc=oc: evac(oc))
            yield (lambda j=j: wload(j + 3))

        QT, KT, VT = vbf(oQKV, 2048), vbf(oQKV + 4096, 2048), vbf(oQKV + 8192, 2048)
        qdst = {"aWq": 0, "aWk": 4096, "aWv": 8192}
        bnm = {"aWq": "abq", "aWk": "abk", "aWv": "abv"}
        qkey = {"aWq": "R_QT", "aWk": "R_KT", "aWv": "R_VT"}

        def mk_ev(nm):
            def ev(oc):
                dstv = vbf(oQKV + qdst[nm] + oc * 512, 256)
                V(lambda e, dstv=dstv, oc=oc, nm=nm: e.tensor_scalar(
                    dstv, pbPROJ[:, 0:256], bcol(bnm[nm], oc), None, ALU.add),
                  reads=["pb3p", "biases"], writes=[qkey[nm]])
            return ev

        def P1_thunks(c):
            ci = c % 2
            xc = v32(oXC + ci * 8192, 2048)
            xck = "a_xc%d" % ci
            xn = vbf(oXN + ci * 4096, 2048)
            xnk = "R_XN%d" % ci
            srcx = xT.rearrange("(kc p) t -> p kc t", p=128)[:, :, c * CH:(c + 1) * CH]
            yield lambda: p.dma("SY", "xin%d" % ci,
                                lambda e, srcx=srcx, xc=xc: e.dma_start(xc, srcx),
                                writes=[xck])
            xsq = vbf(oSQ, 2048)

            def _ln1(xc=xc, xck=xck, xn=xn, xnk=xnk):
                ln_stats(xc, xck, xsq, "R_SQ", CH, True, 1.0,
                         pbA, pbB, "pb3b", "pb3b", ones_cb)
                ln_apply(r3(xc, CH), xck, r3(xsq, CH), "R_SQ",
                         r3(xn, CH), xnk, CH, pbA, pbB, "pb3b", "pb3b")
            yield _ln1
            yield from projS(jof[("aWq", c)], xn, xnk, mk_ev("aWq"))
            yield from projS(jof[("aWk", c)], xn, xnk, mk_ev("aWk"))
            yield from projS(jof[("aWv", c)], xn, xnk, mk_ev("aWv"))
            par = c % 2
            Qv = Qvp(c)
            Kv = Kvp(c)
            kq, kq2 = "a_Qv%d" % par, "a_Qv2_%d" % par
            kk, kk2 = "a_Kv%d" % par, "a_Kv2_%d" % par
            for m in range(16):
                def _cp(m=m, Qv=Qv, Kv=Kv, kq=kq, kk=kk):
                    def sl(X, m=m):
                        return X[(m % 2) * 64:(m % 2) * 64 + 64,
                                 (m // 2) * 256:(m // 2) * 256 + 256]
                    V(lambda e, m=m, s=sl(QT), Qv=Qv: e.tensor_copy(Qv[0:64, m:4096:16], s),
                      reads=["R_QT"], writes=[kq])
                    V(lambda e, m=m, s=sl(VT): e.tensor_copy(Vvv[0:64, m:4096:16], s),
                      reads=["R_VT"], writes=["a_Vv"])
                    dk = free_dims(Kv[0:64, (127 - m):(128 - m)], [[128, 32], [-16, 8]])
                    sk3 = sl(KT).rearrange("p (a r) -> p a r", a=32)
                    V(lambda e, dk=dk, sk3=sk3: e.tensor_copy(dk, sk3),
                      reads=["R_KT"], writes=[kk])
                    dv = free_dims(Vvv[64:128, (127 - m):(128 - m)], [[128, 32], [-16, 8]])
                    sv3 = sl(VT).rearrange("p (a r) -> p a r", a=32)
                    V(lambda e, dv=dv, sv3=sv3: e.tensor_copy(dv, sv3),
                      reads=["R_VT"], writes=["a_Vvr"])
                yield _cp
            for h4 in range(2):
                yield lambda h4=h4, Qv=Qv, kq=kq, kq2=kq2: V(
                    lambda e, h4=h4, Qv=Qv: e.tensor_copy(
                        Qv[64:128, h4 * 2048:(h4 + 1) * 2048],
                        Qv[0:64, h4 * 2048:(h4 + 1) * 2048]),
                    reads=[kq], writes=[kq2])
                yield lambda h4=h4, Kv=Kv, kk=kk, kk2=kk2: V(
                    lambda e, h4=h4, Kv=Kv: e.tensor_copy(
                        Kv[64:128, h4 * 2048:(h4 + 1) * 2048],
                        Kv[0:64, h4 * 2048:(h4 + 1) * 2048]),
                    reads=[kk], writes=[kk2])

        def P2_thunks(c):
            ci = c % 2
            vt = vtp(c)
            big = bigp(c)
            hk = hkp(c)

            def _band():
                V(lambda e, c=c: e.tensor_copy(relc_sb[:], rel_sb[0:32, c:c + 1]),
                  reads=["rel_sb"], writes=["relc"])
                for pc in range(4):
                    T(lambda e, pc=pc: e.matmul(pbM[0:1, 0:320],
                                                relc_sb[:],
                                                oh_sb[:, pc * 320:(pc + 1) * 320],
                                                start=True, stop=True),
                      reads=["relc", "oh_sb"], writes=["pb3b"])
                    V(lambda e, pc=pc: e.tensor_copy(band_sb[0:1, pc * 320:(pc + 1) * 320],
                                                     pbM[0:1, 0:320]),
                      reads=["pb3b"], writes=["band_sb"])
                p.dma("SY", "bando%d" % ci,
                      lambda e, ci=ci: e.dma_start(bandh[ci:ci + 1, :], band_sb[0:1, :1280]),
                      reads=["band_sb"], writes=["bandh%d" % ci])
            yield _band
            pbT = PB[3][:, 256:512]
            for tt in range(0, 32, 4):
                def _tr(tt=tt):
                    for j in range(4):
                        t = tt + j
                        T(lambda e, t=t, j=j: e.matmul(pbT[:, j * 64:j * 64 + 64],
                                                       Vvv[64:128, t * 128:(t + 1) * 128],
                                                       i64_sb[64:128, :], start=True, stop=True,
                                                       tile_position=(64, 0)),
                          reads=["a_Vvr", "i64_sb"], writes=["pb3t"])
                    dst = free_dims(vt[:, tt * 65:tt * 65 + 1], [[65, 4], [1, 64]])
                    src4 = pbT.rearrange("p (a b) -> p a b", a=4)
                    V(lambda e, dst=dst, src4=src4: e.tensor_copy(dst, src4),
                      reads=["pb3t"], writes=["a_vtok%d" % ci])
                yield _tr

            def _scan():
                zcb = zcol64[:].broadcast_to([64, 4255])
                V(lambda e, zcb=zcb, big=big: e.tensor_tensor_scan(big[0:64, 97:4352], zcb,
                                                                   Vvv[0:64, 0:4255], 0.0,
                                                                   ALU.add, ALU.add),
                  reads=["zc", "a_Vv", "a_Vvp"], writes=["a_ps%d" % ci])
                T(lambda e, big=big: e.matmul(pbM[0:1, 0:64], big[0:64, 4192:4193],
                                              i64_sb[0:64, :], start=True, stop=True),
                  reads=["a_ps%d" % ci, "i64_sb"], writes=["pb3b"])
                V(lambda e, ci=ci: e.tensor_copy(totT[:, ci * 64:ci * 64 + 64], pbM[0:1, 0:64]),
                  reads=["pb3b"], writes=["totT%d" % ci])
                V(lambda e, c=c, ci=ci: e.tensor_scalar(w15row[:, ci * 512:ci * 512 + 512],
                                                        ones512[:],
                                                        wbc[0:1, 240 + c:241 + c], None, ALU.mult),
                  reads=["ones512", "wbc"], writes=["w15%d" % ci])
            yield _scan

            def _hk():
                hsrc = bass.AP(bandh_t, ci * 1280 + 1, [[1, 128], [1, 1152]])
                p.dma("SY", "hank",
                      lambda e, hsrc=hsrc, hk=hk: e.dma_start(hk, hsrc),
                      reads=["bandh%d" % ci], writes=["a_hk%d" % ci])
            yield _hk

        def A_emit(c, feeder):
            par = c % 2
            Qv = Qvp(c)
            Kv = Kvp(c)
            vt = vtp(c)
            big = bigp(c)
            hk = hkp(c)
            kq, kq2 = "a_Qv%d" % par, "a_Qv2_%d" % par
            kk, kk2 = "a_Kv%d" % par, "a_Kv2_%d" % par
            vtk = "a_vtok%d" % par
            psk = "a_ps%d" % par
            outk = "a_out%d" % par
            hkk = "a_hk%d" % par
            totc = totT[:, par * 64:par * 64 + 64]
            w15c = w15row[:, par * 512:par * 512 + 512]
            pend = [None]

            def flush_combine():
                if pend[0] is not None:
                    pend[0]()
                    pend[0] = None

            def draw(n=1):
                for _ in range(n):
                    th = next(feeder, None)
                    if th is not None:
                        th()

            def qg_params(g):
                pvh, bdh = ("pb2a", "pb2b") if g % 2 == 0 else ("pb2b", "pb2a")
                pvo = 0 if g % 2 == 0 else 512
                return pvh, bdh, pvo, 512 - pvo

            def emit_pv(g, pi):
                pvh, _, pvo, _ = qg_params(g)
                gp = g * 16 + pi
                pp = vbf(oPP + (gp % 3) * 2048, 1024)
                ppk = "a_pp%d" % (gp % 3)
                for j, tj in enumerate((2 * pi, 2 * pi + 1)):
                    T(lambda e, tj=tj, pp=pp, j=j, pvo=pvo: e.matmul(
                        PB[2][0:65, pvo:pvo + 512], vt[:, tj * 65:tj * 65 + 65],
                        pp[:, j * 512:(j + 1) * 512],
                        start=(tj == 0), stop=(tj == 31)),
                      reads=[vtk, ppk], writes=[pvh])

            def band_mms(g):
                q0 = g * 512
                _, bdh, _, bdo = qg_params(g)
                mms = []
                first = [True]
                for d0 in (128, 0, -128, -256, -384, -512):
                    k0 = q0 - d0
                    if k0 < 0 or k0 >= 4096:
                        continue
                    t = k0 // 128

                    def mk(t=t, d0=d0, bdo=bdo, bdh=bdh):
                        fb = first[0]
                        first[0] = False
                        def go(fb=fb):
                            T(lambda e, t=t, d0=d0, fb=fb, bdo=bdo: e.matmul(
                                PB[2][0:64, bdo:bdo + 512], vt[:, t * 65:t * 65 + 64],
                                hk[:, 512 + d0:512 + d0 + 512],
                                start=fb, stop=False),
                              reads=[vtk, hkk], writes=[bdh])
                        return go
                    mms.append(mk())

                def w15mm():
                    T(lambda e, bdo=bdo: e.matmul(PB[2][0:64, bdo:bdo + 512], totc, w15c,
                                                  start=False, stop=True),
                      reads=["totT%d" % par, "w15%d" % par], writes=[bdh])
                mms.append(w15mm)
                return mms

            def emit_o2e(g):
                _, bdh, _, bdo = qg_params(g)
                o2e = v32(oSC + 2048, 512)[0:64, :]
                V(lambda e, o2e=o2e, bdo=bdo: e.tensor_copy(o2e, PB[2][0:64, bdo:bdo + 512]),
                  reads=[bdh], writes=["R_O2"])

            def emit_recip(g):
                pvh, _, pvo, _ = qg_params(g)
                V(lambda e, pvo=pvo: e.reciprocal(recip_sb[:], PB[2][64:65, pvo:pvo + 512]),
                  reads=[pvh], writes=["recip"])

            def mk_combine(g):
                q0 = g * 512
                pvh, _, pvo, _ = qg_params(g)

                def go():
                    T(lambda e: e.matmul(pbM[0:64, 0:512], ones_r[0:1, 0:64],
                                         recip_sb[:], start=True, stop=True),
                      reads=["ones_r", "recip"], writes=["pb3b"])
                    V(lambda e: e.tensor_copy(rb_sb[:], pbM[0:64, 0:512]),
                      reads=["pb3b"], writes=["rb"])
                    o1 = v32(oSC, 512)[0:64, :]
                    o2 = v32(oSC + 2048, 512)[0:64, :]
                    V(lambda e, o1=o1: e.tensor_tensor(o1, PB[2][0:64, pvo:pvo + 512],
                                                       rb_sb[:], ALU.mult),
                      reads=[pvh, "rb"], writes=["R_O1"])
                    V(lambda e, o2=o2, o1=o1: e.tensor_tensor(
                        o2, o1, o2, ALU.add),
                      reads=["R_O1", "R_O2"], writes=["R_O2"])
                    V(lambda e, o1=o1: e.scalar_tensor_tensor(
                        o1, big[0:64, q0 + 6:q0 + 518],
                        wbc[0:64, 496 + c:497 + c], o2, ALU.mult, ALU.add),
                      reads=[psk, "wbc", "R_O2"], writes=["R_O1"])
                    V(lambda e, o1=o1: e.scalar_tensor_tensor(
                        big[64:128, q0:q0 + 512], big[0:64, q0 + 187:q0 + 699],
                        wbcn[0:64, 240 + c:241 + c], o1, ALU.mult, ALU.add),
                      reads=[psk, "wbcn", "R_O1"], writes=[outk])
                return go

            bandq = []

            for g in range(8):
                q0 = g * 512
                for pi in range(16):
                    t0, t1 = 2 * pi, 2 * pi + 1
                    sp = PB[pi % 2]
                    ska, skb = "pb%da" % (pi % 2), "pb%db" % (pi % 2)
                    gp = g * 16 + pi
                    T(lambda e, t0=t0, sp=sp, q0=q0: e.matmul(
                        sp[:, 0:512], Kv[0:64, t0 * 128:(t0 + 1) * 128],
                        Qv[0:64, q0:q0 + 512], start=True, stop=True,
                        tile_position=(0, 0)),
                      reads=[kk, kq], writes=[ska])
                    T(lambda e, t1=t1, sp=sp, q0=q0: e.matmul(
                        sp[:, 512:1024], Kv[64:128, t1 * 128:(t1 + 1) * 128],
                        Qv[64:128, q0:q0 + 512], start=True, stop=True,
                        tile_position=(64, 0)),
                      reads=[kk2, kq2], writes=[skb])
                    pp = vbf(oPP + (gp % 3) * 2048, 1024)
                    ppk = "a_pp%d" % (gp % 3)
                    S(lambda e, pp=pp, sp=sp: e.activation(
                        pp, sp[:, :], AF.Exp, bias=consts[:, 0:1], scale=consts[:, 1:2]),
                      reads=[ska, skb, "consts"], writes=[ppk])
                    if pi == 0:
                        if g > 0:
                            emit_pv(g - 1, 14)
                            for mm in bandq[3:]:
                                mm()
                            emit_o2e(g - 1)
                        else:
                            draw()
                    elif pi == 1:
                        if g > 0:
                            emit_pv(g - 1, 15)
                            emit_recip(g - 1)
                            pend[0] = mk_combine(g - 1)
                        else:
                            draw()
                    elif pi == 5:
                        emit_pv(g, 3)
                        flush_combine()
                    elif pi == 15:
                        emit_pv(g, 13)
                        bandq = band_mms(g)
                        for mm in bandq[:3]:
                            mm()
                    else:
                        emit_pv(g, pi - 2)
                        draw()
                        if pi in (7, 9, 11, 13):
                            draw()
            emit_pv(7, 14)
            for mm in bandq[3:]:
                mm()
            emit_o2e(7)
            emit_pv(7, 15)
            emit_recip(7)
            flush_combine()
            mk_combine(7)()

        def T_thunks(c):
            ci = c % 2
            xc = v32(oXC + ci * 8192, 2048)
            xck = "a_xc%d" % ci
            xn = vbf(oXN + ci * 4096, 2048)
            xnk = "R_XN%d" % ci
            as2 = v32(oAS, 2048)
            xsq2 = vbf(oSQ, 2048)
            ab = vbf(oXN + ci * 4096, 2048)
            big = bigp(c)
            outk = "a_out%d" % ci
            for m in range(16):
                def _uv(m=m):
                    dsto = OT[(m % 2) * 64:(m % 2) * 64 + 64,
                              (m // 2) * 256:(m // 2) * 256 + 256]
                    V(lambda e, dsto=dsto, m=m, big=big: e.tensor_copy(
                        dsto, big[64:128, m:4096:16]),
                      reads=[outk], writes=["R_OT"])
                yield _uv

            def evo(oc):
                V(lambda e, oc=oc: e.scalar_tensor_tensor(
                    as2[:, oc * 256:(oc + 1) * 256], pbPROJ[:, 0:256], bcol("abo", oc),
                    ab[:, oc * 256:(oc + 1) * 256], ALU.add, ALU.add),
                  reads=["pb3p", "biases", xnk], writes=["R_AS"])
            yield from projS(jof[("aWo", c)], OT, "R_OT", evo)

            def _ln2(ab=ab, xnk=xnk):
                ln_stats(as2, "R_AS", xsq2, "R_SQ", CH, False, 1.0,
                         pbA, pbB, "pb3b", "pb3b", ones_cb)
                ln_apply(r3(as2, CH), "R_AS", r3(xsq2, CH), "R_SQ",
                         r3(ab, CH), xnk, CH, pbA, pbB, "pb3b", "pb3b")
            yield _ln2

            def ev1(oc):
                V(lambda e, oc=oc: e.tensor_scalar(
                    ff1[:, oc * 256:(oc + 1) * 256], pbPROJ[:, 0:256],
                    bcol("ab1", oc), None, ALU.add),
                  reads=["pb3p", "biases"], writes=["R_OT"])
            yield from projS(jof[("aW1", c)], ab, xnk, ev1)

            def ev2(oc):
                V(lambda e, oc=oc: e.scalar_tensor_tensor(
                    as2[:, oc * 256:(oc + 1) * 256], pbPROJ[:, 0:256], bcol("ab2", oc),
                    ab[:, oc * 256:(oc + 1) * 256], ALU.add, ALU.add),
                  reads=["pb3p", "biases", xnk], writes=["R_AS"])
            yield from projS(jof[("aW2", c)], ff1, "R_OT", ev2)

            def _ln3():
                ln_stats(as2, "R_AS", xsq2, "R_SQ", CH, False, 1.0,
                         pbA, pbB, "pb3b", "pb3b", ones_cb)
                as23 = r3(as2, CH)
                xc3 = r3(xc, CH)
                V(lambda e: e.tensor_tensor(as23, as23, bc3g(pbA, CH), ALU.mult),
                  reads=["R_AS", "pb3b"], writes=["R_AS"])
                V(lambda e: e.tensor_tensor(as23, as23, bc3g(pbB, CH), ALU.subtract),
                  reads=["R_AS", "pb3b"], writes=["R_AS"])
                V(lambda e: e.tensor_tensor(as23, as23, xc3, ALU.add),
                  reads=["R_AS", xck], writes=["R_AS"])
                dstx = xbuf.rearrange("(kc p) t -> p kc t", p=128)[:, :, c * CH:(c + 1) * CH]
                p.dma("SY", "xout",
                      lambda e, dstx=dstx, as2=as2: e.dma_start(dstx, as2),
                      reads=["R_AS"], writes=[])
            yield _ln3

        def drain(gen):
            for th in gen:
                th()

        import itertools
        for ci2 in range(2):
            V(lambda e, ci2=ci2: e.memset(
                free_dims(vbf(oVT + ci2 * 4160, 2080)[:, 64:65], [[65, 32]]), 1.0),
              writes=["a_vtok%d" % ci2])
            V(lambda e, ci2=ci2: e.memset(vbf(oBG + ci2 * 8704, 4352)[0:64, 0:97], 0.0),
              writes=["a_ps%d" % ci2])
        V(lambda e: e.memset(Vvv[0:64, 4096:4256], 0.0), writes=["a_Vvp"])
        for j in range(3):
            wload(j)
        drain(P1_thunks(0))
        drain(P2_thunks(0))
        carry = iter(())
        for c in range(NCHUNK):
            feeder = itertools.chain(
                carry, P1_thunks(c + 1) if c + 1 < NCHUNK else iter(()))
            A_emit(c, feeder)
            drain(feeder)
            if c + 1 < NCHUNK:
                drain(P2_thunks(c + 1))
            carry = T_thunks(c)
        drain(carry)

    def phase_ffn(last):
        oXG = 0
        oSQ = oXG + 32768
        oXN = oSQ + 8192
        oH = oXN + 32768
        oWG = oH + 98304
        oGS = oWG + 8192
        assert oGS + 4096 <= ARB, oGS
        oWD = 0
        oXS = oWD + 12288
        src_in = xbuf
        dst_out = yT if last else xbuf
        xnf = vbf(oXN, 16384)
        for g in range(4):
            xg = v32(oXG + (g % 2) * 16384, 4096)
            xgk = "R_FXG%d" % (g % 2)
            srcx = src_in.rearrange("(kc p) t -> p kc t", p=128)[:, :, g * 512:(g + 1) * 512]
            p.dma("SY", "fxin%d" % (g % 2), lambda e, srcx=srcx, xg=xg: e.dma_start(xg, srcx),
                  writes=[xgk])
            xsq = vbf(oSQ, 4096)
            ln_stats(xg, xgk, xsq, "R_FSQ", 512, True, 1.0 / 127.0,
                     PB[3][:, 0:512], PB[3][:, 512:1024], "pb3a", "pb3b", ones_cb)
            xn3 = free_dims(xnf[0:128, g * 512:g * 512 + 1], [[2048, 8], [1, 512]])
            ln_apply(r3(xg, 512), xgk, r3(xsq, 512), "R_FSQ", xn3, "f_xn%d" % g, 512,
                     PB[3][:, 0:512], PB[3][:, 512:1024], "pb3a", "pb3b")

        H = vbf(oH, 49152)
        for hc in range(NHC):
            hi = hc % 2
            wgv = vbf(oWG + hi * 2048, 1024)
            wuv = vbf(oWG + 4096 + hi * 2048, 1024)
            sg = ins["wg"][hc, :, :].rearrange("(kc p) h -> p kc h", p=128)
            su = ins["wu"][hc, :, :].rearrange("(kc p) h -> p kc h", p=128)
            p.dma("G", "fwg%d" % hi, lambda e, sg=sg, wgv=wgv: e.dma_start(wgv, sg),
                  writes=["f_wg%d" % hi])
            p.dma("G", "fwu%d" % hi, lambda e, su=su, wuv=wuv: e.dma_start(wuv, su),
                  writes=["f_wu%d" % hi])
            for seg in range(4):
                sb = seg % 2
                gk = "pb0a" if sb == 0 else "pb0b"
                uk = "pb1a" if sb == 0 else "pb1b"
                gdep = "f_xn%d" % seg
                for kc in range(8):
                    mv = xnf[:, kc * 2048 + seg * 512: kc * 2048 + (seg + 1) * 512]
                    T(lambda e, mv=mv, wgv=wgv, kc=kc, sb=sb: e.matmul(
                        PB[0][:, sb * 512:(sb + 1) * 512],
                        wgv[:, kc * 128:(kc + 1) * 128], mv,
                        start=(kc == 0), stop=(kc == 7)),
                      reads=["f_wg%d" % hi, gdep], writes=[gk])
                for kc in range(8):
                    mv = xnf[:, kc * 2048 + seg * 512: kc * 2048 + (seg + 1) * 512]
                    T(lambda e, mv=mv, wuv=wuv, kc=kc, sb=sb: e.matmul(
                        PB[1][:, sb * 512:(sb + 1) * 512],
                        wuv[:, kc * 128:(kc + 1) * 128], mv,
                        start=(kc == 0), stop=(kc == 7)),
                      reads=["f_wu%d" % hi, gdep], writes=[uk])
                gs = vbf(oGS + sb * 2048, 512)
                S(lambda e, gs=gs, sb=sb: e.activation(gs, PB[0][:, sb * 512:(sb + 1) * 512],
                                                       AF.Silu),
                  reads=[gk], writes=["f_gs%d" % sb])
                hdst = H[:, hc * 2048 + seg * 512: hc * 2048 + (seg + 1) * 512]
                V(lambda e, hdst=hdst, gs=gs, sb=sb: e.tensor_tensor(
                    hdst, gs, PB[1][:, sb * 512:(sb + 1) * 512], ALU.mult),
                  reads=["f_gs%d" % sb, uk], writes=["f_H%d" % hc])

        for oc in range(8):
            oi = oc % 2
            wdv = vbf(oWD + oi * 6144, 3072)
            sd = ins["wd"][oc, :, :].rearrange("(kt p) d -> p kt d", p=128)
            p.dma("G", "fwd%d" % oi, lambda e, sd=sd, wdv=wdv: e.dma_start(wdv, sd),
                  writes=["f_wd%d" % oi, "R_FXG0"])
            for seg in range(4):
                sb = seg % 2
                dk = "pb2a" if sb == 0 else "pb2b"
                for kt in range(NHC):
                    mv = H[:, kt * 2048 + seg * 512: kt * 2048 + (seg + 1) * 512]
                    T(lambda e, kt=kt, mv=mv, wdv=wdv, sb=sb: e.matmul(
                        PB[2][:, sb * 512:(sb + 1) * 512],
                        wdv[:, kt * 128:(kt + 1) * 128], mv,
                        start=(kt == 0), stop=(kt == NHC - 1)),
                      reads=["f_wd%d" % oi, "f_H%d" % kt], writes=[dk])
                xs = v32(oXS + seg * 2048, 512)
                xsk = "R_FXG0" if seg < 2 else "R_FXG1"
                srcx = src_in[oc * 128:(oc + 1) * 128, seg * 512:(seg + 1) * 512]
                p.dma("SY", "fx2%d" % seg,
                      lambda e, srcx=srcx, xs=xs: e.dma_start(xs, srcx),
                      writes=[xsk])
                xo = v32(oXG + 28672 + (seg % 2) * 2048, 512)
                xok = "R_FXG1"
                V(lambda e, xo=xo, xs=xs, sb=sb: e.scalar_tensor_tensor(
                    xo, PB[2][:, sb * 512:(sb + 1) * 512], 1.0 / 127.0, xs,
                    ALU.mult, ALU.add),
                  reads=[dk, xsk], writes=[xok])
                dstx = dst_out[oc * 128:(oc + 1) * 128, seg * 512:(seg + 1) * 512]
                p.dma("SY", "fxo%d" % (seg % 2),
                      lambda e, dstx=dstx, xo=xo: e.dma_start(dstx, xo),
                      reads=[xok], writes=[])

    def phase_attn_b():
        oXG = 0
        oSQ = oXG + 16384
        oCS = oSQ + 16384
        oLN = oCS + 12288
        oRX = oLN + 4096
        oQT = oRX + 32768
        oST = oQT + 32768
        oW = oST + 17408
        oPP = 0
        oXS = 8192
        oXO2 = oXS + 8192
        assert oW + 49152 <= ARB

        rxn = vbf(oRX, 16384)
        sbA = v32(oLN, 512)
        sbB = v32(oLN + 2048, 512)

        def xg_slot(g):
            return v32(oXG if g % 2 == 0 else oQT, 4096)

        def xg_pref(g):
            xg = xg_slot(g)
            srcg = xbuf.rearrange("(kc p) t -> p kc t", p=128)[:, :, g * 512:(g + 1) * 512]
            p.dma("SY", "bxin%d" % (g % 2),
                  lambda e, srcg=srcg, xg=xg: e.dma_start(xg, srcg),
                  writes=["R_BXG%d" % (g % 2)])

        def lnb(g):
            gi = g % 2
            xg = xg_slot(g)
            xgk = "R_BXG%d" % gi
            xsq = v32(oSQ, 4096)
            ln_stats(xg, xgk, xsq, "R_BSQ", 512, True, 1.0,
                     PB[3][:, 0:512], PB[3][:, 512:1024], "pb3a", "pb3b", ones_c)
            V(lambda e: e.tensor_copy(sbA[:], PB[3][:, 0:512]),
              reads=["pb3a"], writes=["b_sbA"])
            V(lambda e: e.tensor_copy(sbB[:], PB[3][:, 512:1024]),
              reads=["pb3b"], writes=["b_sbB"])
            xn = v32(oSQ, 4096)
            ln_apply(r3(xg, 512), xgk, r3(xg, 512), xgk, r3(xn, 512), "R_BSQ", 512,
                     sbA, sbB, "b_sbA", "b_sbB")
            if g + 2 < 4:
                xg_pref(g + 2)
            for cc in range(4):
                ci2 = cc % 2
                EN = V if ci2 == 0 else G
                csrc = ins["cosT"][cc * 128:(cc + 1) * 128, g * 512:(g + 1) * 512]
                ssrc = ins["sinT"][cc * 128:(cc + 1) * 128, g * 512:(g + 1) * 512]
                cosb = v32(oCS + ci2 * 2048, 512)
                sinb = v32(oCS + 6144 + ci2 * 2048, 512)
                p.dma("SY", "bcs%d" % ci2,
                      lambda e, csrc=csrc, cosb=cosb: e.dma_start(cosb, csrc),
                      writes=["b_cos%d" % ci2])
                p.dma("SY", "bsn%d" % ci2,
                      lambda e, ssrc=ssrc, sinb=sinb: e.dma_start(sinb, ssrc),
                      writes=["b_sin%d" % ci2])
                x1 = xn[:, cc * 512:(cc + 1) * 512]
                x2 = xn[:, (cc + 4) * 512:(cc + 5) * 512]
                t1 = v32((oCS + 4096) if ci2 == 0 else (oQT + 16384), 512)
                t2 = v32((oCS + 10240) if ci2 == 0 else (oQT + 18432), 512)
                r1 = rxn[:, cc * 2048 + g * 512: cc * 2048 + (g + 1) * 512]
                r2 = rxn[:, (cc + 4) * 2048 + g * 512: (cc + 4) * 2048 + (g + 1) * 512]
                EN(lambda e, t1=t1, x1=x1, cosb=cosb: e.tensor_tensor(t1, x1, cosb, ALU.mult),
                   reads=["R_BSQ", "b_cos%d" % ci2], writes=["b_t1%d" % ci2])
                EN(lambda e, t2=t2, x2=x2, sinb=sinb: e.tensor_tensor(t2, x2, sinb, ALU.mult),
                   reads=["R_BSQ", "b_sin%d" % ci2], writes=["b_t2%d" % ci2])
                EN(lambda e, r1=r1, t1=t1, t2=t2: e.tensor_tensor(r1, t1, t2, ALU.subtract),
                   reads=["b_t1%d" % ci2, "b_t2%d" % ci2], writes=["R_RX", "R_RXg%d" % g])
                EN(lambda e, t1=t1, x1=x1, sinb=sinb: e.tensor_tensor(t1, x1, sinb, ALU.mult),
                   reads=["R_BSQ", "b_sin%d" % ci2], writes=["b_t1%d" % ci2])
                EN(lambda e, t2=t2, x2=x2, cosb=cosb: e.tensor_tensor(t2, x2, cosb, ALU.mult),
                   reads=["R_BSQ", "b_cos%d" % ci2], writes=["b_t2%d" % ci2])
                EN(lambda e, r2=r2, t1=t1, t2=t2: e.tensor_tensor(r2, t1, t2, ALU.add),
                   reads=["b_t1%d" % ci2, "b_t2%d" % ci2], writes=["R_RX", "R_RXg%d" % g])

        wslots = [vbf(oW, 8192), vbf(oW + 16384, 8192)]
        QTb = vbf(oQT, 16384)
        stg = [vbf(oST + i * 4352, 2048) for i in range(4)]

        def proj_b(nm, sl, tg):
            bias_nm = {"bWq": "bbq", "bWk": "bbk", "bWv": "bbv", "bWo": "bbo"}[nm]
            ccdst = {"bWk": cc_in_k, "bWv": cc_in_v}.get(nm)
            for oc in range(8):
                pbi = (oc + tg) % 2
                pbks = ["pb%da" % pbi, "pb%db" % pbi]
                for kc in range(8):
                    for q in range(2):
                        mv = rxn[:, kc * 2048 + tg * 1024 + q * 512:
                                 kc * 2048 + tg * 1024 + (q + 1) * 512]
                        T(lambda e, kc=kc, oc=oc, mv=mv, q=q, pbi=pbi, sl=sl: e.matmul(
                            PB[pbi][:, q * 512:(q + 1) * 512],
                            wslots[sl][:, kc * 1024 + oc * 128: kc * 1024 + (oc + 1) * 128],
                            mv, start=(kc == 0), stop=(kc == 7)),
                          reads=["R_BW%d" % sl, "R_RXg%d" % (tg * 2 + q)],
                          writes=[pbks[q]])
                if nm == "bWq":
                    dst = QTb[:, oc * 2048 + tg * 1024: oc * 2048 + (tg + 1) * 1024]
                    S(lambda e, dst=dst, oc=oc, pbi=pbi: e.activation(
                        dst, PB[pbi][:, 0:1024], AF.Identity,
                        bias=bcol("bbq", oc), scale=consts[:, 1:2]),
                      reads=pbks + ["biases", "consts"], writes=["b_QT"])
                else:
                    slot = (0 if nm == "bWk" else 2) + (oc + tg) % 2
                    sgb = stg[slot][:, 0:1024]
                    skey = "R_VT%d" % slot
                    S(lambda e, sgb=sgb, oc=oc, pbi=pbi, bias_nm=bias_nm: e.activation(
                        sgb, PB[pbi][:, 0:1024], AF.Identity,
                        bias=bcol(bias_nm, oc), scale=consts[:, 1:2]),
                      reads=pbks + ["biases", "consts"], writes=[skey])
                    dd = ccdst[oc * 128:(oc + 1) * 128, tg * 1024:(tg + 1) * 1024]
                    p.dma("G", "bstg%d" % slot,
                          lambda e, dd=dd, sgb=sgb: e.dma_start(dd, sgb),
                          reads=[skey], writes=["cc_in" + nm])

        srcw = ins["bWk"].rearrange("(kc p) d -> p kc d", p=128)
        p.dma("G", "bw0", lambda e, srcw=srcw: e.dma_start(wslots[0], srcw),
              writes=["R_BW0"])
        srcw = ins["bWv"].rearrange("(kc p) d -> p kc d", p=128)
        p.dma("G", "bw1", lambda e, srcw=srcw: e.dma_start(wslots[1], srcw),
              writes=["R_BW1"])
        xg_pref(0)
        xg_pref(1)
        lnb(0)
        lnb(1)
        proj_b("bWk", 0, 0)
        lnb(2)
        lnb(3)
        proj_b("bWk", 0, 1)
        p.async_op("G", "cck", lambda e: e.collective_compute(
            "AllGather", ALU.bypass, [[0, 1], [2, 3], [4, 5], [6, 7]],
            [cc_in_k.opt()], [cc_out_k.opt()]),
            reads=["cc_inbWk", "R_VT0", "R_VT1"], writes=["cc_k"])
        proj_b("bWv", 1, 0)
        proj_b("bWv", 1, 1)
        p.async_op("G", "ccv", lambda e: e.collective_compute(
            "AllGather", ALU.bypass, [[0, 1], [2, 3], [4, 5], [6, 7]],
            [cc_in_v.opt()], [cc_out_v.opt()]),
            reads=["cc_inbWv", "R_VT2", "R_VT3"], writes=["cc_v"])
        srcw = ins["bWq"].rearrange("(kc p) d -> p kc d", p=128)
        p.dma("G", "bw0b", lambda e, srcw=srcw: e.dma_start(wslots[0], srcw),
              writes=["R_BW0"])
        proj_b("bWq", 0, 0)
        proj_b("bWq", 0, 1)

        woview = wslots[0]
        srcw = ins["bWo"].rearrange("(kc p) d -> p kc d", p=128)
        p.dma("G", "bwo", lambda e, srcw=srcw: e.dma_start(woview, srcw),
              writes=["R_BW0"])

        OTb = vbf(oW + 16384, 16384)
        kp = [vbf(oRX + i * 8192, 4096) for i in range(2)]
        vp = [vbf(oRX + 16384 + i * 8192, 4096) for i in range(2)]
        vtb = [vbf(oST + i * 4352, 2080) for i in range(4)]

        def stage_thunks(gp):
            h0, h1 = 2 * gp, 2 * gp + 1
            pi = gp % 2
            kpk, vpk = "R_RXK%d" % pi, "R_RXV%d" % pi
            for (hh, rows) in ((h0, (0, 64)), (h1, (64, 128))):
                for (hs, cols) in ((0, (0, 2048)), (1, (2048, 4096))):
                    def _ld(hh=hh, rows=rows, hs=hs, cols=cols):
                        srck = cc_out_k[hs * D + hh * 64: hs * D + hh * 64 + 64, :]
                        dstk = kp[pi][rows[0]:rows[1], cols[0]:cols[1]]
                        p.dma("SY", "bkp%d" % pi,
                              lambda e, dstk=dstk, srck=srck: e.dma_start(dstk, srck),
                              reads=["cc_k"], writes=[kpk] + (["R_RX", "R_RXg0", "R_RXg1",
                                                               "R_RXg2", "R_RXg3"] if gp < 2 else []))
                        srcv = cc_out_v[hs * D + hh * 64: hs * D + hh * 64 + 64, :]
                        dstv = vp[pi][rows[0]:rows[1], cols[0]:cols[1]]
                        p.dma("SY", "bvp%d" % pi,
                              lambda e, dstv=dstv, srcv=srcv: e.dma_start(dstv, srcv),
                              reads=["cc_v"], writes=[vpk] + (["R_RX", "R_RXg0", "R_RXg1",
                                                               "R_RXg2", "R_RXg3"] if gp < 2 else []))
                    yield _ld
            s0, s1 = 2 * pi, 2 * pi + 1
            k0n, k1n = "R_VT%d" % s0, "R_VT%d" % s1
            def _ms():
                V(lambda e: e.memset(free_dims(vtb[s0][:, 64:65], [[65, 32]]), 1.0),
                  writes=[k0n])
                V(lambda e: e.memset(free_dims(vtb[s1][:, 64:65], [[65, 32]]), 1.0),
                  writes=[k1n])
            yield _ms
            for tt in range(0, 32, 4):
                def _tr(tt=tt):
                    for t in range(tt, tt + 4):
                        T(lambda e, t=t: e.matmul(
                            PB[3][:, 0:64], vp[pi][0:64, t * 128:(t + 1) * 128],
                            i64_sb[0:64, :], start=True, stop=True,
                            tile_position=(0, 0)),
                          reads=[vpk, "i64_sb"], writes=["pb3a"])
                        V(lambda e, t=t: e.tensor_copy(vtb[s0][:, t * 65:t * 65 + 64],
                                                       PB[3][:, 0:64]),
                          reads=["pb3a"], writes=[k0n])
                        T(lambda e, t=t: e.matmul(
                            PB[3][:, 512:576], vp[pi][64:128, t * 128:(t + 1) * 128],
                            i64_sb[64:128, :], start=True, stop=True,
                            tile_position=(64, 0)),
                          reads=[vpk, "i64_sb"], writes=["pb3b"])
                        V(lambda e, t=t: e.tensor_copy(vtb[s1][:, t * 65:t * 65 + 64],
                                                       PB[3][:, 512:576]),
                          reads=["pb3b"], writes=[k1n])
                yield _tr

        def A_b(gp, feeder, carry):
            h0, h1 = 2 * gp, 2 * gp + 1
            pi = gp % 2
            kpk = "R_RXK%d" % pi
            s0, s1 = 2 * pi, 2 * pi + 1
            k0n, k1n = "R_VT%d" % s0, "R_VT%d" % s1

            def draw():
                th = next(feeder, None)
                if th is not None:
                    th()

            for g in range(4):
                q0g = g * 512
                q_h0 = QTb[0:64, (h0 // 2) * 2048 + q0g: (h0 // 2) * 2048 + q0g + 512]
                q_h1 = QTb[64:128, (h1 // 2) * 2048 + q0g: (h1 // 2) * 2048 + q0g + 512]

                def emit_pvb(t, s0=s0, s1=s1, k0n=k0n, k1n=k1n):
                    pp = vbf(oPP + (t % 4) * 2048, 1024)
                    ppk = "b_pp%d" % (t % 4)
                    T(lambda e, t=t, pp=pp, s0=s0: e.matmul(
                        PB[2][0:65, 0:512], vtb[s0][:, t * 65:t * 65 + 65], pp[:, 0:512],
                        start=(t == 0), stop=(t == 31)),
                      reads=[k0n, ppk], writes=["pb2a"])
                    T(lambda e, t=t, pp=pp, s1=s1: e.matmul(
                        PB[2][0:65, 512:1024], vtb[s1][:, t * 65:t * 65 + 65], pp[:, 512:1024],
                        start=(t == 0), stop=(t == 31)),
                      reads=[k1n, ppk], writes=["pb2b"])
                for t in range(32):
                    sp = PB[t % 2]
                    ska, skb = "pb%da" % (t % 2), "pb%db" % (t % 2)
                    T(lambda e, t=t, sp=sp, q_h0=q_h0, pi=pi: e.matmul(
                        sp[:, 0:512], kp[pi][0:64, t * 128:(t + 1) * 128], q_h0,
                        start=True, stop=True, tile_position=(0, 0)),
                      reads=[kpk, "b_QT"], writes=[ska])
                    T(lambda e, t=t, sp=sp, q_h1=q_h1, pi=pi: e.matmul(
                        sp[:, 512:1024], kp[pi][64:128, t * 128:(t + 1) * 128], q_h1,
                        start=True, stop=True, tile_position=(64, 0)),
                      reads=[kpk, "b_QT"], writes=[skb])
                    pp = vbf(oPP + (t % 4) * 2048, 1024)
                    ppk = "b_pp%d" % (t % 4)
                    S(lambda e, pp=pp, sp=sp: e.activation(
                        pp, sp[:, :], AF.Exp, bias=consts[:, 2:3], scale=consts[:, 3:4]),
                      reads=[ska, skb, "consts"], writes=[ppk])
                    if t < 4:
                        if carry is not None:
                            for op in carry[t]:
                                op()
                        else:
                            draw()
                    elif t < 8:
                        draw()
                    elif t <= 14:
                        emit_pvb(2 * t - 16)
                        emit_pvb(2 * t - 15)
                    else:
                        emit_pvb(t - 1)

                def mk_carry(g=g, q0g=q0g, h0=h0, h1=h1, emit_pvb=emit_pvb):
                    def pv31():
                        emit_pvb(31)

                    def rc(hi2, dstr, dk):
                        V(lambda e, hi2=hi2, dstr=dstr: e.reciprocal(
                            dstr[0:1, 0:512], PB[2][64:65, hi2 * 512: hi2 * 512 + 512]),
                          reads=["pb2a" if hi2 == 0 else "pb2b"], writes=[dk])

                    def cm(hi2, srcr, dk):
                        hh = h0 if hi2 == 0 else h1
                        pvk = "pb2a" if hi2 == 0 else "pb2b"
                        T(lambda e, srcr=srcr: e.matmul(PB[3][0:64, 512:1024], ones_r[0:1, 0:64],
                                                        srcr[0:1, 0:512], start=True, stop=True),
                          reads=["ones_r", dk], writes=["pb3b"])
                        V(lambda e: e.tensor_copy(rb_sb[:], PB[3][0:64, 512:1024]),
                          reads=["pb3b"], writes=["rb"])
                        dsto = OTb[(hh % 2) * 64:(hh % 2) * 64 + 64,
                                   (hh // 2) * 2048 + q0g: (hh // 2) * 2048 + q0g + 512]
                        V(lambda e, dsto=dsto, hi2=hi2: e.tensor_tensor(
                            dsto, PB[2][0:64, hi2 * 512: hi2 * 512 + 512], rb_sb[:], ALU.mult),
                          reads=[pvk, "rb"], writes=["R_BW1"])
                    return [
                        [pv31, lambda: rc(0, recip_sb, "recipA")],
                        [lambda: rc(1, rowA, "recipB")],
                        [lambda: cm(0, recip_sb, "recipA")],
                        [lambda: cm(1, rowA, "recipB")],
                    ]
                carry = mk_carry()
            return carry

        for th in stage_thunks(0):
            th()
        carry = None
        for gp in range(8):
            feeder = stage_thunks(gp + 1) if gp < 7 else iter(())
            carry = A_b(gp, feeder, carry)
            for th in feeder:
                th()
        for ops in carry:
            for op in ops:
                op()

        def evwo(oc, tg, pb, pbks):
            oi = (oc * 2 + tg) % 2
            xs = v32(oXS + oi * 4096, 1024)
            xsk = "R_BXG"
            srcx = xbuf[oc * 128:(oc + 1) * 128, tg * 1024:(tg + 1) * 1024]
            p.dma("SY", "bx2%d" % oi,
                  lambda e, srcx=srcx, xs=xs: e.dma_start(xs, srcx),
                  writes=[xsk, "b_pp2", "b_pp3"])
            xo = v32(oXO2 + oi * 4096, 1024)
            xok = "R_BSQ"
            V(lambda e, xo=xo, xs=xs, pb=pb, oc=oc: e.scalar_tensor_tensor(
                xo, pb, bcol("bbo", oc), xs, ALU.add, ALU.add),
              reads=list(pbks) + ["biases", xsk], writes=["R_BSQ", "R_BXG"])
            dstx = xbuf[oc * 128:(oc + 1) * 128, tg * 1024:(tg + 1) * 1024]
            p.dma("SY", "bxo%d" % oi,
                  lambda e, dstx=dstx, xo=xo: e.dma_start(dstx, xo),
                  reads=[xok], writes=[])
        proj(woview, "R_BW0", OTb, "R_BW1", TOK, evwo)

    def dbg_tap(i):
        if not _DEBUG_TAPS:
            return
        for kc in range(8):
            t = v32(0, 2048)
            p.dma("SY", "dbgl", lambda e, kc=kc, t=t: e.dma_start(
                t, xbuf[kc * 128:(kc + 1) * 128, :]), writes=["dbgt"])
            p.dma("SY", "dbgs", lambda e, kc=kc, i=i, t=t: e.dma_start(
                dbg_taps[i][kc * 128:(kc + 1) * 128, :], t), reads=["dbgt"], writes=[])
        p.barrier()

    phase_attn_a()
    p.barrier()
    dbg_tap(0)
    phase_ffn(last=False)
    p.barrier()
    dbg_tap(1)
    phase_attn_b()
    p.barrier()
    dbg_tap(2)
    phase_ffn(last=True)
    p.barrier()

    p.finalize()
    with nc.Block() as block:
        @block.tensor
        def _(eng):
            p.emit("T", eng, sems)

        @block.vector
        def _(eng):
            p.emit("V", eng, sems)

        @block.scalar
        def _(eng):
            p.emit("S", eng, sems)

        @block.gpsimd
        def _(eng):
            p.emit("G", eng, sems)

        @block.sync
        def _(eng):
            p.emit("SY", eng, sems)
    stack.close()
    return nc


_NC_CACHE = None


def kernel(L0_attn_a_x, L0_attn_a_Wq, L0_attn_a_bq, L0_attn_a_Wk, L0_attn_a_bk,
           L0_attn_a_Wv, L0_attn_a_bv, L0_attn_a_Wo, L0_attn_a_bo,
           L0_attn_a_rel_bias, L0_attn_a_W1, L0_attn_a_b1, L0_attn_a_W2,
           L0_attn_a_b2, L0_attn_b_Wq, L0_attn_b_bq, L0_attn_b_Wk, L0_attn_b_bk,
           L0_attn_b_Wv, L0_attn_b_bv, L0_attn_b_Wo, L0_attn_b_bo,
           L0_ffn_w_gate, L0_ffn_w_up, L0_ffn_w_down):
    global _NC_CACHE
    oh, cosT, sinT, i64 = _host_consts()

    def wT(w):
        return np.ascontiguousarray(np.asarray(w, np.float32).T)

    def bR(b):
        return np.ascontiguousarray(np.asarray(b, np.float32).reshape(8, 128).T)

    x = np.asarray(L0_attn_a_x, np.float32)
    common = {
        "aWq": wT(L0_attn_a_Wq), "aWk": wT(L0_attn_a_Wk), "aWv": wT(L0_attn_a_Wv),
        "aWo": wT(L0_attn_a_Wo), "aW1": wT(L0_attn_a_W1), "aW2": wT(L0_attn_a_W2),
        "bWq": wT(L0_attn_b_Wq), "bWk": wT(L0_attn_b_Wk), "bWv": wT(L0_attn_b_Wv),
        "bWo": wT(L0_attn_b_Wo),
        "abq": bR(L0_attn_a_bq), "abk": bR(L0_attn_a_bk), "abv": bR(L0_attn_a_bv),
        "abo": bR(L0_attn_a_bo), "ab1": bR(L0_attn_a_b1), "ab2": bR(L0_attn_a_b2),
        "bbq": bR(L0_attn_b_bq), "bbk": bR(L0_attn_b_bk), "bbv": bR(L0_attn_b_bv),
        "bbo": bR(L0_attn_b_bo),
        "onehot": oh, "i64": i64,
        "wg": np.ascontiguousarray(
            np.asarray(L0_ffn_w_gate, np.int32).T.reshape(D, NHC, 128).transpose(1, 0, 2)),
        "wu": np.ascontiguousarray(
            np.asarray(L0_ffn_w_up, np.int32).T.reshape(D, NHC, 128).transpose(1, 0, 2)),
        "wd": np.ascontiguousarray(
            np.asarray(L0_ffn_w_down, np.int32).T.reshape(HID, 8, 128).transpose(1, 0, 2)),
    }
    in_maps = []
    for core in range(8):
        b, hh = core // 2, core % 2
        m = dict(common)
        m["xT"] = np.ascontiguousarray(x[b, hh * TOK:(hh + 1) * TOK, :].T)
        relroll = np.ascontiguousarray(
            np.roll(np.asarray(L0_attn_a_rel_bias, np.float32), -8 * hh, axis=1))
        m["relT"] = relroll
        m["relflat"] = np.ascontiguousarray(relroll.reshape(1, 512))
        m["cosT"] = np.ascontiguousarray(cosT[:, hh * TOK:(hh + 1) * TOK])
        m["sinT"] = np.ascontiguousarray(sinT[:, hh * TOK:(hh + 1) * TOK])
        in_maps.append(m)

    if _NC_CACHE is None:
        _NC_CACHE = build_program()
    res = run_bass_kernel_spmd(_NC_CACHE, in_maps, core_ids=list(range(8)))
    out = np.empty((4, 4096, D), np.float32)
    for core in range(8):
        b, hh = core // 2, core % 2
        out[b, hh * TOK:(hh + 1) * TOK, :] = res.results[core]["yT"].T
    return out
